# version 10
# baseline (speedup 1.0000x reference)
import contextlib
import numpy as np
import concourse.bass as bass
import concourse.mybir as mybir
from concourse.bass_utils import run_bass_kernel_spmd

F32 = mybir.dt.float32
BF16 = mybir.dt.bfloat16
AF = mybir.ActivationFunctionType
ALU = mybir.AluOpType
AX = mybir.AxisListType

D = 1024
T = 4096
NS = D // 128
NE = 32
EH = 512
FFN = 4096
NCORES = 8
EPS = 1e-5
TOPK = 4
CAP_DROP = 70.0
SCALE_A = 1.0 / 32.0

ENGS = ("sync", "vector", "scalar", "tensor", "gpsimd")
ALL_PHASES = ("prep", "a1", "a2", "a3", "moe1", "attnb", "moe2", "final")


class Prog:
    def __init__(self, nc):
        self.nc = nc
        self.plan = {e: [] for e in ENGS}
        self.sem_handles = {}
        self.sem_counts = {}

    def register_sem(self, name, handle):
        self.sem_handles[name] = handle
        self.sem_counts[name] = 0

    def bump(self, name, k):
        self.sem_counts[name] += k
        return self.sem_counts[name]

    def count(self, name):
        return self.sem_counts[name]

    def emit(self, eng, fn):
        self.plan[eng].append(fn)

    def dma(self, eng, out, in_, sem):
        h = self.sem_handles[sem]
        tgt = self.bump(sem, 16)

        def f(e, out=out, in_=in_, h=h):
            e.dma_start(out=out, in_=in_).then_inc(h, 16)
        self.emit(eng, f)
        return tgt

    def wait(self, eng, sem, val=None):
        h = self.sem_handles[sem]
        v = self.count(sem) if val is None else val
        if v <= 0:
            return

        def f(e, h=h, v=v):
            e.wait_ge(h, v)
        self.emit(eng, f)

    def op(self, eng, fn, sem=None, k=1):
        if sem is None:
            self.emit(eng, fn)
            return None
        h = self.sem_handles[sem]
        tgt = self.bump(sem, k)

        def f(e, fn=fn, h=h, k=k):
            fn(e).then_inc(h, k)
        self.emit(eng, f)
        return tgt

    def barrier(self):
        h = self.sem_handles["bar"]
        base = self.count("bar")
        for e in ENGS:
            self.bump("bar", 1)
        tgt = base + len(ENGS)
        for e in ENGS:
            def f(eng, h=h, tgt=tgt):
                eng.nop().then_inc(h, 1)
                eng.wait_ge(h, tgt)
            self.emit(e, f)


class Ctx:
    pass


def carve(arena_ap, byte_off, rows, cols, dtype):
    nb = mybir.dt.size(dtype)
    assert byte_off % 2 == 0
    start = byte_off // 2
    width = cols * nb // 2
    sl = arena_ap[0:rows, start:start + width]
    if dtype == BF16:
        return sl
    return sl.bitcast(dtype)


def sl3(ap, s=NS):
    return ap.rearrange("p (s c) -> p s c", s=s)


def dview(dram_ap):
    return dram_ap.rearrange("(s p) t -> p s t", p=128)



def emit_prep(c):
    P, t = c.P, c.t
    c.prep_w_done = {}

    def cast_pm(dst_pm, src, s_count, width):
        P.dma("gpsimd", dst_pm.rearrange("p (s h) -> p s h", s=s_count),
              src.rearrange("(s p) h -> p s h", p=128), "gpc")

    cast_pm(t["wqpm"], t["wq_in"], NS, D)
    cast_pm(t["wkapm"], t["wk_a_in"], NS, D)
    cast_pm(t["wvapm"], t["wv_a_in"], NS, D)
    c.prep_w_done["qkv"] = P.count("gpc")

    stg16 = [carve(c.AR1, 0, 128, 2048, BF16), carve(c.AR1, 4096, 128, 2048, BF16)]
    tstg = [carve(c.AR1, 8192, 128, 2048, BF16), carve(c.AR1, 12288, 128, 2048, BF16)]
    ps_bf2 = [c.psF[:, 0:1024].bitcast(BF16), c.psF[:, 1024:2048].bitcast(BF16)]
    tposes = [
        ("wkb_in", "wkT_bf", 1024, 1024), ("wvb_in", "wvT_bf", 1024, 1024),
        ("wrb_in", "wrT_bf", 1024, 1024), ("wob_in", "woT_bf", 1024, 1024),
        ("wrec_in", "wrecT_bf", 1024, 1024),
        ("wkey_in", "wkeyT_bf", 4096, 1024), ("wval_in", "wvalT_bf", 1024, 4096),
    ]
    ti = 0
    for nm_in, nm_out, R, Cc in tposes:
        srcw, dst = t[nm_in], t[nm_out]
        for r0 in range(0, R, 128):
            for c0 in range(0, Cc, 2048):
                cw = min(2048, Cc - c0)
                b = ti % 2
                ps_bf = ps_bf2[b]
                P.wait("gpsimd", "mm", P.count("mm"))
                ldn = P.dma("gpsimd", stg16[b][:, 0:cw],
                            srcw[r0:r0 + 128, c0:c0 + cw], "gpc")
                P.wait("tensor", "gpc", ldn)
                P.wait("tensor", "st_a", P.count("st_a"))
                P.wait("tensor", "scl", P.count("scl"))
                nblk = cw // 128
                for jb in range(nblk):
                    P.op("tensor",
                         lambda te, o=ps_bf[:, jb * 128:(jb + 1) * 128],
                         i_=stg16[b][:, jb * 128:(jb + 1) * 128],
                         idn=c.s_ident_bf:
                         te.transpose(o, i_, idn))
                P.op("tensor",
                     lambda te, o=c.psF[:, 2048:2112].bitcast(BF16),
                     i_=c.s_ident_bf, idn=c.s_ident_bf:
                     te.transpose(o, i_, idn), sem="mm")
                P.wait("scalar", "mm", P.count("mm"))
                P.op("scalar",
                     lambda sc, o=tstg[b][:, 0:cw], i_=ps_bf[:, 0:cw]:
                     sc.copy(o, i_), sem="scl")
                P.wait("sync", "scl", P.count("scl"))
                for jb in range(nblk):
                    P.dma("sync",
                          dst[c0 + jb * 128:c0 + (jb + 1) * 128, r0:r0 + 128],
                          tstg[b][:, jb * 128:(jb + 1) * 128], "st_a")
                ti += 1
    c.prep_w_done["attnb"] = P.count("st_a")

    P.wait("gpsimd", "st_a", P.count("st_a"))
    for h in range(2):
        cast_pm(t["ws1pm"][h], t["ws1_in"][:, h * EH:(h + 1) * EH], NS, EH)
        cast_pm(t["ws3pm"][h], t["ws3_in"][:, h * EH:(h + 1) * EH], NS, EH)
        cast_pm(t["ws2pm"][h], t["ws2_in"][h * EH:(h + 1) * EH, :], 4, D)
        c.prep_w_done[NE + h] = P.count("gpc")
    for e in range(NE):
        cast_pm(t["w1pm"][e], t["w1_in"][e], NS, EH)
        cast_pm(t["w3pm"][e], t["w3_in"][e], NS, EH)
        cast_pm(t["w2pm"][e], t["w2_in"][e], 4, D)
        c.prep_w_done[e] = P.count("gpc")
    c.prep_w_done["all"] = P.count("gpc")
    P.wait("sync", "st_a", P.count("st_a"))
    P.barrier()



def emit_a1(c):
    P, t = c.P, c.t
    xst = [carve(c.AR1, 0, 128, 1024, F32), carve(c.AR1, 4096, 128, 1024, F32)]
    tmp1 = carve(c.AR1, 8192, 128, 1024, F32)
    lnst = [carve(c.AR1, 12288, 128, 1024, BF16), carve(c.AR1, 14336, 128, 1024, BF16)]
    xTst = [carve(c.AR1, 16384, 128, 1024, F32), carve(c.AR1, 20480, 128, 1024, F32)]
    nmu = carve(c.AR1, 24576, 128, 1, F32)
    vsum = carve(c.AR1, 24608, 128, 1, F32)
    rstd = carve(c.AR1, 24640, 128, 1, F32)
    junk = carve(c.AR1, 28672, 128, 1024, F32)
    ps32 = c.psF[:, 0:1024]
    psbf = c.psF[:, 1024:1536].bitcast(BF16)
    xT_r = dview(t["xT"])

    for ti in range(T // 128):
        b = ti % 2
        if ti >= 2:
            P.wait("sync", "mm", P.count("mm"))
            P.wait("sync", "vec", P.count("vec"))
        ld_n = P.dma("sync", xst[b], t["x_in"][ti * 128:(ti + 1) * 128, :], "ld_a")

        P.wait("tensor", "ld_a", ld_n)
        if ti >= 2:
            P.wait("tensor", "st_a", P.count("st_a"))
        for s in range(NS):
            P.op("tensor",
                 lambda te, o=ps32[:, s * 128:(s + 1) * 128],
                 i_=xst[b][:, s * 128:(s + 1) * 128], idn=c.s_ident_f:
                 te.transpose(o, i_, idn),
                 sem="mm" if s == NS - 1 else None)
        P.wait("scalar", "mm", P.count("mm"))
        evn = P.op("scalar", lambda sc, o=xTst[b], i_=ps32: sc.copy(o, i_),
                   sem="scl")
        P.wait("sync", "scl", evn)
        P.dma("sync", xT_r[:, :, ti * 128:(ti + 1) * 128], sl3(xTst[b]), "st_a")

        P.wait("vector", "ld_a", ld_n)
        P.op("vector", lambda v, o=vsum, i_=xst[b]:
             v.reduce_sum(o, i_, axis=AX.X), sem="wbv")
        P.wait("vector", "wbv")
        P.op("vector", lambda v, o=nmu, i_=vsum:
             v.tensor_scalar(o, i_, -1.0 / D, None, ALU.mult), sem="wbv")
        P.wait("vector", "wbv")
        P.op("vector", lambda v, o=tmp1, i_=xst[b], m=nmu:
             v.tensor_scalar(o, i_, m, None, ALU.add))
        P.op("vector", lambda v, o=junk, i_=tmp1: v.tensor_mul(o, i_, i_))
        P.op("vector", lambda v, o=vsum, i_=junk:
             v.reduce_sum(o, i_, axis=AX.X), sem="vec")
        P.wait("scalar", "vec", P.count("vec"))
        P.op("scalar", lambda sc, o=rstd, i_=vsum, e=c.s_eps:
             sc.activation(o, i_, AF.Sqrt, bias=e, scale=1.0 / D), sem="scl")
        P.wait("vector", "scl", P.count("scl"))
        P.op("vector", lambda v, o=nmu, i_=rstd: v.reciprocal(o, i_), sem="wbv")
        P.wait("vector", "wbv")
        P.op("vector", lambda v, o=lnst[b], i_=tmp1, m=nmu:
             v.tensor_scalar(o, i_, m, None, ALU.mult), sem="vec")
        P.wait("tensor", "vec", P.count("vec"))
        for s in range(NS):
            P.op("tensor",
                 lambda te, o=psbf[:, s * 128:(s + 1) * 128],
                 i_=lnst[b][:, s * 128:(s + 1) * 128], idn=c.s_ident_bf:
                 te.transpose(o, i_, idn),
                 sem="mm" if s == NS - 1 else None)
        P.wait("vector", "mm", P.count("mm"))
        P.op("vector",
             lambda v, o=sl3(c.LN0T, NS)[:, :, ti * 128:(ti + 1) * 128].
             rearrange("p s c -> p (s c)") if False else
             sl3(c.LN0T)[:, :, ti * 128:(ti + 1) * 128],
             i_=sl3(psbf, NS): v.tensor_copy(o, i_), sem="vec")
    P.barrier()



def emit_a2(c):
    P, t = c.P, c.t
    wsl = carve(c.AR4, 0, 128, NS * 1024, BF16)
    qstg = [carve(c.AR1, 0, 128, 512, BF16), carve(c.AR1, 1024, 128, 512, BF16)]
    ps = [c.psF[:, 0:512], c.psF[:, 512:1024]]
    qT = t["qT"]

    P.wait("sync", "gpc", c.prep_w_done["qkv"])

    def load_w(nm):
        P.dma("sync", wsl, t[nm], "ld_a")
        return P.count("ld_a")

    for which, nm in (("k", "wkapm"), ("q", "wqpm")):
        P.wait("sync", "mm", P.count("mm"))
        wn = load_w(nm)
        P.wait("tensor", "ld_a", wn)
        pp = 0
        for tc in range(T // 512):
            for oc in range(NS):
                b = pp % 2
                P.wait("tensor", "ev", P.count("ev"))
                P.wait("tensor", "st_a", P.count("st_a"))
                for s in range(NS):
                    P.op("tensor",
                         lambda te, o=ps[b],
                         l=wsl[:, s * 1024 + oc * 128:s * 1024 + oc * 128 + 128],
                         r=c.LN0T[:, s * T + tc * 512:s * T + tc * 512 + 512],
                         st=(s == 0), sp=(s == NS - 1):
                         te.matmul(o, l, r, start=st, stop=sp,
                                   skip_group_check=True),
                         sem="mm" if s == NS - 1 else None)
                mmn = P.count("mm")
                if which == "k":
                    P.wait("vector", "mm", mmn)
                    P.op("vector",
                         lambda v, o=c.KT[:, oc * T + tc * 512:oc * T + tc * 512 + 512],
                         i_=ps[b]: v.tensor_copy(o, i_), sem="ev")
                else:
                    P.wait("scalar", "mm", mmn)
                    evn = P.op("scalar",
                               lambda sc, o=qstg[b], i_=ps[b]: sc.copy(o, i_),
                               sem="ev")
                    P.wait("sync", "ev", evn)
                    P.dma("sync", qT[oc * 128:(oc + 1) * 128, tc * 512:tc * 512 + 512],
                          qstg[b], "st_a")
                pp += 1

    P.wait("sync", "mm", P.count("mm"))
    P.wait("sync", "st_a", P.count("st_a"))
    wn = load_w("wvapm")
    P.wait("tensor", "ld_a", wn)
    pp = 0
    for ti in range(T // 128):
        for dc in range(2):
            b = pp % 2
            P.wait("tensor", "ev", P.count("ev"))
            for s in range(NS):
                P.op("tensor",
                     lambda te, o=ps[b],
                     l=c.LN0T[:, s * T + ti * 128:s * T + ti * 128 + 128],
                     r=wsl[:, s * 1024 + dc * 512:s * 1024 + dc * 512 + 512],
                     st=(s == 0), sp=(s == NS - 1):
                     te.matmul(o, l, r, start=st, stop=sp,
                               skip_group_check=True),
                     sem="mm" if s == NS - 1 else None)
            P.wait("vector", "mm", P.count("mm"))
            P.op("vector",
                 lambda v, o=c.V[:, ti * 1024 + dc * 512:ti * 1024 + dc * 512 + 512],
                 i_=ps[b]: v.tensor_copy(o, i_), sem="ev")
            pp += 1
    P.barrier()



def emit_a3(c):
    P, t = c.P, c.t
    qtile = [carve(c.AR2, 0, 128, 1024, BF16), carve(c.AR2, 2048, 128, 1024, BF16)]
    ptil = carve(c.AR2, 4096, 128, 512, BF16)
    ptev_big = carve(c.AR2, 28672, 128, 4096, BF16)
    mrow = carve(c.AR2, 6144, 128, 1, F32)
    lrow = carve(c.AR2, 6208, 128, 1, F32)
    tsum = carve(c.AR2, 6272, 128, 1, F32)
    tmax = carve(c.AR2, 6336, 128, 1, F32)
    alpha = carve(c.AR2, 6400, 128, 1, F32)
    nbias = carve(c.AR2, 6464, 128, 1, F32)
    junk = carve(c.AR2, 8192, 128, 512, F32)
    otst = [carve(c.AR2, 12288, 128, 1024, F32), carve(c.AR2, 16384, 128, 1024, F32)]
    xtst = [carve(c.AR2, 20480, 128, 1024, F32), carve(c.AR2, 24576, 128, 1024, F32)]
    ps_s = c.psF[:, 2048:2560]
    ps_pt = c.psF[:, 2560:2816].bitcast(BF16)
    ps_ot = c.psF[:, 0:1024]
    qT_r = dview(t["qT"])
    xT_r = dview(t["xT"])

    for qi in range(T // 128):
        bq = qi % 2
        if qi >= 2:
            P.wait("sync", "mm", P.count("mm"))
        qld = P.dma("sync", sl3(qtile[bq]), qT_r[:, :, qi * 128:(qi + 1) * 128],
                    "ld_a")
        P.op("vector", lambda v, o=mrow: v.memset(o, -1e30))
        P.op("vector", lambda v, o=lrow: v.memset(o, 0.0), sem="wbv")
        P.wait("vector", "wbv")
        P.wait("tensor", "ld_a", qld)

        ngrp = (qi * 128 + 128 + 511) // 512
        for g in range(ngrp):
            k0 = g * 512
            kw = min(512, qi * 128 + 128 - k0)
            P.wait("tensor", "vec", P.count("vec"))
            P.wait("tensor", "scl", P.count("scl"))
            for s in range(NS):
                P.op("tensor",
                     lambda te, o=ps_s[:, 0:kw], l=qtile[bq][:, s * 128:(s + 1) * 128],
                     r=c.KT[:, s * T + k0:s * T + k0 + kw],
                     st=(s == 0), sp=(s == NS - 1):
                     te.matmul(o, l, r, start=st, stop=sp, skip_group_check=True),
                     sem="mm" if s == NS - 1 else None)
            P.wait("vector", "mm", P.count("mm"))
            if kw > (qi * 128 - k0):
                P.op("vector", lambda v, o=ps_s[:, kw - 128:kw],
                     i_=ps_s[:, kw - 128:kw], m=c.s_causal:
                     v.tensor_tensor(o, i_, m, ALU.add))
            P.op("vector", lambda v, o=tmax, i_=ps_s[:, 0:kw]:
                 v.reduce_max(o, i_, axis=AX.X), sem="wbv")
            P.wait("vector", "wbv")
            P.op("vector", lambda v, o=alpha, i_=mrow, m=tmax:
                 v.tensor_scalar(o, i_, m, None, ALU.subtract))
            P.op("vector", lambda v, o=mrow, i_=mrow, m=tmax:
                 v.tensor_scalar(o, i_, m, None, ALU.max), sem="wbv")
            P.wait("vector", "wbv")
            P.op("vector", lambda v, o=alpha, i_=alpha:
                 v.tensor_scalar(o, i_, 0.0, None, ALU.min))
            P.op("vector", lambda v, o=nbias, i_=mrow:
                 v.tensor_scalar(o, i_, -SCALE_A, None, ALU.mult), sem="vec")
            P.wait("scalar", "vec", P.count("vec"))
            P.op("scalar", lambda sc, o=alpha, i_=alpha:
                 sc.activation(o, i_, AF.Exp, scale=SCALE_A), sem="scl")
            P.op("scalar", lambda sc, o=junk[:, 0:kw], i_=ps_s[:, 0:kw],
                 b_=nbias, m=tsum:
                 sc.activation(o, i_, AF.Exp, bias=b_, scale=SCALE_A,
                               accum_out=m), sem="scl")
            P.wait("vector", "scl", P.count("scl"))
            P.op("vector", lambda v, o=lrow, i_=lrow, a=alpha:
                 v.tensor_scalar(o, i_, a, None, ALU.mult), sem="wbv")
            P.wait("vector", "wbv")
            P.op("vector", lambda v, o=lrow, i_=lrow, b_=tsum:
                 v.tensor_scalar(o, i_, b_, None, ALU.add), sem="vec")

        P.wait("scalar", "vec", P.count("vec"))
        P.op("scalar", lambda sc, o=tsum, i_=lrow:
             sc.activation(o, i_, AF.Ln), sem="scl")
        P.wait("vector", "scl", P.count("scl"))
        P.op("vector", lambda v, o=nbias, i_=nbias, m=tsum:
             v.tensor_scalar(o, i_, m, None, ALU.subtract), sem="vec")

        for g in range(ngrp):
            k0 = g * 512
            kw = min(512, qi * 128 + 128 - k0)
            P.wait("tensor", "vec", P.count("vec"))
            P.wait("tensor", "scl", P.count("scl"))
            for s in range(NS):
                P.op("tensor",
                     lambda te, o=ps_s[:, 0:kw], l=qtile[bq][:, s * 128:(s + 1) * 128],
                     r=c.KT[:, s * T + k0:s * T + k0 + kw],
                     st=(s == 0), sp=(s == NS - 1):
                     te.matmul(o, l, r, start=st, stop=sp, skip_group_check=True),
                     sem="mm" if s == NS - 1 else None)
            if kw > (qi * 128 - k0):
                P.wait("vector", "mm", P.count("mm"))
                P.op("vector", lambda v, o=ps_s[:, kw - 128:kw],
                     i_=ps_s[:, kw - 128:kw], m=c.s_causal:
                     v.tensor_tensor(o, i_, m, ALU.add), sem="vec")
                P.wait("scalar", "vec", P.count("vec"))
            else:
                P.wait("scalar", "mm", P.count("mm"))
            P.op("scalar", lambda sc, o=ptil[:, 0:kw], i_=ps_s[:, 0:kw], b_=nbias:
                 sc.activation(o, i_, AF.Exp, bias=b_, scale=SCALE_A), sem="scl")
            P.wait("tensor", "scl", P.count("scl"))
            nkt = (kw + 127) // 128
            for j in range(nkt):
                P.op("tensor",
                     lambda te, o=ps_pt[:, j * 128:(j + 1) * 128],
                     i_=ptil[:, j * 128:(j + 1) * 128], idn=c.s_ident_bf:
                     te.transpose(o, i_, idn),
                     sem="mm" if j == nkt - 1 else None)
            P.wait("vector", "mm", P.count("mm"))
            P.op("vector", lambda v, o=ptev_big[:, k0:k0 + kw], i_=ps_pt[:, 0:kw]:
                 v.tensor_copy(o, i_), sem="vec")
        P.wait("tensor", "vec", P.count("vec"))
        nkt_all = qi + 1
        for dc in range(NS):
            for kt in range(nkt_all):
                P.op("tensor",
                     lambda te, o=ps_ot[:, dc * 128:(dc + 1) * 128],
                     l=c.V[:, kt * 1024 + dc * 128:kt * 1024 + dc * 128 + 128],
                     r=ptev_big[:, kt * 128:(kt + 1) * 128],
                     st=(kt == 0), sp=(kt == nkt_all - 1):
                     te.matmul(o, l, r, start=st, stop=sp,
                               skip_group_check=True),
                     sem="mm" if (dc == NS - 1 and kt == nkt_all - 1) else None)

        bo = qi % 2
        P.wait("sync", "vec", P.count("vec"))
        xld = P.dma("sync", sl3(xtst[bo]), xT_r[:, :, qi * 128:(qi + 1) * 128],
                    "ld_b")
        P.wait("vector", "mm", P.count("mm"))
        P.wait("vector", "ld_b", xld)
        P.wait("vector", "st_a", P.count("st_a"))
        P.op("vector", lambda v, o=otst[bo], i_=xtst[bo], p=ps_ot:
             v.tensor_tensor(o, i_, p, ALU.add), sem="vec")
        P.wait("sync", "vec", P.count("vec"))
        P.dma("sync", xT_r[:, :, qi * 128:(qi + 1) * 128], sl3(otst[bo]), "st_a")
    P.barrier()



def emit_moe(c):
    P, t = c.P, c.t
    GT = 1024
    CH = 256
    NG = T // GT
    NCH = GT // CH
    NPE = NE + 2
    NPAIR = NPE // 2

    ln1t = [carve(c.AR2, 0, 128, NS * GT, BF16),
            carve(c.AR2, 16384, 128, NS * GT, BF16)]
    mw = [carve(c.AR2, 32768, 128, 4 * CH, BF16),
          carve(c.AR2, 34816, 128, 4 * CH, BF16),
          carve(c.AR2, 36864, 128, 4 * CH, BF16),
          carve(c.AR2, 38912, 128, 4 * CH, BF16)]
    logit = carve(c.AR2, 40960, 128, NE, F32)
    lmax = carve(c.AR2, 41088, 128, 1, F32)
    lsum = carve(c.AR2, 41120, 128, 1, F32)
    rrow = carve(c.AR2, 41152, 128, 1, F32)
    srow = carve(c.AR2, 41184, 128, 1, F32)
    eq = carve(c.AR2, 41216, 128, NE, F32)
    idxm = carve(c.AR2, 41344, 128, NE, F32)
    wsel = carve(c.AR2, 41472, 128, NE, F32)
    stat1 = carve(c.AR2, 41600, 1, CH, F32)
    stat2 = carve(c.AR2, 42624, 1, CH, F32)
    wbsb = [carve(c.AR2, 43648, 128, CH, F32),
            carve(c.AR2, 44672, 128, CH, F32)]
    silu1 = [carve(c.AR2, 45696, 128, 4 * CH, BF16),
             carve(c.AR2, 47744, 128, 4 * CH, BF16)]
    h3w = [carve(c.AR2, 49792, 128, 4 * CH, BF16),
           carve(c.AR2, 51840, 128, 4 * CH, BF16)]
    yacc = carve(c.AR1, 0, 128, NS * GT, F32)
    wt32 = [carve(c.AR1, 32768, 32, GT, F32),
            carve(c.AR1, 36864, 32, GT, F32)]
    onehot_dyn = [carve(c.AR1, 40960, 32, 128, F32),
                  carve(c.AR1, 41472, 32, 128, F32)]
    xchk = [carve(c.AR1, 49152, 128, NS * CH, F32),
            carve(c.AR1, 57344, 128, NS * CH, F32)]
    lnf = carve(c.AR4, 0, 128, NS * CH, F32)
    xchk2 = carve(c.AR4, 8192, 128, NS * CH, F32)
    wv = []
    for b in range(2):
        off = b * 24576
        wv.append({
            "w1": carve(c.AR3, off, 128, NS * 512, BF16),
            "w3": carve(c.AR3, off + 8192, 128, NS * 512, BF16),
            "w2": carve(c.AR3, off + 16384, 128, 4 * 1024, BF16),
        })
    ps_y = c.psF[:, 0:2048]
    ps_h1 = c.psF[:, 2048:3072]
    ps_h3 = c.psF[:, 3072:4096]
    ps_st = c.psF[:, 2048:2560]
    ps_lg = c.psF[:, 3072:3104]
    ps_wt = c.psF[0:32, 3104:3232]
    xT_r = dview(t["xT"])


    def load_w13(pe, buf):
        P.wait("sync", "gpc", c.prep_w_done["all"])
        if pe < NE:
            P.dma("sync", wv[buf]["w1"], t["w1pm"][pe], "ld_a")
            P.dma("sync", wv[buf]["w3"], t["w3pm"][pe], "ld_a")
        else:
            h0 = pe - NE
            P.dma("sync", wv[buf]["w1"], t["ws1pm"][h0], "ld_a")
            P.dma("sync", wv[buf]["w3"], t["ws3pm"][h0], "ld_a")
        return P.count("ld_a")

    def load_w2(pe, buf):
        if pe < NE:
            P.dma("sync", wv[buf]["w2"], t["w2pm"][pe], "ld_a")
        else:
            P.dma("sync", wv[buf]["w2"], t["ws2pm"][pe - NE], "ld_a")
        return P.count("ld_a")



    ln1t_ready = {}
    c.prep_psum_free = 0

    def prep_gen(g, gb):
        t0g = g * GT
        xln_done = {}
        P.wait("sync", "pvec", P.count("pvec"))
        xln_done[0] = P.dma("sync", sl3(xchk[0]), xT_r[:, :, t0g:t0g + CH],
                            "ld_b")
        for ch in range(NCH):
            t0 = t0g + ch * CH
            if ch + 1 < NCH:
                P.wait("sync", "pvec", P.count("pvec"))
                xln_done[ch + 1] = P.dma(
                    "sync", sl3(xchk[(ch + 1) % 2]),
                    xT_r[:, :, t0 + CH:t0 + 2 * CH], "ld_b")
            xc = xchk[ch % 2]
            xln = xln_done[ch]
            P.wait("tensor", "ld_b", xln)
            P.wait("tensor", "pvec", P.count("pvec"))
            P.wait("tensor", "pscl", P.count("pscl"))
            P.wait("tensor", "scl", P.count("scl"))
            P.wait("tensor", "wbs", P.count("wbs"))
            for s in range(NS):
                P.op("tensor", lambda te, o=ps_st[0:1, 0:CH], l=c.s_ones,
                     r=xc[:, s * CH:(s + 1) * CH],
                     st=(s == 0), sp=(s == NS - 1):
                     te.matmul(o, l, r, start=st, stop=sp, skip_group_check=True),
                     sem="pmm" if s == NS - 1 else None)
            P.wait("vector", "pmm", P.count("pmm"))
            P.op("vector", lambda v, o=stat1, i_=ps_st[0:1, 0:CH]:
                 v.tensor_scalar(o, i_, -1.0 / D, None, ALU.mult), sem="pvec")
            for s in range(NS):
                P.op("vector", lambda v, o=lnf[:, s * CH:(s + 1) * CH],
                     i_=xc[:, s * CH:(s + 1) * CH]:
                     v.tensor_mul(o, i_, i_),
                     sem="pvec" if s == NS - 1 else None)
            c.prep_psum_free = P.count("pvec")
            yield
            P.wait("tensor", "pvec", P.count("pvec"))
            P.wait("tensor", "scl", P.count("scl"))
            P.wait("tensor", "wbs", P.count("wbs"))
            for s in range(NS):
                P.op("tensor", lambda te, o=ps_st[0:1, CH:2 * CH], l=c.s_ones,
                     r=lnf[:, s * CH:(s + 1) * CH],
                     st=(s == 0), sp=(s == NS - 1):
                     te.matmul(o, l, r, start=st, stop=sp, skip_group_check=True),
                     sem="pmm" if s == NS - 1 else None)
            P.wait("vector", "pmm", P.count("pmm"))
            P.op("vector", lambda v, o=stat2, i_=stat1: v.tensor_mul(o, i_, i_),
                 sem="pvec")
            P.wait("vector", "pvec")
            P.op("vector", lambda v, o=stat2, i_=ps_st[0:1, CH:2 * CH], m=stat2:
                 v.scalar_tensor_tensor(o, i_, 1.0 / D, m, ALU.mult,
                                        ALU.subtract), sem="pvec")
            P.wait("scalar", "pvec", P.count("pvec"))
            P.op("scalar", lambda sc, o=stat2, i_=stat2, e=c.s_eps_row:
                 sc.activation(o, i_, AF.Sqrt, bias=e), sem="pscl")
            P.wait("vector", "pscl", P.count("pscl"))
            P.op("vector", lambda v, o=stat2, i_=stat2:
                 v.reciprocal(o, i_), sem="pvec")
            c.prep_psum_free = P.count("pvec")
            yield
            P.wait("tensor", "pvec", P.count("pvec"))
            P.wait("tensor", "scl", P.count("scl"))
            P.wait("tensor", "wbs", P.count("wbs"))
            P.op("tensor", lambda te, o=ps_st[:, 0:CH], l=c.s_ones_row,
                 r=stat1: te.matmul(o, l, r, start=True, stop=True,
                                    skip_group_check=True))
            P.op("tensor", lambda te, o=ps_st[:, CH:2 * CH], l=c.s_ones_row,
                 r=stat2: te.matmul(o, l, r, start=True, stop=True,
                                    skip_group_check=True), sem="pmm")
            P.wait("vector", "pmm", P.count("pmm"))
            for s in range(NS):
                P.op("vector", lambda v, o=lnf[:, s * CH:(s + 1) * CH],
                     i_=xc[:, s * CH:(s + 1) * CH], m=ps_st[:, 0:CH]:
                     v.tensor_tensor(o, i_, m, ALU.add))
            for s in range(NS):
                P.op("vector", lambda v, o=lnf[:, s * CH:(s + 1) * CH],
                     i_=lnf[:, s * CH:(s + 1) * CH], m=ps_st[:, CH:2 * CH]:
                     v.tensor_tensor(o, i_, m, ALU.mult),
                     sem="pvec" if s == NS - 1 else None)
            for s in range(NS):
                P.op("vector",
                     lambda v, o=sl3(ln1t[gb])[:, s, ch * CH:(ch + 1) * CH],
                     i_=lnf[:, s * CH:(s + 1) * CH]:
                     v.tensor_copy(o, i_), sem="pvec" if s == NS - 1 else None)
            c.prep_psum_free = P.count("pvec")
            yield
            for tt in range(CH // 128):
                P.wait("tensor", "pvec", P.count("pvec"))
                P.wait("tensor", "vec", P.count("vec"))
                for s in range(NS):
                    P.op("tensor", lambda te, o=ps_lg,
                         l=lnf[:, s * CH + tt * 128:s * CH + tt * 128 + 128],
                         r=c.WG[:, s * NE:(s + 1) * NE],
                         st=(s == 0), sp=(s == NS - 1):
                         te.matmul(o, l, r, start=st, stop=sp,
                                   skip_group_check=True),
                         sem="pmm" if s == NS - 1 else None)
                P.wait("vector", "pmm", P.count("pmm"))
                P.op("vector", lambda v, o=lmax, i_=ps_lg:
                     v.reduce_max(o, i_, axis=AX.X, negate=True))
                P.op("vector", lambda v, o=logit, i_=ps_lg:
                     v.tensor_copy(o, i_), sem="pvec")
                P.wait("scalar", "pvec", P.count("pvec"))
                P.op("scalar", lambda sc, o=logit, i_=logit, b_=lmax, m=lsum:
                     sc.activation(o, i_, AF.Exp, bias=b_, accum_out=m),
                     sem="pscl")
                P.wait("vector", "pscl", P.count("pscl"))
                P.op("vector", lambda v, o=wsel: v.memset(o, 0.0), sem="pvec")
                c.prep_psum_free = P.count("pvec")
                yield
                for r_ in range(TOPK):
                    P.wait("vector", "pvec")
                    P.op("vector", lambda v, o=lmax, i_=logit:
                         v.reduce_max(o, i_, axis=AX.X), sem="pvec")
                    P.wait("vector", "pvec")
                    P.op("vector", lambda v, o=eq, i_=logit, m=lmax:
                         v.tensor_scalar(o, i_, m, None, ALU.is_equal),
                         sem="pvec")
                    P.wait("vector", "pvec")
                    P.op("vector", lambda v, o=idxm, i_=eq, m=c.s_iotam64:
                         v.tensor_tensor(o, i_, m, ALU.mult), sem="pvec")
                    P.wait("vector", "pvec")
                    P.op("vector", lambda v, o=idxm, i_=idxm:
                         v.tensor_scalar(o, i_, 64.0, None, ALU.add),
                         sem="pvec")
                    P.wait("vector", "pvec")
                    P.op("vector", lambda v, o=srow, i_=idxm:
                         v.tensor_reduce(o, i_, axis=AX.X, op=ALU.min),
                         sem="pvec")
                    P.wait("vector", "pvec")
                    P.op("vector", lambda v, o=eq, i_=c.s_iota32, m=srow:
                         v.tensor_scalar(o, i_, m, None, ALU.is_equal),
                         sem="pvec")
                    P.wait("vector", "pvec")
                    P.op("vector", lambda v, o=wsel, i_=eq, m=lmax, a=wsel:
                         v.scalar_tensor_tensor(o, i_, m, a, ALU.mult, ALU.add))
                    P.op("vector", lambda v, o=logit, i_=eq, a=logit:
                         v.scalar_tensor_tensor(o, i_, -2.0, a, ALU.mult,
                                                ALU.add), sem="pvec")
                    c.prep_psum_free = P.count("pvec")
                    yield
                P.wait("vector", "pvec")
                P.op("vector", lambda v, o=rrow, i_=lsum: v.reciprocal(o, i_),
                     sem="pvec")
                P.wait("vector", "pvec")
                P.op("vector", lambda v, o=wsel, i_=wsel, m=rrow:
                     v.tensor_scalar(o, i_, m, None, ALU.mult), sem="pvec")
                c.prep_psum_free = P.count("pvec")
                yield
                P.wait("tensor", "pvec", P.count("pvec"))
                P.wait("tensor", "vec", P.count("vec"))
                P.op("tensor", lambda te, o=ps_wt, i_=wsel, idn=c.s_ident_f:
                     te.transpose(o, i_, idn), sem="pmm")
                P.wait("vector", "pmm", P.count("pmm"))
                P.op("vector",
                     lambda v, o=wt32[gb][0:32, ch * CH + tt * 128:
                                          ch * CH + tt * 128 + 128],
                     i_=ps_wt: v.tensor_copy(o, i_), sem="pvec")
                c.prep_psum_free = P.count("pvec")
                yield
        ln1t_ready[g] = (P.count("pvec"), P.count("pmm"))

    def emit_waves(g, gb, gen):
        t0g = g * GT
        P.op("vector", lambda v, o=yacc: v.memset(o, 0.0), sem="vec")
        P.wait("sync", "mm", P.count("mm"))
        w13_done = [load_w13(0, 0), load_w13(1, 1)]
        w2_done = [load_w2(0, 0), load_w2(1, 1)]
        pool_done = {}
        w2mm_done = {}

        def h_phase(pw, ch, ei):
            gci = pw * NCH + ch
            slot = gci % 2
            mwslot = slot * 2 + ei
            pe = 2 * pw + ei
            ch0 = ch * CH
            if ch == 0 and ei == 0 and pw == 0:
                P.wait("tensor", "pvec", ln1t_ready[g][0])
            if ch == 0 and ei == 0:
                P.wait("vector", "mm", P.count("mm"))
                for e2 in range(2):
                    if 2 * pw + e2 < NE:
                        P.op("vector", lambda v, o=onehot_dyn[e2],
                             i_=c.s_iotap, val=float(2 * pw + e2):
                             v.tensor_single_scalar(o, i_, val, ALU.is_equal),
                             sem="vec" if e2 == 1 else None)
                P.wait("tensor", "ld_a", max(w13_done))
                P.wait("tensor", "vec", P.count("vec"))
            P.wait("tensor", "scl", P.count("scl"))
            P.wait("tensor", "vec", P.count("vec"))
            P.wait("tensor", "pvec", c.prep_psum_free)
            if pe < NE:
                P.op("tensor", lambda te, o=ps_h3[:, 3 * CH:4 * CH],
                     l=onehot_dyn[ei],
                     r=wt32[gb][0:32, ch0:ch0 + CH]:
                     te.matmul(o, l, r, start=True, stop=True,
                               skip_group_check=True), sem="mm")
                P.wait("vector", "mm", P.count("mm"))
                P.op("vector", lambda v, o=wbsb[ei], i_=ps_h3[:, 3 * CH:4 * CH]:
                     v.tensor_copy(o, i_), sem="wbs")
            for hc in range(4):
                for s in range(NS):
                    P.op("tensor",
                         lambda te, o=ps_h1[:, hc * CH:(hc + 1) * CH],
                         l=wv[ei]["w1"][:, s * 512 + hc * 128:
                                        s * 512 + hc * 128 + 128],
                         r=ln1t[gb][:, s * GT + ch0:s * GT + ch0 + CH],
                         st=(s == 0), sp=(s == NS - 1):
                         te.matmul(o, l, r, start=st, stop=sp,
                                   skip_group_check=True),
                         sem="mm" if (hc == 3 and s == NS - 1) else None)
            mmh1 = P.count("mm")
            for hc in range(4):
                if hc == 3 and pe < NE:
                    P.wait("tensor", "wbs", P.count("wbs"))
                for s in range(NS):
                    P.op("tensor",
                         lambda te, o=ps_h3[:, hc * CH:(hc + 1) * CH],
                         l=wv[ei]["w3"][:, s * 512 + hc * 128:
                                        s * 512 + hc * 128 + 128],
                         r=ln1t[gb][:, s * GT + ch0:s * GT + ch0 + CH],
                         st=(s == 0), sp=(s == NS - 1):
                         te.matmul(o, l, r, start=st, stop=sp,
                                   skip_group_check=True),
                         sem="mm" if (hc == 3 and s == NS - 1) else None)
            mmn = P.count("mm")
            if ch == NCH - 1 and ei == 1 and 2 * pw + 2 < NPE:
                P.wait("sync", "mm", mmn)
                w13_done[:] = [load_w13(2 * pw + 2, 0),
                               load_w13(2 * pw + 3, 1)]
            P.wait("scalar", "mm", mmh1)
            P.wait("scalar", "pool", P.count("pool"))
            P.op("scalar", lambda sc, o=silu1[ei], i_=ps_h1:
                 sc.activation(o, i_, AF.Silu), sem="scl")
            P.wait("vector", "mm", mmn)
            P.wait("vector", "pool", P.count("pool"))
            if pe < NE:
                P.wait("vector", "wbs", P.count("wbs"))
                P.op("vector",
                     lambda v, o=sl3(h3w[ei], 4), i_=sl3(ps_h3, 4),
                     m=wbsb[ei].unsqueeze(1).broadcast_to((128, 4, CH)):
                     v.tensor_tensor(o, i_, m, ALU.mult), sem="vec")
            else:
                P.op("vector", lambda v, o=h3w[ei], i_=ps_h3:
                     v.tensor_copy(o, i_), sem="vec")
            P.wait("gpsimd", "scl", P.count("scl"))
            P.wait("gpsimd", "vec", P.count("vec"))
            P.op("gpsimd", lambda gp, o=mw[mwslot], i_=silu1[ei], m=h3w[ei]:
                 gp.tensor_tensor(o, i_, m, ALU.mult), sem="pool")
            if ei == 1:
                pool_done[gci] = P.count("pool")

        def w2_batch(pw, ch):
            gci = pw * NCH + ch
            slot = gci % 2
            ch0 = ch * CH
            P.wait("tensor", "pool", pool_done[gci])
            P.wait("tensor", "ld_a", max(w2_done))
            P.wait("tensor", "ev", P.count("ev"))
            for dc in range(NS):
                for ei in range(2):
                    for s in range(4):
                        P.op("tensor",
                             lambda te, o=ps_y[:, dc * CH:(dc + 1) * CH],
                             l=wv[ei]["w2"][:, s * 1024 + dc * 128:
                                            s * 1024 + dc * 128 + 128],
                             r=mw[slot * 2 + ei][:, s * CH:(s + 1) * CH],
                             st=(ei == 0 and s == 0),
                             sp=(ei == 1 and s == 3):
                             te.matmul(o, l, r, start=st, stop=sp,
                                       skip_group_check=True),
                             sem="mm" if (dc == NS - 1 and ei == 1
                                          and s == 3) else None)
            w2mm_done[pw] = P.count("mm")
            P.wait("vector", "mm", P.count("mm"))
            if pw + 1 < NPAIR:
                P.op("vector", lambda v, o=sl3(yacc)[:, :, ch0:ch0 + CH],
                     i_=sl3(yacc)[:, :, ch0:ch0 + CH], p=sl3(ps_y, NS):
                     v.tensor_tensor(o, i_, p, ALU.add), sem="ev")
            else:
                P.wait("sync", "vec", P.count("vec"))
                P.wait("sync", "st_a", P.count("st_a"))
                xl = P.dma("sync", sl3(xchk2),
                           xT_r[:, :, t0g + ch0:t0g + ch0 + CH], "ld_b")
                P.op("vector", lambda v, o=sl3(yacc)[:, :, ch0:ch0 + CH],
                     i_=sl3(yacc)[:, :, ch0:ch0 + CH], p=sl3(ps_y, NS):
                     v.tensor_tensor(o, i_, p, ALU.add), sem="ev")
                P.wait("vector", "ld_b", xl)
                P.wait("vector", "ev", P.count("ev"))
                P.op("vector", lambda v, o=sl3(xchk2), i_=sl3(xchk2),
                     p=sl3(yacc)[:, :, ch0:ch0 + CH]:
                     v.tensor_tensor(o, i_, p, ALU.add), sem="vec")
                P.wait("sync", "vec", P.count("vec"))
                P.dma("sync", xT_r[:, :, t0g + ch0:t0g + ch0 + CH],
                      sl3(xchk2), "st_a")

        def run_w2(task):
            tpw, tch = task
            w2_batch(tpw, tch)
            if tch == NCH - 1 and 2 * tpw + 2 < NPE:
                P.wait("sync", "mm", w2mm_done[tpw])
                w2_done[:] = [load_w2(2 * tpw + 2, 0), load_w2(2 * tpw + 3, 1)]

        pending = []
        for pw in range(NPAIR):
            for ch in range(NCH):
                h_phase(pw, ch, 0)
                if gen is not None:
                    next(gen, None)
                h_phase(pw, ch, 1)
                if gen is not None:
                    next(gen, None)
                pending.append((pw, ch))
                if len(pending) >= 2:
                    run_w2(pending.pop(0))
        while pending:
            run_w2(pending.pop(0))
        if gen is not None:
            for _ in gen:
                pass

    INTERLEAVE = True
    for g in range(NG):
        gb = g % 2
        if g == 0:
            for _ in prep_gen(0, 0):
                pass
        gen = prep_gen(g + 1, (g + 1) % 2) if g + 1 < NG else None
        if not INTERLEAVE and gen is not None:
            for _ in gen:
                pass
            gen = None
        emit_waves(g, gb, gen)
    P.wait("sync", "st_a", P.count("st_a"))
    P.barrier()



def emit_attnb(c):
    P, t = c.P, c.t
    xT_r = dview(t["xT"])
    bT_r = dview(t["bT"])
    hT_r = dview(t["hT"])
    CA = 512
    CB = 256
    NCA = T // CA
    NCB = T // CB

    xa = carve(c.AR1, 41760, 128, NS * CA, F32)
    hca = carve(c.AR2, 32768, 128, NS * CA, F32)
    sqf = hca
    st1a = carve(c.AR2, 49152, 1, CA, F32)
    st2a = carve(c.AR2, 51200, 1, CA, F32)
    ps_a = c.psF[:, 3072:4096]

    b1a_store = {}

    def emit_b1a(ch):
        t0 = ch * CA
        P.wait("sync", "pvec", P.count("pvec"))
        P.wait("sync", "st_a", P.count("st_a"))
        xl = P.dma("sync", sl3(xa), xT_r[:, :, t0:t0 + CA], "ld_b")
        for round_ in range(2):
            src = xa
            P.wait("tensor", "ld_b", xl)
            P.wait("tensor", "pvec", P.count("pvec"))
            P.wait("tensor", "pscl", P.count("pscl"))
            for s in range(NS):
                P.op("tensor", lambda te, o=ps_a[0:1, 0:CA], l=c.s_ones,
                     r=src[:, s * CA:(s + 1) * CA], st=(s == 0), sp=(s == NS - 1):
                     te.matmul(o, l, r, start=st, stop=sp, skip_group_check=True),
                     sem="pmm" if s == NS - 1 else None)
            P.wait("vector", "pmm", P.count("pmm"))
            P.op("vector", lambda v, o=st1a, i_=ps_a[0:1, 0:CA]:
                 v.tensor_scalar(o, i_, -1.0 / D, None, ALU.mult), sem="pvec")
            for s in range(NS):
                P.op("vector", lambda v, o=sqf[:, s * CA:(s + 1) * CA],
                     i_=src[:, s * CA:(s + 1) * CA]:
                     v.tensor_mul(o, i_, i_), sem="pvec" if s == NS - 1 else None)
            P.wait("tensor", "pvec", P.count("pvec"))
            for s in range(NS):
                P.op("tensor", lambda te, o=ps_a[0:1, CA:2 * CA], l=c.s_ones,
                     r=sqf[:, s * CA:(s + 1) * CA], st=(s == 0), sp=(s == NS - 1):
                     te.matmul(o, l, r, start=st, stop=sp, skip_group_check=True),
                     sem="pmm" if s == NS - 1 else None)
            P.wait("vector", "pmm", P.count("pmm"))
            P.op("vector", lambda v, o=st2a, i_=st1a: v.tensor_mul(o, i_, i_),
                 sem="pvec")
            P.wait("vector", "pvec")
            P.op("vector", lambda v, o=st2a, i_=ps_a[0:1, CA:2 * CA], m=st2a:
                 v.scalar_tensor_tensor(o, i_, 1.0 / D, m, ALU.mult,
                                        ALU.subtract), sem="pvec")
            P.wait("scalar", "pvec", P.count("pvec"))
            P.op("scalar", lambda sc, o=st2a, i_=st2a, e=c.s_eps_row:
                 sc.activation(o, i_, AF.Sqrt, bias=e), sem="pscl")
            P.wait("vector", "pscl", P.count("pscl"))
            P.op("vector", lambda v, o=st2a, i_=st2a:
                 v.reciprocal(o, i_), sem="pvec")
            P.wait("tensor", "pvec", P.count("pvec"))
            P.op("tensor", lambda te, o=ps_a[:, 0:CA], l=c.s_ones_row,
                 r=st1a: te.matmul(o, l, r, start=True, stop=True,
                                   skip_group_check=True))
            P.op("tensor", lambda te, o=ps_a[:, CA:2 * CA], l=c.s_ones_row,
                 r=st2a: te.matmul(o, l, r, start=True, stop=True,
                                   skip_group_check=True), sem="pmm")
            P.wait("vector", "pmm", P.count("pmm"))
            dst = xa if round_ == 0 else hca
            if round_ == 1:
                P.wait("vector", "st_a", P.count("st_a"))
            for s in range(NS):
                P.op("vector", lambda v, o=dst[:, s * CA:(s + 1) * CA],
                     i_=src[:, s * CA:(s + 1) * CA], m=ps_a[:, 0:CA]:
                     v.tensor_tensor(o, i_, m, ALU.add))
            for s in range(NS):
                P.op("vector", lambda v, o=dst[:, s * CA:(s + 1) * CA],
                     i_=dst[:, s * CA:(s + 1) * CA], m=ps_a[:, CA:2 * CA]:
                     v.tensor_tensor(o, i_, m, ALU.mult),
                     sem="pvec" if s == NS - 1 else None)
            if round_ == 1:
                for s in range(NS):
                    P.op("vector", lambda v, o=hca[:, s * CA:(s + 1) * CA],
                         i_=hca[:, s * CA:(s + 1) * CA],
                         w_=c.s_vec["ln1_w"][:, s:s + 1],
                         b_=c.s_vec["ln1_b"][:, s:s + 1]:
                         v.tensor_scalar(o, i_, w_, b_, ALU.mult, ALU.add),
                         sem="pvec" if s == NS - 1 else None)
        P.wait("sync", "pvec", P.count("pvec"))
        P.dma("sync", bT_r[:, :, t0:t0 + CA], sl3(xa), "st_a")
        P.dma("sync", hT_r[:, :, t0:t0 + CA], sl3(hca), "st_a")
        b1a_store[ch] = P.count("st_a")

    wk_sb = carve(c.AR3, 0, 128, NS * 1024, BF16)
    wvv_sb = carve(c.AR3, 16384, 128, NS * 1024, BF16)
    wr_sb = carve(c.AR3, 32768, 128, NS * 1024, BF16)
    wo_sb = carve(c.AR3, 49152, 128, NS * 1024, BF16)
    for nm, sb in (("wkT_bf", wk_sb), ("wvT_bf", wvv_sb), ("wrT_bf", wr_sb),
                   ("woT_bf", wo_sb)):
        P.dma("sync", sl3(sb), dview(t[nm]), "ld_a")
    wld = P.count("ld_a")

    hcc0 = carve(c.AR1, 0, 128, NS * 257, F32)
    hcc = [hcc0, hcc0]
    kf = [carve(c.AR1, 8224, 128, NS * CB, F32),
          carve(c.AR1, 16416, 128, NS * CB, F32)]
    vf = [carve(c.AR1, 24608, 128, NS * CB, F32),
          carve(c.AR1, 32800, 128, NS * CB, F32)]
    srf = [carve(c.AR2, 0, 128, NS * CB, F32),
           carve(c.AR2, 8192, 128, NS * CB, F32)]
    xkb = [carve(c.AR2, 16384, 128, NS * CB, BF16),
           carve(c.AR2, 20480, 128, NS * CB, BF16)]
    xvb = [carve(c.AR2, 24576, 128, NS * CB, BF16),
           carve(c.AR2, 53248, 128, NS * CB, BF16)]
    xrb = [carve(c.AR2, 28672, 128, NS * CB, BF16),
           carve(c.AR2, 57344, 128, NS * CB, BF16)]
    e0 = carve(c.AR2, 61440, 128, CB, F32)
    e1 = carve(c.AR2, 62464, 128, CB, F32)
    ffb = carve(c.AR2, 63488, 128, CB, F32)
    yA = carve(c.AR4, 0, 128, CB + 1, F32)
    yB = carve(c.AR4, 1056, 128, CB + 1, F32)
    den = carve(c.AR4, 2112, 128, CB, F32)
    rwk = [carve(c.AR4, 3136, 128, NS * CB, BF16),
           carve(c.AR4, 7232, 128, NS * CB, BF16)]
    e0b2 = [carve(c.AR4, 11328, 128, CB, F32),
            carve(c.AR4, 12352, 128, CB, F32)]
    e1b2 = [carve(c.AR4, 13376, 128, CB, F32),
            carve(c.AR4, 14400, 128, CB, F32)]
    btc = carve(c.AR1, 58144, 128, 4 * CB, F32)
    ps_m = c.psF[:, 0:2048]

    b1b_state = {"mix": {}, "k": {}, "v": {}, "r": {}, "wkv": {}}

    lamb = carve(c.AR2, 64512, 128, NS, F32)
    kru = carve(c.AR2, 64544, 128, NS, F32)
    Ah = carve(c.AR2, 64576, 128, NS, F32)
    Bh = carve(c.AR2, 64608, 128, NS, F32)
    Mh = carve(c.AR2, 64640, 128, NS, F32)
    mu8 = carve(c.AR2, 64672, 128, NS, F32)
    nmu8 = carve(c.AR2, 64704, 128, NS, F32)
    sc8 = carve(c.AR2, 64736, 128, NS, F32)
    tf8 = carve(c.AR2, 64768, 128, NS, F32)

    P.wait("scalar", "ld_a", wld)
    P.op("scalar", lambda sc, o=lamb, i_=c.s_vec["time_decay"]:
         sc.activation(o, i_, AF.Exp), sem="scl")
    P.wait("scalar", "scl", P.count("scl"))
    P.op("scalar", lambda sc, o=lamb, i_=lamb:
         sc.activation(o, i_, AF.Exp, scale=-1.0), sem="scl")
    P.op("vector", lambda v, o=kru, i_=c.s_vec["time_first"]:
         v.tensor_scalar(o, i_, 0.0, None, ALU.max))
    P.op("vector", lambda v, o=Ah: v.memset(o, 0.0))
    P.op("vector", lambda v, o=Bh: v.memset(o, 0.0))
    P.op("vector", lambda v, o=Mh: v.memset(o, -1e30), sem="vec")

    def b1b_mix(ch):
        b = ch % 2
        t0 = ch * CB
        P.wait("sync", "vec", b1b_state["mix"].get(ch - 1, 0))
        P.wait("sync", "st_a", b1a_store[(t0 + CB - 1) // CA])
        if ch == 0:
            P.op("vector", lambda v, o=hcc[0]: v.memset(o, 0.0), sem="vec")
            P.wait("sync", "vec", P.count("vec"))
            hl = P.dma("sync", sl3(hcc[0], NS)[:, :, 1:CB + 1],
                       hT_r[:, :, 0:CB], "ld_a")
        else:
            hl = P.dma("sync", sl3(hcc[b], NS), hT_r[:, :, t0 - 1:t0 + CB],
                       "ld_a")
        P.wait("vector", "ld_a", hl)
        P.wait("vector", "mm", b1b_state["r"].get(ch - 2, 0))
        W1 = CB + 1
        for nm, dst in (("att_tmk", xkb), ("att_tmv", xvb), ("att_tmr", xrb)):
            for s in range(NS):
                P.op("vector",
                     lambda v, o=ffb, h_=hcc[b][:, s * W1 + 1:s * W1 + W1],
                     hh=hcc[b][:, s * W1:s * W1 + CB]:
                     v.tensor_sub(o, h_, hh))
                P.op("vector",
                     lambda v, o=dst[b][:, s * CB:(s + 1) * CB], i_=ffb,
                     w_=c.s_vec[nm][:, s:s + 1],
                     hh=hcc[b][:, s * W1:s * W1 + CB]:
                     v.scalar_tensor_tensor(o, i_, w_, hh, ALU.mult, ALU.add),
                     sem="vec" if (nm == "att_tmr" and s == NS - 1) else None)
        b1b_state["mix"][ch] = P.count("vec")

    def b1b_mms(ch):
        b = ch % 2
        for which, src_b, wsb, dstf in (("k", xkb, wk_sb, kf),
                                        ("v", xvb, wvv_sb, vf),
                                        ("r", xrb, wr_sb, srf)):
            P.wait("tensor", "vec", b1b_state["mix"][ch])
            P.wait("tensor", "ev", P.count("ev"))
            P.wait("tensor", "vec", b1b_state["wkv"].get(ch - 2, 0))
            for oc in range(NS):
                for s in range(NS):
                    P.op("tensor",
                         lambda te, o=ps_m[:, oc * CB:oc * CB + CB],
                         l=wsb[:, s * 1024 + oc * 128:s * 1024 + oc * 128 + 128],
                         r=src_b[b][:, s * CB:(s + 1) * CB],
                         st=(s == 0), sp=(s == NS - 1):
                         te.matmul(o, l, r, start=st, stop=sp,
                                   skip_group_check=True),
                         sem="mm" if (oc == NS - 1 and s == NS - 1) else None)
            P.wait("scalar", "mm", P.count("mm"))
            if which == "r":
                P.op("scalar", lambda sc, o=srf[b], i_=ps_m:
                     sc.activation(o, i_, AF.Sigmoid), sem="ev")
            else:
                P.op("scalar", lambda sc, o=dstf[b], i_=ps_m:
                     sc.copy(o, i_), sem="ev")
            b1b_state[which][ch] = P.count("mm")
            P.wait("tensor", "ev", P.count("ev"))

    def b1b_wkv(ch):
        b = ch % 2
        P.wait("vector", "ev", P.count("ev"))
        for s in range(NS):
            P.op("vector", lambda v, o=mu8[:, s:s + 1],
                 i_=kf[b][:, s * CB:(s + 1) * CB]:
                 v.reduce_max(o, i_, axis=AX.X),
                 sem="wbv" if s == NS - 1 else None)
        P.wait("vector", "wbv")
        P.op("vector", lambda v, o=mu8, i_=mu8, m=kru:
             v.tensor_tensor(o, i_, m, ALU.add))
        P.op("vector", lambda v, o=sc8, i_=Mh:
             v.tensor_scalar(o, i_, -CAP_DROP, None, ALU.add), sem="wbv")
        P.wait("vector", "wbv")
        P.op("vector", lambda v, o=mu8, i_=mu8, m=sc8:
             v.tensor_tensor(o, i_, m, ALU.max), sem="wbv")
        P.wait("vector", "wbv")
        P.op("vector", lambda v, o=nmu8, i_=mu8:
             v.tensor_scalar(o, i_, -1.0, None, ALU.mult))
        P.op("vector", lambda v, o=sc8, i_=Mh, m=mu8:
             v.tensor_tensor(o, i_, m, ALU.subtract))
        P.op("vector", lambda v, o=tf8, i_=c.s_vec["time_first"][:, 0:NS],
             m=mu8: v.tensor_tensor(o, i_, m, ALU.subtract), sem="vec")
        P.wait("scalar", "vec", P.count("vec"))
        P.op("scalar", lambda sc, o=sc8, i_=sc8:
             sc.activation(o, i_, AF.Exp), sem="scl")
        P.wait("vector", "scl", P.count("scl"))
        P.op("vector", lambda v, o=Ah, i_=Ah, m=sc8:
             v.tensor_tensor(o, i_, m, ALU.mult))
        P.op("vector", lambda v, o=Bh, i_=Bh, m=sc8:
             v.tensor_tensor(o, i_, m, ALU.mult))
        P.op("vector", lambda v, o=Mh, i_=mu8:
             v.tensor_copy(o, i_), sem="wbv")
        P.wait("vector", "wbv")
        ev_slab = {}
        for s in range(NS):
            eb = s % 2
            kfs = kf[b][:, s * CB:(s + 1) * CB]
            vfs = vf[b][:, s * CB:(s + 1) * CB]
            if s >= 2:
                P.wait("scalar", "vec", ev_slab[s - 2])
            P.op("scalar", lambda sc, o=e0b2[eb], i_=kfs, b_=nmu8[:, s:s + 1]:
                 sc.activation(o, i_, AF.Exp, bias=b_))
            P.op("scalar", lambda sc, o=e1b2[eb], i_=kfs, b_=tf8[:, s:s + 1]:
                 sc.activation(o, i_, AF.Exp, bias=b_), sem="scl")
            P.wait("vector", "scl", P.count("scl"))
            P.op("vector", lambda v, o=ffb, i_=e0b2[eb], m=vfs:
                 v.tensor_tensor(o, i_, m, ALU.mult))
            P.op("vector", lambda v, o=yA[:, 0:1], i_=Ah[:, s:s + 1]:
                 v.tensor_copy(o, i_))
            P.op("vector", lambda v, o=yB[:, 0:1], i_=Bh[:, s:s + 1]:
                 v.tensor_copy(o, i_), sem="wbv")
            P.wait("vector", "wbv")
            P.op("vector", lambda v, o=yA[:, 1:CB + 1],
                 d0=lamb[:, s:s + 1].broadcast_to((128, CB)), d1=ffb,
                 init=Ah[:, s:s + 1]:
                 v.tensor_tensor_scan(o, d0, d1, init, ALU.mult, ALU.add))
            P.op("vector", lambda v, o=yB[:, 1:CB + 1],
                 d0=lamb[:, s:s + 1].broadcast_to((128, CB)), d1=e0b2[eb],
                 init=Bh[:, s:s + 1]:
                 v.tensor_tensor_scan(o, d0, d1, init, ALU.mult, ALU.add),
                 sem="vec")
            P.wait("vector", "vec")
            P.op("vector", lambda v, o=den, i_=yB[:, 0:CB], m=e1b2[eb]:
                 v.tensor_tensor(o, i_, m, ALU.add))
            P.op("vector", lambda v, o=den, i_=den: v.reciprocal(o, i_))
            P.op("vector", lambda v, o=e1b2[eb], i_=e1b2[eb], m=vfs:
                 v.tensor_tensor(o, i_, m, ALU.mult))
            P.op("vector", lambda v, o=e1b2[eb], i_=e1b2[eb], m=yA[:, 0:CB]:
                 v.tensor_tensor(o, i_, m, ALU.add), sem="wbv")
            P.wait("vector", "wbv")
            P.op("vector", lambda v, o=e1b2[eb], i_=e1b2[eb], m=den:
                 v.tensor_tensor(o, i_, m, ALU.mult))
            P.op("vector", lambda v, o=rwk[b][:, s * CB:(s + 1) * CB],
                 i_=e1b2[eb], m=srf[b][:, s * CB:(s + 1) * CB]:
                 v.tensor_tensor(o, i_, m, ALU.mult))
            P.op("vector", lambda v, o=Ah[:, s:s + 1], i_=yA[:, CB:CB + 1]:
                 v.tensor_copy(o, i_))
            P.op("vector", lambda v, o=Bh[:, s:s + 1], i_=yB[:, CB:CB + 1]:
                 v.tensor_copy(o, i_), sem="vec")
            ev_slab[s] = P.count("vec")
        b1b_state["wkv"][ch] = P.count("vec")

    def b1b_wo(ch):
        b = ch % 2
        t0 = ch * CB
        P.wait("tensor", "vec", b1b_state["wkv"][ch])
        P.wait("tensor", "ev", P.count("ev"))
        for oc in range(NS):
            for s in range(NS):
                P.op("tensor",
                     lambda te, o=ps_m[:, oc * CB:oc * CB + CB],
                     l=wo_sb[:, s * 1024 + oc * 128:s * 1024 + oc * 128 + 128],
                     r=rwk[b][:, s * CB:(s + 1) * CB],
                     st=(s == 0), sp=(s == NS - 1):
                     te.matmul(o, l, r, start=st, stop=sp,
                               skip_group_check=True),
                     sem="mm" if (oc == NS - 1 and s == NS - 1) else None)
        mmw = P.count("mm")
        for hh2 in range(2):
            P.wait("sync", "ev", P.count("ev"))
            bl = P.dma("sync", sl3(btc, 4),
                       bT_r[:, hh2 * 4:(hh2 + 1) * 4, t0:t0 + CB], "ld_b")
            P.wait("vector", "mm", mmw)
            P.wait("vector", "ld_b", bl)
            P.op("vector", lambda v, o=btc, i_=btc,
                 p=ps_m[:, hh2 * 1024:(hh2 + 1) * 1024]:
                 v.tensor_tensor(o, i_, p, ALU.add), sem="ev")
            P.wait("sync", "ev", P.count("ev"))
            P.dma("sync", bT_r[:, hh2 * 4:(hh2 + 1) * 4, t0:t0 + CB],
                  sl3(btc, 4), "st_a")

    emit_b1a(0)
    for ch in range(NCB):
        if ch % 2 == 0 and ch // 2 + 1 < NCA:
            emit_b1a(ch // 2 + 1)
        b1b_mix(ch)
        b1b_mms(ch)
        b1b_wkv(ch)
        b1b_wo(ch)
    P.barrier()


    CH2 = 256
    NCH2 = T // CH2
    wkey_sb = carve(c.AR3, 0, 128, NS * FFN, BF16)
    wval_sb = carve(c.AR1, 0, 128, 32 * 1024, BF16)
    wrec_sb = carve(c.AR4, 0, 128, NS * 1024, BF16)
    P.dma("sync", sl3(wkey_sb), dview(t["wkeyT_bf"]), "ld_a")
    wkey_n = P.count("ld_a")
    P.dma("sync", sl3(wval_sb, s=32),
          t["wvalT_bf"].rearrange("(s p) q -> p s q", p=128), "ld_a")
    wval_n = P.count("ld_a")
    P.dma("sync", sl3(wrec_sb), dview(t["wrecT_bf"]), "ld_a")
    wrec_n = P.count("ld_a")

    bcc = carve(c.AR2, 0, 128, NS * 257, F32)
    h2c = carve(c.AR2, 8224, 128, NS * 257, F32)
    sq2 = carve(c.AR2, 16448, 128, NS * CH2, F32)
    xk2 = carve(c.AR2, 24640, 128, NS * CH2, BF16)
    xr2 = carve(c.AR2, 28736, 128, NS * CH2, BF16)
    kkr = carve(c.AR2, 32832, 128, CH2, F32)
    kkr_b = carve(c.AR2, 58432, 128, CH2, F32)
    kk2 = carve(c.AR2, 33856, 128, 32 * CH2, BF16)
    rf = carve(c.AR2, 50240, 128, NS * CH2, F32)
    st1b = carve(c.AR2, 62528, 1, 257, F32)
    st2b = carve(c.AR2, 63680, 1, 257, F32)
    ps_kk = [c.psF[:, 0:256], c.psF[:, 512:768]]
    ps_kv = c.psF[:, 2048:4096]
    ps_rr = c.psF[:, 1024:2048]
    ps_stb = c.psF[:, 0:1024]

    for ch in range(NCH2):
        t0 = ch * CH2
        P.wait("sync", "vec", P.count("vec"))
        if ch == 0:
            P.op("vector", lambda v, o=bcc: v.memset(o, 0.0), sem="vec")
            P.wait("sync", "vec", P.count("vec"))
            bl = P.dma("sync", sl3(bcc, NS)[:, :, 1:257], bT_r[:, :, 0:CH2],
                       "ld_a")
        else:
            bl = P.dma("sync", sl3(bcc, NS), bT_r[:, :, t0 - 1:t0 + CH2], "ld_a")
        P.wait("tensor", "ld_a", bl)
        P.wait("tensor", "vec", P.count("vec"))
        P.wait("tensor", "scl", P.count("scl"))
        W = 257
        for s in range(NS):
            P.op("tensor", lambda te, o=ps_stb[0:1, 0:W], l=c.s_ones,
                 r=bcc[:, s * 257:(s + 1) * 257], st=(s == 0), sp=(s == NS - 1):
                 te.matmul(o, l, r, start=st, stop=sp, skip_group_check=True),
                 sem="mm" if s == NS - 1 else None)
        P.wait("vector", "mm", P.count("mm"))
        P.op("vector", lambda v, o=st1b, i_=ps_stb[0:1, 0:W]:
             v.tensor_scalar(o, i_, -1.0 / D, None, ALU.mult), sem="vec")
        for s in range(NS):
            P.op("vector", lambda v, o=h2c[:, s * 257:(s + 1) * 257],
                 i_=bcc[:, s * 257:(s + 1) * 257]:
                 v.tensor_mul(o, i_, i_), sem="vec" if s == NS - 1 else None)
        P.wait("tensor", "vec", P.count("vec"))
        for s in range(NS):
            P.op("tensor", lambda te, o=ps_stb[0:1, 512:512 + W], l=c.s_ones,
                 r=h2c[:, s * 257:(s + 1) * 257], st=(s == 0), sp=(s == NS - 1):
                 te.matmul(o, l, r, start=st, stop=sp, skip_group_check=True),
                 sem="mm" if s == NS - 1 else None)
        P.wait("vector", "mm", P.count("mm"))
        P.op("vector", lambda v, o=st2b, i_=st1b: v.tensor_mul(o, i_, i_),
             sem="wbv")
        P.wait("vector", "wbv")
        P.op("vector", lambda v, o=st2b, i_=ps_stb[0:1, 512:512 + W], m=st2b:
             v.scalar_tensor_tensor(o, i_, 1.0 / D, m, ALU.mult, ALU.subtract),
             sem="vec")
        P.wait("scalar", "vec", P.count("vec"))
        P.op("scalar", lambda sc, o=st2b, i_=st2b, e=c.s_eps_row:
             sc.activation(o, i_, AF.Sqrt, bias=e), sem="scl")
        P.wait("vector", "scl", P.count("scl"))
        P.op("vector", lambda v, o=st2b, i_=st2b: v.reciprocal(o, i_), sem="vec")
        P.wait("tensor", "vec", P.count("vec"))
        P.op("tensor", lambda te, o=ps_stb[:, 0:W].opt(), l=c.s_ones_row,
             r=st1b: te.matmul(o, l, r, start=True, stop=True,
                               skip_group_check=True))
        P.op("tensor", lambda te, o=ps_stb[:, 512:512 + W], l=c.s_ones_row,
             r=st2b: te.matmul(o, l, r, start=True, stop=True,
                               skip_group_check=True), sem="mm")
        P.wait("vector", "mm", P.count("mm"))
        for s in range(NS):
            P.op("vector", lambda v, o=h2c[:, s * 257:(s + 1) * 257],
                 i_=bcc[:, s * 257:(s + 1) * 257], m=ps_stb[:, 0:W]:
                 v.tensor_tensor(o, i_, m, ALU.add))
        for s in range(NS):
            P.op("vector", lambda v, o=h2c[:, s * 257:(s + 1) * 257],
                 i_=h2c[:, s * 257:(s + 1) * 257], m=ps_stb[:, 512:512 + W]:
                 v.tensor_tensor(o, i_, m, ALU.mult))
        for s in range(NS):
            P.op("vector", lambda v, o=h2c[:, s * 257:(s + 1) * 257],
                 i_=h2c[:, s * 257:(s + 1) * 257],
                 w_=c.s_vec["ln2_w"][:, s:s + 1], b_=c.s_vec["ln2_b"][:, s:s + 1]:
                 v.tensor_scalar(o, i_, w_, b_, ALU.mult, ALU.add),
                 sem="vec" if s == NS - 1 else None)
        if ch == 0:
            P.op("vector", lambda v, o=sl3(h2c, NS)[:, :, 0:1]:
                 v.memset(o, 0.0), sem="wbv")
            P.wait("vector", "wbv")
        for nm, dst in (("ffn_tmk", xk2), ("ffn_tmr", xr2)):
            for s in range(NS):
                P.op("vector",
                     lambda v, o=kkr, h_=h2c[:, s * 257 + 1:s * 257 + 257],
                     hh=h2c[:, s * 257:s * 257 + 256]:
                     v.tensor_sub(o, h_, hh))
                P.op("vector",
                     lambda v, o=dst[:, s * CH2:(s + 1) * CH2], i_=kkr,
                     w_=c.s_vec[nm][:, s:s + 1],
                     hh=h2c[:, s * 257:s * 257 + 256]:
                     v.scalar_tensor_tensor(o, i_, w_, hh, ALU.mult, ALU.add),
                     sem="vec" if (nm == "ffn_tmr" and s == NS - 1) else None)
        mixn = P.count("vec")
        P.wait("tensor", "vec", mixn)
        if ch == 0:
            P.wait("tensor", "ld_a", wkey_n)
        ev_after = {}
        kkr2 = [kkr, kkr_b]
        for oc in range(32):
            pb = oc % 2
            if oc >= 2:
                P.wait("tensor", "ev", ev_after[oc - 2])
            for s in range(NS):
                P.op("tensor", lambda te, o=ps_kk[pb],
                     l=wkey_sb[:, s * FFN + oc * 128:s * FFN + oc * 128 + 128],
                     r=xk2[:, s * CH2:(s + 1) * CH2],
                     st=(s == 0), sp=(s == NS - 1):
                     te.matmul(o, l, r, start=st, stop=sp,
                               skip_group_check=True),
                     sem="mm" if s == NS - 1 else None)
            P.wait("scalar", "mm", P.count("mm"))
            if oc >= 2:
                P.wait("scalar", "ev", ev_after[oc - 2])
            P.op("scalar", lambda sc, o=kkr2[pb], i_=ps_kk[pb]:
                 sc.activation(o, i_, AF.Relu), sem="scl")
            P.wait("vector", "scl", P.count("scl"))
            P.op("vector", lambda v, o=kk2[:, oc * CH2:(oc + 1) * CH2],
                 i_=kkr2[pb], m=kkr2[pb]:
                 v.tensor_tensor(o, i_, m, ALU.mult), sem="ev")
            ev_after[oc] = P.count("ev")
        P.wait("tensor", "ev", P.count("ev"))
        if ch == 0:
            P.wait("tensor", "ld_a", wval_n)
        for oc in range(NS):
            for s in range(32):
                P.op("tensor", lambda te, o=ps_kv[:, oc * 256:oc * 256 + 256],
                     l=wval_sb[:, s * 1024 + oc * 128:s * 1024 + oc * 128 + 128],
                     r=kk2[:, s * CH2:(s + 1) * CH2],
                     st=(s == 0), sp=(s == 31):
                     te.matmul(o, l, r, start=st, stop=sp,
                               skip_group_check=True),
                     sem="mm" if (oc == NS - 1 and s == 31) else None)
        rr_slots = [c.psF[:, 1024:1280], c.psF[:, 1536:1792]]
        if ch == 0:
            P.wait("tensor", "ld_a", wrec_n)
        scl_after_r = {}
        for oc in range(NS):
            pb = oc % 2
            if oc >= 2:
                P.wait("tensor", "scl", scl_after_r[oc - 2])
            for s in range(NS):
                P.op("tensor", lambda te, o=rr_slots[pb],
                     l=wrec_sb[:, s * 1024 + oc * 128:s * 1024 + oc * 128 + 128],
                     r=xr2[:, s * CH2:(s + 1) * CH2],
                     st=(s == 0), sp=(s == NS - 1):
                     te.matmul(o, l, r, start=st, stop=sp,
                               skip_group_check=True),
                     sem="mm" if s == NS - 1 else None)
            P.wait("scalar", "mm", P.count("mm"))
            P.op("scalar", lambda sc, o=rf[:, oc * CH2:(oc + 1) * CH2],
                 i_=rr_slots[pb]:
                 sc.activation(o, i_, AF.Sigmoid), sem="scl")
            scl_after_r[oc] = P.count("scl")
        P.wait("tensor", "scl", P.count("scl"))
        P.wait("sync", "scl", P.count("scl"))
        P.wait("sync", "st_a", P.count("st_a"))
        xl = P.dma("sync", sl3(sq2), xT_r[:, :, t0:t0 + CH2], "ld_b")
        P.wait("vector", "mm", P.count("mm"))
        P.wait("vector", "scl", P.count("scl"))
        P.wait("vector", "ld_b", xl)
        P.op("vector", lambda v, o=rf, i_=rf, p=ps_kv:
             v.tensor_tensor(o, i_, p, ALU.mult), sem="ev")
        P.op("vector", lambda v, o=sq2, i_=sq2,
             p=sl3(bcc, NS)[:, :, 1:257].rearrange("p s c -> p (s c)")
             if False else sq2: v.tensor_copy(o, i_)) if False else None
        for s in range(NS):
            P.op("vector", lambda v, o=sq2[:, s * CH2:(s + 1) * CH2],
                 i_=sq2[:, s * CH2:(s + 1) * CH2],
                 m=bcc[:, s * 257 + 1:s * 257 + 257]:
                 v.tensor_tensor(o, i_, m, ALU.add))
        P.op("vector", lambda v, o=sq2, i_=sq2, m=rf:
             v.tensor_tensor(o, i_, m, ALU.add), sem="vec")
        P.wait("sync", "vec", P.count("vec"))
        P.dma("sync", xT_r[:, :, t0:t0 + CH2], sl3(sq2), "st_a")
        P.wait("tensor", "vec", P.count("vec"))
    P.barrier()



def emit_final(c):
    P, t = c.P, c.t
    xT_r = dview(t["xT"])
    xst = [carve(c.AR1, 0, 128, 1024, F32), carve(c.AR1, 4096, 128, 1024, F32)]
    ost = [carve(c.AR1, 8192, 128, 1024, F32), carve(c.AR1, 12288, 128, 1024, F32)]
    ps32 = c.psF[:, 0:1024]
    for ti in range(T // 128):
        b = ti % 2
        if ti >= 2:
            P.wait("sync", "mm", P.count("mm"))
        xl = P.dma("sync", sl3(xst[b]), xT_r[:, :, ti * 128:(ti + 1) * 128], "ld_a")
        P.wait("tensor", "ld_a", xl)
        P.wait("tensor", "scl", P.count("scl"))
        for s in range(NS):
            P.op("tensor", lambda te, o=ps32[:, s * 128:(s + 1) * 128],
                 i_=xst[b][:, s * 128:(s + 1) * 128], idn=c.s_ident_f:
                 te.transpose(o, i_, idn), sem="mm" if s == NS - 1 else None)
        P.wait("scalar", "mm", P.count("mm"))
        P.wait("scalar", "st_a", P.count("st_a"))
        evn = P.op("scalar", lambda sc, o=ost[b], i_=ps32: sc.copy(o, i_),
                   sem="scl")
        P.wait("sync", "scl", evn)
        P.dma("sync", t["out"][ti * 128:(ti + 1) * 128, :], ost[b], "st_a")
    P.wait("sync", "st_a", P.count("st_a"))



def build_program(phases=ALL_PHASES, debug_names=(), debug_sbuf=()):
    nc = bass.Bass("TRN2", target_bir_lowering=False, debug=False,
                   num_devices=NCORES, dynamic_dma_scratch_size=8192)

    t = {}
    t["x_in"] = nc.dram_tensor("x", [T, D], F32, kind="ExternalInput").ap()
    t["wq_in"] = nc.dram_tensor("wq", [D, D], F32, kind="ExternalInput").ap()
    t["wk_a_in"] = nc.dram_tensor("wk_a", [D, D], F32, kind="ExternalInput").ap()
    t["wv_a_in"] = nc.dram_tensor("wv_a", [D, D], F32, kind="ExternalInput").ap()
    VEC_NAMES = ["ln1_w", "ln1_b", "ln2_w", "ln2_b", "att_tmk", "att_tmv",
                 "att_tmr", "time_decay", "time_first", "ffn_tmk", "ffn_tmr"]
    for nm in VEC_NAMES:
        t[nm + "_in"] = nc.dram_tensor(nm, [128, NS], F32, kind="ExternalInput").ap()
    for nm, shape in [("wkb", [D, D]), ("wvb", [D, D]), ("wrb", [D, D]),
                      ("wob", [D, D]), ("wkey", [FFN, D]), ("wrec", [D, D]),
                      ("wval", [D, FFN]), ("wg", [D, NE]),
                      ("w1", [NE, D, EH]), ("w2", [NE, EH, D]), ("w3", [NE, D, EH]),
                      ("ws1", [D, D]), ("ws2", [D, D]), ("ws3", [D, D])]:
        t[nm + "_in"] = nc.dram_tensor(nm, shape, F32, kind="ExternalInput").ap()
    t["ident_f_in"] = nc.dram_tensor("ident_f", [128, 128], F32, kind="ExternalInput").ap()
    t["ident_b_in"] = nc.dram_tensor("ident_b", [128, 128], BF16, kind="ExternalInput").ap()
    t["ones_in"] = nc.dram_tensor("ones", [128, 128], F32, kind="ExternalInput").ap()
    t["causal_in"] = nc.dram_tensor("causal", [128, 128], F32, kind="ExternalInput").ap()
    t["onehot_in"] = nc.dram_tensor("onehot", [NE, NE * 128], F32, kind="ExternalInput").ap()
    t["iota32_in"] = nc.dram_tensor("iota32", [128, 2 * NE], F32, kind="ExternalInput").ap()
    t["iotap_in"] = nc.dram_tensor("iotap", [32, 128], F32, kind="ExternalInput").ap()
    t["eps_in"] = nc.dram_tensor("epsv", [128, 2], F32, kind="ExternalInput").ap()
    t["out"] = nc.dram_tensor("out", [T, D], F32, kind="ExternalOutput").ap()

    t["xT"] = nc.dram_tensor("xT", [D, T], F32, kind="Internal").ap()
    t["bT"] = nc.dram_tensor("bT", [D, T], F32, kind="Internal").ap()
    t["hT"] = nc.dram_tensor("hT", [D, T], F32, kind="Internal").ap()
    t["qT"] = nc.dram_tensor("qT", [D, T], BF16, kind="Internal").ap()
    for nm, shape in [("wqpm", [128, NS * D]), ("wkapm", [128, NS * D]),
                      ("wvapm", [128, NS * D]),
                      ("wkT_bf", [D, D]), ("wvT_bf", [D, D]), ("wrT_bf", [D, D]),
                      ("woT_bf", [D, D]), ("wrecT_bf", [D, D]),
                      ("wkeyT_bf", [D, FFN]), ("wvalT_bf", [FFN, D]),
                      ("w1pm", [NE, 128, NS * EH]), ("w2pm", [NE, 128, 4 * D]),
                      ("w3pm", [NE, 128, NS * EH]),
                      ("ws1pm", [2, 128, NS * EH]), ("ws2pm", [2, 128, 4 * D]),
                      ("ws3pm", [2, 128, NS * EH])]:
        t[nm] = nc.dram_tensor(nm, shape, BF16, kind="Internal").ap()

    sem_names = ["bar", "ld_a", "ld_b", "st_a", "st_b", "mm", "ev", "vec",
                 "scl", "pool", "wbv", "wbs", "gpc", "pmm", "pvec", "pscl"]

    with contextlib.ExitStack() as stack:
        AR1h = stack.enter_context(nc.sbuf_tensor("AR1", [128, 32768], BF16))
        AR2h = stack.enter_context(nc.sbuf_tensor("AR2", [128, 32768], BF16))
        AR3h = stack.enter_context(nc.sbuf_tensor("AR3", [128, 32768], BF16))
        AR4h = stack.enter_context(nc.sbuf_tensor("AR4", [128, 8192], BF16))
        identfh = stack.enter_context(nc.sbuf_tensor("sb_idf", [128, 128], F32))
        identbh = stack.enter_context(nc.sbuf_tensor("sb_idb", [128, 128], BF16))
        onesh = stack.enter_context(nc.sbuf_tensor("sb_ones", [128, 128], F32))
        causalh = stack.enter_context(nc.sbuf_tensor("sb_causal", [128, 128], F32))
        iotah = stack.enter_context(nc.sbuf_tensor("sb_iota", [128, 2 * NE], F32))
        iotaph = stack.enter_context(nc.sbuf_tensor("sb_iotap", [32, 128], F32))
        epsh = stack.enter_context(nc.sbuf_tensor("sb_eps", [128, 2], F32))
        wgh = stack.enter_context(nc.sbuf_tensor("sb_wg", [128, NS * NE], F32))
        vech = stack.enter_context(nc.sbuf_tensor("sb_vecs", [128, 11 * NS], F32))
        psFh = stack.enter_context(nc.psum_tensor("psF", [128, 4096], F32))

        blk = stack.enter_context(nc.Block())
        sems = {nm: stack.enter_context(nc.semaphore(nm)) for nm in sem_names}

        P = Prog(nc)
        for nm in sem_names:
            P.register_sem(nm, sems[nm])

        c = Ctx()
        c.P, c.t, c.nc = P, t, nc
        c.AR1, c.AR2, c.AR3, c.AR4 = AR1h[:], AR2h[:], AR3h[:], AR4h[:]
        c.LN0T = AR2h[:]
        c.KT = AR3h[:]
        c.V = AR1h[:]
        c.s_ident_f, c.s_ident_bf = identfh[:], identbh[:]
        c.s_ones = onesh[:, 0:1]
        c.s_ones_row = onesh[0:1, :]
        c.s_causal = causalh[:]
        c.s_iota32 = iotah[:, 0:NE]
        c.s_iotap = iotaph[:]
        c.s_iotam64 = iotah[:, NE:2 * NE]
        c.s_eps = epsh[:, 0:1]
        c.s_eps_row = epsh[0:1, 1:2]
        c.WG = wgh[:]
        c.psF = psFh[:]
        c.s_vec = {nm: vech[:, i * NS:(i + 1) * NS]
                   for i, nm in enumerate(VEC_NAMES)}

        P.dma("sync", c.s_ident_f, t["ident_f_in"], "ld_a")
        P.dma("sync", c.s_ident_bf, t["ident_b_in"], "ld_a")
        P.dma("sync", onesh[:], t["ones_in"], "ld_a")
        P.dma("sync", c.s_causal, t["causal_in"], "ld_a")
        P.dma("sync", iotah[:], t["iota32_in"], "ld_a")
        P.dma("sync", iotaph[:], t["iotap_in"], "ld_a")
        P.dma("sync", epsh[:], t["eps_in"], "ld_a")
        for s in range(NS):
            P.dma("sync", wgh[:, s * NE:(s + 1) * NE],
                  t["wg_in"][s * 128:(s + 1) * 128, :], "ld_a")
        for nm in VEC_NAMES:
            P.dma("sync", c.s_vec[nm], t[nm + "_in"], "ld_a")
        P.wait("sync", "ld_a", P.count("ld_a"))
        P.barrier()

        if "prep" in phases:
            emit_prep(c)
        if "a1" in phases:
            emit_a1(c)
        if "a2" in phases:
            emit_a2(c)
        if "a3" in phases:
            emit_a3(c)
        if "moe1" in phases:
            emit_moe(c)
        if "attnb" in phases:
            emit_attnb(c)
        if "moe2" in phases:
            emit_moe(c)
        if "final" in phases:
            emit_final(c)
        else:
            for ti in range(0, T, 128):
                sb = carve(c.AR4, 0, 128, 1024, F32)
                P.wait("sync", "st_b", P.count("st_b"))
                P.dma("sync", sb, t["x_in"][ti:ti + 128, :], "ld_b")
                P.wait("gpsimd", "ld_b", P.count("ld_b"))
                P.dma("gpsimd", t["out"][ti:ti + 128, :], sb, "st_b")
            P.wait("sync", "st_b", P.count("st_b"))

        if debug_names:
            P.barrier()
            for nm in debug_names:
                src = t[nm]
                dbg = nc.dram_tensor("dbg_" + nm, list(src.shape), src.dtype,
                                     kind="ExternalOutput").ap()
                P.dma("gpsimd", dbg, src, "st_b")
            P.wait("gpsimd", "st_b", P.count("st_b"))
        if debug_sbuf:
            P.barrier()
            for nm, ap in debug_sbuf:
                dbg = nc.dram_tensor("dbg_" + nm, list(ap(c).shape), ap(c).dtype,
                                     kind="ExternalOutput").ap()
                P.dma("gpsimd", dbg, ap(c), "st_b")
            P.wait("gpsimd", "st_b", P.count("st_b"))

        for name, eng_sel in (("sync", blk.sync), ("vector", blk.vector),
                              ("scalar", blk.scalar), ("tensor", blk.tensor),
                              ("gpsimd", blk.gpsimd)):
            fns = P.plan[name]
            eng_sel(lambda e, fns=fns: [f(e) for f in fns])

    return nc


def make_inputs(inputs):
    x = np.ascontiguousarray(inputs["L0_attn_a_embeddings"], dtype=np.float32)

    def slabvec(v):
        return np.ascontiguousarray(np.asarray(v, np.float32).reshape(NS, 128).T)

    base = {
        "wq": inputs["L0_attn_a_W_Q"], "wk_a": inputs["L0_attn_a_W_K"],
        "wv_a": inputs["L0_attn_a_W_V"],
        "wkb": inputs["L0_attn_b_Wk"], "wvb": inputs["L0_attn_b_Wv"],
        "wrb": inputs["L0_attn_b_Wr"], "wob": inputs["L0_attn_b_Wo"],
        "wkey": inputs["L0_attn_b_Wkey"], "wrec": inputs["L0_attn_b_Wrec"],
        "wval": inputs["L0_attn_b_Wval"],
        "wg": inputs["L0_ffn_Wg"], "w1": inputs["L0_ffn_W1"],
        "w2": inputs["L0_ffn_W2"], "w3": inputs["L0_ffn_W3"],
        "ws1": inputs["L0_ffn_Ws1"], "ws2": inputs["L0_ffn_Ws2"],
        "ws3": inputs["L0_ffn_Ws3"],
    }
    base = {k: np.ascontiguousarray(v, dtype=np.float32) for k, v in base.items()}
    for nm, key in [("ln1_w", "L0_attn_b_ln1_w"), ("ln1_b", "L0_attn_b_ln1_b"),
                    ("ln2_w", "L0_attn_b_ln2_w"), ("ln2_b", "L0_attn_b_ln2_b"),
                    ("att_tmk", "L0_attn_b_att_tmk"),
                    ("att_tmv", "L0_attn_b_att_tmv"),
                    ("att_tmr", "L0_attn_b_att_tmr"),
                    ("time_decay", "L0_attn_b_time_decay"),
                    ("time_first", "L0_attn_b_time_first"),
                    ("ffn_tmk", "L0_attn_b_ffn_tmk"),
                    ("ffn_tmr", "L0_attn_b_ffn_tmr")]:
        base[nm] = slabvec(inputs[key])

    import ml_dtypes
    base["ident_f"] = np.eye(128, dtype=np.float32)
    base["ident_b"] = np.eye(128).astype(ml_dtypes.bfloat16)
    base["ones"] = np.ones((128, 128), np.float32)
    cz = np.zeros((128, 128), np.float32)
    i_idx = np.arange(128)[:, None]
    j_idx = np.arange(128)[None, :]
    cz[j_idx > i_idx] = -1e30 / SCALE_A
    base["causal"] = cz
    oh = np.zeros((NE, NE * 128), np.float32)
    for e in range(NE):
        oh[e, e * 128:(e + 1) * 128] = 1.0
    base["onehot"] = oh
    io = np.tile(np.arange(NE, dtype=np.float32), (128, 1))
    base["iota32"] = np.ascontiguousarray(
        np.concatenate([io, io - 64.0], axis=1))
    base["iotap"] = np.ascontiguousarray(
        np.tile(np.arange(32, dtype=np.float32)[:, None], (1, 128)))
    base["epsv"] = np.full((128, 2), EPS, np.float32)

    in_maps = []
    for cc in range(NCORES):
        m = dict(base)
        m["x"] = np.ascontiguousarray(x[cc])
        in_maps.append(m)
    return in_maps


def kernel(**inputs):
    in_maps = make_inputs(inputs)
    nc = build_program()
    res = run_bass_kernel_spmd(nc, in_maps, core_ids=list(range(NCORES)))
    out = np.stack([res.results[cc]["out"] for cc in range(NCORES)], axis=0)
    return out.astype(np.float32)


# version 11
# speedup vs baseline: 1.0545x; 1.0545x over previous
import contextlib
import numpy as np
import concourse.bass as bass
import concourse.mybir as mybir
from concourse.bass_utils import run_bass_kernel_spmd

F32 = mybir.dt.float32
BF16 = mybir.dt.bfloat16
AF = mybir.ActivationFunctionType
ALU = mybir.AluOpType
AX = mybir.AxisListType

D = 1024
T = 4096
NS = D // 128
NE = 32
EH = 512
FFN = 4096
NCORES = 8
EPS = 1e-5
TOPK = 4
CAP_DROP = 70.0
SCALE_A = 1.0 / 32.0

ENGS = ("sync", "vector", "scalar", "tensor", "gpsimd")
ALL_PHASES = ("prep", "a1", "a2", "a3", "moe1", "attnb", "moe2", "final")


class Prog:
    def __init__(self, nc):
        self.nc = nc
        self.plan = {e: [] for e in ENGS}
        self.sem_handles = {}
        self.sem_counts = {}

    def register_sem(self, name, handle):
        self.sem_handles[name] = handle
        self.sem_counts[name] = 0

    def bump(self, name, k):
        self.sem_counts[name] += k
        return self.sem_counts[name]

    def count(self, name):
        return self.sem_counts[name]

    def emit(self, eng, fn):
        self.plan[eng].append(fn)

    def dma(self, eng, out, in_, sem):
        h = self.sem_handles[sem]
        tgt = self.bump(sem, 16)

        def f(e, out=out, in_=in_, h=h):
            e.dma_start(out=out, in_=in_).then_inc(h, 16)
        self.emit(eng, f)
        return tgt

    def wait(self, eng, sem, val=None):
        h = self.sem_handles[sem]
        v = self.count(sem) if val is None else val
        if v <= 0:
            return

        def f(e, h=h, v=v):
            e.wait_ge(h, v)
        self.emit(eng, f)

    def op(self, eng, fn, sem=None, k=1):
        if sem is None:
            self.emit(eng, fn)
            return None
        h = self.sem_handles[sem]
        tgt = self.bump(sem, k)

        def f(e, fn=fn, h=h, k=k):
            fn(e).then_inc(h, k)
        self.emit(eng, f)
        return tgt

    def barrier(self):
        h = self.sem_handles["bar"]
        base = self.count("bar")
        for e in ENGS:
            self.bump("bar", 1)
        tgt = base + len(ENGS)
        for e in ENGS:
            def f(eng, h=h, tgt=tgt):
                eng.nop().then_inc(h, 1)
                eng.wait_ge(h, tgt)
            self.emit(e, f)


class Ctx:
    pass


def carve(arena_ap, byte_off, rows, cols, dtype):
    nb = mybir.dt.size(dtype)
    assert byte_off % 2 == 0
    start = byte_off // 2
    width = cols * nb // 2
    sl = arena_ap[0:rows, start:start + width]
    if dtype == BF16:
        return sl
    return sl.bitcast(dtype)


def sl3(ap, s=NS):
    return ap.rearrange("p (s c) -> p s c", s=s)


def dview(dram_ap):
    return dram_ap.rearrange("(s p) t -> p s t", p=128)



def emit_prep(c):
    P, t = c.P, c.t
    c.prep_w_done = {}

    def cast_pm(dst_pm, src, s_count, width):
        P.dma("gpsimd", dst_pm.rearrange("p (s h) -> p s h", s=s_count),
              src.rearrange("(s p) h -> p s h", p=128), "gpc")

    cast_pm(t["wqpm"], t["wq_in"], NS, D)
    cast_pm(t["wkapm"], t["wk_a_in"], NS, D)
    cast_pm(t["wvapm"], t["wv_a_in"], NS, D)
    c.prep_w_done["qkv"] = P.count("gpc")

    stg16 = [carve(c.AR1, 0, 128, 2048, BF16), carve(c.AR1, 4096, 128, 2048, BF16)]
    tstg = [carve(c.AR1, 8192, 128, 2048, BF16), carve(c.AR1, 12288, 128, 2048, BF16)]
    ps_bf2 = [c.psF[:, 0:1024].bitcast(BF16), c.psF[:, 1024:2048].bitcast(BF16)]
    tposes = [
        ("wkb_in", "wkT_bf", 1024, 1024), ("wvb_in", "wvT_bf", 1024, 1024),
        ("wrb_in", "wrT_bf", 1024, 1024), ("wob_in", "woT_bf", 1024, 1024),
        ("wrec_in", "wrecT_bf", 1024, 1024),
        ("wkey_in", "wkeyT_bf", 4096, 1024), ("wval_in", "wvalT_bf", 1024, 4096),
    ]
    ti = 0
    for nm_in, nm_out, R, Cc in tposes:
        srcw, dst = t[nm_in], t[nm_out]
        for r0 in range(0, R, 128):
            for c0 in range(0, Cc, 2048):
                cw = min(2048, Cc - c0)
                b = ti % 2
                ps_bf = ps_bf2[b]
                P.wait("gpsimd", "mm", P.count("mm"))
                ldn = P.dma("gpsimd", stg16[b][:, 0:cw],
                            srcw[r0:r0 + 128, c0:c0 + cw], "gpc")
                P.wait("tensor", "gpc", ldn)
                P.wait("tensor", "st_a", P.count("st_a"))
                P.wait("tensor", "scl", P.count("scl"))
                nblk = cw // 128
                for jb in range(nblk):
                    P.op("tensor",
                         lambda te, o=ps_bf[:, jb * 128:(jb + 1) * 128],
                         i_=stg16[b][:, jb * 128:(jb + 1) * 128],
                         idn=c.s_ident_bf:
                         te.transpose(o, i_, idn))
                P.op("tensor",
                     lambda te, o=c.psF[:, 2048:2112].bitcast(BF16),
                     i_=c.s_ident_bf, idn=c.s_ident_bf:
                     te.transpose(o, i_, idn), sem="mm")
                P.wait("scalar", "mm", P.count("mm"))
                P.op("scalar",
                     lambda sc, o=tstg[b][:, 0:cw], i_=ps_bf[:, 0:cw]:
                     sc.copy(o, i_), sem="scl")
                P.wait("sync", "scl", P.count("scl"))
                for jb in range(nblk):
                    P.dma("sync",
                          dst[c0 + jb * 128:c0 + (jb + 1) * 128, r0:r0 + 128],
                          tstg[b][:, jb * 128:(jb + 1) * 128], "st_a")
                ti += 1
    c.prep_w_done["attnb"] = P.count("st_a")

    P.wait("gpsimd", "st_a", P.count("st_a"))
    for h in range(2):
        cast_pm(t["ws1pm"][h], t["ws1_in"][:, h * EH:(h + 1) * EH], NS, EH)
        cast_pm(t["ws3pm"][h], t["ws3_in"][:, h * EH:(h + 1) * EH], NS, EH)
        cast_pm(t["ws2pm"][h], t["ws2_in"][h * EH:(h + 1) * EH, :], 4, D)
        c.prep_w_done[NE + h] = P.count("gpc")
    for e in range(NE):
        cast_pm(t["w1pm"][e], t["w1_in"][e], NS, EH)
        cast_pm(t["w3pm"][e], t["w3_in"][e], NS, EH)
        cast_pm(t["w2pm"][e], t["w2_in"][e], 4, D)
        c.prep_w_done[e] = P.count("gpc")
    c.prep_w_done["all"] = P.count("gpc")
    P.wait("sync", "st_a", P.count("st_a"))
    P.barrier()



def emit_a1(c):
    P, t = c.P, c.t
    xst = [carve(c.AR1, 0, 128, 1024, F32), carve(c.AR1, 4096, 128, 1024, F32)]
    tmp1 = carve(c.AR1, 8192, 128, 1024, F32)
    lnst = [carve(c.AR1, 12288, 128, 1024, BF16), carve(c.AR1, 14336, 128, 1024, BF16)]
    xTst = [carve(c.AR1, 16384, 128, 1024, F32), carve(c.AR1, 20480, 128, 1024, F32)]
    nmu = carve(c.AR1, 24576, 128, 1, F32)
    vsum = carve(c.AR1, 24608, 128, 1, F32)
    rstd = carve(c.AR1, 24640, 128, 1, F32)
    junk = carve(c.AR1, 28672, 128, 1024, F32)
    ps32 = c.psF[:, 0:1024]
    psbf = c.psF[:, 1024:1536].bitcast(BF16)
    xT_r = dview(t["xT"])

    for ti in range(T // 128):
        b = ti % 2
        if ti >= 2:
            P.wait("sync", "mm", P.count("mm"))
            P.wait("sync", "vec", P.count("vec"))
        ld_n = P.dma("sync", xst[b], t["x_in"][ti * 128:(ti + 1) * 128, :], "ld_a")

        P.wait("tensor", "ld_a", ld_n)
        if ti >= 2:
            P.wait("tensor", "st_a", P.count("st_a"))
        for s in range(NS):
            P.op("tensor",
                 lambda te, o=ps32[:, s * 128:(s + 1) * 128],
                 i_=xst[b][:, s * 128:(s + 1) * 128], idn=c.s_ident_f:
                 te.transpose(o, i_, idn),
                 sem="mm" if s == NS - 1 else None)
        P.wait("scalar", "mm", P.count("mm"))
        evn = P.op("scalar", lambda sc, o=xTst[b], i_=ps32: sc.copy(o, i_),
                   sem="scl")
        P.wait("sync", "scl", evn)
        P.dma("sync", xT_r[:, :, ti * 128:(ti + 1) * 128], sl3(xTst[b]), "st_a")

        P.wait("vector", "ld_a", ld_n)
        P.op("vector", lambda v, o=vsum, i_=xst[b]:
             v.reduce_sum(o, i_, axis=AX.X), sem="wbv")
        P.wait("vector", "wbv")
        P.op("vector", lambda v, o=nmu, i_=vsum:
             v.tensor_scalar(o, i_, -1.0 / D, None, ALU.mult), sem="wbv")
        P.wait("vector", "wbv")
        P.op("vector", lambda v, o=tmp1, i_=xst[b], m=nmu:
             v.tensor_scalar(o, i_, m, None, ALU.add))
        P.op("vector", lambda v, o=junk, i_=tmp1: v.tensor_mul(o, i_, i_))
        P.op("vector", lambda v, o=vsum, i_=junk:
             v.reduce_sum(o, i_, axis=AX.X), sem="vec")
        P.wait("scalar", "vec", P.count("vec"))
        P.op("scalar", lambda sc, o=rstd, i_=vsum, e=c.s_eps:
             sc.activation(o, i_, AF.Sqrt, bias=e, scale=1.0 / D), sem="scl")
        P.wait("vector", "scl", P.count("scl"))
        P.op("vector", lambda v, o=nmu, i_=rstd: v.reciprocal(o, i_), sem="wbv")
        P.wait("vector", "wbv")
        P.op("vector", lambda v, o=lnst[b], i_=tmp1, m=nmu:
             v.tensor_scalar(o, i_, m, None, ALU.mult), sem="vec")
        P.wait("tensor", "vec", P.count("vec"))
        for s in range(NS):
            P.op("tensor",
                 lambda te, o=psbf[:, s * 128:(s + 1) * 128],
                 i_=lnst[b][:, s * 128:(s + 1) * 128], idn=c.s_ident_bf:
                 te.transpose(o, i_, idn),
                 sem="mm" if s == NS - 1 else None)
        P.wait("vector", "mm", P.count("mm"))
        P.op("vector",
             lambda v, o=sl3(c.LN0T, NS)[:, :, ti * 128:(ti + 1) * 128].
             rearrange("p s c -> p (s c)") if False else
             sl3(c.LN0T)[:, :, ti * 128:(ti + 1) * 128],
             i_=sl3(psbf, NS): v.tensor_copy(o, i_), sem="vec")
    P.barrier()



def emit_a2(c):
    P, t = c.P, c.t
    wsl = carve(c.AR4, 0, 128, NS * 1024, BF16)
    qstg = [carve(c.AR1, 0, 128, 512, BF16), carve(c.AR1, 1024, 128, 512, BF16)]
    ps = [c.psF[:, 0:512], c.psF[:, 512:1024]]
    qT = t["qT"]

    P.wait("sync", "gpc", c.prep_w_done["qkv"])

    def load_w(nm):
        P.dma("sync", wsl, t[nm], "ld_a")
        return P.count("ld_a")

    for which, nm in (("k", "wkapm"), ("q", "wqpm")):
        P.wait("sync", "mm", P.count("mm"))
        wn = load_w(nm)
        P.wait("tensor", "ld_a", wn)
        pp = 0
        for tc in range(T // 512):
            for oc in range(NS):
                b = pp % 2
                P.wait("tensor", "ev", P.count("ev"))
                P.wait("tensor", "st_a", P.count("st_a"))
                for s in range(NS):
                    P.op("tensor",
                         lambda te, o=ps[b],
                         l=wsl[:, s * 1024 + oc * 128:s * 1024 + oc * 128 + 128],
                         r=c.LN0T[:, s * T + tc * 512:s * T + tc * 512 + 512],
                         st=(s == 0), sp=(s == NS - 1):
                         te.matmul(o, l, r, start=st, stop=sp,
                                   skip_group_check=True),
                         sem="mm" if s == NS - 1 else None)
                mmn = P.count("mm")
                if which == "k":
                    P.wait("vector", "mm", mmn)
                    P.op("vector",
                         lambda v, o=c.KT[:, oc * T + tc * 512:oc * T + tc * 512 + 512],
                         i_=ps[b]: v.tensor_copy(o, i_), sem="ev")
                else:
                    P.wait("scalar", "mm", mmn)
                    evn = P.op("scalar",
                               lambda sc, o=qstg[b], i_=ps[b]: sc.copy(o, i_),
                               sem="ev")
                    P.wait("sync", "ev", evn)
                    P.dma("sync", qT[oc * 128:(oc + 1) * 128, tc * 512:tc * 512 + 512],
                          qstg[b], "st_a")
                pp += 1

    P.wait("sync", "mm", P.count("mm"))
    P.wait("sync", "st_a", P.count("st_a"))
    wn = load_w("wvapm")
    P.wait("tensor", "ld_a", wn)
    pp = 0
    for ti in range(T // 128):
        for dc in range(2):
            b = pp % 2
            P.wait("tensor", "ev", P.count("ev"))
            for s in range(NS):
                P.op("tensor",
                     lambda te, o=ps[b],
                     l=c.LN0T[:, s * T + ti * 128:s * T + ti * 128 + 128],
                     r=wsl[:, s * 1024 + dc * 512:s * 1024 + dc * 512 + 512],
                     st=(s == 0), sp=(s == NS - 1):
                     te.matmul(o, l, r, start=st, stop=sp,
                               skip_group_check=True),
                     sem="mm" if s == NS - 1 else None)
            P.wait("vector", "mm", P.count("mm"))
            P.op("vector",
                 lambda v, o=c.V[:, ti * 1024 + dc * 512:ti * 1024 + dc * 512 + 512],
                 i_=ps[b]: v.tensor_copy(o, i_), sem="ev")
            pp += 1
    P.barrier()



def emit_a3(c):
    P, t = c.P, c.t
    qtile = [carve(c.AR2, 0, 128, 1024, BF16), carve(c.AR2, 2048, 128, 1024, BF16)]
    ptil = carve(c.AR2, 4096, 128, 512, BF16)
    ptev_big = carve(c.AR2, 28672, 128, 4096, BF16)
    mrow = carve(c.AR2, 6144, 128, 1, F32)
    lrow = carve(c.AR2, 6208, 128, 1, F32)
    tsum = carve(c.AR2, 6272, 128, 1, F32)
    tmax = carve(c.AR2, 6336, 128, 1, F32)
    alpha = carve(c.AR2, 6400, 128, 1, F32)
    nbias = carve(c.AR2, 6464, 128, 1, F32)
    junk = carve(c.AR2, 8192, 128, 512, F32)
    otst = [carve(c.AR2, 12288, 128, 1024, F32), carve(c.AR2, 16384, 128, 1024, F32)]
    xtst = [carve(c.AR2, 20480, 128, 1024, F32), carve(c.AR2, 24576, 128, 1024, F32)]
    ps_s = c.psF[:, 2048:2560]
    ps_pt = c.psF[:, 2560:2816].bitcast(BF16)
    ps_ot = c.psF[:, 0:1024]
    qT_r = dview(t["qT"])
    xT_r = dview(t["xT"])

    for qi in range(T // 128):
        bq = qi % 2
        if qi >= 2:
            P.wait("sync", "mm", P.count("mm"))
        qld = P.dma("sync", sl3(qtile[bq]), qT_r[:, :, qi * 128:(qi + 1) * 128],
                    "ld_a")
        P.op("vector", lambda v, o=mrow: v.memset(o, -1e30))
        P.op("vector", lambda v, o=lrow: v.memset(o, 0.0), sem="wbv")
        P.wait("vector", "wbv")
        P.wait("tensor", "ld_a", qld)

        ngrp = (qi * 128 + 128 + 511) // 512
        for g in range(ngrp):
            k0 = g * 512
            kw = min(512, qi * 128 + 128 - k0)
            P.wait("tensor", "vec", P.count("vec"))
            P.wait("tensor", "scl", P.count("scl"))
            for s in range(NS):
                P.op("tensor",
                     lambda te, o=ps_s[:, 0:kw], l=qtile[bq][:, s * 128:(s + 1) * 128],
                     r=c.KT[:, s * T + k0:s * T + k0 + kw],
                     st=(s == 0), sp=(s == NS - 1):
                     te.matmul(o, l, r, start=st, stop=sp, skip_group_check=True),
                     sem="mm" if s == NS - 1 else None)
            P.wait("vector", "mm", P.count("mm"))
            if kw > (qi * 128 - k0):
                P.op("vector", lambda v, o=ps_s[:, kw - 128:kw],
                     i_=ps_s[:, kw - 128:kw], m=c.s_causal:
                     v.tensor_tensor(o, i_, m, ALU.add))
            P.op("vector", lambda v, o=tmax, i_=ps_s[:, 0:kw]:
                 v.reduce_max(o, i_, axis=AX.X), sem="wbv")
            P.wait("vector", "wbv")
            P.op("vector", lambda v, o=alpha, i_=mrow, m=tmax:
                 v.tensor_scalar(o, i_, m, None, ALU.subtract))
            P.op("vector", lambda v, o=mrow, i_=mrow, m=tmax:
                 v.tensor_scalar(o, i_, m, None, ALU.max), sem="wbv")
            P.wait("vector", "wbv")
            P.op("vector", lambda v, o=alpha, i_=alpha:
                 v.tensor_scalar(o, i_, 0.0, None, ALU.min))
            P.op("vector", lambda v, o=nbias, i_=mrow:
                 v.tensor_scalar(o, i_, -SCALE_A, None, ALU.mult), sem="vec")
            P.wait("scalar", "vec", P.count("vec"))
            P.op("scalar", lambda sc, o=alpha, i_=alpha:
                 sc.activation(o, i_, AF.Exp, scale=SCALE_A), sem="scl")
            P.op("scalar", lambda sc, o=junk[:, 0:kw], i_=ps_s[:, 0:kw],
                 b_=nbias, m=tsum:
                 sc.activation(o, i_, AF.Exp, bias=b_, scale=SCALE_A,
                               accum_out=m), sem="scl")
            P.wait("vector", "scl", P.count("scl"))
            P.op("vector", lambda v, o=lrow, i_=lrow, a=alpha:
                 v.tensor_scalar(o, i_, a, None, ALU.mult), sem="wbv")
            P.wait("vector", "wbv")
            P.op("vector", lambda v, o=lrow, i_=lrow, b_=tsum:
                 v.tensor_scalar(o, i_, b_, None, ALU.add), sem="vec")

        P.wait("scalar", "vec", P.count("vec"))
        P.op("scalar", lambda sc, o=tsum, i_=lrow:
             sc.activation(o, i_, AF.Ln), sem="scl")
        P.wait("vector", "scl", P.count("scl"))
        P.op("vector", lambda v, o=nbias, i_=nbias, m=tsum:
             v.tensor_scalar(o, i_, m, None, ALU.subtract), sem="vec")

        for g in range(ngrp):
            k0 = g * 512
            kw = min(512, qi * 128 + 128 - k0)
            P.wait("tensor", "vec", P.count("vec"))
            P.wait("tensor", "scl", P.count("scl"))
            for s in range(NS):
                P.op("tensor",
                     lambda te, o=ps_s[:, 0:kw], l=qtile[bq][:, s * 128:(s + 1) * 128],
                     r=c.KT[:, s * T + k0:s * T + k0 + kw],
                     st=(s == 0), sp=(s == NS - 1):
                     te.matmul(o, l, r, start=st, stop=sp, skip_group_check=True),
                     sem="mm" if s == NS - 1 else None)
            if kw > (qi * 128 - k0):
                P.wait("vector", "mm", P.count("mm"))
                P.op("vector", lambda v, o=ps_s[:, kw - 128:kw],
                     i_=ps_s[:, kw - 128:kw], m=c.s_causal:
                     v.tensor_tensor(o, i_, m, ALU.add), sem="vec")
                P.wait("scalar", "vec", P.count("vec"))
            else:
                P.wait("scalar", "mm", P.count("mm"))
            P.op("scalar", lambda sc, o=ptil[:, 0:kw], i_=ps_s[:, 0:kw], b_=nbias:
                 sc.activation(o, i_, AF.Exp, bias=b_, scale=SCALE_A), sem="scl")
            P.wait("tensor", "scl", P.count("scl"))
            nkt = (kw + 127) // 128
            for j in range(nkt):
                P.op("tensor",
                     lambda te, o=ps_pt[:, j * 128:(j + 1) * 128],
                     i_=ptil[:, j * 128:(j + 1) * 128], idn=c.s_ident_bf:
                     te.transpose(o, i_, idn),
                     sem="mm" if j == nkt - 1 else None)
            P.wait("vector", "mm", P.count("mm"))
            P.op("vector", lambda v, o=ptev_big[:, k0:k0 + kw], i_=ps_pt[:, 0:kw]:
                 v.tensor_copy(o, i_), sem="vec")
        P.wait("tensor", "vec", P.count("vec"))
        nkt_all = qi + 1
        for dc in range(NS):
            for kt in range(nkt_all):
                P.op("tensor",
                     lambda te, o=ps_ot[:, dc * 128:(dc + 1) * 128],
                     l=c.V[:, kt * 1024 + dc * 128:kt * 1024 + dc * 128 + 128],
                     r=ptev_big[:, kt * 128:(kt + 1) * 128],
                     st=(kt == 0), sp=(kt == nkt_all - 1):
                     te.matmul(o, l, r, start=st, stop=sp,
                               skip_group_check=True),
                     sem="mm" if (dc == NS - 1 and kt == nkt_all - 1) else None)

        bo = qi % 2
        P.wait("sync", "vec", P.count("vec"))
        xld = P.dma("sync", sl3(xtst[bo]), xT_r[:, :, qi * 128:(qi + 1) * 128],
                    "ld_b")
        P.wait("vector", "mm", P.count("mm"))
        P.wait("vector", "ld_b", xld)
        P.wait("vector", "st_a", P.count("st_a"))
        P.op("vector", lambda v, o=otst[bo], i_=xtst[bo], p=ps_ot:
             v.tensor_tensor(o, i_, p, ALU.add), sem="vec")
        P.wait("sync", "vec", P.count("vec"))
        P.dma("sync", xT_r[:, :, qi * 128:(qi + 1) * 128], sl3(otst[bo]), "st_a")
    P.barrier()



def emit_moe(c):
    P, t = c.P, c.t
    GT = 1024
    CH = 256
    NG = T // GT
    NCH = GT // CH
    NPE = NE + 2
    NPAIR = NPE // 2

    ln1t = [carve(c.AR2, 0, 128, NS * GT, BF16),
            carve(c.AR2, 16384, 128, NS * GT, BF16)]
    mw = [carve(c.AR2, 32768, 128, 4 * CH, BF16),
          carve(c.AR2, 34816, 128, 4 * CH, BF16),
          carve(c.AR2, 36864, 128, 4 * CH, BF16),
          carve(c.AR2, 38912, 128, 4 * CH, BF16)]
    logit = carve(c.AR2, 40960, 128, NE, F32)
    lmax = carve(c.AR2, 41088, 128, 1, F32)
    lsum = carve(c.AR2, 41120, 128, 1, F32)
    rrow = carve(c.AR2, 41152, 128, 1, F32)
    srow = carve(c.AR2, 41184, 128, 1, F32)
    eq = carve(c.AR2, 41216, 128, NE, F32)
    idxm = carve(c.AR2, 41344, 128, NE, F32)
    wsel = carve(c.AR2, 41472, 128, NE, F32)
    stat1 = carve(c.AR2, 41600, 1, CH, F32)
    stat2 = carve(c.AR2, 42624, 1, CH, F32)
    wbsb = [carve(c.AR2, 43648, 128, CH, F32),
            carve(c.AR2, 44672, 128, CH, F32)]
    silu1 = [carve(c.AR2, 45696, 128, 4 * CH, BF16),
             carve(c.AR2, 47744, 128, 4 * CH, BF16)]
    h3w = [carve(c.AR2, 49792, 128, 4 * CH, BF16),
           carve(c.AR2, 51840, 128, 4 * CH, BF16)]
    yacc = carve(c.AR1, 0, 128, NS * GT, F32)
    wt32 = [carve(c.AR1, 32768, 32, GT, F32),
            carve(c.AR1, 36864, 32, GT, F32)]
    onehot_dyn = [carve(c.AR1, 40960, 32, 128, F32),
                  carve(c.AR1, 41472, 32, 128, F32)]
    xchk = [carve(c.AR1, 49152, 128, NS * CH, F32),
            carve(c.AR1, 57344, 128, NS * CH, F32)]
    lnf = carve(c.AR4, 0, 128, NS * CH, F32)
    xchk2 = carve(c.AR4, 8192, 128, NS * CH, F32)
    wv = []
    for b in range(2):
        off = b * 24576
        wv.append({
            "w1": carve(c.AR3, off, 128, NS * 512, BF16),
            "w3": carve(c.AR3, off + 8192, 128, NS * 512, BF16),
            "w2": carve(c.AR3, off + 16384, 128, 4 * 1024, BF16),
        })
    ps_y = c.psF[:, 0:2048]
    ps_h1 = c.psF[:, 2048:3072]
    ps_h3 = c.psF[:, 3072:4096]
    ps_st = c.psF[:, 2048:2560]
    ps_lg = c.psF[:, 3072:3104]
    ps_wt = c.psF[0:32, 3104:3232]
    xT_r = dview(t["xT"])


    def load_w13(pe, buf):
        P.wait("sync", "gpc", c.prep_w_done["all"])
        if pe < NE:
            P.dma("sync", wv[buf]["w1"], t["w1pm"][pe], "ld_a")
            n1 = P.count("ld_a")
            P.dma("sync", wv[buf]["w3"], t["w3pm"][pe], "ld_a")
        else:
            h0 = pe - NE
            P.dma("sync", wv[buf]["w1"], t["ws1pm"][h0], "ld_a")
            n1 = P.count("ld_a")
            P.dma("sync", wv[buf]["w3"], t["ws3pm"][h0], "ld_a")
        return (n1, P.count("ld_a"))

    def load_w2(pe, buf):
        if pe < NE:
            P.dma("sync", wv[buf]["w2"], t["w2pm"][pe], "ld_a")
        else:
            P.dma("sync", wv[buf]["w2"], t["ws2pm"][pe - NE], "ld_a")
        return P.count("ld_a")



    ln1t_ready = {}
    c.prep_psum_free = 0

    def prep_gen(g, gb):
        t0g = g * GT
        xln_done = {}
        P.wait("sync", "pvec", P.count("pvec"))
        xln_done[0] = P.dma("sync", sl3(xchk[0]), xT_r[:, :, t0g:t0g + CH],
                            "ld_b")
        for ch in range(NCH):
            t0 = t0g + ch * CH
            if ch + 1 < NCH:
                P.wait("sync", "pvec", P.count("pvec"))
                xln_done[ch + 1] = P.dma(
                    "sync", sl3(xchk[(ch + 1) % 2]),
                    xT_r[:, :, t0 + CH:t0 + 2 * CH], "ld_b")
            xc = xchk[ch % 2]
            xln = xln_done[ch]
            P.wait("tensor", "ld_b", xln)
            P.wait("tensor", "pvec", P.count("pvec"))
            P.wait("tensor", "pscl", P.count("pscl"))
            P.wait("tensor", "scl", P.count("scl"))
            P.wait("tensor", "wbs", P.count("wbs"))
            for s in range(NS):
                P.op("tensor", lambda te, o=ps_st[0:1, 0:CH], l=c.s_ones,
                     r=xc[:, s * CH:(s + 1) * CH],
                     st=(s == 0), sp=(s == NS - 1):
                     te.matmul(o, l, r, start=st, stop=sp, skip_group_check=True),
                     sem="pmm" if s == NS - 1 else None)
            P.wait("vector", "pmm", P.count("pmm"))
            P.op("vector", lambda v, o=stat1, i_=ps_st[0:1, 0:CH]:
                 v.tensor_scalar(o, i_, -1.0 / D, None, ALU.mult), sem="pvec")
            for s in range(NS):
                P.op("vector", lambda v, o=lnf[:, s * CH:(s + 1) * CH],
                     i_=xc[:, s * CH:(s + 1) * CH]:
                     v.tensor_mul(o, i_, i_),
                     sem="pvec" if s == NS - 1 else None)
            c.prep_psum_free = P.count("pvec")
            yield
            P.wait("tensor", "pvec", P.count("pvec"))
            P.wait("tensor", "scl", P.count("scl"))
            P.wait("tensor", "wbs", P.count("wbs"))
            for s in range(NS):
                P.op("tensor", lambda te, o=ps_st[0:1, CH:2 * CH], l=c.s_ones,
                     r=lnf[:, s * CH:(s + 1) * CH],
                     st=(s == 0), sp=(s == NS - 1):
                     te.matmul(o, l, r, start=st, stop=sp, skip_group_check=True),
                     sem="pmm" if s == NS - 1 else None)
            P.wait("vector", "pmm", P.count("pmm"))
            P.op("vector", lambda v, o=stat2, i_=stat1: v.tensor_mul(o, i_, i_),
                 sem="pvec")
            P.wait("vector", "pvec")
            P.op("vector", lambda v, o=stat2, i_=ps_st[0:1, CH:2 * CH], m=stat2:
                 v.scalar_tensor_tensor(o, i_, 1.0 / D, m, ALU.mult,
                                        ALU.subtract), sem="pvec")
            P.wait("scalar", "pvec", P.count("pvec"))
            P.op("scalar", lambda sc, o=stat2, i_=stat2, e=c.s_eps_row:
                 sc.activation(o, i_, AF.Sqrt, bias=e), sem="pscl")
            P.wait("vector", "pscl", P.count("pscl"))
            P.op("vector", lambda v, o=stat2, i_=stat2:
                 v.reciprocal(o, i_), sem="pvec")
            c.prep_psum_free = P.count("pvec")
            yield
            P.wait("tensor", "pvec", P.count("pvec"))
            P.wait("tensor", "scl", P.count("scl"))
            P.wait("tensor", "wbs", P.count("wbs"))
            P.op("tensor", lambda te, o=ps_st[:, 0:CH], l=c.s_ones_row,
                 r=stat1: te.matmul(o, l, r, start=True, stop=True,
                                    skip_group_check=True))
            P.op("tensor", lambda te, o=ps_st[:, CH:2 * CH], l=c.s_ones_row,
                 r=stat2: te.matmul(o, l, r, start=True, stop=True,
                                    skip_group_check=True), sem="pmm")
            P.wait("vector", "pmm", P.count("pmm"))
            for s in range(NS):
                P.op("vector", lambda v, o=lnf[:, s * CH:(s + 1) * CH],
                     i_=xc[:, s * CH:(s + 1) * CH], m=ps_st[:, 0:CH]:
                     v.tensor_tensor(o, i_, m, ALU.add))
            for s in range(NS):
                P.op("vector", lambda v, o=lnf[:, s * CH:(s + 1) * CH],
                     i_=lnf[:, s * CH:(s + 1) * CH], m=ps_st[:, CH:2 * CH]:
                     v.tensor_tensor(o, i_, m, ALU.mult),
                     sem="pvec" if s == NS - 1 else None)
            for s in range(NS):
                P.op("vector",
                     lambda v, o=sl3(ln1t[gb])[:, s, ch * CH:(ch + 1) * CH],
                     i_=lnf[:, s * CH:(s + 1) * CH]:
                     v.tensor_copy(o, i_), sem="pvec" if s == NS - 1 else None)
            c.prep_psum_free = P.count("pvec")
            yield
            for tt in range(CH // 128):
                P.wait("tensor", "pvec", P.count("pvec"))
                P.wait("tensor", "vec", P.count("vec"))
                for s in range(NS):
                    P.op("tensor", lambda te, o=ps_lg,
                         l=lnf[:, s * CH + tt * 128:s * CH + tt * 128 + 128],
                         r=c.WG[:, s * NE:(s + 1) * NE],
                         st=(s == 0), sp=(s == NS - 1):
                         te.matmul(o, l, r, start=st, stop=sp,
                                   skip_group_check=True),
                         sem="pmm" if s == NS - 1 else None)
                P.wait("vector", "pmm", P.count("pmm"))
                P.op("vector", lambda v, o=lmax, i_=ps_lg:
                     v.reduce_max(o, i_, axis=AX.X, negate=True))
                P.op("vector", lambda v, o=logit, i_=ps_lg:
                     v.tensor_copy(o, i_), sem="pvec")
                P.wait("scalar", "pvec", P.count("pvec"))
                P.op("scalar", lambda sc, o=logit, i_=logit, b_=lmax, m=lsum:
                     sc.activation(o, i_, AF.Exp, bias=b_, accum_out=m),
                     sem="pscl")
                P.wait("vector", "pscl", P.count("pscl"))
                P.op("vector", lambda v, o=wsel: v.memset(o, 0.0), sem="pvec")
                for r_ in range(TOPK):
                    P.wait("vector", "pvec")
                    P.op("vector", lambda v, o=lmax, i_=logit:
                         v.reduce_max(o, i_, axis=AX.X), sem="pvec")
                    P.wait("vector", "pvec")
                    P.op("vector", lambda v, o=eq, i_=logit, m=lmax:
                         v.tensor_scalar(o, i_, m, None, ALU.is_equal),
                         sem="pvec")
                    P.wait("vector", "pvec")
                    P.op("vector", lambda v, o=idxm, i_=eq, m=c.s_iotam64:
                         v.tensor_tensor(o, i_, m, ALU.mult), sem="pvec")
                    P.wait("vector", "pvec")
                    P.op("vector", lambda v, o=idxm, i_=idxm:
                         v.tensor_scalar(o, i_, 64.0, None, ALU.add),
                         sem="pvec")
                    P.wait("vector", "pvec")
                    P.op("vector", lambda v, o=srow, i_=idxm:
                         v.tensor_reduce(o, i_, axis=AX.X, op=ALU.min),
                         sem="pvec")
                    P.wait("vector", "pvec")
                    P.op("vector", lambda v, o=eq, i_=c.s_iota32, m=srow:
                         v.tensor_scalar(o, i_, m, None, ALU.is_equal),
                         sem="pvec")
                    P.wait("vector", "pvec")
                    P.op("vector", lambda v, o=wsel, i_=eq, m=lmax, a=wsel:
                         v.scalar_tensor_tensor(o, i_, m, a, ALU.mult, ALU.add))
                    P.op("vector", lambda v, o=logit, i_=eq, a=logit:
                         v.scalar_tensor_tensor(o, i_, -2.0, a, ALU.mult,
                                                ALU.add), sem="pvec")
                P.wait("vector", "pvec")
                P.op("vector", lambda v, o=rrow, i_=lsum: v.reciprocal(o, i_),
                     sem="pvec")
                P.wait("vector", "pvec")
                P.op("vector", lambda v, o=wsel, i_=wsel, m=rrow:
                     v.tensor_scalar(o, i_, m, None, ALU.mult), sem="pvec")
                c.prep_psum_free = P.count("pvec")
                yield
                P.wait("tensor", "pvec", P.count("pvec"))
                P.wait("tensor", "vec", P.count("vec"))
                P.op("tensor", lambda te, o=ps_wt, i_=wsel, idn=c.s_ident_f:
                     te.transpose(o, i_, idn), sem="pmm")
                P.wait("vector", "pmm", P.count("pmm"))
                P.op("vector",
                     lambda v, o=wt32[gb][0:32, ch * CH + tt * 128:
                                          ch * CH + tt * 128 + 128],
                     i_=ps_wt: v.tensor_copy(o, i_), sem="pvec")
                c.prep_psum_free = P.count("pvec")
                yield
        ln1t_ready[g] = (P.count("pvec"), P.count("pmm"))

    def emit_waves(g, gb, gen):
        t0g = g * GT
        P.op("vector", lambda v, o=yacc: v.memset(o, 0.0), sem="vec")
        P.wait("sync", "mm", P.count("mm"))
        w13_done = [load_w13(0, 0), load_w13(1, 1)]
        w2_done = [load_w2(0, 0), load_w2(1, 1)]
        pool_done = {}
        w2mm_done = {}

        def h_phase(pw, ch, ei):
            gci = pw * NCH + ch
            slot = gci % 2
            mwslot = slot * 2 + ei
            pe = 2 * pw + ei
            ch0 = ch * CH
            if ch == 0 and ei == 0 and pw == 0:
                P.wait("tensor", "pvec", ln1t_ready[g][0])
            if ch == 0 and ei == 0:
                P.wait("vector", "mm", P.count("mm"))
                for e2 in range(2):
                    if 2 * pw + e2 < NE:
                        P.op("vector", lambda v, o=onehot_dyn[e2],
                             i_=c.s_iotap, val=float(2 * pw + e2):
                             v.tensor_single_scalar(o, i_, val, ALU.is_equal),
                             sem="vec" if e2 == 1 else None)
                P.wait("tensor", "vec", P.count("vec"))
            P.wait("tensor", "scl", P.count("scl"))
            P.wait("tensor", "vec", P.count("vec"))
            P.wait("tensor", "pvec", c.prep_psum_free)
            if pe < NE:
                P.op("tensor", lambda te, o=ps_h3[:, 3 * CH:4 * CH],
                     l=onehot_dyn[ei],
                     r=wt32[gb][0:32, ch0:ch0 + CH]:
                     te.matmul(o, l, r, start=True, stop=True,
                               skip_group_check=True), sem="mm")
                P.wait("vector", "mm", P.count("mm"))
                P.op("vector", lambda v, o=wbsb[ei], i_=ps_h3[:, 3 * CH:4 * CH]:
                     v.tensor_copy(o, i_), sem="wbs")
            if ch == 0:
                P.wait("tensor", "ld_a", w13_done[ei][0])
            for hc in range(4):
                for s in range(NS):
                    P.op("tensor",
                         lambda te, o=ps_h1[:, hc * CH:(hc + 1) * CH],
                         l=wv[ei]["w1"][:, s * 512 + hc * 128:
                                        s * 512 + hc * 128 + 128],
                         r=ln1t[gb][:, s * GT + ch0:s * GT + ch0 + CH],
                         st=(s == 0), sp=(s == NS - 1):
                         te.matmul(o, l, r, start=st, stop=sp,
                                   skip_group_check=True),
                         sem="mm" if (hc == 3 and s == NS - 1) else None)
            mmh1 = P.count("mm")
            if ch == 0:
                P.wait("tensor", "ld_a", w13_done[ei][1])
            for hc in range(4):
                if hc == 3 and pe < NE:
                    P.wait("tensor", "wbs", P.count("wbs"))
                for s in range(NS):
                    P.op("tensor",
                         lambda te, o=ps_h3[:, hc * CH:(hc + 1) * CH],
                         l=wv[ei]["w3"][:, s * 512 + hc * 128:
                                        s * 512 + hc * 128 + 128],
                         r=ln1t[gb][:, s * GT + ch0:s * GT + ch0 + CH],
                         st=(s == 0), sp=(s == NS - 1):
                         te.matmul(o, l, r, start=st, stop=sp,
                                   skip_group_check=True),
                         sem="mm" if (hc == 3 and s == NS - 1) else None)
            mmn = P.count("mm")
            if ch == NCH - 1 and ei == 1 and 2 * pw + 2 < NPE:
                P.wait("sync", "mm", mmn)
                w13_done[:] = [load_w13(2 * pw + 2, 0),
                               load_w13(2 * pw + 3, 1)]
            P.wait("scalar", "mm", mmh1)
            P.wait("scalar", "pool", P.count("pool"))
            P.op("scalar", lambda sc, o=silu1[ei], i_=ps_h1:
                 sc.activation(o, i_, AF.Silu), sem="scl")
            P.wait("vector", "mm", mmn)
            P.wait("vector", "pool", P.count("pool"))
            if pe < NE:
                P.wait("vector", "wbs", P.count("wbs"))
                P.op("vector",
                     lambda v, o=sl3(h3w[ei], 4), i_=sl3(ps_h3, 4),
                     m=wbsb[ei].unsqueeze(1).broadcast_to((128, 4, CH)):
                     v.tensor_tensor(o, i_, m, ALU.mult), sem="vec")
            else:
                P.op("vector", lambda v, o=h3w[ei], i_=ps_h3:
                     v.tensor_copy(o, i_), sem="vec")
            P.wait("gpsimd", "scl", P.count("scl"))
            P.wait("gpsimd", "vec", P.count("vec"))
            P.op("gpsimd", lambda gp, o=mw[mwslot], i_=silu1[ei], m=h3w[ei]:
                 gp.tensor_tensor(o, i_, m, ALU.mult), sem="pool")
            if ei == 1:
                pool_done[gci] = P.count("pool")

        def w2_batch(pw, ch):
            gci = pw * NCH + ch
            slot = gci % 2
            ch0 = ch * CH
            P.wait("tensor", "pool", pool_done[gci])
            P.wait("tensor", "ld_a", max(w2_done))
            P.wait("tensor", "ev", P.count("ev"))
            for dc in range(NS):
                for ei in range(2):
                    for s in range(4):
                        P.op("tensor",
                             lambda te, o=ps_y[:, dc * CH:(dc + 1) * CH],
                             l=wv[ei]["w2"][:, s * 1024 + dc * 128:
                                            s * 1024 + dc * 128 + 128],
                             r=mw[slot * 2 + ei][:, s * CH:(s + 1) * CH],
                             st=(ei == 0 and s == 0),
                             sp=(ei == 1 and s == 3):
                             te.matmul(o, l, r, start=st, stop=sp,
                                       skip_group_check=True),
                             sem="mm" if (dc == NS - 1 and ei == 1
                                          and s == 3) else None)
            w2mm_done[pw] = P.count("mm")
            P.wait("vector", "mm", P.count("mm"))
            if pw + 1 < NPAIR:
                P.op("vector", lambda v, o=sl3(yacc)[:, :, ch0:ch0 + CH],
                     i_=sl3(yacc)[:, :, ch0:ch0 + CH], p=sl3(ps_y, NS):
                     v.tensor_tensor(o, i_, p, ALU.add), sem="ev")
            else:
                P.wait("sync", "vec", P.count("vec"))
                P.wait("sync", "st_a", P.count("st_a"))
                xl = P.dma("sync", sl3(xchk2),
                           xT_r[:, :, t0g + ch0:t0g + ch0 + CH], "ld_b")
                P.op("vector", lambda v, o=sl3(yacc)[:, :, ch0:ch0 + CH],
                     i_=sl3(yacc)[:, :, ch0:ch0 + CH], p=sl3(ps_y, NS):
                     v.tensor_tensor(o, i_, p, ALU.add), sem="ev")
                P.wait("vector", "ld_b", xl)
                P.wait("vector", "ev", P.count("ev"))
                P.op("vector", lambda v, o=sl3(xchk2), i_=sl3(xchk2),
                     p=sl3(yacc)[:, :, ch0:ch0 + CH]:
                     v.tensor_tensor(o, i_, p, ALU.add), sem="vec")
                P.wait("sync", "vec", P.count("vec"))
                P.dma("sync", xT_r[:, :, t0g + ch0:t0g + ch0 + CH],
                      sl3(xchk2), "st_a")

        def run_w2(task):
            tpw, tch = task
            w2_batch(tpw, tch)
            if tch == NCH - 1 and 2 * tpw + 2 < NPE:
                P.wait("sync", "mm", w2mm_done[tpw])
                w2_done[:] = [load_w2(2 * tpw + 2, 0), load_w2(2 * tpw + 3, 1)]

        pending = []
        for pw in range(NPAIR):
            for ch in range(NCH):
                h_phase(pw, ch, 0)
                if gen is not None:
                    next(gen, None)
                h_phase(pw, ch, 1)
                if gen is not None:
                    next(gen, None)
                pending.append((pw, ch))
                if len(pending) >= 2:
                    run_w2(pending.pop(0))
        while pending:
            run_w2(pending.pop(0))
        if gen is not None:
            for _ in gen:
                pass

    INTERLEAVE = True
    for g in range(NG):
        gb = g % 2
        if g == 0:
            for _ in prep_gen(0, 0):
                pass
        gen = prep_gen(g + 1, (g + 1) % 2) if g + 1 < NG else None
        if not INTERLEAVE and gen is not None:
            for _ in gen:
                pass
            gen = None
        emit_waves(g, gb, gen)
    P.wait("sync", "st_a", P.count("st_a"))
    P.barrier()



def emit_attnb(c):
    P, t = c.P, c.t
    xT_r = dview(t["xT"])
    bT_r = dview(t["bT"])
    hT_r = dview(t["hT"])
    CA = 512
    CB = 256
    NCA = T // CA
    NCB = T // CB

    xa = carve(c.AR1, 41760, 128, NS * CA, F32)
    hca = carve(c.AR2, 32768, 128, NS * CA, F32)
    sqf = hca
    st1a = carve(c.AR2, 49152, 1, CA, F32)
    st2a = carve(c.AR2, 51200, 1, CA, F32)
    ps_a = c.psF[:, 3072:4096]

    b1a_store = {}

    def emit_b1a(ch):
        t0 = ch * CA
        P.wait("sync", "pvec", P.count("pvec"))
        P.wait("sync", "st_a", P.count("st_a"))
        xl = P.dma("sync", sl3(xa), xT_r[:, :, t0:t0 + CA], "ld_b")
        for round_ in range(2):
            src = xa
            P.wait("tensor", "ld_b", xl)
            P.wait("tensor", "pvec", P.count("pvec"))
            P.wait("tensor", "pscl", P.count("pscl"))
            for s in range(NS):
                P.op("tensor", lambda te, o=ps_a[0:1, 0:CA], l=c.s_ones,
                     r=src[:, s * CA:(s + 1) * CA], st=(s == 0), sp=(s == NS - 1):
                     te.matmul(o, l, r, start=st, stop=sp, skip_group_check=True),
                     sem="pmm" if s == NS - 1 else None)
            P.wait("vector", "pmm", P.count("pmm"))
            P.op("vector", lambda v, o=st1a, i_=ps_a[0:1, 0:CA]:
                 v.tensor_scalar(o, i_, -1.0 / D, None, ALU.mult), sem="pvec")
            for s in range(NS):
                P.op("vector", lambda v, o=sqf[:, s * CA:(s + 1) * CA],
                     i_=src[:, s * CA:(s + 1) * CA]:
                     v.tensor_mul(o, i_, i_), sem="pvec" if s == NS - 1 else None)
            P.wait("tensor", "pvec", P.count("pvec"))
            for s in range(NS):
                P.op("tensor", lambda te, o=ps_a[0:1, CA:2 * CA], l=c.s_ones,
                     r=sqf[:, s * CA:(s + 1) * CA], st=(s == 0), sp=(s == NS - 1):
                     te.matmul(o, l, r, start=st, stop=sp, skip_group_check=True),
                     sem="pmm" if s == NS - 1 else None)
            P.wait("vector", "pmm", P.count("pmm"))
            P.op("vector", lambda v, o=st2a, i_=st1a: v.tensor_mul(o, i_, i_),
                 sem="pvec")
            P.wait("vector", "pvec")
            P.op("vector", lambda v, o=st2a, i_=ps_a[0:1, CA:2 * CA], m=st2a:
                 v.scalar_tensor_tensor(o, i_, 1.0 / D, m, ALU.mult,
                                        ALU.subtract), sem="pvec")
            P.wait("scalar", "pvec", P.count("pvec"))
            P.op("scalar", lambda sc, o=st2a, i_=st2a, e=c.s_eps_row:
                 sc.activation(o, i_, AF.Sqrt, bias=e), sem="pscl")
            P.wait("vector", "pscl", P.count("pscl"))
            P.op("vector", lambda v, o=st2a, i_=st2a:
                 v.reciprocal(o, i_), sem="pvec")
            P.wait("tensor", "pvec", P.count("pvec"))
            P.op("tensor", lambda te, o=ps_a[:, 0:CA], l=c.s_ones_row,
                 r=st1a: te.matmul(o, l, r, start=True, stop=True,
                                   skip_group_check=True))
            P.op("tensor", lambda te, o=ps_a[:, CA:2 * CA], l=c.s_ones_row,
                 r=st2a: te.matmul(o, l, r, start=True, stop=True,
                                   skip_group_check=True), sem="pmm")
            P.wait("vector", "pmm", P.count("pmm"))
            dst = xa if round_ == 0 else hca
            if round_ == 1:
                P.wait("vector", "st_a", P.count("st_a"))
            for s in range(NS):
                P.op("vector", lambda v, o=dst[:, s * CA:(s + 1) * CA],
                     i_=src[:, s * CA:(s + 1) * CA], m=ps_a[:, 0:CA]:
                     v.tensor_tensor(o, i_, m, ALU.add))
            for s in range(NS):
                P.op("vector", lambda v, o=dst[:, s * CA:(s + 1) * CA],
                     i_=dst[:, s * CA:(s + 1) * CA], m=ps_a[:, CA:2 * CA]:
                     v.tensor_tensor(o, i_, m, ALU.mult),
                     sem="pvec" if s == NS - 1 else None)
            if round_ == 1:
                for s in range(NS):
                    P.op("vector", lambda v, o=hca[:, s * CA:(s + 1) * CA],
                         i_=hca[:, s * CA:(s + 1) * CA],
                         w_=c.s_vec["ln1_w"][:, s:s + 1],
                         b_=c.s_vec["ln1_b"][:, s:s + 1]:
                         v.tensor_scalar(o, i_, w_, b_, ALU.mult, ALU.add),
                         sem="pvec" if s == NS - 1 else None)
        P.wait("sync", "pvec", P.count("pvec"))
        P.dma("sync", bT_r[:, :, t0:t0 + CA], sl3(xa), "st_a")
        P.dma("sync", hT_r[:, :, t0:t0 + CA], sl3(hca), "st_a")
        b1a_store[ch] = P.count("st_a")

    wk_sb = carve(c.AR3, 0, 128, NS * 1024, BF16)
    wvv_sb = carve(c.AR3, 16384, 128, NS * 1024, BF16)
    wr_sb = carve(c.AR3, 32768, 128, NS * 1024, BF16)
    wo_sb = carve(c.AR3, 49152, 128, NS * 1024, BF16)
    for nm, sb in (("wkT_bf", wk_sb), ("wvT_bf", wvv_sb), ("wrT_bf", wr_sb),
                   ("woT_bf", wo_sb)):
        P.dma("sync", sl3(sb), dview(t[nm]), "ld_a")
    wld = P.count("ld_a")

    hcc0 = carve(c.AR1, 0, 128, NS * 257, F32)
    hcc = [hcc0, hcc0]
    kf = [carve(c.AR1, 8224, 128, NS * CB, F32),
          carve(c.AR1, 16416, 128, NS * CB, F32)]
    vf = [carve(c.AR1, 24608, 128, NS * CB, F32),
          carve(c.AR1, 32800, 128, NS * CB, F32)]
    srf = [carve(c.AR2, 0, 128, NS * CB, F32),
           carve(c.AR2, 8192, 128, NS * CB, F32)]
    xkb = [carve(c.AR2, 16384, 128, NS * CB, BF16),
           carve(c.AR2, 20480, 128, NS * CB, BF16)]
    xvb = [carve(c.AR2, 24576, 128, NS * CB, BF16),
           carve(c.AR2, 53248, 128, NS * CB, BF16)]
    xrb = [carve(c.AR2, 28672, 128, NS * CB, BF16),
           carve(c.AR2, 57344, 128, NS * CB, BF16)]
    e0 = carve(c.AR2, 61440, 128, CB, F32)
    e1 = carve(c.AR2, 62464, 128, CB, F32)
    ffb = carve(c.AR2, 63488, 128, CB, F32)
    yA = carve(c.AR4, 0, 128, CB + 1, F32)
    yB = carve(c.AR4, 1056, 128, CB + 1, F32)
    den = carve(c.AR4, 2112, 128, CB, F32)
    rwk = [carve(c.AR4, 3136, 128, NS * CB, BF16),
           carve(c.AR4, 7232, 128, NS * CB, BF16)]
    e0b2 = [carve(c.AR4, 11328, 128, CB, F32),
            carve(c.AR4, 12352, 128, CB, F32)]
    e1b2 = [carve(c.AR4, 13376, 128, CB, F32),
            carve(c.AR4, 14400, 128, CB, F32)]
    btc = carve(c.AR1, 58144, 128, 4 * CB, F32)
    ps_m = c.psF[:, 0:2048]

    b1b_state = {"mix": {}, "k": {}, "v": {}, "r": {}, "wkv": {}}

    lamb = carve(c.AR2, 64512, 128, NS, F32)
    kru = carve(c.AR2, 64544, 128, NS, F32)
    Ah = carve(c.AR2, 64576, 128, NS, F32)
    Bh = carve(c.AR2, 64608, 128, NS, F32)
    Mh = carve(c.AR2, 64640, 128, NS, F32)
    mu8 = carve(c.AR2, 64672, 128, NS, F32)
    nmu8 = carve(c.AR2, 64704, 128, NS, F32)
    sc8 = carve(c.AR2, 64736, 128, NS, F32)
    tf8 = carve(c.AR2, 64768, 128, NS, F32)

    P.wait("scalar", "ld_a", wld)
    P.op("scalar", lambda sc, o=lamb, i_=c.s_vec["time_decay"]:
         sc.activation(o, i_, AF.Exp), sem="scl")
    P.wait("scalar", "scl", P.count("scl"))
    P.op("scalar", lambda sc, o=lamb, i_=lamb:
         sc.activation(o, i_, AF.Exp, scale=-1.0), sem="scl")
    P.op("vector", lambda v, o=kru, i_=c.s_vec["time_first"]:
         v.tensor_scalar(o, i_, 0.0, None, ALU.max))
    P.op("vector", lambda v, o=Ah: v.memset(o, 0.0))
    P.op("vector", lambda v, o=Bh: v.memset(o, 0.0))
    P.op("vector", lambda v, o=Mh: v.memset(o, -1e30), sem="vec")

    def b1b_mix(ch):
        b = ch % 2
        t0 = ch * CB
        P.wait("sync", "vec", b1b_state["mix"].get(ch - 1, 0))
        P.wait("sync", "st_a", b1a_store[(t0 + CB - 1) // CA])
        if ch == 0:
            P.op("vector", lambda v, o=hcc[0]: v.memset(o, 0.0), sem="vec")
            P.wait("sync", "vec", P.count("vec"))
            hl = P.dma("sync", sl3(hcc[0], NS)[:, :, 1:CB + 1],
                       hT_r[:, :, 0:CB], "ld_a")
        else:
            hl = P.dma("sync", sl3(hcc[b], NS), hT_r[:, :, t0 - 1:t0 + CB],
                       "ld_a")
        P.wait("vector", "ld_a", hl)
        P.wait("vector", "mm", b1b_state["r"].get(ch - 2, 0))
        W1 = CB + 1
        for nm, dst in (("att_tmk", xkb), ("att_tmv", xvb), ("att_tmr", xrb)):
            for s in range(NS):
                P.op("vector",
                     lambda v, o=ffb, h_=hcc[b][:, s * W1 + 1:s * W1 + W1],
                     hh=hcc[b][:, s * W1:s * W1 + CB]:
                     v.tensor_sub(o, h_, hh))
                P.op("vector",
                     lambda v, o=dst[b][:, s * CB:(s + 1) * CB], i_=ffb,
                     w_=c.s_vec[nm][:, s:s + 1],
                     hh=hcc[b][:, s * W1:s * W1 + CB]:
                     v.scalar_tensor_tensor(o, i_, w_, hh, ALU.mult, ALU.add),
                     sem="vec" if (nm == "att_tmr" and s == NS - 1) else None)
        b1b_state["mix"][ch] = P.count("vec")

    def b1b_mms(ch):
        b = ch % 2
        for which, src_b, wsb, dstf in (("k", xkb, wk_sb, kf),
                                        ("v", xvb, wvv_sb, vf),
                                        ("r", xrb, wr_sb, srf)):
            P.wait("tensor", "vec", b1b_state["mix"][ch])
            P.wait("tensor", "ev", P.count("ev"))
            P.wait("tensor", "vec", b1b_state["wkv"].get(ch - 2, 0))
            for oc in range(NS):
                for s in range(NS):
                    P.op("tensor",
                         lambda te, o=ps_m[:, oc * CB:oc * CB + CB],
                         l=wsb[:, s * 1024 + oc * 128:s * 1024 + oc * 128 + 128],
                         r=src_b[b][:, s * CB:(s + 1) * CB],
                         st=(s == 0), sp=(s == NS - 1):
                         te.matmul(o, l, r, start=st, stop=sp,
                                   skip_group_check=True),
                         sem="mm" if (oc == NS - 1 and s == NS - 1) else None)
            P.wait("scalar", "mm", P.count("mm"))
            if which == "r":
                P.op("scalar", lambda sc, o=srf[b], i_=ps_m:
                     sc.activation(o, i_, AF.Sigmoid), sem="ev")
            else:
                P.op("scalar", lambda sc, o=dstf[b], i_=ps_m:
                     sc.copy(o, i_), sem="ev")
            b1b_state[which][ch] = P.count("mm")
            P.wait("tensor", "ev", P.count("ev"))

    def b1b_wkv(ch):
        b = ch % 2
        P.wait("vector", "ev", P.count("ev"))
        for s in range(NS):
            P.op("vector", lambda v, o=mu8[:, s:s + 1],
                 i_=kf[b][:, s * CB:(s + 1) * CB]:
                 v.reduce_max(o, i_, axis=AX.X),
                 sem="wbv" if s == NS - 1 else None)
        P.wait("vector", "wbv")
        P.op("vector", lambda v, o=mu8, i_=mu8, m=kru:
             v.tensor_tensor(o, i_, m, ALU.add))
        P.op("vector", lambda v, o=sc8, i_=Mh:
             v.tensor_scalar(o, i_, -CAP_DROP, None, ALU.add), sem="wbv")
        P.wait("vector", "wbv")
        P.op("vector", lambda v, o=mu8, i_=mu8, m=sc8:
             v.tensor_tensor(o, i_, m, ALU.max), sem="wbv")
        P.wait("vector", "wbv")
        P.op("vector", lambda v, o=nmu8, i_=mu8:
             v.tensor_scalar(o, i_, -1.0, None, ALU.mult))
        P.op("vector", lambda v, o=sc8, i_=Mh, m=mu8:
             v.tensor_tensor(o, i_, m, ALU.subtract))
        P.op("vector", lambda v, o=tf8, i_=c.s_vec["time_first"][:, 0:NS],
             m=mu8: v.tensor_tensor(o, i_, m, ALU.subtract), sem="vec")
        P.wait("scalar", "vec", P.count("vec"))
        P.op("scalar", lambda sc, o=sc8, i_=sc8:
             sc.activation(o, i_, AF.Exp), sem="scl")
        P.wait("vector", "scl", P.count("scl"))
        P.op("vector", lambda v, o=Ah, i_=Ah, m=sc8:
             v.tensor_tensor(o, i_, m, ALU.mult))
        P.op("vector", lambda v, o=Bh, i_=Bh, m=sc8:
             v.tensor_tensor(o, i_, m, ALU.mult))
        P.op("vector", lambda v, o=Mh, i_=mu8:
             v.tensor_copy(o, i_), sem="wbv")
        P.wait("vector", "wbv")
        ev_slab = {}
        for s in range(NS):
            eb = s % 2
            kfs = kf[b][:, s * CB:(s + 1) * CB]
            vfs = vf[b][:, s * CB:(s + 1) * CB]
            if s >= 2:
                P.wait("scalar", "vec", ev_slab[s - 2])
            P.op("scalar", lambda sc, o=e0b2[eb], i_=kfs, b_=nmu8[:, s:s + 1]:
                 sc.activation(o, i_, AF.Exp, bias=b_))
            P.op("scalar", lambda sc, o=e1b2[eb], i_=kfs, b_=tf8[:, s:s + 1]:
                 sc.activation(o, i_, AF.Exp, bias=b_), sem="scl")
            P.wait("vector", "scl", P.count("scl"))
            P.op("vector", lambda v, o=ffb, i_=e0b2[eb], m=vfs:
                 v.tensor_tensor(o, i_, m, ALU.mult))
            P.op("vector", lambda v, o=yA[:, 0:1], i_=Ah[:, s:s + 1]:
                 v.tensor_copy(o, i_))
            P.op("vector", lambda v, o=yB[:, 0:1], i_=Bh[:, s:s + 1]:
                 v.tensor_copy(o, i_), sem="wbv")
            P.wait("vector", "wbv")
            P.op("vector", lambda v, o=yA[:, 1:CB + 1],
                 d0=lamb[:, s:s + 1].broadcast_to((128, CB)), d1=ffb,
                 init=Ah[:, s:s + 1]:
                 v.tensor_tensor_scan(o, d0, d1, init, ALU.mult, ALU.add))
            P.op("vector", lambda v, o=yB[:, 1:CB + 1],
                 d0=lamb[:, s:s + 1].broadcast_to((128, CB)), d1=e0b2[eb],
                 init=Bh[:, s:s + 1]:
                 v.tensor_tensor_scan(o, d0, d1, init, ALU.mult, ALU.add),
                 sem="vec")
            P.wait("vector", "vec")
            P.op("vector", lambda v, o=den, i_=yB[:, 0:CB], m=e1b2[eb]:
                 v.tensor_tensor(o, i_, m, ALU.add))
            P.op("vector", lambda v, o=den, i_=den: v.reciprocal(o, i_))
            P.op("vector", lambda v, o=e1b2[eb], i_=e1b2[eb], m=vfs:
                 v.tensor_tensor(o, i_, m, ALU.mult))
            P.op("vector", lambda v, o=e1b2[eb], i_=e1b2[eb], m=yA[:, 0:CB]:
                 v.tensor_tensor(o, i_, m, ALU.add), sem="wbv")
            P.wait("vector", "wbv")
            P.op("vector", lambda v, o=e1b2[eb], i_=e1b2[eb], m=den:
                 v.tensor_tensor(o, i_, m, ALU.mult))
            P.op("vector", lambda v, o=rwk[b][:, s * CB:(s + 1) * CB],
                 i_=e1b2[eb], m=srf[b][:, s * CB:(s + 1) * CB]:
                 v.tensor_tensor(o, i_, m, ALU.mult))
            P.op("vector", lambda v, o=Ah[:, s:s + 1], i_=yA[:, CB:CB + 1]:
                 v.tensor_copy(o, i_))
            P.op("vector", lambda v, o=Bh[:, s:s + 1], i_=yB[:, CB:CB + 1]:
                 v.tensor_copy(o, i_), sem="vec")
            ev_slab[s] = P.count("vec")
        b1b_state["wkv"][ch] = P.count("vec")

    def b1b_wo(ch):
        b = ch % 2
        t0 = ch * CB
        P.wait("tensor", "vec", b1b_state["wkv"][ch])
        P.wait("tensor", "ev", P.count("ev"))
        for oc in range(NS):
            for s in range(NS):
                P.op("tensor",
                     lambda te, o=ps_m[:, oc * CB:oc * CB + CB],
                     l=wo_sb[:, s * 1024 + oc * 128:s * 1024 + oc * 128 + 128],
                     r=rwk[b][:, s * CB:(s + 1) * CB],
                     st=(s == 0), sp=(s == NS - 1):
                     te.matmul(o, l, r, start=st, stop=sp,
                               skip_group_check=True),
                     sem="mm" if (oc == NS - 1 and s == NS - 1) else None)
        mmw = P.count("mm")
        for hh2 in range(2):
            P.wait("sync", "ev", P.count("ev"))
            bl = P.dma("sync", sl3(btc, 4),
                       bT_r[:, hh2 * 4:(hh2 + 1) * 4, t0:t0 + CB], "ld_b")
            P.wait("vector", "mm", mmw)
            P.wait("vector", "ld_b", bl)
            P.op("vector", lambda v, o=btc, i_=btc,
                 p=ps_m[:, hh2 * 1024:(hh2 + 1) * 1024]:
                 v.tensor_tensor(o, i_, p, ALU.add), sem="ev")
            P.wait("sync", "ev", P.count("ev"))
            P.dma("sync", bT_r[:, hh2 * 4:(hh2 + 1) * 4, t0:t0 + CB],
                  sl3(btc, 4), "st_a")

    emit_b1a(0)
    for ch in range(NCB):
        if ch % 2 == 0 and ch // 2 + 1 < NCA:
            emit_b1a(ch // 2 + 1)
        b1b_mix(ch)
        b1b_mms(ch)
        b1b_wkv(ch)
        b1b_wo(ch)
    P.barrier()


    CH2 = 256
    NCH2 = T // CH2
    wkey_sb = carve(c.AR3, 0, 128, NS * FFN, BF16)
    wval_sb = carve(c.AR1, 0, 128, 32 * 1024, BF16)
    wrec_sb = carve(c.AR4, 0, 128, NS * 1024, BF16)
    P.dma("sync", sl3(wkey_sb), dview(t["wkeyT_bf"]), "ld_a")
    wkey_n = P.count("ld_a")
    P.dma("sync", sl3(wval_sb, s=32),
          t["wvalT_bf"].rearrange("(s p) q -> p s q", p=128), "ld_a")
    wval_n = P.count("ld_a")
    P.dma("sync", sl3(wrec_sb), dview(t["wrecT_bf"]), "ld_a")
    wrec_n = P.count("ld_a")

    bcc = carve(c.AR2, 0, 128, NS * 257, F32)
    h2c = carve(c.AR2, 8224, 128, NS * 257, F32)
    sq2 = carve(c.AR2, 16448, 128, NS * CH2, F32)
    xk2 = carve(c.AR2, 24640, 128, NS * CH2, BF16)
    xr2 = carve(c.AR2, 28736, 128, NS * CH2, BF16)
    kkr = carve(c.AR2, 32832, 128, CH2, F32)
    kkr_b = carve(c.AR2, 58432, 128, CH2, F32)
    kk2 = carve(c.AR2, 33856, 128, 32 * CH2, BF16)
    rf = carve(c.AR2, 50240, 128, NS * CH2, F32)
    st1b = carve(c.AR2, 62528, 1, 257, F32)
    st2b = carve(c.AR2, 63680, 1, 257, F32)
    ps_kk = [c.psF[:, 0:256], c.psF[:, 512:768]]
    ps_kv = c.psF[:, 2048:4096]
    ps_rr = c.psF[:, 1024:2048]
    ps_stb = c.psF[:, 0:1024]

    for ch in range(NCH2):
        t0 = ch * CH2
        P.wait("sync", "vec", P.count("vec"))
        if ch == 0:
            P.op("vector", lambda v, o=bcc: v.memset(o, 0.0), sem="vec")
            P.wait("sync", "vec", P.count("vec"))
            bl = P.dma("sync", sl3(bcc, NS)[:, :, 1:257], bT_r[:, :, 0:CH2],
                       "ld_a")
        else:
            bl = P.dma("sync", sl3(bcc, NS), bT_r[:, :, t0 - 1:t0 + CH2], "ld_a")
        P.wait("tensor", "ld_a", bl)
        P.wait("tensor", "vec", P.count("vec"))
        P.wait("tensor", "scl", P.count("scl"))
        W = 257
        for s in range(NS):
            P.op("tensor", lambda te, o=ps_stb[0:1, 0:W], l=c.s_ones,
                 r=bcc[:, s * 257:(s + 1) * 257], st=(s == 0), sp=(s == NS - 1):
                 te.matmul(o, l, r, start=st, stop=sp, skip_group_check=True),
                 sem="mm" if s == NS - 1 else None)
        P.wait("vector", "mm", P.count("mm"))
        P.op("vector", lambda v, o=st1b, i_=ps_stb[0:1, 0:W]:
             v.tensor_scalar(o, i_, -1.0 / D, None, ALU.mult), sem="vec")
        for s in range(NS):
            P.op("vector", lambda v, o=h2c[:, s * 257:(s + 1) * 257],
                 i_=bcc[:, s * 257:(s + 1) * 257]:
                 v.tensor_mul(o, i_, i_), sem="vec" if s == NS - 1 else None)
        P.wait("tensor", "vec", P.count("vec"))
        for s in range(NS):
            P.op("tensor", lambda te, o=ps_stb[0:1, 512:512 + W], l=c.s_ones,
                 r=h2c[:, s * 257:(s + 1) * 257], st=(s == 0), sp=(s == NS - 1):
                 te.matmul(o, l, r, start=st, stop=sp, skip_group_check=True),
                 sem="mm" if s == NS - 1 else None)
        P.wait("vector", "mm", P.count("mm"))
        P.op("vector", lambda v, o=st2b, i_=st1b: v.tensor_mul(o, i_, i_),
             sem="wbv")
        P.wait("vector", "wbv")
        P.op("vector", lambda v, o=st2b, i_=ps_stb[0:1, 512:512 + W], m=st2b:
             v.scalar_tensor_tensor(o, i_, 1.0 / D, m, ALU.mult, ALU.subtract),
             sem="vec")
        P.wait("scalar", "vec", P.count("vec"))
        P.op("scalar", lambda sc, o=st2b, i_=st2b, e=c.s_eps_row:
             sc.activation(o, i_, AF.Sqrt, bias=e), sem="scl")
        P.wait("vector", "scl", P.count("scl"))
        P.op("vector", lambda v, o=st2b, i_=st2b: v.reciprocal(o, i_), sem="vec")
        P.wait("tensor", "vec", P.count("vec"))
        P.op("tensor", lambda te, o=ps_stb[:, 0:W].opt(), l=c.s_ones_row,
             r=st1b: te.matmul(o, l, r, start=True, stop=True,
                               skip_group_check=True))
        P.op("tensor", lambda te, o=ps_stb[:, 512:512 + W], l=c.s_ones_row,
             r=st2b: te.matmul(o, l, r, start=True, stop=True,
                               skip_group_check=True), sem="mm")
        P.wait("vector", "mm", P.count("mm"))
        for s in range(NS):
            P.op("vector", lambda v, o=h2c[:, s * 257:(s + 1) * 257],
                 i_=bcc[:, s * 257:(s + 1) * 257], m=ps_stb[:, 0:W]:
                 v.tensor_tensor(o, i_, m, ALU.add))
        for s in range(NS):
            P.op("vector", lambda v, o=h2c[:, s * 257:(s + 1) * 257],
                 i_=h2c[:, s * 257:(s + 1) * 257], m=ps_stb[:, 512:512 + W]:
                 v.tensor_tensor(o, i_, m, ALU.mult))
        for s in range(NS):
            P.op("vector", lambda v, o=h2c[:, s * 257:(s + 1) * 257],
                 i_=h2c[:, s * 257:(s + 1) * 257],
                 w_=c.s_vec["ln2_w"][:, s:s + 1], b_=c.s_vec["ln2_b"][:, s:s + 1]:
                 v.tensor_scalar(o, i_, w_, b_, ALU.mult, ALU.add),
                 sem="vec" if s == NS - 1 else None)
        if ch == 0:
            P.op("vector", lambda v, o=sl3(h2c, NS)[:, :, 0:1]:
                 v.memset(o, 0.0), sem="wbv")
            P.wait("vector", "wbv")
        for nm, dst in (("ffn_tmk", xk2), ("ffn_tmr", xr2)):
            for s in range(NS):
                P.op("vector",
                     lambda v, o=kkr, h_=h2c[:, s * 257 + 1:s * 257 + 257],
                     hh=h2c[:, s * 257:s * 257 + 256]:
                     v.tensor_sub(o, h_, hh))
                P.op("vector",
                     lambda v, o=dst[:, s * CH2:(s + 1) * CH2], i_=kkr,
                     w_=c.s_vec[nm][:, s:s + 1],
                     hh=h2c[:, s * 257:s * 257 + 256]:
                     v.scalar_tensor_tensor(o, i_, w_, hh, ALU.mult, ALU.add),
                     sem="vec" if (nm == "ffn_tmr" and s == NS - 1) else None)
        mixn = P.count("vec")
        P.wait("tensor", "vec", mixn)
        if ch == 0:
            P.wait("tensor", "ld_a", wkey_n)
        ev_after = {}
        kkr2 = [kkr, kkr_b]
        for oc in range(32):
            pb = oc % 2
            if oc >= 2:
                P.wait("tensor", "ev", ev_after[oc - 2])
            for s in range(NS):
                P.op("tensor", lambda te, o=ps_kk[pb],
                     l=wkey_sb[:, s * FFN + oc * 128:s * FFN + oc * 128 + 128],
                     r=xk2[:, s * CH2:(s + 1) * CH2],
                     st=(s == 0), sp=(s == NS - 1):
                     te.matmul(o, l, r, start=st, stop=sp,
                               skip_group_check=True),
                     sem="mm" if s == NS - 1 else None)
            P.wait("scalar", "mm", P.count("mm"))
            if oc >= 2:
                P.wait("scalar", "ev", ev_after[oc - 2])
            P.op("scalar", lambda sc, o=kkr2[pb], i_=ps_kk[pb]:
                 sc.activation(o, i_, AF.Relu), sem="scl")
            P.wait("vector", "scl", P.count("scl"))
            P.op("vector", lambda v, o=kk2[:, oc * CH2:(oc + 1) * CH2],
                 i_=kkr2[pb], m=kkr2[pb]:
                 v.tensor_tensor(o, i_, m, ALU.mult), sem="ev")
            ev_after[oc] = P.count("ev")
        P.wait("tensor", "ev", P.count("ev"))
        if ch == 0:
            P.wait("tensor", "ld_a", wval_n)
        for oc in range(NS):
            for s in range(32):
                P.op("tensor", lambda te, o=ps_kv[:, oc * 256:oc * 256 + 256],
                     l=wval_sb[:, s * 1024 + oc * 128:s * 1024 + oc * 128 + 128],
                     r=kk2[:, s * CH2:(s + 1) * CH2],
                     st=(s == 0), sp=(s == 31):
                     te.matmul(o, l, r, start=st, stop=sp,
                               skip_group_check=True),
                     sem="mm" if (oc == NS - 1 and s == 31) else None)
        rr_slots = [c.psF[:, 1024:1280], c.psF[:, 1536:1792]]
        if ch == 0:
            P.wait("tensor", "ld_a", wrec_n)
        scl_after_r = {}
        for oc in range(NS):
            pb = oc % 2
            if oc >= 2:
                P.wait("tensor", "scl", scl_after_r[oc - 2])
            for s in range(NS):
                P.op("tensor", lambda te, o=rr_slots[pb],
                     l=wrec_sb[:, s * 1024 + oc * 128:s * 1024 + oc * 128 + 128],
                     r=xr2[:, s * CH2:(s + 1) * CH2],
                     st=(s == 0), sp=(s == NS - 1):
                     te.matmul(o, l, r, start=st, stop=sp,
                               skip_group_check=True),
                     sem="mm" if s == NS - 1 else None)
            P.wait("scalar", "mm", P.count("mm"))
            P.op("scalar", lambda sc, o=rf[:, oc * CH2:(oc + 1) * CH2],
                 i_=rr_slots[pb]:
                 sc.activation(o, i_, AF.Sigmoid), sem="scl")
            scl_after_r[oc] = P.count("scl")
        P.wait("tensor", "scl", P.count("scl"))
        P.wait("sync", "scl", P.count("scl"))
        P.wait("sync", "st_a", P.count("st_a"))
        xl = P.dma("sync", sl3(sq2), xT_r[:, :, t0:t0 + CH2], "ld_b")
        P.wait("vector", "mm", P.count("mm"))
        P.wait("vector", "scl", P.count("scl"))
        P.wait("vector", "ld_b", xl)
        P.op("vector", lambda v, o=rf, i_=rf, p=ps_kv:
             v.tensor_tensor(o, i_, p, ALU.mult), sem="ev")
        P.op("vector", lambda v, o=sq2, i_=sq2,
             p=sl3(bcc, NS)[:, :, 1:257].rearrange("p s c -> p (s c)")
             if False else sq2: v.tensor_copy(o, i_)) if False else None
        for s in range(NS):
            P.op("vector", lambda v, o=sq2[:, s * CH2:(s + 1) * CH2],
                 i_=sq2[:, s * CH2:(s + 1) * CH2],
                 m=bcc[:, s * 257 + 1:s * 257 + 257]:
                 v.tensor_tensor(o, i_, m, ALU.add))
        P.op("vector", lambda v, o=sq2, i_=sq2, m=rf:
             v.tensor_tensor(o, i_, m, ALU.add), sem="vec")
        P.wait("sync", "vec", P.count("vec"))
        P.dma("sync", xT_r[:, :, t0:t0 + CH2], sl3(sq2), "st_a")
        P.wait("tensor", "vec", P.count("vec"))
    P.barrier()



def emit_final(c):
    P, t = c.P, c.t
    xT_r = dview(t["xT"])
    xst = [carve(c.AR1, 0, 128, 1024, F32), carve(c.AR1, 4096, 128, 1024, F32)]
    ost = [carve(c.AR1, 8192, 128, 1024, F32), carve(c.AR1, 12288, 128, 1024, F32)]
    ps32 = c.psF[:, 0:1024]
    for ti in range(T // 128):
        b = ti % 2
        if ti >= 2:
            P.wait("sync", "mm", P.count("mm"))
        xl = P.dma("sync", sl3(xst[b]), xT_r[:, :, ti * 128:(ti + 1) * 128], "ld_a")
        P.wait("tensor", "ld_a", xl)
        P.wait("tensor", "scl", P.count("scl"))
        for s in range(NS):
            P.op("tensor", lambda te, o=ps32[:, s * 128:(s + 1) * 128],
                 i_=xst[b][:, s * 128:(s + 1) * 128], idn=c.s_ident_f:
                 te.transpose(o, i_, idn), sem="mm" if s == NS - 1 else None)
        P.wait("scalar", "mm", P.count("mm"))
        P.wait("scalar", "st_a", P.count("st_a"))
        evn = P.op("scalar", lambda sc, o=ost[b], i_=ps32: sc.copy(o, i_),
                   sem="scl")
        P.wait("sync", "scl", evn)
        P.dma("sync", t["out"][ti * 128:(ti + 1) * 128, :], ost[b], "st_a")
    P.wait("sync", "st_a", P.count("st_a"))



def build_program(phases=ALL_PHASES, debug_names=(), debug_sbuf=()):
    nc = bass.Bass("TRN2", target_bir_lowering=False, debug=False,
                   num_devices=NCORES, dynamic_dma_scratch_size=8192)

    t = {}
    t["x_in"] = nc.dram_tensor("x", [T, D], F32, kind="ExternalInput").ap()
    t["wq_in"] = nc.dram_tensor("wq", [D, D], F32, kind="ExternalInput").ap()
    t["wk_a_in"] = nc.dram_tensor("wk_a", [D, D], F32, kind="ExternalInput").ap()
    t["wv_a_in"] = nc.dram_tensor("wv_a", [D, D], F32, kind="ExternalInput").ap()
    VEC_NAMES = ["ln1_w", "ln1_b", "ln2_w", "ln2_b", "att_tmk", "att_tmv",
                 "att_tmr", "time_decay", "time_first", "ffn_tmk", "ffn_tmr"]
    for nm in VEC_NAMES:
        t[nm + "_in"] = nc.dram_tensor(nm, [128, NS], F32, kind="ExternalInput").ap()
    for nm, shape in [("wkb", [D, D]), ("wvb", [D, D]), ("wrb", [D, D]),
                      ("wob", [D, D]), ("wkey", [FFN, D]), ("wrec", [D, D]),
                      ("wval", [D, FFN]), ("wg", [D, NE]),
                      ("w1", [NE, D, EH]), ("w2", [NE, EH, D]), ("w3", [NE, D, EH]),
                      ("ws1", [D, D]), ("ws2", [D, D]), ("ws3", [D, D])]:
        t[nm + "_in"] = nc.dram_tensor(nm, shape, F32, kind="ExternalInput").ap()
    t["ident_f_in"] = nc.dram_tensor("ident_f", [128, 128], F32, kind="ExternalInput").ap()
    t["ident_b_in"] = nc.dram_tensor("ident_b", [128, 128], BF16, kind="ExternalInput").ap()
    t["ones_in"] = nc.dram_tensor("ones", [128, 128], F32, kind="ExternalInput").ap()
    t["causal_in"] = nc.dram_tensor("causal", [128, 128], F32, kind="ExternalInput").ap()
    t["onehot_in"] = nc.dram_tensor("onehot", [NE, NE * 128], F32, kind="ExternalInput").ap()
    t["iota32_in"] = nc.dram_tensor("iota32", [128, 2 * NE], F32, kind="ExternalInput").ap()
    t["iotap_in"] = nc.dram_tensor("iotap", [32, 128], F32, kind="ExternalInput").ap()
    t["eps_in"] = nc.dram_tensor("epsv", [128, 2], F32, kind="ExternalInput").ap()
    t["out"] = nc.dram_tensor("out", [T, D], F32, kind="ExternalOutput").ap()

    t["xT"] = nc.dram_tensor("xT", [D, T], F32, kind="Internal").ap()
    t["bT"] = nc.dram_tensor("bT", [D, T], F32, kind="Internal").ap()
    t["hT"] = nc.dram_tensor("hT", [D, T], F32, kind="Internal").ap()
    t["qT"] = nc.dram_tensor("qT", [D, T], BF16, kind="Internal").ap()
    for nm, shape in [("wqpm", [128, NS * D]), ("wkapm", [128, NS * D]),
                      ("wvapm", [128, NS * D]),
                      ("wkT_bf", [D, D]), ("wvT_bf", [D, D]), ("wrT_bf", [D, D]),
                      ("woT_bf", [D, D]), ("wrecT_bf", [D, D]),
                      ("wkeyT_bf", [D, FFN]), ("wvalT_bf", [FFN, D]),
                      ("w1pm", [NE, 128, NS * EH]), ("w2pm", [NE, 128, 4 * D]),
                      ("w3pm", [NE, 128, NS * EH]),
                      ("ws1pm", [2, 128, NS * EH]), ("ws2pm", [2, 128, 4 * D]),
                      ("ws3pm", [2, 128, NS * EH])]:
        t[nm] = nc.dram_tensor(nm, shape, BF16, kind="Internal").ap()

    sem_names = ["bar", "ld_a", "ld_b", "st_a", "st_b", "mm", "ev", "vec",
                 "scl", "pool", "wbv", "wbs", "gpc", "pmm", "pvec", "pscl"]

    with contextlib.ExitStack() as stack:
        AR1h = stack.enter_context(nc.sbuf_tensor("AR1", [128, 32768], BF16))
        AR2h = stack.enter_context(nc.sbuf_tensor("AR2", [128, 32768], BF16))
        AR3h = stack.enter_context(nc.sbuf_tensor("AR3", [128, 32768], BF16))
        AR4h = stack.enter_context(nc.sbuf_tensor("AR4", [128, 8192], BF16))
        identfh = stack.enter_context(nc.sbuf_tensor("sb_idf", [128, 128], F32))
        identbh = stack.enter_context(nc.sbuf_tensor("sb_idb", [128, 128], BF16))
        onesh = stack.enter_context(nc.sbuf_tensor("sb_ones", [128, 128], F32))
        causalh = stack.enter_context(nc.sbuf_tensor("sb_causal", [128, 128], F32))
        iotah = stack.enter_context(nc.sbuf_tensor("sb_iota", [128, 2 * NE], F32))
        iotaph = stack.enter_context(nc.sbuf_tensor("sb_iotap", [32, 128], F32))
        epsh = stack.enter_context(nc.sbuf_tensor("sb_eps", [128, 2], F32))
        wgh = stack.enter_context(nc.sbuf_tensor("sb_wg", [128, NS * NE], F32))
        vech = stack.enter_context(nc.sbuf_tensor("sb_vecs", [128, 11 * NS], F32))
        psFh = stack.enter_context(nc.psum_tensor("psF", [128, 4096], F32))

        blk = stack.enter_context(nc.Block())
        sems = {nm: stack.enter_context(nc.semaphore(nm)) for nm in sem_names}

        P = Prog(nc)
        for nm in sem_names:
            P.register_sem(nm, sems[nm])

        c = Ctx()
        c.P, c.t, c.nc = P, t, nc
        c.AR1, c.AR2, c.AR3, c.AR4 = AR1h[:], AR2h[:], AR3h[:], AR4h[:]
        c.LN0T = AR2h[:]
        c.KT = AR3h[:]
        c.V = AR1h[:]
        c.s_ident_f, c.s_ident_bf = identfh[:], identbh[:]
        c.s_ones = onesh[:, 0:1]
        c.s_ones_row = onesh[0:1, :]
        c.s_causal = causalh[:]
        c.s_iota32 = iotah[:, 0:NE]
        c.s_iotap = iotaph[:]
        c.s_iotam64 = iotah[:, NE:2 * NE]
        c.s_eps = epsh[:, 0:1]
        c.s_eps_row = epsh[0:1, 1:2]
        c.WG = wgh[:]
        c.psF = psFh[:]
        c.s_vec = {nm: vech[:, i * NS:(i + 1) * NS]
                   for i, nm in enumerate(VEC_NAMES)}

        P.dma("sync", c.s_ident_f, t["ident_f_in"], "ld_a")
        P.dma("sync", c.s_ident_bf, t["ident_b_in"], "ld_a")
        P.dma("sync", onesh[:], t["ones_in"], "ld_a")
        P.dma("sync", c.s_causal, t["causal_in"], "ld_a")
        P.dma("sync", iotah[:], t["iota32_in"], "ld_a")
        P.dma("sync", iotaph[:], t["iotap_in"], "ld_a")
        P.dma("sync", epsh[:], t["eps_in"], "ld_a")
        for s in range(NS):
            P.dma("sync", wgh[:, s * NE:(s + 1) * NE],
                  t["wg_in"][s * 128:(s + 1) * 128, :], "ld_a")
        for nm in VEC_NAMES:
            P.dma("sync", c.s_vec[nm], t[nm + "_in"], "ld_a")
        P.wait("sync", "ld_a", P.count("ld_a"))
        P.barrier()

        if "prep" in phases:
            emit_prep(c)
        if "a1" in phases:
            emit_a1(c)
        if "a2" in phases:
            emit_a2(c)
        if "a3" in phases:
            emit_a3(c)
        if "moe1" in phases:
            emit_moe(c)
        if "attnb" in phases:
            emit_attnb(c)
        if "moe2" in phases:
            emit_moe(c)
        if "final" in phases:
            emit_final(c)
        else:
            for ti in range(0, T, 128):
                sb = carve(c.AR4, 0, 128, 1024, F32)
                P.wait("sync", "st_b", P.count("st_b"))
                P.dma("sync", sb, t["x_in"][ti:ti + 128, :], "ld_b")
                P.wait("gpsimd", "ld_b", P.count("ld_b"))
                P.dma("gpsimd", t["out"][ti:ti + 128, :], sb, "st_b")
            P.wait("sync", "st_b", P.count("st_b"))

        if debug_names:
            P.barrier()
            for nm in debug_names:
                src = t[nm]
                dbg = nc.dram_tensor("dbg_" + nm, list(src.shape), src.dtype,
                                     kind="ExternalOutput").ap()
                P.dma("gpsimd", dbg, src, "st_b")
            P.wait("gpsimd", "st_b", P.count("st_b"))
        if debug_sbuf:
            P.barrier()
            for nm, ap in debug_sbuf:
                dbg = nc.dram_tensor("dbg_" + nm, list(ap(c).shape), ap(c).dtype,
                                     kind="ExternalOutput").ap()
                P.dma("gpsimd", dbg, ap(c), "st_b")
            P.wait("gpsimd", "st_b", P.count("st_b"))

        for name, eng_sel in (("sync", blk.sync), ("vector", blk.vector),
                              ("scalar", blk.scalar), ("tensor", blk.tensor),
                              ("gpsimd", blk.gpsimd)):
            fns = P.plan[name]
            eng_sel(lambda e, fns=fns: [f(e) for f in fns])

    return nc


def make_inputs(inputs):
    x = np.ascontiguousarray(inputs["L0_attn_a_embeddings"], dtype=np.float32)

    def slabvec(v):
        return np.ascontiguousarray(np.asarray(v, np.float32).reshape(NS, 128).T)

    base = {
        "wq": inputs["L0_attn_a_W_Q"], "wk_a": inputs["L0_attn_a_W_K"],
        "wv_a": inputs["L0_attn_a_W_V"],
        "wkb": inputs["L0_attn_b_Wk"], "wvb": inputs["L0_attn_b_Wv"],
        "wrb": inputs["L0_attn_b_Wr"], "wob": inputs["L0_attn_b_Wo"],
        "wkey": inputs["L0_attn_b_Wkey"], "wrec": inputs["L0_attn_b_Wrec"],
        "wval": inputs["L0_attn_b_Wval"],
        "wg": inputs["L0_ffn_Wg"], "w1": inputs["L0_ffn_W1"],
        "w2": inputs["L0_ffn_W2"], "w3": inputs["L0_ffn_W3"],
        "ws1": inputs["L0_ffn_Ws1"], "ws2": inputs["L0_ffn_Ws2"],
        "ws3": inputs["L0_ffn_Ws3"],
    }
    base = {k: np.ascontiguousarray(v, dtype=np.float32) for k, v in base.items()}
    for nm, key in [("ln1_w", "L0_attn_b_ln1_w"), ("ln1_b", "L0_attn_b_ln1_b"),
                    ("ln2_w", "L0_attn_b_ln2_w"), ("ln2_b", "L0_attn_b_ln2_b"),
                    ("att_tmk", "L0_attn_b_att_tmk"),
                    ("att_tmv", "L0_attn_b_att_tmv"),
                    ("att_tmr", "L0_attn_b_att_tmr"),
                    ("time_decay", "L0_attn_b_time_decay"),
                    ("time_first", "L0_attn_b_time_first"),
                    ("ffn_tmk", "L0_attn_b_ffn_tmk"),
                    ("ffn_tmr", "L0_attn_b_ffn_tmr")]:
        base[nm] = slabvec(inputs[key])

    import ml_dtypes
    base["ident_f"] = np.eye(128, dtype=np.float32)
    base["ident_b"] = np.eye(128).astype(ml_dtypes.bfloat16)
    base["ones"] = np.ones((128, 128), np.float32)
    cz = np.zeros((128, 128), np.float32)
    i_idx = np.arange(128)[:, None]
    j_idx = np.arange(128)[None, :]
    cz[j_idx > i_idx] = -1e30 / SCALE_A
    base["causal"] = cz
    oh = np.zeros((NE, NE * 128), np.float32)
    for e in range(NE):
        oh[e, e * 128:(e + 1) * 128] = 1.0
    base["onehot"] = oh
    io = np.tile(np.arange(NE, dtype=np.float32), (128, 1))
    base["iota32"] = np.ascontiguousarray(
        np.concatenate([io, io - 64.0], axis=1))
    base["iotap"] = np.ascontiguousarray(
        np.tile(np.arange(32, dtype=np.float32)[:, None], (1, 128)))
    base["epsv"] = np.full((128, 2), EPS, np.float32)

    in_maps = []
    for cc in range(NCORES):
        m = dict(base)
        m["x"] = np.ascontiguousarray(x[cc])
        in_maps.append(m)
    return in_maps


def kernel(**inputs):
    in_maps = make_inputs(inputs)
    nc = build_program()
    res = run_bass_kernel_spmd(nc, in_maps, core_ids=list(range(NCORES)))
    out = np.stack([res.results[cc]["out"] for cc in range(NCORES)], axis=0)
    return out.astype(np.float32)


# version 12
# speedup vs baseline: 1.0887x; 1.0325x over previous
import contextlib
import numpy as np
import concourse.bass as bass
import concourse.mybir as mybir
from concourse.bass_utils import run_bass_kernel_spmd

F32 = mybir.dt.float32
BF16 = mybir.dt.bfloat16
AF = mybir.ActivationFunctionType
ALU = mybir.AluOpType
AX = mybir.AxisListType

D = 1024
T = 4096
NS = D // 128
NE = 32
EH = 512
FFN = 4096
NCORES = 8
EPS = 1e-5
TOPK = 4
CAP_DROP = 70.0
SCALE_A = 1.0 / 32.0

ENGS = ("sync", "vector", "scalar", "tensor", "gpsimd")
ALL_PHASES = ("prep", "a1", "a2", "a3", "moe1", "attnb", "moe2", "final")


class Prog:
    def __init__(self, nc):
        self.nc = nc
        self.plan = {e: [] for e in ENGS}
        self.sem_handles = {}
        self.sem_counts = {}

    def register_sem(self, name, handle):
        self.sem_handles[name] = handle
        self.sem_counts[name] = 0

    def bump(self, name, k):
        self.sem_counts[name] += k
        return self.sem_counts[name]

    def count(self, name):
        return self.sem_counts[name]

    def emit(self, eng, fn):
        self.plan[eng].append(fn)

    def dma(self, eng, out, in_, sem):
        h = self.sem_handles[sem]
        tgt = self.bump(sem, 16)

        def f(e, out=out, in_=in_, h=h):
            e.dma_start(out=out, in_=in_).then_inc(h, 16)
        self.emit(eng, f)
        return tgt

    def wait(self, eng, sem, val=None):
        h = self.sem_handles[sem]
        v = self.count(sem) if val is None else val
        if v <= 0:
            return

        def f(e, h=h, v=v):
            e.wait_ge(h, v)
        self.emit(eng, f)

    def op(self, eng, fn, sem=None, k=1):
        if sem is None:
            self.emit(eng, fn)
            return None
        h = self.sem_handles[sem]
        tgt = self.bump(sem, k)

        def f(e, fn=fn, h=h, k=k):
            fn(e).then_inc(h, k)
        self.emit(eng, f)
        return tgt

    def barrier(self):
        h = self.sem_handles["bar"]
        base = self.count("bar")
        for e in ENGS:
            self.bump("bar", 1)
        tgt = base + len(ENGS)
        for e in ENGS:
            def f(eng, h=h, tgt=tgt):
                eng.nop().then_inc(h, 1)
                eng.wait_ge(h, tgt)
            self.emit(e, f)


class Ctx:
    pass


def carve(arena_ap, byte_off, rows, cols, dtype):
    nb = mybir.dt.size(dtype)
    assert byte_off % 2 == 0
    start = byte_off // 2
    width = cols * nb // 2
    sl = arena_ap[0:rows, start:start + width]
    if dtype == BF16:
        return sl
    return sl.bitcast(dtype)


def sl3(ap, s=NS):
    return ap.rearrange("p (s c) -> p s c", s=s)


def dview(dram_ap):
    return dram_ap.rearrange("(s p) t -> p s t", p=128)



def emit_prep(c):
    P, t = c.P, c.t
    c.prep_w_done = {}

    def cast_pm(dst_pm, src, s_count, width):
        P.dma("gpsimd", dst_pm.rearrange("p (s h) -> p s h", s=s_count),
              src.rearrange("(s p) h -> p s h", p=128), "gpc")

    cast_pm(t["wqpm"], t["wq_in"], NS, D)
    cast_pm(t["wkapm"], t["wk_a_in"], NS, D)
    cast_pm(t["wvapm"], t["wv_a_in"], NS, D)
    c.prep_w_done["qkv"] = P.count("gpc")

    stg16 = [carve(c.AR1, 0, 128, 2048, BF16), carve(c.AR1, 4096, 128, 2048, BF16)]
    tstg = [carve(c.AR1, 8192, 128, 2048, BF16), carve(c.AR1, 12288, 128, 2048, BF16)]
    ps_bf2 = [c.psF[:, 0:1024].bitcast(BF16), c.psF[:, 1024:2048].bitcast(BF16)]
    tposes = [
        ("wkb_in", "wkT_bf", 1024, 1024), ("wvb_in", "wvT_bf", 1024, 1024),
        ("wrb_in", "wrT_bf", 1024, 1024), ("wob_in", "woT_bf", 1024, 1024),
        ("wrec_in", "wrecT_bf", 1024, 1024),
        ("wkey_in", "wkeyT_bf", 4096, 1024), ("wval_in", "wvalT_bf", 1024, 4096),
    ]
    ti = 0
    for nm_in, nm_out, R, Cc in tposes:
        srcw, dst = t[nm_in], t[nm_out]
        for r0 in range(0, R, 128):
            for c0 in range(0, Cc, 2048):
                cw = min(2048, Cc - c0)
                b = ti % 2
                ps_bf = ps_bf2[b]
                P.wait("gpsimd", "mm", P.count("mm"))
                ldn = P.dma("gpsimd", stg16[b][:, 0:cw],
                            srcw[r0:r0 + 128, c0:c0 + cw], "gpc")
                P.wait("tensor", "gpc", ldn)
                P.wait("tensor", "st_a", P.count("st_a"))
                P.wait("tensor", "scl", P.count("scl"))
                nblk = cw // 128
                for jb in range(nblk):
                    P.op("tensor",
                         lambda te, o=ps_bf[:, jb * 128:(jb + 1) * 128],
                         i_=stg16[b][:, jb * 128:(jb + 1) * 128],
                         idn=c.s_ident_bf:
                         te.transpose(o, i_, idn))
                P.op("tensor",
                     lambda te, o=c.psF[:, 2048:2112].bitcast(BF16),
                     i_=c.s_ident_bf, idn=c.s_ident_bf:
                     te.transpose(o, i_, idn), sem="mm")
                P.wait("scalar", "mm", P.count("mm"))
                P.op("scalar",
                     lambda sc, o=tstg[b][:, 0:cw], i_=ps_bf[:, 0:cw]:
                     sc.copy(o, i_), sem="scl")
                P.wait("sync", "scl", P.count("scl"))
                for jb in range(nblk):
                    P.dma("sync",
                          dst[c0 + jb * 128:c0 + (jb + 1) * 128, r0:r0 + 128],
                          tstg[b][:, jb * 128:(jb + 1) * 128], "st_a")
                ti += 1
    c.prep_w_done["attnb"] = P.count("st_a")

    P.wait("gpsimd", "st_a", P.count("st_a"))
    for h in range(2):
        cast_pm(t["ws1pm"][h], t["ws1_in"][:, h * EH:(h + 1) * EH], NS, EH)
        cast_pm(t["ws3pm"][h], t["ws3_in"][:, h * EH:(h + 1) * EH], NS, EH)
        cast_pm(t["ws2pm"][h], t["ws2_in"][h * EH:(h + 1) * EH, :], 4, D)
        c.prep_w_done[NE + h] = P.count("gpc")
    for e in range(NE):
        cast_pm(t["w1pm"][e], t["w1_in"][e], NS, EH)
        cast_pm(t["w3pm"][e], t["w3_in"][e], NS, EH)
        cast_pm(t["w2pm"][e], t["w2_in"][e], 4, D)
        c.prep_w_done[e] = P.count("gpc")
    c.prep_w_done["all"] = P.count("gpc")
    P.wait("sync", "st_a", P.count("st_a"))
    P.barrier()



def emit_a1(c):
    P, t = c.P, c.t
    xst = [carve(c.AR1, 0, 128, 1024, F32), carve(c.AR1, 4096, 128, 1024, F32)]
    tmp1 = carve(c.AR1, 8192, 128, 1024, F32)
    lnst = [carve(c.AR1, 12288, 128, 1024, BF16), carve(c.AR1, 14336, 128, 1024, BF16)]
    xTst = [carve(c.AR1, 16384, 128, 1024, F32), carve(c.AR1, 20480, 128, 1024, F32)]
    nmu = carve(c.AR1, 24576, 128, 1, F32)
    vsum = carve(c.AR1, 24608, 128, 1, F32)
    rstd = carve(c.AR1, 24640, 128, 1, F32)
    junk = carve(c.AR1, 28672, 128, 1024, F32)
    ps32 = c.psF[:, 0:1024]
    psbf = c.psF[:, 1024:1536].bitcast(BF16)
    xT_r = dview(t["xT"])

    for ti in range(T // 128):
        b = ti % 2
        if ti >= 2:
            P.wait("sync", "mm", P.count("mm"))
            P.wait("sync", "vec", P.count("vec"))
        ld_n = P.dma("sync", xst[b], t["x_in"][ti * 128:(ti + 1) * 128, :], "ld_a")

        P.wait("tensor", "ld_a", ld_n)
        if ti >= 2:
            P.wait("tensor", "st_a", P.count("st_a"))
        for s in range(NS):
            P.op("tensor",
                 lambda te, o=ps32[:, s * 128:(s + 1) * 128],
                 i_=xst[b][:, s * 128:(s + 1) * 128], idn=c.s_ident_f:
                 te.transpose(o, i_, idn),
                 sem="mm" if s == NS - 1 else None)
        P.wait("scalar", "mm", P.count("mm"))
        evn = P.op("scalar", lambda sc, o=xTst[b], i_=ps32: sc.copy(o, i_),
                   sem="scl")
        P.wait("sync", "scl", evn)
        P.dma("sync", xT_r[:, :, ti * 128:(ti + 1) * 128], sl3(xTst[b]), "st_a")

        P.wait("vector", "ld_a", ld_n)
        P.op("vector", lambda v, o=vsum, i_=xst[b]:
             v.reduce_sum(o, i_, axis=AX.X), sem="wbv")
        P.wait("vector", "wbv")
        P.op("vector", lambda v, o=nmu, i_=vsum:
             v.tensor_scalar(o, i_, -1.0 / D, None, ALU.mult), sem="wbv")
        P.wait("vector", "wbv")
        P.op("vector", lambda v, o=tmp1, i_=xst[b], m=nmu:
             v.tensor_scalar(o, i_, m, None, ALU.add))
        P.op("vector", lambda v, o=junk, i_=tmp1: v.tensor_mul(o, i_, i_))
        P.op("vector", lambda v, o=vsum, i_=junk:
             v.reduce_sum(o, i_, axis=AX.X), sem="vec")
        P.wait("scalar", "vec", P.count("vec"))
        P.op("scalar", lambda sc, o=rstd, i_=vsum, e=c.s_eps:
             sc.activation(o, i_, AF.Sqrt, bias=e, scale=1.0 / D), sem="scl")
        P.wait("vector", "scl", P.count("scl"))
        P.op("vector", lambda v, o=nmu, i_=rstd: v.reciprocal(o, i_), sem="wbv")
        P.wait("vector", "wbv")
        P.op("vector", lambda v, o=lnst[b], i_=tmp1, m=nmu:
             v.tensor_scalar(o, i_, m, None, ALU.mult), sem="vec")
        P.wait("tensor", "vec", P.count("vec"))
        for s in range(NS):
            P.op("tensor",
                 lambda te, o=psbf[:, s * 128:(s + 1) * 128],
                 i_=lnst[b][:, s * 128:(s + 1) * 128], idn=c.s_ident_bf:
                 te.transpose(o, i_, idn),
                 sem="mm" if s == NS - 1 else None)
        P.wait("vector", "mm", P.count("mm"))
        P.op("vector",
             lambda v, o=sl3(c.LN0T, NS)[:, :, ti * 128:(ti + 1) * 128].
             rearrange("p s c -> p (s c)") if False else
             sl3(c.LN0T)[:, :, ti * 128:(ti + 1) * 128],
             i_=sl3(psbf, NS): v.tensor_copy(o, i_), sem="vec")
    P.barrier()



def emit_a2(c):
    P, t = c.P, c.t
    wsl = carve(c.AR4, 0, 128, NS * 1024, BF16)
    qstg = [carve(c.AR1, 0, 128, 512, BF16), carve(c.AR1, 1024, 128, 512, BF16)]
    ps = [c.psF[:, 0:512], c.psF[:, 512:1024]]
    qT = t["qT"]

    P.wait("sync", "gpc", c.prep_w_done["qkv"])

    def load_w(nm):
        P.dma("sync", wsl, t[nm], "ld_a")
        return P.count("ld_a")

    for which, nm in (("k", "wkapm"), ("q", "wqpm")):
        P.wait("sync", "mm", P.count("mm"))
        wn = load_w(nm)
        P.wait("tensor", "ld_a", wn)
        pp = 0
        for tc in range(T // 512):
            for oc in range(NS):
                b = pp % 2
                P.wait("tensor", "ev", P.count("ev"))
                P.wait("tensor", "st_a", P.count("st_a"))
                for s in range(NS):
                    P.op("tensor",
                         lambda te, o=ps[b],
                         l=wsl[:, s * 1024 + oc * 128:s * 1024 + oc * 128 + 128],
                         r=c.LN0T[:, s * T + tc * 512:s * T + tc * 512 + 512],
                         st=(s == 0), sp=(s == NS - 1):
                         te.matmul(o, l, r, start=st, stop=sp,
                                   skip_group_check=True),
                         sem="mm" if s == NS - 1 else None)
                mmn = P.count("mm")
                if which == "k":
                    P.wait("vector", "mm", mmn)
                    P.op("vector",
                         lambda v, o=c.KT[:, oc * T + tc * 512:oc * T + tc * 512 + 512],
                         i_=ps[b]: v.tensor_copy(o, i_), sem="ev")
                else:
                    P.wait("scalar", "mm", mmn)
                    evn = P.op("scalar",
                               lambda sc, o=qstg[b], i_=ps[b]: sc.copy(o, i_),
                               sem="ev")
                    P.wait("sync", "ev", evn)
                    P.dma("sync", qT[oc * 128:(oc + 1) * 128, tc * 512:tc * 512 + 512],
                          qstg[b], "st_a")
                pp += 1

    P.wait("sync", "mm", P.count("mm"))
    P.wait("sync", "st_a", P.count("st_a"))
    wn = load_w("wvapm")
    P.wait("tensor", "ld_a", wn)
    pp = 0
    for ti in range(T // 128):
        for dc in range(2):
            b = pp % 2
            P.wait("tensor", "ev", P.count("ev"))
            for s in range(NS):
                P.op("tensor",
                     lambda te, o=ps[b],
                     l=c.LN0T[:, s * T + ti * 128:s * T + ti * 128 + 128],
                     r=wsl[:, s * 1024 + dc * 512:s * 1024 + dc * 512 + 512],
                     st=(s == 0), sp=(s == NS - 1):
                     te.matmul(o, l, r, start=st, stop=sp,
                               skip_group_check=True),
                     sem="mm" if s == NS - 1 else None)
            P.wait("vector", "mm", P.count("mm"))
            P.op("vector",
                 lambda v, o=c.V[:, ti * 1024 + dc * 512:ti * 1024 + dc * 512 + 512],
                 i_=ps[b]: v.tensor_copy(o, i_), sem="ev")
            pp += 1
    P.barrier()



def emit_a3(c):
    P, t = c.P, c.t
    qtile = [carve(c.AR2, 0, 128, 1024, BF16), carve(c.AR2, 2048, 128, 1024, BF16)]
    ptil = carve(c.AR2, 4096, 128, 512, BF16)
    ptev_big = carve(c.AR2, 28672, 128, 4096, BF16)
    mrow = carve(c.AR2, 6144, 128, 1, F32)
    lrow = carve(c.AR2, 6208, 128, 1, F32)
    tsum = carve(c.AR2, 6272, 128, 1, F32)
    tmax = carve(c.AR2, 6336, 128, 1, F32)
    alpha = carve(c.AR2, 6400, 128, 1, F32)
    nbias = carve(c.AR2, 6464, 128, 1, F32)
    junk = carve(c.AR2, 8192, 128, 512, F32)
    otst = [carve(c.AR2, 12288, 128, 1024, F32), carve(c.AR2, 16384, 128, 1024, F32)]
    xtst = [carve(c.AR2, 20480, 128, 1024, F32), carve(c.AR2, 24576, 128, 1024, F32)]
    ps_s = c.psF[:, 2048:2560]
    ps_pt = c.psF[:, 2560:2816].bitcast(BF16)
    ps_ot = c.psF[:, 0:1024]
    qT_r = dview(t["qT"])
    xT_r = dview(t["xT"])

    for qi in range(T // 128):
        bq = qi % 2
        if qi >= 2:
            P.wait("sync", "mm", P.count("mm"))
        qld = P.dma("sync", sl3(qtile[bq]), qT_r[:, :, qi * 128:(qi + 1) * 128],
                    "ld_a")
        P.op("vector", lambda v, o=mrow: v.memset(o, -1e30))
        P.op("vector", lambda v, o=lrow: v.memset(o, 0.0), sem="wbv")
        P.wait("vector", "wbv")
        P.wait("tensor", "ld_a", qld)

        ngrp = (qi * 128 + 128 + 511) // 512
        for g in range(ngrp):
            k0 = g * 512
            kw = min(512, qi * 128 + 128 - k0)
            P.wait("tensor", "vec", P.count("vec"))
            P.wait("tensor", "scl", P.count("scl"))
            for s in range(NS):
                P.op("tensor",
                     lambda te, o=ps_s[:, 0:kw], l=qtile[bq][:, s * 128:(s + 1) * 128],
                     r=c.KT[:, s * T + k0:s * T + k0 + kw],
                     st=(s == 0), sp=(s == NS - 1):
                     te.matmul(o, l, r, start=st, stop=sp, skip_group_check=True),
                     sem="mm" if s == NS - 1 else None)
            P.wait("vector", "mm", P.count("mm"))
            if kw > (qi * 128 - k0):
                P.op("vector", lambda v, o=ps_s[:, kw - 128:kw],
                     i_=ps_s[:, kw - 128:kw], m=c.s_causal:
                     v.tensor_tensor(o, i_, m, ALU.add))
            P.op("vector", lambda v, o=tmax, i_=ps_s[:, 0:kw]:
                 v.reduce_max(o, i_, axis=AX.X), sem="wbv")
            P.wait("vector", "wbv")
            P.op("vector", lambda v, o=alpha, i_=mrow, m=tmax:
                 v.tensor_scalar(o, i_, m, None, ALU.subtract))
            P.op("vector", lambda v, o=mrow, i_=mrow, m=tmax:
                 v.tensor_scalar(o, i_, m, None, ALU.max), sem="wbv")
            P.wait("vector", "wbv")
            P.op("vector", lambda v, o=alpha, i_=alpha:
                 v.tensor_scalar(o, i_, 0.0, None, ALU.min))
            P.op("vector", lambda v, o=nbias, i_=mrow:
                 v.tensor_scalar(o, i_, -SCALE_A, None, ALU.mult), sem="vec")
            P.wait("scalar", "vec", P.count("vec"))
            P.op("scalar", lambda sc, o=alpha, i_=alpha:
                 sc.activation(o, i_, AF.Exp, scale=SCALE_A), sem="scl")
            P.op("scalar", lambda sc, o=junk[:, 0:kw], i_=ps_s[:, 0:kw],
                 b_=nbias, m=tsum:
                 sc.activation(o, i_, AF.Exp, bias=b_, scale=SCALE_A,
                               accum_out=m), sem="scl")
            P.wait("vector", "scl", P.count("scl"))
            P.op("vector", lambda v, o=lrow, i_=lrow, a=alpha:
                 v.tensor_scalar(o, i_, a, None, ALU.mult), sem="wbv")
            P.wait("vector", "wbv")
            P.op("vector", lambda v, o=lrow, i_=lrow, b_=tsum:
                 v.tensor_scalar(o, i_, b_, None, ALU.add), sem="vec")

        P.wait("scalar", "vec", P.count("vec"))
        P.op("scalar", lambda sc, o=tsum, i_=lrow:
             sc.activation(o, i_, AF.Ln), sem="scl")
        P.wait("vector", "scl", P.count("scl"))
        P.op("vector", lambda v, o=nbias, i_=nbias, m=tsum:
             v.tensor_scalar(o, i_, m, None, ALU.subtract), sem="vec")

        for g in range(ngrp):
            k0 = g * 512
            kw = min(512, qi * 128 + 128 - k0)
            P.wait("tensor", "vec", P.count("vec"))
            P.wait("tensor", "scl", P.count("scl"))
            for s in range(NS):
                P.op("tensor",
                     lambda te, o=ps_s[:, 0:kw], l=qtile[bq][:, s * 128:(s + 1) * 128],
                     r=c.KT[:, s * T + k0:s * T + k0 + kw],
                     st=(s == 0), sp=(s == NS - 1):
                     te.matmul(o, l, r, start=st, stop=sp, skip_group_check=True),
                     sem="mm" if s == NS - 1 else None)
            if kw > (qi * 128 - k0):
                P.wait("vector", "mm", P.count("mm"))
                P.op("vector", lambda v, o=ps_s[:, kw - 128:kw],
                     i_=ps_s[:, kw - 128:kw], m=c.s_causal:
                     v.tensor_tensor(o, i_, m, ALU.add), sem="vec")
                P.wait("scalar", "vec", P.count("vec"))
            else:
                P.wait("scalar", "mm", P.count("mm"))
            P.op("scalar", lambda sc, o=ptil[:, 0:kw], i_=ps_s[:, 0:kw], b_=nbias:
                 sc.activation(o, i_, AF.Exp, bias=b_, scale=SCALE_A), sem="scl")
            P.wait("tensor", "scl", P.count("scl"))
            nkt = (kw + 127) // 128
            for j in range(nkt):
                P.op("tensor",
                     lambda te, o=ps_pt[:, j * 128:(j + 1) * 128],
                     i_=ptil[:, j * 128:(j + 1) * 128], idn=c.s_ident_bf:
                     te.transpose(o, i_, idn),
                     sem="mm" if j == nkt - 1 else None)
            P.wait("vector", "mm", P.count("mm"))
            P.op("vector", lambda v, o=ptev_big[:, k0:k0 + kw], i_=ps_pt[:, 0:kw]:
                 v.tensor_copy(o, i_), sem="vec")
        P.wait("tensor", "vec", P.count("vec"))
        nkt_all = qi + 1
        for dc in range(NS):
            for kt in range(nkt_all):
                P.op("tensor",
                     lambda te, o=ps_ot[:, dc * 128:(dc + 1) * 128],
                     l=c.V[:, kt * 1024 + dc * 128:kt * 1024 + dc * 128 + 128],
                     r=ptev_big[:, kt * 128:(kt + 1) * 128],
                     st=(kt == 0), sp=(kt == nkt_all - 1):
                     te.matmul(o, l, r, start=st, stop=sp,
                               skip_group_check=True),
                     sem="mm" if (dc == NS - 1 and kt == nkt_all - 1) else None)

        bo = qi % 2
        P.wait("sync", "vec", P.count("vec"))
        xld = P.dma("sync", sl3(xtst[bo]), xT_r[:, :, qi * 128:(qi + 1) * 128],
                    "ld_b")
        P.wait("vector", "mm", P.count("mm"))
        P.wait("vector", "ld_b", xld)
        P.wait("vector", "st_a", P.count("st_a"))
        P.op("vector", lambda v, o=otst[bo], i_=xtst[bo], p=ps_ot:
             v.tensor_tensor(o, i_, p, ALU.add), sem="vec")
        P.wait("sync", "vec", P.count("vec"))
        P.dma("sync", xT_r[:, :, qi * 128:(qi + 1) * 128], sl3(otst[bo]), "st_a")
    P.barrier()



def emit_moe(c):
    P, t = c.P, c.t
    GT = 1024
    CH = 256
    NG = T // GT
    NCH = GT // CH
    NPE = NE + 2
    NPAIR = NPE // 2

    ln1t = [carve(c.AR2, 0, 128, NS * GT, BF16),
            carve(c.AR2, 16384, 128, NS * GT, BF16)]
    mw = [carve(c.AR2, 32768, 128, 4 * CH, BF16),
          carve(c.AR2, 34816, 128, 4 * CH, BF16),
          carve(c.AR2, 36864, 128, 4 * CH, BF16),
          carve(c.AR2, 38912, 128, 4 * CH, BF16)]
    logit = carve(c.AR2, 40960, 128, NE, F32)
    lmax = carve(c.AR2, 41088, 128, 1, F32)
    lsum = carve(c.AR2, 41120, 128, 1, F32)
    rrow = carve(c.AR2, 41152, 128, 1, F32)
    srow = carve(c.AR2, 41184, 128, 1, F32)
    eq = carve(c.AR2, 41216, 128, NE, F32)
    idxm = carve(c.AR2, 41344, 128, NE, F32)
    wsel = carve(c.AR2, 41472, 128, NE, F32)
    stat1 = carve(c.AR2, 41600, 1, CH, F32)
    stat2 = carve(c.AR2, 42624, 1, CH, F32)
    wbsb = [carve(c.AR2, 53888, 128, GT, F32),
            carve(c.AR2, 57984, 128, GT, F32)]
    silu1 = [carve(c.AR2, 45696, 128, 4 * CH, BF16),
             carve(c.AR2, 47744, 128, 4 * CH, BF16)]
    h3w = [carve(c.AR2, 49792, 128, 4 * CH, BF16),
           carve(c.AR2, 51840, 128, 4 * CH, BF16)]
    yacc = carve(c.AR1, 0, 128, NS * GT, F32)
    wt32 = [carve(c.AR1, 32768, 32, GT, F32),
            carve(c.AR1, 36864, 32, GT, F32)]
    onehot_dyn = [carve(c.AR1, 40960, 32, 128, F32),
                  carve(c.AR1, 41472, 32, 128, F32)]
    xchk = [carve(c.AR1, 49152, 128, NS * CH, F32),
            carve(c.AR1, 57344, 128, NS * CH, F32)]
    lnf = carve(c.AR4, 0, 128, NS * CH, F32)
    xchk2 = carve(c.AR4, 8192, 128, NS * CH, F32)
    wv = []
    for b in range(2):
        off = b * 24576
        wv.append({
            "w1": carve(c.AR3, off, 128, NS * 512, BF16),
            "w3": carve(c.AR3, off + 8192, 128, NS * 512, BF16),
            "w2": carve(c.AR3, off + 16384, 128, 4 * 1024, BF16),
        })
    ps_y = c.psF[:, 0:2048]
    ps_h1 = c.psF[:, 2048:3072]
    ps_h3 = c.psF[:, 3072:4096]
    ps_st = c.psF[:, 2560:3072]
    ps_lg = c.psF[:, 2560:2592]
    ps_wt = c.psF[0:32, 2592:2720]
    xT_r = dview(t["xT"])


    def load_w13(pe, buf):
        P.wait("sync", "gpc", c.prep_w_done["all"])
        if pe < NE:
            P.dma("sync", wv[buf]["w1"], t["w1pm"][pe], "ld_a")
            n1 = P.count("ld_a")
            P.dma("sync", wv[buf]["w3"], t["w3pm"][pe], "ld_a")
        else:
            h0 = pe - NE
            P.dma("sync", wv[buf]["w1"], t["ws1pm"][h0], "ld_a")
            n1 = P.count("ld_a")
            P.dma("sync", wv[buf]["w3"], t["ws3pm"][h0], "ld_a")
        return (n1, P.count("ld_a"))

    def load_w2(pe, buf):
        if pe < NE:
            P.dma("sync", wv[buf]["w2"], t["w2pm"][pe], "ld_a")
        else:
            P.dma("sync", wv[buf]["w2"], t["ws2pm"][pe - NE], "ld_a")
        return P.count("ld_a")



    ln1t_ready = {}
    c.prep_psum_free = 0
    evst = {"silu": [0, 0], "h3w": [0, 0], "bcast": 0}
    poolw = {0: [0, 0], 1: [0, 0]}

    def prep_gen(g, gb):
        t0g = g * GT
        xln_done = {}
        P.wait("sync", "pvec", P.count("pvec"))
        xln_done[0] = P.dma("sync", sl3(xchk[0]), xT_r[:, :, t0g:t0g + CH],
                            "ld_b")
        for ch in range(NCH):
            t0 = t0g + ch * CH
            if ch + 1 < NCH:
                P.wait("sync", "pvec", P.count("pvec"))
                xln_done[ch + 1] = P.dma(
                    "sync", sl3(xchk[(ch + 1) % 2]),
                    xT_r[:, :, t0 + CH:t0 + 2 * CH], "ld_b")
            xc = xchk[ch % 2]
            xln = xln_done[ch]
            P.wait("tensor", "ld_b", xln)
            P.wait("tensor", "pvec", P.count("pvec"))
            P.wait("tensor", "pscl", P.count("pscl"))
            P.wait("tensor", "scl", P.count("scl"))
            for s in range(NS):
                P.op("tensor", lambda te, o=ps_st[0:1, 0:CH], l=c.s_ones,
                     r=xc[:, s * CH:(s + 1) * CH],
                     st=(s == 0), sp=(s == NS - 1):
                     te.matmul(o, l, r, start=st, stop=sp, skip_group_check=True),
                     sem="pmm" if s == NS - 1 else None)
            P.wait("vector", "pmm", P.count("pmm"))
            P.op("vector", lambda v, o=stat1, i_=ps_st[0:1, 0:CH]:
                 v.tensor_scalar(o, i_, -1.0 / D, None, ALU.mult), sem="pvec")
            for s in range(NS):
                P.op("vector", lambda v, o=lnf[:, s * CH:(s + 1) * CH],
                     i_=xc[:, s * CH:(s + 1) * CH]:
                     v.tensor_mul(o, i_, i_),
                     sem="pvec" if s == NS - 1 else None)
            c.prep_psum_free = P.count("pvec")
            yield
            P.wait("tensor", "pvec", P.count("pvec"))
            P.wait("tensor", "scl", P.count("scl"))
            for s in range(NS):
                P.op("tensor", lambda te, o=ps_st[0:1, CH:2 * CH], l=c.s_ones,
                     r=lnf[:, s * CH:(s + 1) * CH],
                     st=(s == 0), sp=(s == NS - 1):
                     te.matmul(o, l, r, start=st, stop=sp, skip_group_check=True),
                     sem="pmm" if s == NS - 1 else None)
            P.wait("vector", "pmm", P.count("pmm"))
            P.op("vector", lambda v, o=stat2, i_=stat1: v.tensor_mul(o, i_, i_),
                 sem="pvec")
            P.wait("vector", "pvec")
            P.op("vector", lambda v, o=stat2, i_=ps_st[0:1, CH:2 * CH], m=stat2:
                 v.scalar_tensor_tensor(o, i_, 1.0 / D, m, ALU.mult,
                                        ALU.subtract), sem="pvec")
            P.wait("scalar", "pvec", P.count("pvec"))
            P.op("scalar", lambda sc, o=stat2, i_=stat2, e=c.s_eps_row:
                 sc.activation(o, i_, AF.Sqrt, bias=e), sem="pscl")
            P.wait("vector", "pscl", P.count("pscl"))
            P.op("vector", lambda v, o=stat2, i_=stat2:
                 v.reciprocal(o, i_), sem="pvec")
            c.prep_psum_free = P.count("pvec")
            yield
            P.wait("tensor", "pvec", P.count("pvec"))
            P.wait("tensor", "scl", P.count("scl"))
            P.op("tensor", lambda te, o=ps_st[:, 0:CH], l=c.s_ones_row,
                 r=stat1: te.matmul(o, l, r, start=True, stop=True,
                                    skip_group_check=True))
            P.op("tensor", lambda te, o=ps_st[:, CH:2 * CH], l=c.s_ones_row,
                 r=stat2: te.matmul(o, l, r, start=True, stop=True,
                                    skip_group_check=True), sem="pmm")
            P.wait("vector", "pmm", P.count("pmm"))
            for s in range(NS):
                P.op("vector", lambda v, o=lnf[:, s * CH:(s + 1) * CH],
                     i_=xc[:, s * CH:(s + 1) * CH], m=ps_st[:, 0:CH]:
                     v.tensor_tensor(o, i_, m, ALU.add))
            for s in range(NS):
                P.op("vector", lambda v, o=lnf[:, s * CH:(s + 1) * CH],
                     i_=lnf[:, s * CH:(s + 1) * CH], m=ps_st[:, CH:2 * CH]:
                     v.tensor_tensor(o, i_, m, ALU.mult),
                     sem="pvec" if s == NS - 1 else None)
            for s in range(NS):
                P.op("vector",
                     lambda v, o=sl3(ln1t[gb])[:, s, ch * CH:(ch + 1) * CH],
                     i_=lnf[:, s * CH:(s + 1) * CH]:
                     v.tensor_copy(o, i_), sem="pvec" if s == NS - 1 else None)
            c.prep_psum_free = P.count("pvec")
            yield
            for tt in range(CH // 128):
                P.wait("tensor", "pvec", P.count("pvec"))
                P.wait("tensor", "scl", P.count("scl"))
                for s in range(NS):
                    P.op("tensor", lambda te, o=ps_lg,
                         l=lnf[:, s * CH + tt * 128:s * CH + tt * 128 + 128],
                         r=c.WG[:, s * NE:(s + 1) * NE],
                         st=(s == 0), sp=(s == NS - 1):
                         te.matmul(o, l, r, start=st, stop=sp,
                                   skip_group_check=True),
                         sem="pmm" if s == NS - 1 else None)
                P.wait("vector", "pmm", P.count("pmm"))
                P.op("vector", lambda v, o=lmax, i_=ps_lg:
                     v.reduce_max(o, i_, axis=AX.X, negate=True))
                P.op("vector", lambda v, o=logit, i_=ps_lg:
                     v.tensor_copy(o, i_), sem="pvec")
                P.wait("scalar", "pvec", P.count("pvec"))
                P.op("scalar", lambda sc, o=logit, i_=logit, b_=lmax, m=lsum:
                     sc.activation(o, i_, AF.Exp, bias=b_, accum_out=m),
                     sem="pscl")
                P.wait("vector", "pscl", P.count("pscl"))
                P.op("vector", lambda v, o=wsel: v.memset(o, 0.0), sem="pvec")
                for r_ in range(TOPK):
                    P.wait("vector", "pvec")
                    P.op("vector", lambda v, o=lmax, i_=logit:
                         v.reduce_max(o, i_, axis=AX.X), sem="pvec")
                    P.wait("vector", "pvec")
                    P.op("vector", lambda v, o=eq, i_=logit, m=lmax:
                         v.tensor_scalar(o, i_, m, None, ALU.is_equal),
                         sem="pvec")
                    P.wait("vector", "pvec")
                    P.op("vector", lambda v, o=idxm, i_=eq, m=c.s_iotam64:
                         v.tensor_tensor(o, i_, m, ALU.mult), sem="pvec")
                    P.wait("vector", "pvec")
                    P.op("vector", lambda v, o=idxm, i_=idxm:
                         v.tensor_scalar(o, i_, 64.0, None, ALU.add),
                         sem="pvec")
                    P.wait("vector", "pvec")
                    P.op("vector", lambda v, o=srow, i_=idxm:
                         v.tensor_reduce(o, i_, axis=AX.X, op=ALU.min),
                         sem="pvec")
                    P.wait("vector", "pvec")
                    P.op("vector", lambda v, o=eq, i_=c.s_iota32, m=srow:
                         v.tensor_scalar(o, i_, m, None, ALU.is_equal),
                         sem="pvec")
                    P.wait("vector", "pvec")
                    P.op("vector", lambda v, o=wsel, i_=eq, m=lmax, a=wsel:
                         v.scalar_tensor_tensor(o, i_, m, a, ALU.mult, ALU.add))
                    P.op("vector", lambda v, o=logit, i_=eq, a=logit:
                         v.scalar_tensor_tensor(o, i_, -2.0, a, ALU.mult,
                                                ALU.add), sem="pvec")
                P.wait("vector", "pvec")
                P.op("vector", lambda v, o=rrow, i_=lsum: v.reciprocal(o, i_),
                     sem="pvec")
                P.wait("vector", "pvec")
                P.op("vector", lambda v, o=wsel, i_=wsel, m=rrow:
                     v.tensor_scalar(o, i_, m, None, ALU.mult), sem="pvec")
                c.prep_psum_free = P.count("pvec")
                yield
                P.wait("tensor", "pvec", P.count("pvec"))
                P.wait("tensor", "scl", P.count("scl"))
                P.op("tensor", lambda te, o=ps_wt, i_=wsel, idn=c.s_ident_f:
                     te.transpose(o, i_, idn), sem="pmm")
                P.wait("vector", "pmm", P.count("pmm"))
                P.op("vector",
                     lambda v, o=wt32[gb][0:32, ch * CH + tt * 128:
                                          ch * CH + tt * 128 + 128],
                     i_=ps_wt: v.tensor_copy(o, i_), sem="pvec")
                c.prep_psum_free = P.count("pvec")
                yield
        ln1t_ready[g] = (P.count("pvec"), P.count("pmm"))

    def emit_waves(g, gb, gen):
        t0g = g * GT
        P.op("vector", lambda v, o=yacc: v.memset(o, 0.0), sem="vec")
        P.wait("sync", "mm", P.count("mm"))
        w13_done = [load_w13(0, 0), load_w13(1, 1)]
        w2_done = [load_w2(0, 0), load_w2(1, 1)]
        pool_done = {}
        w2mm_done = {}

        def h_phase(pw, ch, ei):
            gci = pw * NCH + ch
            slot = gci % 2
            mwslot = slot * 2 + ei
            pe = 2 * pw + ei
            ch0 = ch * CH
            if ch == 0 and ei == 0 and pw == 0:
                P.wait("tensor", "pvec", ln1t_ready[g][0])
            if ch == 0 and ei == 0:
                P.wait("vector", "mm", P.count("mm"))
                for e2 in range(2):
                    if 2 * pw + e2 < NE:
                        P.op("vector", lambda v, o=onehot_dyn[e2],
                             i_=c.s_iotap, val=float(2 * pw + e2):
                             v.tensor_single_scalar(o, i_, val, ALU.is_equal),
                             sem="vec" if e2 == 1 else None)
                if 2 * pw < NE:
                    P.wait("tensor", "vec", P.count("vec"))
                    wbs_th = {}
                    for e2 in range(2):
                        if 2 * pw + e2 >= NE:
                            continue
                        for th in range(2):
                            if e2 == 1:
                                P.wait("tensor", "wbs", wbs_th[th])
                            P.op("tensor",
                                 lambda te, o=ps_h3[:, th * 512:th * 512 + 512],
                                 l=onehot_dyn[e2],
                                 r=wt32[gb][0:32, th * 512:th * 512 + 512]:
                                 te.matmul(o, l, r, start=True, stop=True,
                                           skip_group_check=True), sem="mm")
                            P.wait("vector", "mm", P.count("mm"))
                            P.op("vector",
                                 lambda v,
                                 o=wbsb[e2][:, th * 512:th * 512 + 512],
                                 i_=ps_h3[:, th * 512:th * 512 + 512]:
                                 v.tensor_copy(o, i_), sem="wbs")
                            wbs_th[th] = P.count("wbs")
                    evst["bcast"] = P.count("wbs")
            for half in range(2):
                h0, h1c = 2 * half * CH, (2 * half + 2) * CH
                if ch == 0 and half == 0:
                    P.wait("tensor", "ld_a", w13_done[ei][0])
                    P.wait("tensor", "ld_a", w13_done[ei][1])
                P.wait("tensor", "scl", evst["silu"][half])
                if half == 1:
                    P.wait("tensor", "pvec", c.prep_psum_free)
                for hc in (2 * half, 2 * half + 1):
                    for s in range(NS):
                        P.op("tensor",
                             lambda te, o=ps_h1[:, hc * CH:(hc + 1) * CH],
                             l=wv[ei]["w1"][:, s * 512 + hc * 128:
                                            s * 512 + hc * 128 + 128],
                             r=ln1t[gb][:, s * GT + ch0:s * GT + ch0 + CH],
                             st=(s == 0), sp=(s == NS - 1):
                             te.matmul(o, l, r, start=st, stop=sp,
                                       skip_group_check=True),
                             sem="mm" if (hc == 2 * half + 1 and s == NS - 1)
                             else None)
                mmh1 = P.count("mm")
                P.wait("tensor", "vec", evst["h3w"][half])
                P.wait("tensor", "wbs", evst["bcast"])
                for hc in (2 * half, 2 * half + 1):
                    for s in range(NS):
                        P.op("tensor",
                             lambda te, o=ps_h3[:, hc * CH:(hc + 1) * CH],
                             l=wv[ei]["w3"][:, s * 512 + hc * 128:
                                            s * 512 + hc * 128 + 128],
                             r=ln1t[gb][:, s * GT + ch0:s * GT + ch0 + CH],
                             st=(s == 0), sp=(s == NS - 1):
                             te.matmul(o, l, r, start=st, stop=sp,
                                       skip_group_check=True),
                             sem="mm" if (hc == 2 * half + 1 and s == NS - 1)
                             else None)
                mmh3 = P.count("mm")
                P.wait("scalar", "mm", mmh1)
                P.wait("scalar", "pool", poolw[ei][half])
                P.op("scalar", lambda sc, o=silu1[ei][:, h0:h1c],
                     i_=ps_h1[:, h0:h1c]:
                     sc.activation(o, i_, AF.Silu), sem="scl")
                evst["silu"][half] = P.count("scl")
                P.wait("vector", "mm", mmh3)
                P.wait("vector", "pool", poolw[ei][half])
                if pe < NE:
                    P.op("vector",
                         lambda v,
                         o=sl3(h3w[ei], 4)[:, 2 * half:2 * half + 2, :],
                         i_=sl3(ps_h3, 4)[:, 2 * half:2 * half + 2, :],
                         m=wbsb[ei][:, ch0:ch0 + CH].unsqueeze(1)
                         .broadcast_to((128, 2, CH)):
                         v.tensor_tensor(o, i_, m, ALU.mult), sem="vec")
                else:
                    P.op("vector", lambda v, o=h3w[ei][:, h0:h1c],
                         i_=ps_h3[:, h0:h1c]:
                         v.tensor_copy(o, i_), sem="vec")
                evst["h3w"][half] = P.count("vec")
                P.wait("gpsimd", "scl", evst["silu"][half])
                P.wait("gpsimd", "vec", evst["h3w"][half])
                P.op("gpsimd", lambda gp, o=mw[mwslot][:, h0:h1c],
                     i_=silu1[ei][:, h0:h1c], m=h3w[ei][:, h0:h1c]:
                     gp.tensor_tensor(o, i_, m, ALU.mult), sem="pool")
                poolw[ei][half] = P.count("pool")
            if ei == 1:
                pool_done[gci] = P.count("pool")
            if ch == NCH - 1 and ei == 1 and 2 * pw + 2 < NPE:
                P.wait("sync", "mm", P.count("mm"))
                w13_done[:] = [load_w13(2 * pw + 2, 0),
                               load_w13(2 * pw + 3, 1)]

        def w2_batch(pw, ch):
            gci = pw * NCH + ch
            slot = gci % 2
            ch0 = ch * CH
            P.wait("tensor", "pool", pool_done[gci])
            P.wait("tensor", "ld_a", max(w2_done))
            P.wait("tensor", "ev", P.count("ev"))
            for dc in range(NS):
                for ei in range(2):
                    for s in range(4):
                        P.op("tensor",
                             lambda te, o=ps_y[:, dc * CH:(dc + 1) * CH],
                             l=wv[ei]["w2"][:, s * 1024 + dc * 128:
                                            s * 1024 + dc * 128 + 128],
                             r=mw[slot * 2 + ei][:, s * CH:(s + 1) * CH],
                             st=(ei == 0 and s == 0),
                             sp=(ei == 1 and s == 3):
                             te.matmul(o, l, r, start=st, stop=sp,
                                       skip_group_check=True),
                             sem="mm" if (dc == NS - 1 and ei == 1
                                          and s == 3) else None)
            w2mm_done[pw] = P.count("mm")
            P.wait("vector", "mm", P.count("mm"))
            if pw + 1 < NPAIR:
                P.op("vector", lambda v, o=sl3(yacc)[:, :, ch0:ch0 + CH],
                     i_=sl3(yacc)[:, :, ch0:ch0 + CH], p=sl3(ps_y, NS):
                     v.tensor_tensor(o, i_, p, ALU.add), sem="ev")
            else:
                P.wait("sync", "vec", P.count("vec"))
                P.wait("sync", "st_a", P.count("st_a"))
                xl = P.dma("sync", sl3(xchk2),
                           xT_r[:, :, t0g + ch0:t0g + ch0 + CH], "ld_b")
                P.op("vector", lambda v, o=sl3(yacc)[:, :, ch0:ch0 + CH],
                     i_=sl3(yacc)[:, :, ch0:ch0 + CH], p=sl3(ps_y, NS):
                     v.tensor_tensor(o, i_, p, ALU.add), sem="ev")
                P.wait("vector", "ld_b", xl)
                P.wait("vector", "ev", P.count("ev"))
                P.op("vector", lambda v, o=sl3(xchk2), i_=sl3(xchk2),
                     p=sl3(yacc)[:, :, ch0:ch0 + CH]:
                     v.tensor_tensor(o, i_, p, ALU.add), sem="vec")
                P.wait("sync", "vec", P.count("vec"))
                P.dma("sync", xT_r[:, :, t0g + ch0:t0g + ch0 + CH],
                      sl3(xchk2), "st_a")

        def run_w2(task):
            tpw, tch = task
            w2_batch(tpw, tch)
            if tch == NCH - 1 and 2 * tpw + 2 < NPE:
                P.wait("sync", "mm", w2mm_done[tpw])
                w2_done[:] = [load_w2(2 * tpw + 2, 0), load_w2(2 * tpw + 3, 1)]

        pending = []
        for pw in range(NPAIR):
            for ch in range(NCH):
                h_phase(pw, ch, 0)
                if gen is not None:
                    next(gen, None)
                h_phase(pw, ch, 1)
                if gen is not None:
                    next(gen, None)
                pending.append((pw, ch))
                if len(pending) >= 2:
                    run_w2(pending.pop(0))
        while pending:
            run_w2(pending.pop(0))
        if gen is not None:
            for _ in gen:
                pass

    INTERLEAVE = True
    for g in range(NG):
        gb = g % 2
        if g == 0:
            for _ in prep_gen(0, 0):
                pass
        gen = prep_gen(g + 1, (g + 1) % 2) if g + 1 < NG else None
        if not INTERLEAVE and gen is not None:
            for _ in gen:
                pass
            gen = None
        emit_waves(g, gb, gen)
    P.wait("sync", "st_a", P.count("st_a"))
    P.barrier()



def emit_attnb(c):
    P, t = c.P, c.t
    xT_r = dview(t["xT"])
    bT_r = dview(t["bT"])
    hT_r = dview(t["hT"])
    CA = 512
    CB = 256
    NCA = T // CA
    NCB = T // CB

    xa = carve(c.AR1, 41760, 128, NS * CA, F32)
    hca = carve(c.AR2, 32768, 128, NS * CA, F32)
    sqf = hca
    st1a = carve(c.AR2, 49152, 1, CA, F32)
    st2a = carve(c.AR2, 51200, 1, CA, F32)
    ps_a = c.psF[:, 3072:4096]

    b1a_store = {}

    def emit_b1a(ch):
        t0 = ch * CA
        P.wait("sync", "pvec", P.count("pvec"))
        P.wait("sync", "st_a", P.count("st_a"))
        xl = P.dma("sync", sl3(xa), xT_r[:, :, t0:t0 + CA], "ld_b")
        for round_ in range(2):
            src = xa
            P.wait("tensor", "ld_b", xl)
            P.wait("tensor", "pvec", P.count("pvec"))
            P.wait("tensor", "pscl", P.count("pscl"))
            for s in range(NS):
                P.op("tensor", lambda te, o=ps_a[0:1, 0:CA], l=c.s_ones,
                     r=src[:, s * CA:(s + 1) * CA], st=(s == 0), sp=(s == NS - 1):
                     te.matmul(o, l, r, start=st, stop=sp, skip_group_check=True),
                     sem="pmm" if s == NS - 1 else None)
            P.wait("vector", "pmm", P.count("pmm"))
            P.op("vector", lambda v, o=st1a, i_=ps_a[0:1, 0:CA]:
                 v.tensor_scalar(o, i_, -1.0 / D, None, ALU.mult), sem="pvec")
            for s in range(NS):
                P.op("vector", lambda v, o=sqf[:, s * CA:(s + 1) * CA],
                     i_=src[:, s * CA:(s + 1) * CA]:
                     v.tensor_mul(o, i_, i_), sem="pvec" if s == NS - 1 else None)
            P.wait("tensor", "pvec", P.count("pvec"))
            for s in range(NS):
                P.op("tensor", lambda te, o=ps_a[0:1, CA:2 * CA], l=c.s_ones,
                     r=sqf[:, s * CA:(s + 1) * CA], st=(s == 0), sp=(s == NS - 1):
                     te.matmul(o, l, r, start=st, stop=sp, skip_group_check=True),
                     sem="pmm" if s == NS - 1 else None)
            P.wait("vector", "pmm", P.count("pmm"))
            P.op("vector", lambda v, o=st2a, i_=st1a: v.tensor_mul(o, i_, i_),
                 sem="pvec")
            P.wait("vector", "pvec")
            P.op("vector", lambda v, o=st2a, i_=ps_a[0:1, CA:2 * CA], m=st2a:
                 v.scalar_tensor_tensor(o, i_, 1.0 / D, m, ALU.mult,
                                        ALU.subtract), sem="pvec")
            P.wait("scalar", "pvec", P.count("pvec"))
            P.op("scalar", lambda sc, o=st2a, i_=st2a, e=c.s_eps_row:
                 sc.activation(o, i_, AF.Sqrt, bias=e), sem="pscl")
            P.wait("vector", "pscl", P.count("pscl"))
            P.op("vector", lambda v, o=st2a, i_=st2a:
                 v.reciprocal(o, i_), sem="pvec")
            P.wait("tensor", "pvec", P.count("pvec"))
            P.op("tensor", lambda te, o=ps_a[:, 0:CA], l=c.s_ones_row,
                 r=st1a: te.matmul(o, l, r, start=True, stop=True,
                                   skip_group_check=True))
            P.op("tensor", lambda te, o=ps_a[:, CA:2 * CA], l=c.s_ones_row,
                 r=st2a: te.matmul(o, l, r, start=True, stop=True,
                                   skip_group_check=True), sem="pmm")
            P.wait("vector", "pmm", P.count("pmm"))
            dst = xa if round_ == 0 else hca
            if round_ == 1:
                P.wait("vector", "st_a", P.count("st_a"))
            for s in range(NS):
                P.op("vector", lambda v, o=dst[:, s * CA:(s + 1) * CA],
                     i_=src[:, s * CA:(s + 1) * CA], m=ps_a[:, 0:CA]:
                     v.tensor_tensor(o, i_, m, ALU.add))
            for s in range(NS):
                P.op("vector", lambda v, o=dst[:, s * CA:(s + 1) * CA],
                     i_=dst[:, s * CA:(s + 1) * CA], m=ps_a[:, CA:2 * CA]:
                     v.tensor_tensor(o, i_, m, ALU.mult),
                     sem="pvec" if s == NS - 1 else None)
            if round_ == 1:
                for s in range(NS):
                    P.op("vector", lambda v, o=hca[:, s * CA:(s + 1) * CA],
                         i_=hca[:, s * CA:(s + 1) * CA],
                         w_=c.s_vec["ln1_w"][:, s:s + 1],
                         b_=c.s_vec["ln1_b"][:, s:s + 1]:
                         v.tensor_scalar(o, i_, w_, b_, ALU.mult, ALU.add),
                         sem="pvec" if s == NS - 1 else None)
        P.wait("sync", "pvec", P.count("pvec"))
        P.dma("sync", bT_r[:, :, t0:t0 + CA], sl3(xa), "st_a")
        P.dma("sync", hT_r[:, :, t0:t0 + CA], sl3(hca), "st_a")
        b1a_store[ch] = P.count("st_a")

    wk_sb = carve(c.AR3, 0, 128, NS * 1024, BF16)
    wvv_sb = carve(c.AR3, 16384, 128, NS * 1024, BF16)
    wr_sb = carve(c.AR3, 32768, 128, NS * 1024, BF16)
    wo_sb = carve(c.AR3, 49152, 128, NS * 1024, BF16)
    for nm, sb in (("wkT_bf", wk_sb), ("wvT_bf", wvv_sb), ("wrT_bf", wr_sb),
                   ("woT_bf", wo_sb)):
        P.dma("sync", sl3(sb), dview(t[nm]), "ld_a")
    wld = P.count("ld_a")

    hcc0 = carve(c.AR1, 0, 128, NS * 257, F32)
    hcc = [hcc0, hcc0]
    kf = [carve(c.AR1, 8224, 128, NS * CB, F32),
          carve(c.AR1, 16416, 128, NS * CB, F32)]
    vf = [carve(c.AR1, 24608, 128, NS * CB, F32),
          carve(c.AR1, 32800, 128, NS * CB, F32)]
    srf = [carve(c.AR2, 0, 128, NS * CB, F32),
           carve(c.AR2, 8192, 128, NS * CB, F32)]
    xkb = [carve(c.AR2, 16384, 128, NS * CB, BF16),
           carve(c.AR2, 20480, 128, NS * CB, BF16)]
    xvb = [carve(c.AR2, 24576, 128, NS * CB, BF16),
           carve(c.AR2, 53248, 128, NS * CB, BF16)]
    xrb = [carve(c.AR2, 28672, 128, NS * CB, BF16),
           carve(c.AR2, 57344, 128, NS * CB, BF16)]
    e0 = carve(c.AR2, 61440, 128, CB, F32)
    e1 = carve(c.AR2, 62464, 128, CB, F32)
    ffb = carve(c.AR2, 63488, 128, CB, F32)
    yA = carve(c.AR4, 0, 128, CB + 1, F32)
    yB = carve(c.AR4, 1056, 128, CB + 1, F32)
    den = carve(c.AR4, 2112, 128, CB, F32)
    rwk = [carve(c.AR4, 3136, 128, NS * CB, BF16),
           carve(c.AR4, 7232, 128, NS * CB, BF16)]
    e0b2 = [carve(c.AR4, 11328, 128, CB, F32),
            carve(c.AR4, 12352, 128, CB, F32)]
    e1b2 = [carve(c.AR4, 13376, 128, CB, F32),
            carve(c.AR4, 14400, 128, CB, F32)]
    btc = carve(c.AR1, 58144, 128, 4 * CB, F32)
    ps_m = c.psF[:, 0:2048]

    b1b_state = {"mix": {}, "k": {}, "v": {}, "r": {}, "wkv": {}}

    lamb = carve(c.AR2, 64512, 128, NS, F32)
    kru = carve(c.AR2, 64544, 128, NS, F32)
    Ah = carve(c.AR2, 64576, 128, NS, F32)
    Bh = carve(c.AR2, 64608, 128, NS, F32)
    Mh = carve(c.AR2, 64640, 128, NS, F32)
    mu8 = carve(c.AR2, 64672, 128, NS, F32)
    nmu8 = carve(c.AR2, 64704, 128, NS, F32)
    sc8 = carve(c.AR2, 64736, 128, NS, F32)
    tf8 = carve(c.AR2, 64768, 128, NS, F32)

    P.wait("scalar", "ld_a", wld)
    P.op("scalar", lambda sc, o=lamb, i_=c.s_vec["time_decay"]:
         sc.activation(o, i_, AF.Exp), sem="scl")
    P.wait("scalar", "scl", P.count("scl"))
    P.op("scalar", lambda sc, o=lamb, i_=lamb:
         sc.activation(o, i_, AF.Exp, scale=-1.0), sem="scl")
    P.op("vector", lambda v, o=kru, i_=c.s_vec["time_first"]:
         v.tensor_scalar(o, i_, 0.0, None, ALU.max))
    P.op("vector", lambda v, o=Ah: v.memset(o, 0.0))
    P.op("vector", lambda v, o=Bh: v.memset(o, 0.0))
    P.op("vector", lambda v, o=Mh: v.memset(o, -1e30), sem="vec")

    def b1b_mix(ch):
        b = ch % 2
        t0 = ch * CB
        P.wait("sync", "vec", b1b_state["mix"].get(ch - 1, 0))
        P.wait("sync", "st_a", b1a_store[(t0 + CB - 1) // CA])
        if ch == 0:
            P.op("vector", lambda v, o=hcc[0]: v.memset(o, 0.0), sem="vec")
            P.wait("sync", "vec", P.count("vec"))
            hl = P.dma("sync", sl3(hcc[0], NS)[:, :, 1:CB + 1],
                       hT_r[:, :, 0:CB], "ld_a")
        else:
            hl = P.dma("sync", sl3(hcc[b], NS), hT_r[:, :, t0 - 1:t0 + CB],
                       "ld_a")
        P.wait("vector", "ld_a", hl)
        P.wait("vector", "mm", b1b_state["r"].get(ch - 2, 0))
        W1 = CB + 1
        for nm, dst in (("att_tmk", xkb), ("att_tmv", xvb), ("att_tmr", xrb)):
            for s in range(NS):
                P.op("vector",
                     lambda v, o=ffb, h_=hcc[b][:, s * W1 + 1:s * W1 + W1],
                     hh=hcc[b][:, s * W1:s * W1 + CB]:
                     v.tensor_sub(o, h_, hh))
                P.op("vector",
                     lambda v, o=dst[b][:, s * CB:(s + 1) * CB], i_=ffb,
                     w_=c.s_vec[nm][:, s:s + 1],
                     hh=hcc[b][:, s * W1:s * W1 + CB]:
                     v.scalar_tensor_tensor(o, i_, w_, hh, ALU.mult, ALU.add),
                     sem="vec" if (nm == "att_tmr" and s == NS - 1) else None)
        b1b_state["mix"][ch] = P.count("vec")

    def b1b_mms(ch):
        b = ch % 2
        for which, src_b, wsb, dstf in (("k", xkb, wk_sb, kf),
                                        ("v", xvb, wvv_sb, vf),
                                        ("r", xrb, wr_sb, srf)):
            P.wait("tensor", "vec", b1b_state["mix"][ch])
            P.wait("tensor", "ev", P.count("ev"))
            P.wait("tensor", "vec", b1b_state["wkv"].get(ch - 2, 0))
            for oc in range(NS):
                for s in range(NS):
                    P.op("tensor",
                         lambda te, o=ps_m[:, oc * CB:oc * CB + CB],
                         l=wsb[:, s * 1024 + oc * 128:s * 1024 + oc * 128 + 128],
                         r=src_b[b][:, s * CB:(s + 1) * CB],
                         st=(s == 0), sp=(s == NS - 1):
                         te.matmul(o, l, r, start=st, stop=sp,
                                   skip_group_check=True),
                         sem="mm" if (oc == NS - 1 and s == NS - 1) else None)
            P.wait("scalar", "mm", P.count("mm"))
            if which == "r":
                P.op("scalar", lambda sc, o=srf[b], i_=ps_m:
                     sc.activation(o, i_, AF.Sigmoid), sem="ev")
            else:
                P.op("scalar", lambda sc, o=dstf[b], i_=ps_m:
                     sc.copy(o, i_), sem="ev")
            b1b_state[which][ch] = P.count("mm")
            P.wait("tensor", "ev", P.count("ev"))

    def b1b_wkv(ch):
        b = ch % 2
        P.wait("vector", "ev", P.count("ev"))
        for s in range(NS):
            P.op("vector", lambda v, o=mu8[:, s:s + 1],
                 i_=kf[b][:, s * CB:(s + 1) * CB]:
                 v.reduce_max(o, i_, axis=AX.X),
                 sem="wbv" if s == NS - 1 else None)
        P.wait("vector", "wbv")
        P.op("vector", lambda v, o=mu8, i_=mu8, m=kru:
             v.tensor_tensor(o, i_, m, ALU.add))
        P.op("vector", lambda v, o=sc8, i_=Mh:
             v.tensor_scalar(o, i_, -CAP_DROP, None, ALU.add), sem="wbv")
        P.wait("vector", "wbv")
        P.op("vector", lambda v, o=mu8, i_=mu8, m=sc8:
             v.tensor_tensor(o, i_, m, ALU.max), sem="wbv")
        P.wait("vector", "wbv")
        P.op("vector", lambda v, o=nmu8, i_=mu8:
             v.tensor_scalar(o, i_, -1.0, None, ALU.mult))
        P.op("vector", lambda v, o=sc8, i_=Mh, m=mu8:
             v.tensor_tensor(o, i_, m, ALU.subtract))
        P.op("vector", lambda v, o=tf8, i_=c.s_vec["time_first"][:, 0:NS],
             m=mu8: v.tensor_tensor(o, i_, m, ALU.subtract), sem="vec")
        P.wait("scalar", "vec", P.count("vec"))
        P.op("scalar", lambda sc, o=sc8, i_=sc8:
             sc.activation(o, i_, AF.Exp), sem="scl")
        P.wait("vector", "scl", P.count("scl"))
        P.op("vector", lambda v, o=Ah, i_=Ah, m=sc8:
             v.tensor_tensor(o, i_, m, ALU.mult))
        P.op("vector", lambda v, o=Bh, i_=Bh, m=sc8:
             v.tensor_tensor(o, i_, m, ALU.mult))
        P.op("vector", lambda v, o=Mh, i_=mu8:
             v.tensor_copy(o, i_), sem="wbv")
        P.wait("vector", "wbv")
        ev_slab = {}
        for s in range(NS):
            eb = s % 2
            kfs = kf[b][:, s * CB:(s + 1) * CB]
            vfs = vf[b][:, s * CB:(s + 1) * CB]
            if s >= 2:
                P.wait("scalar", "vec", ev_slab[s - 2])
            P.op("scalar", lambda sc, o=e0b2[eb], i_=kfs, b_=nmu8[:, s:s + 1]:
                 sc.activation(o, i_, AF.Exp, bias=b_))
            P.op("scalar", lambda sc, o=e1b2[eb], i_=kfs, b_=tf8[:, s:s + 1]:
                 sc.activation(o, i_, AF.Exp, bias=b_), sem="scl")
            P.wait("vector", "scl", P.count("scl"))
            P.op("vector", lambda v, o=ffb, i_=e0b2[eb], m=vfs:
                 v.tensor_tensor(o, i_, m, ALU.mult))
            P.op("vector", lambda v, o=yA[:, 0:1], i_=Ah[:, s:s + 1]:
                 v.tensor_copy(o, i_))
            P.op("vector", lambda v, o=yB[:, 0:1], i_=Bh[:, s:s + 1]:
                 v.tensor_copy(o, i_), sem="wbv")
            P.wait("vector", "wbv")
            P.op("vector", lambda v, o=yA[:, 1:CB + 1],
                 d0=lamb[:, s:s + 1].broadcast_to((128, CB)), d1=ffb,
                 init=Ah[:, s:s + 1]:
                 v.tensor_tensor_scan(o, d0, d1, init, ALU.mult, ALU.add))
            P.op("vector", lambda v, o=yB[:, 1:CB + 1],
                 d0=lamb[:, s:s + 1].broadcast_to((128, CB)), d1=e0b2[eb],
                 init=Bh[:, s:s + 1]:
                 v.tensor_tensor_scan(o, d0, d1, init, ALU.mult, ALU.add),
                 sem="vec")
            P.wait("vector", "vec")
            P.op("vector", lambda v, o=den, i_=yB[:, 0:CB], m=e1b2[eb]:
                 v.tensor_tensor(o, i_, m, ALU.add))
            P.op("vector", lambda v, o=den, i_=den: v.reciprocal(o, i_))
            P.op("vector", lambda v, o=e1b2[eb], i_=e1b2[eb], m=vfs:
                 v.tensor_tensor(o, i_, m, ALU.mult))
            P.op("vector", lambda v, o=e1b2[eb], i_=e1b2[eb], m=yA[:, 0:CB]:
                 v.tensor_tensor(o, i_, m, ALU.add), sem="wbv")
            P.wait("vector", "wbv")
            P.op("vector", lambda v, o=e1b2[eb], i_=e1b2[eb], m=den:
                 v.tensor_tensor(o, i_, m, ALU.mult))
            P.op("vector", lambda v, o=rwk[b][:, s * CB:(s + 1) * CB],
                 i_=e1b2[eb], m=srf[b][:, s * CB:(s + 1) * CB]:
                 v.tensor_tensor(o, i_, m, ALU.mult))
            P.op("vector", lambda v, o=Ah[:, s:s + 1], i_=yA[:, CB:CB + 1]:
                 v.tensor_copy(o, i_))
            P.op("vector", lambda v, o=Bh[:, s:s + 1], i_=yB[:, CB:CB + 1]:
                 v.tensor_copy(o, i_), sem="vec")
            ev_slab[s] = P.count("vec")
        b1b_state["wkv"][ch] = P.count("vec")

    def b1b_wo(ch):
        b = ch % 2
        t0 = ch * CB
        P.wait("tensor", "vec", b1b_state["wkv"][ch])
        P.wait("tensor", "ev", P.count("ev"))
        for oc in range(NS):
            for s in range(NS):
                P.op("tensor",
                     lambda te, o=ps_m[:, oc * CB:oc * CB + CB],
                     l=wo_sb[:, s * 1024 + oc * 128:s * 1024 + oc * 128 + 128],
                     r=rwk[b][:, s * CB:(s + 1) * CB],
                     st=(s == 0), sp=(s == NS - 1):
                     te.matmul(o, l, r, start=st, stop=sp,
                               skip_group_check=True),
                     sem="mm" if (oc == NS - 1 and s == NS - 1) else None)
        mmw = P.count("mm")
        for hh2 in range(2):
            P.wait("sync", "ev", P.count("ev"))
            bl = P.dma("sync", sl3(btc, 4),
                       bT_r[:, hh2 * 4:(hh2 + 1) * 4, t0:t0 + CB], "ld_b")
            P.wait("vector", "mm", mmw)
            P.wait("vector", "ld_b", bl)
            P.op("vector", lambda v, o=btc, i_=btc,
                 p=ps_m[:, hh2 * 1024:(hh2 + 1) * 1024]:
                 v.tensor_tensor(o, i_, p, ALU.add), sem="ev")
            P.wait("sync", "ev", P.count("ev"))
            P.dma("sync", bT_r[:, hh2 * 4:(hh2 + 1) * 4, t0:t0 + CB],
                  sl3(btc, 4), "st_a")

    emit_b1a(0)
    for ch in range(NCB):
        if ch % 2 == 0 and ch // 2 + 1 < NCA:
            emit_b1a(ch // 2 + 1)
        b1b_mix(ch)
        b1b_mms(ch)
        b1b_wkv(ch)
        b1b_wo(ch)
    P.barrier()


    CH2 = 256
    NCH2 = T // CH2
    wkey_sb = carve(c.AR3, 0, 128, NS * FFN, BF16)
    wval_sb = carve(c.AR1, 0, 128, 32 * 1024, BF16)
    wrec_sb = carve(c.AR4, 0, 128, NS * 1024, BF16)
    P.dma("sync", sl3(wkey_sb), dview(t["wkeyT_bf"]), "ld_a")
    wkey_n = P.count("ld_a")
    P.dma("sync", sl3(wval_sb, s=32),
          t["wvalT_bf"].rearrange("(s p) q -> p s q", p=128), "ld_a")
    wval_n = P.count("ld_a")
    P.dma("sync", sl3(wrec_sb), dview(t["wrecT_bf"]), "ld_a")
    wrec_n = P.count("ld_a")

    bcc = carve(c.AR2, 0, 128, NS * 257, F32)
    h2c = carve(c.AR2, 8224, 128, NS * 257, F32)
    sq2 = carve(c.AR2, 16448, 128, NS * CH2, F32)
    xk2 = carve(c.AR2, 24640, 128, NS * CH2, BF16)
    xr2 = carve(c.AR2, 28736, 128, NS * CH2, BF16)
    kkr = carve(c.AR2, 32832, 128, CH2, F32)
    kkr_b = carve(c.AR2, 58432, 128, CH2, F32)
    kk2 = carve(c.AR2, 33856, 128, 32 * CH2, BF16)
    rf = carve(c.AR2, 50240, 128, NS * CH2, F32)
    st1b = carve(c.AR2, 62528, 1, 257, F32)
    st2b = carve(c.AR2, 63680, 1, 257, F32)
    ps_kk = [c.psF[:, 0:256], c.psF[:, 512:768]]
    ps_kv = c.psF[:, 2048:4096]
    ps_rr = c.psF[:, 1024:2048]
    ps_stb = c.psF[:, 0:1024]

    for ch in range(NCH2):
        t0 = ch * CH2
        P.wait("sync", "vec", P.count("vec"))
        if ch == 0:
            P.op("vector", lambda v, o=bcc: v.memset(o, 0.0), sem="vec")
            P.wait("sync", "vec", P.count("vec"))
            bl = P.dma("sync", sl3(bcc, NS)[:, :, 1:257], bT_r[:, :, 0:CH2],
                       "ld_a")
        else:
            bl = P.dma("sync", sl3(bcc, NS), bT_r[:, :, t0 - 1:t0 + CH2], "ld_a")
        P.wait("tensor", "ld_a", bl)
        P.wait("tensor", "vec", P.count("vec"))
        P.wait("tensor", "scl", P.count("scl"))
        W = 257
        for s in range(NS):
            P.op("tensor", lambda te, o=ps_stb[0:1, 0:W], l=c.s_ones,
                 r=bcc[:, s * 257:(s + 1) * 257], st=(s == 0), sp=(s == NS - 1):
                 te.matmul(o, l, r, start=st, stop=sp, skip_group_check=True),
                 sem="mm" if s == NS - 1 else None)
        P.wait("vector", "mm", P.count("mm"))
        P.op("vector", lambda v, o=st1b, i_=ps_stb[0:1, 0:W]:
             v.tensor_scalar(o, i_, -1.0 / D, None, ALU.mult), sem="vec")
        for s in range(NS):
            P.op("vector", lambda v, o=h2c[:, s * 257:(s + 1) * 257],
                 i_=bcc[:, s * 257:(s + 1) * 257]:
                 v.tensor_mul(o, i_, i_), sem="vec" if s == NS - 1 else None)
        P.wait("tensor", "vec", P.count("vec"))
        for s in range(NS):
            P.op("tensor", lambda te, o=ps_stb[0:1, 512:512 + W], l=c.s_ones,
                 r=h2c[:, s * 257:(s + 1) * 257], st=(s == 0), sp=(s == NS - 1):
                 te.matmul(o, l, r, start=st, stop=sp, skip_group_check=True),
                 sem="mm" if s == NS - 1 else None)
        P.wait("vector", "mm", P.count("mm"))
        P.op("vector", lambda v, o=st2b, i_=st1b: v.tensor_mul(o, i_, i_),
             sem="wbv")
        P.wait("vector", "wbv")
        P.op("vector", lambda v, o=st2b, i_=ps_stb[0:1, 512:512 + W], m=st2b:
             v.scalar_tensor_tensor(o, i_, 1.0 / D, m, ALU.mult, ALU.subtract),
             sem="vec")
        P.wait("scalar", "vec", P.count("vec"))
        P.op("scalar", lambda sc, o=st2b, i_=st2b, e=c.s_eps_row:
             sc.activation(o, i_, AF.Sqrt, bias=e), sem="scl")
        P.wait("vector", "scl", P.count("scl"))
        P.op("vector", lambda v, o=st2b, i_=st2b: v.reciprocal(o, i_), sem="vec")
        P.wait("tensor", "vec", P.count("vec"))
        P.op("tensor", lambda te, o=ps_stb[:, 0:W].opt(), l=c.s_ones_row,
             r=st1b: te.matmul(o, l, r, start=True, stop=True,
                               skip_group_check=True))
        P.op("tensor", lambda te, o=ps_stb[:, 512:512 + W], l=c.s_ones_row,
             r=st2b: te.matmul(o, l, r, start=True, stop=True,
                               skip_group_check=True), sem="mm")
        P.wait("vector", "mm", P.count("mm"))
        for s in range(NS):
            P.op("vector", lambda v, o=h2c[:, s * 257:(s + 1) * 257],
                 i_=bcc[:, s * 257:(s + 1) * 257], m=ps_stb[:, 0:W]:
                 v.tensor_tensor(o, i_, m, ALU.add))
        for s in range(NS):
            P.op("vector", lambda v, o=h2c[:, s * 257:(s + 1) * 257],
                 i_=h2c[:, s * 257:(s + 1) * 257], m=ps_stb[:, 512:512 + W]:
                 v.tensor_tensor(o, i_, m, ALU.mult))
        for s in range(NS):
            P.op("vector", lambda v, o=h2c[:, s * 257:(s + 1) * 257],
                 i_=h2c[:, s * 257:(s + 1) * 257],
                 w_=c.s_vec["ln2_w"][:, s:s + 1], b_=c.s_vec["ln2_b"][:, s:s + 1]:
                 v.tensor_scalar(o, i_, w_, b_, ALU.mult, ALU.add),
                 sem="vec" if s == NS - 1 else None)
        if ch == 0:
            P.op("vector", lambda v, o=sl3(h2c, NS)[:, :, 0:1]:
                 v.memset(o, 0.0), sem="wbv")
            P.wait("vector", "wbv")
        for nm, dst in (("ffn_tmk", xk2), ("ffn_tmr", xr2)):
            for s in range(NS):
                P.op("vector",
                     lambda v, o=kkr, h_=h2c[:, s * 257 + 1:s * 257 + 257],
                     hh=h2c[:, s * 257:s * 257 + 256]:
                     v.tensor_sub(o, h_, hh))
                P.op("vector",
                     lambda v, o=dst[:, s * CH2:(s + 1) * CH2], i_=kkr,
                     w_=c.s_vec[nm][:, s:s + 1],
                     hh=h2c[:, s * 257:s * 257 + 256]:
                     v.scalar_tensor_tensor(o, i_, w_, hh, ALU.mult, ALU.add),
                     sem="vec" if (nm == "ffn_tmr" and s == NS - 1) else None)
        mixn = P.count("vec")
        P.wait("tensor", "vec", mixn)
        if ch == 0:
            P.wait("tensor", "ld_a", wkey_n)
        ev_after = {}
        kkr2 = [kkr, kkr_b]
        for oc in range(32):
            pb = oc % 2
            if oc >= 2:
                P.wait("tensor", "ev", ev_after[oc - 2])
            for s in range(NS):
                P.op("tensor", lambda te, o=ps_kk[pb],
                     l=wkey_sb[:, s * FFN + oc * 128:s * FFN + oc * 128 + 128],
                     r=xk2[:, s * CH2:(s + 1) * CH2],
                     st=(s == 0), sp=(s == NS - 1):
                     te.matmul(o, l, r, start=st, stop=sp,
                               skip_group_check=True),
                     sem="mm" if s == NS - 1 else None)
            P.wait("scalar", "mm", P.count("mm"))
            if oc >= 2:
                P.wait("scalar", "ev", ev_after[oc - 2])
            P.op("scalar", lambda sc, o=kkr2[pb], i_=ps_kk[pb]:
                 sc.activation(o, i_, AF.Relu), sem="scl")
            P.wait("vector", "scl", P.count("scl"))
            P.op("vector", lambda v, o=kk2[:, oc * CH2:(oc + 1) * CH2],
                 i_=kkr2[pb], m=kkr2[pb]:
                 v.tensor_tensor(o, i_, m, ALU.mult), sem="ev")
            ev_after[oc] = P.count("ev")
        P.wait("tensor", "ev", P.count("ev"))
        if ch == 0:
            P.wait("tensor", "ld_a", wval_n)
        for oc in range(NS):
            for s in range(32):
                P.op("tensor", lambda te, o=ps_kv[:, oc * 256:oc * 256 + 256],
                     l=wval_sb[:, s * 1024 + oc * 128:s * 1024 + oc * 128 + 128],
                     r=kk2[:, s * CH2:(s + 1) * CH2],
                     st=(s == 0), sp=(s == 31):
                     te.matmul(o, l, r, start=st, stop=sp,
                               skip_group_check=True),
                     sem="mm" if (oc == NS - 1 and s == 31) else None)
        rr_slots = [c.psF[:, 1024:1280], c.psF[:, 1536:1792]]
        if ch == 0:
            P.wait("tensor", "ld_a", wrec_n)
        scl_after_r = {}
        for oc in range(NS):
            pb = oc % 2
            if oc >= 2:
                P.wait("tensor", "scl", scl_after_r[oc - 2])
            for s in range(NS):
                P.op("tensor", lambda te, o=rr_slots[pb],
                     l=wrec_sb[:, s * 1024 + oc * 128:s * 1024 + oc * 128 + 128],
                     r=xr2[:, s * CH2:(s + 1) * CH2],
                     st=(s == 0), sp=(s == NS - 1):
                     te.matmul(o, l, r, start=st, stop=sp,
                               skip_group_check=True),
                     sem="mm" if s == NS - 1 else None)
            P.wait("scalar", "mm", P.count("mm"))
            P.op("scalar", lambda sc, o=rf[:, oc * CH2:(oc + 1) * CH2],
                 i_=rr_slots[pb]:
                 sc.activation(o, i_, AF.Sigmoid), sem="scl")
            scl_after_r[oc] = P.count("scl")
        P.wait("tensor", "scl", P.count("scl"))
        P.wait("sync", "scl", P.count("scl"))
        P.wait("sync", "st_a", P.count("st_a"))
        xl = P.dma("sync", sl3(sq2), xT_r[:, :, t0:t0 + CH2], "ld_b")
        P.wait("vector", "mm", P.count("mm"))
        P.wait("vector", "scl", P.count("scl"))
        P.wait("vector", "ld_b", xl)
        P.op("vector", lambda v, o=rf, i_=rf, p=ps_kv:
             v.tensor_tensor(o, i_, p, ALU.mult), sem="ev")
        P.op("vector", lambda v, o=sq2, i_=sq2,
             p=sl3(bcc, NS)[:, :, 1:257].rearrange("p s c -> p (s c)")
             if False else sq2: v.tensor_copy(o, i_)) if False else None
        for s in range(NS):
            P.op("vector", lambda v, o=sq2[:, s * CH2:(s + 1) * CH2],
                 i_=sq2[:, s * CH2:(s + 1) * CH2],
                 m=bcc[:, s * 257 + 1:s * 257 + 257]:
                 v.tensor_tensor(o, i_, m, ALU.add))
        P.op("vector", lambda v, o=sq2, i_=sq2, m=rf:
             v.tensor_tensor(o, i_, m, ALU.add), sem="vec")
        P.wait("sync", "vec", P.count("vec"))
        P.dma("sync", xT_r[:, :, t0:t0 + CH2], sl3(sq2), "st_a")
        P.wait("tensor", "vec", P.count("vec"))
    P.barrier()



def emit_final(c):
    P, t = c.P, c.t
    xT_r = dview(t["xT"])
    xst = [carve(c.AR1, 0, 128, 1024, F32), carve(c.AR1, 4096, 128, 1024, F32)]
    ost = [carve(c.AR1, 8192, 128, 1024, F32), carve(c.AR1, 12288, 128, 1024, F32)]
    ps32 = c.psF[:, 0:1024]
    for ti in range(T // 128):
        b = ti % 2
        if ti >= 2:
            P.wait("sync", "mm", P.count("mm"))
        xl = P.dma("sync", sl3(xst[b]), xT_r[:, :, ti * 128:(ti + 1) * 128], "ld_a")
        P.wait("tensor", "ld_a", xl)
        P.wait("tensor", "scl", P.count("scl"))
        for s in range(NS):
            P.op("tensor", lambda te, o=ps32[:, s * 128:(s + 1) * 128],
                 i_=xst[b][:, s * 128:(s + 1) * 128], idn=c.s_ident_f:
                 te.transpose(o, i_, idn), sem="mm" if s == NS - 1 else None)
        P.wait("scalar", "mm", P.count("mm"))
        P.wait("scalar", "st_a", P.count("st_a"))
        evn = P.op("scalar", lambda sc, o=ost[b], i_=ps32: sc.copy(o, i_),
                   sem="scl")
        P.wait("sync", "scl", evn)
        P.dma("sync", t["out"][ti * 128:(ti + 1) * 128, :], ost[b], "st_a")
    P.wait("sync", "st_a", P.count("st_a"))



def build_program(phases=ALL_PHASES, debug_names=(), debug_sbuf=()):
    nc = bass.Bass("TRN2", target_bir_lowering=False, debug=False,
                   num_devices=NCORES, dynamic_dma_scratch_size=8192)

    t = {}
    t["x_in"] = nc.dram_tensor("x", [T, D], F32, kind="ExternalInput").ap()
    t["wq_in"] = nc.dram_tensor("wq", [D, D], F32, kind="ExternalInput").ap()
    t["wk_a_in"] = nc.dram_tensor("wk_a", [D, D], F32, kind="ExternalInput").ap()
    t["wv_a_in"] = nc.dram_tensor("wv_a", [D, D], F32, kind="ExternalInput").ap()
    VEC_NAMES = ["ln1_w", "ln1_b", "ln2_w", "ln2_b", "att_tmk", "att_tmv",
                 "att_tmr", "time_decay", "time_first", "ffn_tmk", "ffn_tmr"]
    for nm in VEC_NAMES:
        t[nm + "_in"] = nc.dram_tensor(nm, [128, NS], F32, kind="ExternalInput").ap()
    for nm, shape in [("wkb", [D, D]), ("wvb", [D, D]), ("wrb", [D, D]),
                      ("wob", [D, D]), ("wkey", [FFN, D]), ("wrec", [D, D]),
                      ("wval", [D, FFN]), ("wg", [D, NE]),
                      ("w1", [NE, D, EH]), ("w2", [NE, EH, D]), ("w3", [NE, D, EH]),
                      ("ws1", [D, D]), ("ws2", [D, D]), ("ws3", [D, D])]:
        t[nm + "_in"] = nc.dram_tensor(nm, shape, F32, kind="ExternalInput").ap()
    t["ident_f_in"] = nc.dram_tensor("ident_f", [128, 128], F32, kind="ExternalInput").ap()
    t["ident_b_in"] = nc.dram_tensor("ident_b", [128, 128], BF16, kind="ExternalInput").ap()
    t["ones_in"] = nc.dram_tensor("ones", [128, 128], F32, kind="ExternalInput").ap()
    t["causal_in"] = nc.dram_tensor("causal", [128, 128], F32, kind="ExternalInput").ap()
    t["onehot_in"] = nc.dram_tensor("onehot", [NE, NE * 128], F32, kind="ExternalInput").ap()
    t["iota32_in"] = nc.dram_tensor("iota32", [128, 2 * NE], F32, kind="ExternalInput").ap()
    t["iotap_in"] = nc.dram_tensor("iotap", [32, 128], F32, kind="ExternalInput").ap()
    t["eps_in"] = nc.dram_tensor("epsv", [128, 2], F32, kind="ExternalInput").ap()
    t["out"] = nc.dram_tensor("out", [T, D], F32, kind="ExternalOutput").ap()

    t["xT"] = nc.dram_tensor("xT", [D, T], F32, kind="Internal").ap()
    t["bT"] = nc.dram_tensor("bT", [D, T], F32, kind="Internal").ap()
    t["hT"] = nc.dram_tensor("hT", [D, T], F32, kind="Internal").ap()
    t["qT"] = nc.dram_tensor("qT", [D, T], BF16, kind="Internal").ap()
    for nm, shape in [("wqpm", [128, NS * D]), ("wkapm", [128, NS * D]),
                      ("wvapm", [128, NS * D]),
                      ("wkT_bf", [D, D]), ("wvT_bf", [D, D]), ("wrT_bf", [D, D]),
                      ("woT_bf", [D, D]), ("wrecT_bf", [D, D]),
                      ("wkeyT_bf", [D, FFN]), ("wvalT_bf", [FFN, D]),
                      ("w1pm", [NE, 128, NS * EH]), ("w2pm", [NE, 128, 4 * D]),
                      ("w3pm", [NE, 128, NS * EH]),
                      ("ws1pm", [2, 128, NS * EH]), ("ws2pm", [2, 128, 4 * D]),
                      ("ws3pm", [2, 128, NS * EH])]:
        t[nm] = nc.dram_tensor(nm, shape, BF16, kind="Internal").ap()

    sem_names = ["bar", "ld_a", "ld_b", "st_a", "st_b", "mm", "ev", "vec",
                 "scl", "pool", "wbv", "wbs", "gpc", "pmm", "pvec", "pscl"]

    with contextlib.ExitStack() as stack:
        AR1h = stack.enter_context(nc.sbuf_tensor("AR1", [128, 32768], BF16))
        AR2h = stack.enter_context(nc.sbuf_tensor("AR2", [128, 32768], BF16))
        AR3h = stack.enter_context(nc.sbuf_tensor("AR3", [128, 32768], BF16))
        AR4h = stack.enter_context(nc.sbuf_tensor("AR4", [128, 8192], BF16))
        identfh = stack.enter_context(nc.sbuf_tensor("sb_idf", [128, 128], F32))
        identbh = stack.enter_context(nc.sbuf_tensor("sb_idb", [128, 128], BF16))
        onesh = stack.enter_context(nc.sbuf_tensor("sb_ones", [128, 128], F32))
        causalh = stack.enter_context(nc.sbuf_tensor("sb_causal", [128, 128], F32))
        iotah = stack.enter_context(nc.sbuf_tensor("sb_iota", [128, 2 * NE], F32))
        iotaph = stack.enter_context(nc.sbuf_tensor("sb_iotap", [32, 128], F32))
        epsh = stack.enter_context(nc.sbuf_tensor("sb_eps", [128, 2], F32))
        wgh = stack.enter_context(nc.sbuf_tensor("sb_wg", [128, NS * NE], F32))
        vech = stack.enter_context(nc.sbuf_tensor("sb_vecs", [128, 11 * NS], F32))
        psFh = stack.enter_context(nc.psum_tensor("psF", [128, 4096], F32))

        blk = stack.enter_context(nc.Block())
        sems = {nm: stack.enter_context(nc.semaphore(nm)) for nm in sem_names}

        P = Prog(nc)
        for nm in sem_names:
            P.register_sem(nm, sems[nm])

        c = Ctx()
        c.P, c.t, c.nc = P, t, nc
        c.AR1, c.AR2, c.AR3, c.AR4 = AR1h[:], AR2h[:], AR3h[:], AR4h[:]
        c.LN0T = AR2h[:]
        c.KT = AR3h[:]
        c.V = AR1h[:]
        c.s_ident_f, c.s_ident_bf = identfh[:], identbh[:]
        c.s_ones = onesh[:, 0:1]
        c.s_ones_row = onesh[0:1, :]
        c.s_causal = causalh[:]
        c.s_iota32 = iotah[:, 0:NE]
        c.s_iotap = iotaph[:]
        c.s_iotam64 = iotah[:, NE:2 * NE]
        c.s_eps = epsh[:, 0:1]
        c.s_eps_row = epsh[0:1, 1:2]
        c.WG = wgh[:]
        c.psF = psFh[:]
        c.s_vec = {nm: vech[:, i * NS:(i + 1) * NS]
                   for i, nm in enumerate(VEC_NAMES)}

        P.dma("sync", c.s_ident_f, t["ident_f_in"], "ld_a")
        P.dma("sync", c.s_ident_bf, t["ident_b_in"], "ld_a")
        P.dma("sync", onesh[:], t["ones_in"], "ld_a")
        P.dma("sync", c.s_causal, t["causal_in"], "ld_a")
        P.dma("sync", iotah[:], t["iota32_in"], "ld_a")
        P.dma("sync", iotaph[:], t["iotap_in"], "ld_a")
        P.dma("sync", epsh[:], t["eps_in"], "ld_a")
        for s in range(NS):
            P.dma("sync", wgh[:, s * NE:(s + 1) * NE],
                  t["wg_in"][s * 128:(s + 1) * 128, :], "ld_a")
        for nm in VEC_NAMES:
            P.dma("sync", c.s_vec[nm], t[nm + "_in"], "ld_a")
        P.wait("sync", "ld_a", P.count("ld_a"))
        P.barrier()

        if "prep" in phases:
            emit_prep(c)
        if "a1" in phases:
            emit_a1(c)
        if "a2" in phases:
            emit_a2(c)
        if "a3" in phases:
            emit_a3(c)
        if "moe1" in phases:
            emit_moe(c)
        if "attnb" in phases:
            emit_attnb(c)
        if "moe2" in phases:
            emit_moe(c)
        if "final" in phases:
            emit_final(c)
        else:
            for ti in range(0, T, 128):
                sb = carve(c.AR4, 0, 128, 1024, F32)
                P.wait("sync", "st_b", P.count("st_b"))
                P.dma("sync", sb, t["x_in"][ti:ti + 128, :], "ld_b")
                P.wait("gpsimd", "ld_b", P.count("ld_b"))
                P.dma("gpsimd", t["out"][ti:ti + 128, :], sb, "st_b")
            P.wait("sync", "st_b", P.count("st_b"))

        if debug_names:
            P.barrier()
            for nm in debug_names:
                src = t[nm]
                dbg = nc.dram_tensor("dbg_" + nm, list(src.shape), src.dtype,
                                     kind="ExternalOutput").ap()
                P.dma("gpsimd", dbg, src, "st_b")
            P.wait("gpsimd", "st_b", P.count("st_b"))
        if debug_sbuf:
            P.barrier()
            for nm, ap in debug_sbuf:
                dbg = nc.dram_tensor("dbg_" + nm, list(ap(c).shape), ap(c).dtype,
                                     kind="ExternalOutput").ap()
                P.dma("gpsimd", dbg, ap(c), "st_b")
            P.wait("gpsimd", "st_b", P.count("st_b"))

        for name, eng_sel in (("sync", blk.sync), ("vector", blk.vector),
                              ("scalar", blk.scalar), ("tensor", blk.tensor),
                              ("gpsimd", blk.gpsimd)):
            fns = P.plan[name]
            eng_sel(lambda e, fns=fns: [f(e) for f in fns])

    return nc


def make_inputs(inputs):
    x = np.ascontiguousarray(inputs["L0_attn_a_embeddings"], dtype=np.float32)

    def slabvec(v):
        return np.ascontiguousarray(np.asarray(v, np.float32).reshape(NS, 128).T)

    base = {
        "wq": inputs["L0_attn_a_W_Q"], "wk_a": inputs["L0_attn_a_W_K"],
        "wv_a": inputs["L0_attn_a_W_V"],
        "wkb": inputs["L0_attn_b_Wk"], "wvb": inputs["L0_attn_b_Wv"],
        "wrb": inputs["L0_attn_b_Wr"], "wob": inputs["L0_attn_b_Wo"],
        "wkey": inputs["L0_attn_b_Wkey"], "wrec": inputs["L0_attn_b_Wrec"],
        "wval": inputs["L0_attn_b_Wval"],
        "wg": inputs["L0_ffn_Wg"], "w1": inputs["L0_ffn_W1"],
        "w2": inputs["L0_ffn_W2"], "w3": inputs["L0_ffn_W3"],
        "ws1": inputs["L0_ffn_Ws1"], "ws2": inputs["L0_ffn_Ws2"],
        "ws3": inputs["L0_ffn_Ws3"],
    }
    base = {k: np.ascontiguousarray(v, dtype=np.float32) for k, v in base.items()}
    for nm, key in [("ln1_w", "L0_attn_b_ln1_w"), ("ln1_b", "L0_attn_b_ln1_b"),
                    ("ln2_w", "L0_attn_b_ln2_w"), ("ln2_b", "L0_attn_b_ln2_b"),
                    ("att_tmk", "L0_attn_b_att_tmk"),
                    ("att_tmv", "L0_attn_b_att_tmv"),
                    ("att_tmr", "L0_attn_b_att_tmr"),
                    ("time_decay", "L0_attn_b_time_decay"),
                    ("time_first", "L0_attn_b_time_first"),
                    ("ffn_tmk", "L0_attn_b_ffn_tmk"),
                    ("ffn_tmr", "L0_attn_b_ffn_tmr")]:
        base[nm] = slabvec(inputs[key])

    import ml_dtypes
    base["ident_f"] = np.eye(128, dtype=np.float32)
    base["ident_b"] = np.eye(128).astype(ml_dtypes.bfloat16)
    base["ones"] = np.ones((128, 128), np.float32)
    cz = np.zeros((128, 128), np.float32)
    i_idx = np.arange(128)[:, None]
    j_idx = np.arange(128)[None, :]
    cz[j_idx > i_idx] = -1e30 / SCALE_A
    base["causal"] = cz
    oh = np.zeros((NE, NE * 128), np.float32)
    for e in range(NE):
        oh[e, e * 128:(e + 1) * 128] = 1.0
    base["onehot"] = oh
    io = np.tile(np.arange(NE, dtype=np.float32), (128, 1))
    base["iota32"] = np.ascontiguousarray(
        np.concatenate([io, io - 64.0], axis=1))
    base["iotap"] = np.ascontiguousarray(
        np.tile(np.arange(32, dtype=np.float32)[:, None], (1, 128)))
    base["epsv"] = np.full((128, 2), EPS, np.float32)

    in_maps = []
    for cc in range(NCORES):
        m = dict(base)
        m["x"] = np.ascontiguousarray(x[cc])
        in_maps.append(m)
    return in_maps


def kernel(**inputs):
    in_maps = make_inputs(inputs)
    nc = build_program()
    res = run_bass_kernel_spmd(nc, in_maps, core_ids=list(range(NCORES)))
    out = np.stack([res.results[cc]["out"] for cc in range(NCORES)], axis=0)
    return out.astype(np.float32)


# version 13
# speedup vs baseline: 1.1094x; 1.0190x over previous
import contextlib
import numpy as np
import concourse.bass as bass
import concourse.mybir as mybir
from concourse.bass_utils import run_bass_kernel_spmd

F32 = mybir.dt.float32
BF16 = mybir.dt.bfloat16
AF = mybir.ActivationFunctionType
ALU = mybir.AluOpType
AX = mybir.AxisListType

D = 1024
T = 4096
NS = D // 128
NE = 32
EH = 512
FFN = 4096
NCORES = 8
EPS = 1e-5
TOPK = 4
CAP_DROP = 70.0
SCALE_A = 1.0 / 32.0

ENGS = ("sync", "vector", "scalar", "tensor", "gpsimd")
ALL_PHASES = ("prep", "a1", "a2", "a3", "moe1", "attnb", "moe2", "final")


class Prog:
    def __init__(self, nc):
        self.nc = nc
        self.plan = {e: [] for e in ENGS}
        self.sem_handles = {}
        self.sem_counts = {}

    def register_sem(self, name, handle):
        self.sem_handles[name] = handle
        self.sem_counts[name] = 0

    def bump(self, name, k):
        self.sem_counts[name] += k
        return self.sem_counts[name]

    def count(self, name):
        return self.sem_counts[name]

    def emit(self, eng, fn):
        self.plan[eng].append(fn)

    def dma(self, eng, out, in_, sem):
        h = self.sem_handles[sem]
        tgt = self.bump(sem, 16)

        def f(e, out=out, in_=in_, h=h):
            e.dma_start(out=out, in_=in_).then_inc(h, 16)
        self.emit(eng, f)
        return tgt

    def wait(self, eng, sem, val=None):
        h = self.sem_handles[sem]
        v = self.count(sem) if val is None else val
        if v <= 0:
            return

        def f(e, h=h, v=v):
            e.wait_ge(h, v)
        self.emit(eng, f)

    def op(self, eng, fn, sem=None, k=1):
        if sem is None:
            self.emit(eng, fn)
            return None
        h = self.sem_handles[sem]
        tgt = self.bump(sem, k)

        def f(e, fn=fn, h=h, k=k):
            fn(e).then_inc(h, k)
        self.emit(eng, f)
        return tgt

    def barrier(self):
        h = self.sem_handles["bar"]
        base = self.count("bar")
        for e in ENGS:
            self.bump("bar", 1)
        tgt = base + len(ENGS)
        for e in ENGS:
            def f(eng, h=h, tgt=tgt):
                eng.nop().then_inc(h, 1)
                eng.wait_ge(h, tgt)
            self.emit(e, f)


class Ctx:
    pass


def carve(arena_ap, byte_off, rows, cols, dtype):
    nb = mybir.dt.size(dtype)
    assert byte_off % 2 == 0
    start = byte_off // 2
    width = cols * nb // 2
    sl = arena_ap[0:rows, start:start + width]
    if dtype == BF16:
        return sl
    return sl.bitcast(dtype)


def sl3(ap, s=NS):
    return ap.rearrange("p (s c) -> p s c", s=s)


def dview(dram_ap):
    return dram_ap.rearrange("(s p) t -> p s t", p=128)



def emit_prep(c):
    P, t = c.P, c.t
    c.prep_w_done = {}

    def cast_pm(dst_pm, src, s_count, width):
        P.dma("gpsimd", dst_pm.rearrange("p (s h) -> p s h", s=s_count),
              src.rearrange("(s p) h -> p s h", p=128), "gpc")

    cast_pm(t["wqpm"], t["wq_in"], NS, D)
    cast_pm(t["wkapm"], t["wk_a_in"], NS, D)
    cast_pm(t["wvapm"], t["wv_a_in"], NS, D)
    c.prep_w_done["qkv"] = P.count("gpc")

    stg16 = [carve(c.AR1, 0, 128, 2048, BF16), carve(c.AR1, 4096, 128, 2048, BF16)]
    tstg = [carve(c.AR1, 8192, 128, 2048, BF16), carve(c.AR1, 12288, 128, 2048, BF16)]
    ps_bf2 = [c.psF[:, 0:1024].bitcast(BF16), c.psF[:, 1024:2048].bitcast(BF16)]
    tposes = [
        ("wkb_in", "wkT_bf", 1024, 1024), ("wvb_in", "wvT_bf", 1024, 1024),
        ("wrb_in", "wrT_bf", 1024, 1024), ("wob_in", "woT_bf", 1024, 1024),
        ("wrec_in", "wrecT_bf", 1024, 1024),
        ("wkey_in", "wkeyT_bf", 4096, 1024), ("wval_in", "wvalT_bf", 1024, 4096),
    ]
    ti = 0
    for nm_in, nm_out, R, Cc in tposes:
        srcw, dst = t[nm_in], t[nm_out]
        for r0 in range(0, R, 128):
            for c0 in range(0, Cc, 2048):
                cw = min(2048, Cc - c0)
                b = ti % 2
                ps_bf = ps_bf2[b]
                P.wait("gpsimd", "mm", P.count("mm"))
                ldn = P.dma("gpsimd", stg16[b][:, 0:cw],
                            srcw[r0:r0 + 128, c0:c0 + cw], "gpc")
                P.wait("tensor", "gpc", ldn)
                P.wait("tensor", "st_a", P.count("st_a"))
                P.wait("tensor", "scl", P.count("scl"))
                nblk = cw // 128
                for jb in range(nblk):
                    P.op("tensor",
                         lambda te, o=ps_bf[:, jb * 128:(jb + 1) * 128],
                         i_=stg16[b][:, jb * 128:(jb + 1) * 128],
                         idn=c.s_ident_bf:
                         te.transpose(o, i_, idn))
                P.op("tensor",
                     lambda te, o=c.psF[:, 2048:2112].bitcast(BF16),
                     i_=c.s_ident_bf, idn=c.s_ident_bf:
                     te.transpose(o, i_, idn), sem="mm")
                P.wait("scalar", "mm", P.count("mm"))
                P.op("scalar",
                     lambda sc, o=tstg[b][:, 0:cw], i_=ps_bf[:, 0:cw]:
                     sc.copy(o, i_), sem="scl")
                P.wait("sync", "scl", P.count("scl"))
                for jb in range(nblk):
                    P.dma("sync",
                          dst[c0 + jb * 128:c0 + (jb + 1) * 128, r0:r0 + 128],
                          tstg[b][:, jb * 128:(jb + 1) * 128], "st_a")
                ti += 1
    c.prep_w_done["attnb"] = P.count("st_a")

    P.wait("gpsimd", "st_a", P.count("st_a"))
    for h in range(2):
        cast_pm(t["ws1pm"][h], t["ws1_in"][:, h * EH:(h + 1) * EH], NS, EH)
        cast_pm(t["ws3pm"][h], t["ws3_in"][:, h * EH:(h + 1) * EH], NS, EH)
        cast_pm(t["ws2pm"][h], t["ws2_in"][h * EH:(h + 1) * EH, :], 4, D)
        c.prep_w_done[NE + h] = P.count("gpc")
    for e in range(NE):
        cast_pm(t["w1pm"][e], t["w1_in"][e], NS, EH)
        cast_pm(t["w3pm"][e], t["w3_in"][e], NS, EH)
        cast_pm(t["w2pm"][e], t["w2_in"][e], 4, D)
        c.prep_w_done[e] = P.count("gpc")
    c.prep_w_done["all"] = P.count("gpc")
    P.wait("sync", "st_a", P.count("st_a"))
    P.barrier()



def emit_a1(c):
    P, t = c.P, c.t
    xst = [carve(c.AR1, 0, 128, 1024, F32), carve(c.AR1, 4096, 128, 1024, F32)]
    tmp1 = carve(c.AR1, 8192, 128, 1024, F32)
    lnst = [carve(c.AR1, 12288, 128, 1024, BF16), carve(c.AR1, 14336, 128, 1024, BF16)]
    xTst = [carve(c.AR1, 16384, 128, 1024, F32), carve(c.AR1, 20480, 128, 1024, F32)]
    nmu = carve(c.AR1, 24576, 128, 1, F32)
    vsum = carve(c.AR1, 24608, 128, 1, F32)
    rstd = carve(c.AR1, 24640, 128, 1, F32)
    junk = carve(c.AR1, 28672, 128, 1024, F32)
    ps32 = c.psF[:, 0:1024]
    psbf = c.psF[:, 1024:1536].bitcast(BF16)
    xT_r = dview(t["xT"])

    for ti in range(T // 128):
        b = ti % 2
        if ti >= 2:
            P.wait("sync", "mm", P.count("mm"))
            P.wait("sync", "vec", P.count("vec"))
        ld_n = P.dma("sync", xst[b], t["x_in"][ti * 128:(ti + 1) * 128, :], "ld_a")

        P.wait("tensor", "ld_a", ld_n)
        if ti >= 2:
            P.wait("tensor", "st_a", P.count("st_a"))
        for s in range(NS):
            P.op("tensor",
                 lambda te, o=ps32[:, s * 128:(s + 1) * 128],
                 i_=xst[b][:, s * 128:(s + 1) * 128], idn=c.s_ident_f:
                 te.transpose(o, i_, idn),
                 sem="mm" if s == NS - 1 else None)
        P.wait("scalar", "mm", P.count("mm"))
        evn = P.op("scalar", lambda sc, o=xTst[b], i_=ps32: sc.copy(o, i_),
                   sem="scl")
        P.wait("sync", "scl", evn)
        P.dma("sync", xT_r[:, :, ti * 128:(ti + 1) * 128], sl3(xTst[b]), "st_a")

        P.wait("vector", "ld_a", ld_n)
        P.op("vector", lambda v, o=vsum, i_=xst[b]:
             v.reduce_sum(o, i_, axis=AX.X), sem="wbv")
        P.wait("vector", "wbv")
        P.op("vector", lambda v, o=nmu, i_=vsum:
             v.tensor_scalar(o, i_, -1.0 / D, None, ALU.mult), sem="wbv")
        P.wait("vector", "wbv")
        P.op("vector", lambda v, o=tmp1, i_=xst[b], m=nmu:
             v.tensor_scalar(o, i_, m, None, ALU.add))
        P.op("vector", lambda v, o=junk, i_=tmp1: v.tensor_mul(o, i_, i_))
        P.op("vector", lambda v, o=vsum, i_=junk:
             v.reduce_sum(o, i_, axis=AX.X), sem="vec")
        P.wait("scalar", "vec", P.count("vec"))
        P.op("scalar", lambda sc, o=rstd, i_=vsum, e=c.s_eps:
             sc.activation(o, i_, AF.Sqrt, bias=e, scale=1.0 / D), sem="scl")
        P.wait("vector", "scl", P.count("scl"))
        P.op("vector", lambda v, o=nmu, i_=rstd: v.reciprocal(o, i_), sem="wbv")
        P.wait("vector", "wbv")
        P.op("vector", lambda v, o=lnst[b], i_=tmp1, m=nmu:
             v.tensor_scalar(o, i_, m, None, ALU.mult), sem="vec")
        P.wait("tensor", "vec", P.count("vec"))
        for s in range(NS):
            P.op("tensor",
                 lambda te, o=psbf[:, s * 128:(s + 1) * 128],
                 i_=lnst[b][:, s * 128:(s + 1) * 128], idn=c.s_ident_bf:
                 te.transpose(o, i_, idn),
                 sem="mm" if s == NS - 1 else None)
        P.wait("vector", "mm", P.count("mm"))
        P.op("vector",
             lambda v, o=sl3(c.LN0T, NS)[:, :, ti * 128:(ti + 1) * 128].
             rearrange("p s c -> p (s c)") if False else
             sl3(c.LN0T)[:, :, ti * 128:(ti + 1) * 128],
             i_=sl3(psbf, NS): v.tensor_copy(o, i_), sem="vec")
    P.barrier()



def emit_a2(c):
    P, t = c.P, c.t
    wsl = carve(c.AR4, 0, 128, NS * 1024, BF16)
    qstg = [carve(c.AR1, 0, 128, 512, BF16), carve(c.AR1, 1024, 128, 512, BF16)]
    ps = [c.psF[:, 0:512], c.psF[:, 512:1024]]
    qT = t["qT"]

    P.wait("sync", "gpc", c.prep_w_done["qkv"])

    def load_w(nm):
        P.dma("sync", wsl, t[nm], "ld_a")
        return P.count("ld_a")

    for which, nm in (("k", "wkapm"), ("q", "wqpm")):
        P.wait("sync", "mm", P.count("mm"))
        wn = load_w(nm)
        P.wait("tensor", "ld_a", wn)
        pp = 0
        for tc in range(T // 512):
            for oc in range(NS):
                b = pp % 2
                P.wait("tensor", "ev", P.count("ev"))
                P.wait("tensor", "st_a", P.count("st_a"))
                for s in range(NS):
                    P.op("tensor",
                         lambda te, o=ps[b],
                         l=wsl[:, s * 1024 + oc * 128:s * 1024 + oc * 128 + 128],
                         r=c.LN0T[:, s * T + tc * 512:s * T + tc * 512 + 512],
                         st=(s == 0), sp=(s == NS - 1):
                         te.matmul(o, l, r, start=st, stop=sp,
                                   skip_group_check=True),
                         sem="mm" if s == NS - 1 else None)
                mmn = P.count("mm")
                if which == "k":
                    P.wait("vector", "mm", mmn)
                    P.op("vector",
                         lambda v, o=c.KT[:, oc * T + tc * 512:oc * T + tc * 512 + 512],
                         i_=ps[b]: v.tensor_copy(o, i_), sem="ev")
                else:
                    P.wait("scalar", "mm", mmn)
                    evn = P.op("scalar",
                               lambda sc, o=qstg[b], i_=ps[b]: sc.copy(o, i_),
                               sem="ev")
                    P.wait("sync", "ev", evn)
                    P.dma("sync", qT[oc * 128:(oc + 1) * 128, tc * 512:tc * 512 + 512],
                          qstg[b], "st_a")
                pp += 1

    P.wait("sync", "mm", P.count("mm"))
    P.wait("sync", "st_a", P.count("st_a"))
    wn = load_w("wvapm")
    P.wait("tensor", "ld_a", wn)
    pp = 0
    for ti in range(T // 128):
        for dc in range(2):
            b = pp % 2
            P.wait("tensor", "ev", P.count("ev"))
            for s in range(NS):
                P.op("tensor",
                     lambda te, o=ps[b],
                     l=c.LN0T[:, s * T + ti * 128:s * T + ti * 128 + 128],
                     r=wsl[:, s * 1024 + dc * 512:s * 1024 + dc * 512 + 512],
                     st=(s == 0), sp=(s == NS - 1):
                     te.matmul(o, l, r, start=st, stop=sp,
                               skip_group_check=True),
                     sem="mm" if s == NS - 1 else None)
            P.wait("vector", "mm", P.count("mm"))
            P.op("vector",
                 lambda v, o=c.V[:, ti * 1024 + dc * 512:ti * 1024 + dc * 512 + 512],
                 i_=ps[b]: v.tensor_copy(o, i_), sem="ev")
            pp += 1
    P.barrier()



def emit_a3(c):
    P, t = c.P, c.t
    qtile = [carve(c.AR2, 0, 128, 1024, BF16), carve(c.AR2, 2048, 128, 1024, BF16)]
    ptil = carve(c.AR2, 4096, 128, 512, BF16)
    ptev_big = carve(c.AR2, 28672, 128, 4096, BF16)
    mrow = carve(c.AR2, 6144, 128, 1, F32)
    lrow = carve(c.AR2, 6208, 128, 1, F32)
    tsum = carve(c.AR2, 6272, 128, 1, F32)
    tmax = carve(c.AR2, 6336, 128, 1, F32)
    alpha = carve(c.AR2, 6400, 128, 1, F32)
    nbias = carve(c.AR2, 6464, 128, 1, F32)
    junk = carve(c.AR2, 8192, 128, 512, F32)
    otst = [carve(c.AR2, 12288, 128, 1024, F32), carve(c.AR2, 16384, 128, 1024, F32)]
    xtst = [carve(c.AR2, 20480, 128, 1024, F32), carve(c.AR2, 24576, 128, 1024, F32)]
    ps_s = c.psF[:, 2048:2560]
    ps_pt = c.psF[:, 2560:2816].bitcast(BF16)
    ps_ot = c.psF[:, 0:1024]
    qT_r = dview(t["qT"])
    xT_r = dview(t["xT"])

    for qi in range(T // 128):
        bq = qi % 2
        if qi >= 2:
            P.wait("sync", "mm", P.count("mm"))
        qld = P.dma("sync", sl3(qtile[bq]), qT_r[:, :, qi * 128:(qi + 1) * 128],
                    "ld_a")
        P.op("vector", lambda v, o=mrow: v.memset(o, -1e30))
        P.op("vector", lambda v, o=lrow: v.memset(o, 0.0), sem="wbv")
        P.wait("vector", "wbv")
        P.wait("tensor", "ld_a", qld)

        ngrp = (qi * 128 + 128 + 511) // 512
        for g in range(ngrp):
            k0 = g * 512
            kw = min(512, qi * 128 + 128 - k0)
            P.wait("tensor", "vec", P.count("vec"))
            P.wait("tensor", "scl", P.count("scl"))
            for s in range(NS):
                P.op("tensor",
                     lambda te, o=ps_s[:, 0:kw], l=qtile[bq][:, s * 128:(s + 1) * 128],
                     r=c.KT[:, s * T + k0:s * T + k0 + kw],
                     st=(s == 0), sp=(s == NS - 1):
                     te.matmul(o, l, r, start=st, stop=sp, skip_group_check=True),
                     sem="mm" if s == NS - 1 else None)
            P.wait("vector", "mm", P.count("mm"))
            if kw > (qi * 128 - k0):
                P.op("vector", lambda v, o=ps_s[:, kw - 128:kw],
                     i_=ps_s[:, kw - 128:kw], m=c.s_causal:
                     v.tensor_tensor(o, i_, m, ALU.add))
            P.op("vector", lambda v, o=tmax, i_=ps_s[:, 0:kw]:
                 v.reduce_max(o, i_, axis=AX.X), sem="wbv")
            P.wait("vector", "wbv")
            P.op("vector", lambda v, o=alpha, i_=mrow, m=tmax:
                 v.tensor_scalar(o, i_, m, None, ALU.subtract))
            P.op("vector", lambda v, o=mrow, i_=mrow, m=tmax:
                 v.tensor_scalar(o, i_, m, None, ALU.max), sem="wbv")
            P.wait("vector", "wbv")
            P.op("vector", lambda v, o=alpha, i_=alpha:
                 v.tensor_scalar(o, i_, 0.0, None, ALU.min))
            P.op("vector", lambda v, o=nbias, i_=mrow:
                 v.tensor_scalar(o, i_, -SCALE_A, None, ALU.mult), sem="vec")
            P.wait("scalar", "vec", P.count("vec"))
            P.op("scalar", lambda sc, o=alpha, i_=alpha:
                 sc.activation(o, i_, AF.Exp, scale=SCALE_A), sem="scl")
            P.op("scalar", lambda sc, o=junk[:, 0:kw], i_=ps_s[:, 0:kw],
                 b_=nbias, m=tsum:
                 sc.activation(o, i_, AF.Exp, bias=b_, scale=SCALE_A,
                               accum_out=m), sem="scl")
            P.wait("vector", "scl", P.count("scl"))
            P.op("vector", lambda v, o=lrow, i_=lrow, a=alpha:
                 v.tensor_scalar(o, i_, a, None, ALU.mult), sem="wbv")
            P.wait("vector", "wbv")
            P.op("vector", lambda v, o=lrow, i_=lrow, b_=tsum:
                 v.tensor_scalar(o, i_, b_, None, ALU.add), sem="vec")

        P.wait("scalar", "vec", P.count("vec"))
        P.op("scalar", lambda sc, o=tsum, i_=lrow:
             sc.activation(o, i_, AF.Ln), sem="scl")
        P.wait("vector", "scl", P.count("scl"))
        P.op("vector", lambda v, o=nbias, i_=nbias, m=tsum:
             v.tensor_scalar(o, i_, m, None, ALU.subtract), sem="vec")

        for g in range(ngrp):
            k0 = g * 512
            kw = min(512, qi * 128 + 128 - k0)
            P.wait("tensor", "vec", P.count("vec"))
            P.wait("tensor", "scl", P.count("scl"))
            for s in range(NS):
                P.op("tensor",
                     lambda te, o=ps_s[:, 0:kw], l=qtile[bq][:, s * 128:(s + 1) * 128],
                     r=c.KT[:, s * T + k0:s * T + k0 + kw],
                     st=(s == 0), sp=(s == NS - 1):
                     te.matmul(o, l, r, start=st, stop=sp, skip_group_check=True),
                     sem="mm" if s == NS - 1 else None)
            if kw > (qi * 128 - k0):
                P.wait("vector", "mm", P.count("mm"))
                P.op("vector", lambda v, o=ps_s[:, kw - 128:kw],
                     i_=ps_s[:, kw - 128:kw], m=c.s_causal:
                     v.tensor_tensor(o, i_, m, ALU.add), sem="vec")
                P.wait("scalar", "vec", P.count("vec"))
            else:
                P.wait("scalar", "mm", P.count("mm"))
            P.op("scalar", lambda sc, o=ptil[:, 0:kw], i_=ps_s[:, 0:kw], b_=nbias:
                 sc.activation(o, i_, AF.Exp, bias=b_, scale=SCALE_A), sem="scl")
            P.wait("tensor", "scl", P.count("scl"))
            nkt = (kw + 127) // 128
            for j in range(nkt):
                P.op("tensor",
                     lambda te, o=ps_pt[:, j * 128:(j + 1) * 128],
                     i_=ptil[:, j * 128:(j + 1) * 128], idn=c.s_ident_bf:
                     te.transpose(o, i_, idn),
                     sem="mm" if j == nkt - 1 else None)
            P.wait("vector", "mm", P.count("mm"))
            P.op("vector", lambda v, o=ptev_big[:, k0:k0 + kw], i_=ps_pt[:, 0:kw]:
                 v.tensor_copy(o, i_), sem="vec")
        P.wait("tensor", "vec", P.count("vec"))
        nkt_all = qi + 1
        for dc in range(NS):
            for kt in range(nkt_all):
                P.op("tensor",
                     lambda te, o=ps_ot[:, dc * 128:(dc + 1) * 128],
                     l=c.V[:, kt * 1024 + dc * 128:kt * 1024 + dc * 128 + 128],
                     r=ptev_big[:, kt * 128:(kt + 1) * 128],
                     st=(kt == 0), sp=(kt == nkt_all - 1):
                     te.matmul(o, l, r, start=st, stop=sp,
                               skip_group_check=True),
                     sem="mm" if (dc == NS - 1 and kt == nkt_all - 1) else None)

        bo = qi % 2
        P.wait("sync", "vec", P.count("vec"))
        xld = P.dma("sync", sl3(xtst[bo]), xT_r[:, :, qi * 128:(qi + 1) * 128],
                    "ld_b")
        P.wait("vector", "mm", P.count("mm"))
        P.wait("vector", "ld_b", xld)
        P.wait("vector", "st_a", P.count("st_a"))
        P.op("vector", lambda v, o=otst[bo], i_=xtst[bo], p=ps_ot:
             v.tensor_tensor(o, i_, p, ALU.add), sem="vec")
        P.wait("sync", "vec", P.count("vec"))
        P.dma("sync", xT_r[:, :, qi * 128:(qi + 1) * 128], sl3(otst[bo]), "st_a")
    P.barrier()



def emit_moe(c):
    P, t = c.P, c.t
    GT = 1024
    CH = 256
    NG = T // GT
    NCH = GT // CH
    NPE = NE + 2
    NPAIR = NPE // 2

    ln1t = [carve(c.AR2, 0, 128, NS * GT, BF16),
            carve(c.AR2, 16384, 128, NS * GT, BF16)]
    mw = [carve(c.AR2, 32768, 128, 4 * CH, BF16),
          carve(c.AR2, 34816, 128, 4 * CH, BF16),
          carve(c.AR2, 36864, 128, 4 * CH, BF16),
          carve(c.AR2, 38912, 128, 4 * CH, BF16)]
    logit = carve(c.AR2, 40960, 128, NE, F32)
    lmax = carve(c.AR2, 41088, 128, 1, F32)
    lsum = carve(c.AR2, 41120, 128, 1, F32)
    rrow = carve(c.AR2, 41152, 128, 1, F32)
    srow = carve(c.AR2, 41184, 128, 1, F32)
    eq = carve(c.AR2, 41216, 128, NE, F32)
    idxm = carve(c.AR2, 41344, 128, NE, F32)
    wsel = carve(c.AR2, 41472, 128, NE, F32)
    stat1 = carve(c.AR2, 41600, 1, CH, F32)
    stat2 = carve(c.AR2, 42624, 1, CH, F32)
    wbsb = [carve(c.AR2, 53888, 128, GT, F32),
            carve(c.AR2, 57984, 128, GT, F32)]
    silu1 = [carve(c.AR2, 45696, 128, 4 * CH, BF16),
             carve(c.AR2, 47744, 128, 4 * CH, BF16)]
    h3w = [carve(c.AR2, 49792, 128, 4 * CH, BF16),
           carve(c.AR2, 51840, 128, 4 * CH, BF16)]
    yacc = carve(c.AR1, 0, 128, NS * GT, F32)
    wt32 = [carve(c.AR1, 32768, 32, GT, F32),
            carve(c.AR1, 36864, 32, GT, F32)]
    onehot_dyn = [carve(c.AR1, 40960, 32, 128, F32),
                  carve(c.AR1, 41472, 32, 128, F32)]
    xchk = [carve(c.AR1, 49152, 128, NS * CH, F32),
            carve(c.AR1, 57344, 128, NS * CH, F32)]
    lnf = carve(c.AR4, 0, 128, NS * CH, F32)
    xchk2 = carve(c.AR4, 8192, 128, NS * CH, F32)
    wv = []
    for b in range(2):
        off = b * 24576
        wv.append({
            "w1": carve(c.AR3, off, 128, NS * 512, BF16),
            "w3": carve(c.AR3, off + 8192, 128, NS * 512, BF16),
            "w2": carve(c.AR3, off + 16384, 128, 4 * 1024, BF16),
        })
    ps_y = c.psF[:, 0:2048]
    ps_h1 = c.psF[:, 2048:3072]
    ps_h3 = c.psF[:, 3072:4096]
    ps_st = c.psF[:, 2560:3072]
    ps_lg = c.psF[:, 2560:2592]
    ps_wt = c.psF[0:32, 2592:2720]
    xT_r = dview(t["xT"])


    def load_w13(pe, buf):
        P.wait("sync", "gpc", c.prep_w_done["all"])
        if pe < NE:
            P.dma("sync", wv[buf]["w1"], t["w1pm"][pe], "ld_a")
            n1 = P.count("ld_a")
            P.dma("sync", wv[buf]["w3"], t["w3pm"][pe], "ld_a")
        else:
            h0 = pe - NE
            P.dma("sync", wv[buf]["w1"], t["ws1pm"][h0], "ld_a")
            n1 = P.count("ld_a")
            P.dma("sync", wv[buf]["w3"], t["ws3pm"][h0], "ld_a")
        return (n1, P.count("ld_a"))

    def load_w2(pe, buf):
        if pe < NE:
            P.dma("sync", wv[buf]["w2"], t["w2pm"][pe], "ld_a")
        else:
            P.dma("sync", wv[buf]["w2"], t["ws2pm"][pe - NE], "ld_a")
        return P.count("ld_a")



    ln1t_ready = {}
    c.prep_psum_free = 0
    evst = {"silu": [0, 0], "h3w": [0, 0], "bcast": 0}
    poolw = {0: [0, 0], 1: [0, 0]}

    def prep_gen(g, gb):
        t0g = g * GT
        xln_done = {}
        P.wait("sync", "pvec", P.count("pvec"))
        xln_done[0] = P.dma("sync", sl3(xchk[0]), xT_r[:, :, t0g:t0g + CH],
                            "ld_b")
        for ch in range(NCH):
            t0 = t0g + ch * CH
            if ch + 1 < NCH:
                P.wait("sync", "pvec", P.count("pvec"))
                xln_done[ch + 1] = P.dma(
                    "sync", sl3(xchk[(ch + 1) % 2]),
                    xT_r[:, :, t0 + CH:t0 + 2 * CH], "ld_b")
            xc = xchk[ch % 2]
            xln = xln_done[ch]
            P.wait("tensor", "ld_b", xln)
            P.wait("tensor", "pvec", P.count("pvec"))
            P.wait("tensor", "pscl", P.count("pscl"))
            P.wait("tensor", "scl", P.count("scl"))
            for s in range(NS):
                P.op("tensor", lambda te, o=ps_st[0:1, 0:CH], l=c.s_ones,
                     r=xc[:, s * CH:(s + 1) * CH],
                     st=(s == 0), sp=(s == NS - 1):
                     te.matmul(o, l, r, start=st, stop=sp, skip_group_check=True),
                     sem="pmm" if s == NS - 1 else None)
            P.wait("vector", "pmm", P.count("pmm"))
            P.op("vector", lambda v, o=stat1, i_=ps_st[0:1, 0:CH]:
                 v.tensor_scalar(o, i_, -1.0 / D, None, ALU.mult), sem="pvec")
            for s in range(NS):
                P.op("vector", lambda v, o=lnf[:, s * CH:(s + 1) * CH],
                     i_=xc[:, s * CH:(s + 1) * CH]:
                     v.tensor_mul(o, i_, i_),
                     sem="pvec" if s == NS - 1 else None)
            c.prep_psum_free = P.count("pvec")
            yield
            P.wait("tensor", "pvec", P.count("pvec"))
            P.wait("tensor", "scl", P.count("scl"))
            for s in range(NS):
                P.op("tensor", lambda te, o=ps_st[0:1, CH:2 * CH], l=c.s_ones,
                     r=lnf[:, s * CH:(s + 1) * CH],
                     st=(s == 0), sp=(s == NS - 1):
                     te.matmul(o, l, r, start=st, stop=sp, skip_group_check=True),
                     sem="pmm" if s == NS - 1 else None)
            P.wait("vector", "pmm", P.count("pmm"))
            P.op("vector", lambda v, o=stat2, i_=stat1: v.tensor_mul(o, i_, i_),
                 sem="pvec")
            P.wait("vector", "pvec")
            P.op("vector", lambda v, o=stat2, i_=ps_st[0:1, CH:2 * CH], m=stat2:
                 v.scalar_tensor_tensor(o, i_, 1.0 / D, m, ALU.mult,
                                        ALU.subtract), sem="pvec")
            P.wait("scalar", "pvec", P.count("pvec"))
            P.op("scalar", lambda sc, o=stat2, i_=stat2, e=c.s_eps_row:
                 sc.activation(o, i_, AF.Sqrt, bias=e), sem="pscl")
            P.wait("vector", "pscl", P.count("pscl"))
            P.op("vector", lambda v, o=stat2, i_=stat2:
                 v.reciprocal(o, i_), sem="pvec")
            c.prep_psum_free = P.count("pvec")
            yield
            P.wait("tensor", "pvec", P.count("pvec"))
            P.wait("tensor", "scl", P.count("scl"))
            P.op("tensor", lambda te, o=ps_st[:, 0:CH], l=c.s_ones_row,
                 r=stat1: te.matmul(o, l, r, start=True, stop=True,
                                    skip_group_check=True))
            P.op("tensor", lambda te, o=ps_st[:, CH:2 * CH], l=c.s_ones_row,
                 r=stat2: te.matmul(o, l, r, start=True, stop=True,
                                    skip_group_check=True), sem="pmm")
            P.wait("vector", "pmm", P.count("pmm"))
            for s in range(NS):
                P.op("vector", lambda v, o=lnf[:, s * CH:(s + 1) * CH],
                     i_=xc[:, s * CH:(s + 1) * CH], m=ps_st[:, 0:CH]:
                     v.tensor_tensor(o, i_, m, ALU.add))
            for s in range(NS):
                P.op("vector", lambda v, o=lnf[:, s * CH:(s + 1) * CH],
                     i_=lnf[:, s * CH:(s + 1) * CH], m=ps_st[:, CH:2 * CH]:
                     v.tensor_tensor(o, i_, m, ALU.mult),
                     sem="pvec" if s == NS - 1 else None)
            for s in range(NS):
                P.op("vector",
                     lambda v, o=sl3(ln1t[gb])[:, s, ch * CH:(ch + 1) * CH],
                     i_=lnf[:, s * CH:(s + 1) * CH]:
                     v.tensor_copy(o, i_), sem="pvec" if s == NS - 1 else None)
            c.prep_psum_free = P.count("pvec")
            yield
            for tt in range(CH // 128):
                P.wait("tensor", "pvec", P.count("pvec"))
                P.wait("tensor", "scl", P.count("scl"))
                for s in range(NS):
                    P.op("tensor", lambda te, o=ps_lg,
                         l=lnf[:, s * CH + tt * 128:s * CH + tt * 128 + 128],
                         r=c.WG[:, s * NE:(s + 1) * NE],
                         st=(s == 0), sp=(s == NS - 1):
                         te.matmul(o, l, r, start=st, stop=sp,
                                   skip_group_check=True),
                         sem="pmm" if s == NS - 1 else None)
                P.wait("vector", "pmm", P.count("pmm"))
                P.op("vector", lambda v, o=lmax, i_=ps_lg:
                     v.reduce_max(o, i_, axis=AX.X, negate=True))
                P.op("vector", lambda v, o=logit, i_=ps_lg:
                     v.tensor_copy(o, i_), sem="pvec")
                P.wait("scalar", "pvec", P.count("pvec"))
                P.op("scalar", lambda sc, o=logit, i_=logit, b_=lmax, m=lsum:
                     sc.activation(o, i_, AF.Exp, bias=b_, accum_out=m),
                     sem="pscl")
                P.wait("vector", "pscl", P.count("pscl"))
                P.op("vector", lambda v, o=wsel: v.memset(o, 0.0), sem="pvec")
                for r_ in range(TOPK):
                    P.wait("vector", "pvec")
                    P.op("vector", lambda v, o=lmax, i_=logit:
                         v.reduce_max(o, i_, axis=AX.X), sem="pvec")
                    P.wait("vector", "pvec")
                    P.op("vector", lambda v, o=eq, i_=logit, m=lmax:
                         v.tensor_scalar(o, i_, m, None, ALU.is_equal),
                         sem="pvec")
                    P.wait("vector", "pvec")
                    P.op("vector", lambda v, o=idxm, i_=eq, m=c.s_iotam64:
                         v.tensor_tensor(o, i_, m, ALU.mult), sem="pvec")
                    P.wait("vector", "pvec")
                    P.op("vector", lambda v, o=idxm, i_=idxm:
                         v.tensor_scalar(o, i_, 64.0, None, ALU.add),
                         sem="pvec")
                    P.wait("vector", "pvec")
                    P.op("vector", lambda v, o=srow, i_=idxm:
                         v.tensor_reduce(o, i_, axis=AX.X, op=ALU.min),
                         sem="pvec")
                    P.wait("vector", "pvec")
                    P.op("vector", lambda v, o=eq, i_=c.s_iota32, m=srow:
                         v.tensor_scalar(o, i_, m, None, ALU.is_equal),
                         sem="pvec")
                    P.wait("vector", "pvec")
                    P.op("vector", lambda v, o=wsel, i_=eq, m=lmax, a=wsel:
                         v.scalar_tensor_tensor(o, i_, m, a, ALU.mult, ALU.add))
                    P.op("vector", lambda v, o=logit, i_=eq, a=logit:
                         v.scalar_tensor_tensor(o, i_, -2.0, a, ALU.mult,
                                                ALU.add), sem="pvec")
                P.wait("vector", "pvec")
                P.op("vector", lambda v, o=rrow, i_=lsum: v.reciprocal(o, i_),
                     sem="pvec")
                P.wait("vector", "pvec")
                P.op("vector", lambda v, o=wsel, i_=wsel, m=rrow:
                     v.tensor_scalar(o, i_, m, None, ALU.mult), sem="pvec")
                c.prep_psum_free = P.count("pvec")
                yield
                P.wait("tensor", "pvec", P.count("pvec"))
                P.wait("tensor", "scl", P.count("scl"))
                P.op("tensor", lambda te, o=ps_wt, i_=wsel, idn=c.s_ident_f:
                     te.transpose(o, i_, idn), sem="pmm")
                P.wait("vector", "pmm", P.count("pmm"))
                P.op("vector",
                     lambda v, o=wt32[gb][0:32, ch * CH + tt * 128:
                                          ch * CH + tt * 128 + 128],
                     i_=ps_wt: v.tensor_copy(o, i_), sem="pvec")
                c.prep_psum_free = P.count("pvec")
                yield
        ln1t_ready[g] = (P.count("pvec"), P.count("pmm"))

    def emit_waves(g, gb, gen):
        t0g = g * GT
        P.op("vector", lambda v, o=yacc: v.memset(o, 0.0), sem="vec")
        P.wait("sync", "mm", P.count("mm"))
        w13_done = [load_w13(0, 0), load_w13(1, 1)]
        w2_done = [load_w2(0, 0), load_w2(1, 1)]
        pool_done = {}
        w2mm_done = {}

        def h_phase(pw, ch, ei):
            gci = pw * NCH + ch
            slot = gci % 2
            mwslot = slot * 2 + ei
            pe = 2 * pw + ei
            ch0 = ch * CH
            if ch == 0 and ei == 0 and pw == 0:
                P.wait("tensor", "pvec", ln1t_ready[g][0])
            if ch == 0 and ei == 0:
                P.wait("vector", "mm", P.count("mm"))
                for e2 in range(2):
                    if 2 * pw + e2 < NE:
                        P.op("vector", lambda v, o=onehot_dyn[e2],
                             i_=c.s_iotap, val=float(2 * pw + e2):
                             v.tensor_single_scalar(o, i_, val, ALU.is_equal),
                             sem="vec" if e2 == 1 else None)
                if 2 * pw < NE:
                    P.wait("tensor", "vec", P.count("vec"))
                    wbs_th = {}
                    for e2 in range(2):
                        if 2 * pw + e2 >= NE:
                            continue
                        for th in range(2):
                            if e2 == 1:
                                P.wait("tensor", "wbs", wbs_th[th])
                            P.op("tensor",
                                 lambda te, o=ps_h3[:, th * 512:th * 512 + 512],
                                 l=onehot_dyn[e2],
                                 r=wt32[gb][0:32, th * 512:th * 512 + 512]:
                                 te.matmul(o, l, r, start=True, stop=True,
                                           skip_group_check=True), sem="mm")
                            P.wait("vector", "mm", P.count("mm"))
                            P.op("vector",
                                 lambda v,
                                 o=wbsb[e2][:, th * 512:th * 512 + 512],
                                 i_=ps_h3[:, th * 512:th * 512 + 512]:
                                 v.tensor_copy(o, i_), sem="wbs")
                            wbs_th[th] = P.count("wbs")
                    evst["bcast"] = P.count("wbs")
            for half in range(2):
                h0, h1c = 2 * half * CH, (2 * half + 2) * CH
                if ch == 0 and half == 0:
                    P.wait("tensor", "ld_a", w13_done[ei][0])
                    P.wait("tensor", "ld_a", w13_done[ei][1])
                P.wait("tensor", "scl", evst["silu"][half])
                if half == 1:
                    P.wait("tensor", "pvec", c.prep_psum_free)
                for hc in (2 * half, 2 * half + 1):
                    for s in range(NS):
                        P.op("tensor",
                             lambda te, o=ps_h1[:, hc * CH:(hc + 1) * CH],
                             l=wv[ei]["w1"][:, s * 512 + hc * 128:
                                            s * 512 + hc * 128 + 128],
                             r=ln1t[gb][:, s * GT + ch0:s * GT + ch0 + CH],
                             st=(s == 0), sp=(s == NS - 1):
                             te.matmul(o, l, r, start=st, stop=sp,
                                       skip_group_check=True),
                             sem="mm" if (hc == 2 * half + 1 and s == NS - 1)
                             else None)
                mmh1 = P.count("mm")
                P.wait("tensor", "vec", evst["h3w"][half])
                P.wait("tensor", "wbs", evst["bcast"])
                for hc in (2 * half, 2 * half + 1):
                    for s in range(NS):
                        P.op("tensor",
                             lambda te, o=ps_h3[:, hc * CH:(hc + 1) * CH],
                             l=wv[ei]["w3"][:, s * 512 + hc * 128:
                                            s * 512 + hc * 128 + 128],
                             r=ln1t[gb][:, s * GT + ch0:s * GT + ch0 + CH],
                             st=(s == 0), sp=(s == NS - 1):
                             te.matmul(o, l, r, start=st, stop=sp,
                                       skip_group_check=True),
                             sem="mm" if (hc == 2 * half + 1 and s == NS - 1)
                             else None)
                mmh3 = P.count("mm")
                P.wait("scalar", "mm", mmh1)
                P.wait("scalar", "pool", poolw[ei][half])
                P.op("scalar", lambda sc, o=silu1[ei][:, h0:h1c],
                     i_=ps_h1[:, h0:h1c]:
                     sc.activation(o, i_, AF.Silu), sem="scl")
                evst["silu"][half] = P.count("scl")
                P.wait("vector", "mm", mmh3)
                P.wait("vector", "pool", poolw[ei][half])
                if pe < NE:
                    P.op("vector",
                         lambda v,
                         o=sl3(h3w[ei], 4)[:, 2 * half:2 * half + 2, :],
                         i_=sl3(ps_h3, 4)[:, 2 * half:2 * half + 2, :],
                         m=wbsb[ei][:, ch0:ch0 + CH].unsqueeze(1)
                         .broadcast_to((128, 2, CH)):
                         v.tensor_tensor(o, i_, m, ALU.mult), sem="vec")
                else:
                    P.op("vector", lambda v, o=h3w[ei][:, h0:h1c],
                         i_=ps_h3[:, h0:h1c]:
                         v.tensor_copy(o, i_), sem="vec")
                evst["h3w"][half] = P.count("vec")
                P.wait("gpsimd", "scl", evst["silu"][half])
                P.wait("gpsimd", "vec", evst["h3w"][half])
                P.op("gpsimd", lambda gp, o=mw[mwslot][:, h0:h1c],
                     i_=silu1[ei][:, h0:h1c], m=h3w[ei][:, h0:h1c]:
                     gp.tensor_tensor(o, i_, m, ALU.mult), sem="pool")
                poolw[ei][half] = P.count("pool")
            if ei == 1:
                pool_done[gci] = P.count("pool")
            if ch == NCH - 1 and ei == 1 and 2 * pw + 2 < NPE:
                P.wait("sync", "mm", P.count("mm"))
                w13_done[:] = [load_w13(2 * pw + 2, 0),
                               load_w13(2 * pw + 3, 1)]

        def w2_batch(pw, ch):
            gci = pw * NCH + ch
            slot = gci % 2
            ch0 = ch * CH
            P.wait("tensor", "pool", pool_done[gci])
            P.wait("tensor", "ld_a", max(w2_done))
            P.wait("tensor", "ev", P.count("ev"))
            for dc in range(NS):
                for ei in range(2):
                    for s in range(4):
                        P.op("tensor",
                             lambda te, o=ps_y[:, dc * CH:(dc + 1) * CH],
                             l=wv[ei]["w2"][:, s * 1024 + dc * 128:
                                            s * 1024 + dc * 128 + 128],
                             r=mw[slot * 2 + ei][:, s * CH:(s + 1) * CH],
                             st=(ei == 0 and s == 0),
                             sp=(ei == 1 and s == 3):
                             te.matmul(o, l, r, start=st, stop=sp,
                                       skip_group_check=True),
                             sem="mm" if (dc == NS - 1 and ei == 1
                                          and s == 3) else None)
            w2mm_done[pw] = P.count("mm")
            P.wait("vector", "mm", P.count("mm"))
            if pw + 1 < NPAIR:
                P.op("vector", lambda v, o=sl3(yacc)[:, :, ch0:ch0 + CH],
                     i_=sl3(yacc)[:, :, ch0:ch0 + CH], p=sl3(ps_y, NS):
                     v.tensor_tensor(o, i_, p, ALU.add), sem="ev")
            else:
                P.wait("sync", "vec", P.count("vec"))
                P.wait("sync", "st_a", P.count("st_a"))
                xl = P.dma("sync", sl3(xchk2),
                           xT_r[:, :, t0g + ch0:t0g + ch0 + CH], "ld_b")
                P.op("vector", lambda v, o=sl3(yacc)[:, :, ch0:ch0 + CH],
                     i_=sl3(yacc)[:, :, ch0:ch0 + CH], p=sl3(ps_y, NS):
                     v.tensor_tensor(o, i_, p, ALU.add), sem="ev")
                P.wait("vector", "ld_b", xl)
                P.wait("vector", "ev", P.count("ev"))
                P.op("vector", lambda v, o=sl3(xchk2), i_=sl3(xchk2),
                     p=sl3(yacc)[:, :, ch0:ch0 + CH]:
                     v.tensor_tensor(o, i_, p, ALU.add), sem="vec")
                P.wait("sync", "vec", P.count("vec"))
                P.dma("sync", xT_r[:, :, t0g + ch0:t0g + ch0 + CH],
                      sl3(xchk2), "st_a")

        def run_w2(task):
            tpw, tch = task
            w2_batch(tpw, tch)
            if tch == NCH - 1 and 2 * tpw + 2 < NPE:
                P.wait("sync", "mm", w2mm_done[tpw])
                w2_done[:] = [load_w2(2 * tpw + 2, 0), load_w2(2 * tpw + 3, 1)]

        pending = []
        for pw in range(NPAIR):
            for ch in range(NCH):
                h_phase(pw, ch, 0)
                if gen is not None:
                    next(gen, None)
                h_phase(pw, ch, 1)
                if gen is not None:
                    next(gen, None)
                pending.append((pw, ch))
                if len(pending) >= 2:
                    run_w2(pending.pop(0))
        while pending:
            run_w2(pending.pop(0))
        if gen is not None:
            for _ in gen:
                pass

    INTERLEAVE = True
    for g in range(NG):
        gb = g % 2
        if g == 0:
            for _ in prep_gen(0, 0):
                pass
        gen = prep_gen(g + 1, (g + 1) % 2) if g + 1 < NG else None
        if not INTERLEAVE and gen is not None:
            for _ in gen:
                pass
            gen = None
        emit_waves(g, gb, gen)
    P.wait("sync", "st_a", P.count("st_a"))
    P.barrier()



def emit_attnb(c):
    P, t = c.P, c.t
    xT_r = dview(t["xT"])
    bT_r = dview(t["bT"])
    hT_r = dview(t["hT"])
    CA = 512
    CB = 256
    NCA = T // CA
    NCB = T // CB

    xa = carve(c.AR1, 41760, 128, NS * CA, F32)
    hca = carve(c.AR2, 32768, 128, NS * CA, F32)
    sqf = hca
    st1a = carve(c.AR2, 49152, 1, CA, F32)
    st2a = carve(c.AR2, 51200, 1, CA, F32)
    ps_a = c.psF[:, 3072:4096]

    b1a_store = {}

    def emit_b1a(ch):
        t0 = ch * CA
        P.wait("sync", "pvec", P.count("pvec"))
        P.wait("sync", "st_a", P.count("st_a"))
        xl = P.dma("sync", sl3(xa), xT_r[:, :, t0:t0 + CA], "ld_b")
        for round_ in range(2):
            src = xa
            P.wait("tensor", "ld_b", xl)
            P.wait("tensor", "pvec", P.count("pvec"))
            P.wait("tensor", "pscl", P.count("pscl"))
            for s in range(NS):
                P.op("tensor", lambda te, o=ps_a[0:1, 0:CA], l=c.s_ones,
                     r=src[:, s * CA:(s + 1) * CA], st=(s == 0), sp=(s == NS - 1):
                     te.matmul(o, l, r, start=st, stop=sp, skip_group_check=True),
                     sem="pmm" if s == NS - 1 else None)
            P.wait("vector", "pmm", P.count("pmm"))
            P.op("vector", lambda v, o=st1a, i_=ps_a[0:1, 0:CA]:
                 v.tensor_scalar(o, i_, -1.0 / D, None, ALU.mult), sem="pvec")
            for s in range(NS):
                P.op("vector", lambda v, o=sqf[:, s * CA:(s + 1) * CA],
                     i_=src[:, s * CA:(s + 1) * CA]:
                     v.tensor_mul(o, i_, i_), sem="pvec" if s == NS - 1 else None)
            P.wait("tensor", "pvec", P.count("pvec"))
            for s in range(NS):
                P.op("tensor", lambda te, o=ps_a[0:1, CA:2 * CA], l=c.s_ones,
                     r=sqf[:, s * CA:(s + 1) * CA], st=(s == 0), sp=(s == NS - 1):
                     te.matmul(o, l, r, start=st, stop=sp, skip_group_check=True),
                     sem="pmm" if s == NS - 1 else None)
            P.wait("vector", "pmm", P.count("pmm"))
            P.op("vector", lambda v, o=st2a, i_=st1a: v.tensor_mul(o, i_, i_),
                 sem="pvec")
            P.wait("vector", "pvec")
            P.op("vector", lambda v, o=st2a, i_=ps_a[0:1, CA:2 * CA], m=st2a:
                 v.scalar_tensor_tensor(o, i_, 1.0 / D, m, ALU.mult,
                                        ALU.subtract), sem="pvec")
            P.wait("scalar", "pvec", P.count("pvec"))
            P.op("scalar", lambda sc, o=st2a, i_=st2a, e=c.s_eps_row:
                 sc.activation(o, i_, AF.Sqrt, bias=e), sem="pscl")
            P.wait("vector", "pscl", P.count("pscl"))
            P.op("vector", lambda v, o=st2a, i_=st2a:
                 v.reciprocal(o, i_), sem="pvec")
            P.wait("tensor", "pvec", P.count("pvec"))
            P.op("tensor", lambda te, o=ps_a[:, 0:CA], l=c.s_ones_row,
                 r=st1a: te.matmul(o, l, r, start=True, stop=True,
                                   skip_group_check=True))
            P.op("tensor", lambda te, o=ps_a[:, CA:2 * CA], l=c.s_ones_row,
                 r=st2a: te.matmul(o, l, r, start=True, stop=True,
                                   skip_group_check=True), sem="pmm")
            P.wait("vector", "pmm", P.count("pmm"))
            dst = xa if round_ == 0 else hca
            if round_ == 1:
                P.wait("vector", "st_a", P.count("st_a"))
            for s in range(NS):
                P.op("vector", lambda v, o=dst[:, s * CA:(s + 1) * CA],
                     i_=src[:, s * CA:(s + 1) * CA], m=ps_a[:, 0:CA]:
                     v.tensor_tensor(o, i_, m, ALU.add))
            for s in range(NS):
                P.op("vector", lambda v, o=dst[:, s * CA:(s + 1) * CA],
                     i_=dst[:, s * CA:(s + 1) * CA], m=ps_a[:, CA:2 * CA]:
                     v.tensor_tensor(o, i_, m, ALU.mult),
                     sem="pvec" if s == NS - 1 else None)
            if round_ == 1:
                for s in range(NS):
                    P.op("vector", lambda v, o=hca[:, s * CA:(s + 1) * CA],
                         i_=hca[:, s * CA:(s + 1) * CA],
                         w_=c.s_vec["ln1_w"][:, s:s + 1],
                         b_=c.s_vec["ln1_b"][:, s:s + 1]:
                         v.tensor_scalar(o, i_, w_, b_, ALU.mult, ALU.add),
                         sem="pvec" if s == NS - 1 else None)
        P.wait("sync", "pvec", P.count("pvec"))
        P.dma("sync", bT_r[:, :, t0:t0 + CA], sl3(xa), "st_a")
        P.dma("sync", hT_r[:, :, t0:t0 + CA], sl3(hca), "st_a")
        b1a_store[ch] = P.count("st_a")

    wk_sb = carve(c.AR3, 0, 128, NS * 1024, BF16)
    wvv_sb = carve(c.AR3, 16384, 128, NS * 1024, BF16)
    wr_sb = carve(c.AR3, 32768, 128, NS * 1024, BF16)
    wo_sb = carve(c.AR3, 49152, 128, NS * 1024, BF16)
    for nm, sb in (("wkT_bf", wk_sb), ("wvT_bf", wvv_sb), ("wrT_bf", wr_sb),
                   ("woT_bf", wo_sb)):
        P.dma("sync", sl3(sb), dview(t[nm]), "ld_a")
    wld = P.count("ld_a")

    hcc0 = carve(c.AR1, 0, 128, NS * 257, F32)
    hcc = [hcc0, hcc0]
    kf = [carve(c.AR1, 8224, 128, NS * CB, F32),
          carve(c.AR1, 16416, 128, NS * CB, F32)]
    vf = [carve(c.AR1, 24608, 128, NS * CB, F32),
          carve(c.AR1, 32800, 128, NS * CB, F32)]
    srf = [carve(c.AR2, 0, 128, NS * CB, F32),
           carve(c.AR2, 8192, 128, NS * CB, F32)]
    xkb = [carve(c.AR2, 16384, 128, NS * CB, BF16),
           carve(c.AR2, 20480, 128, NS * CB, BF16)]
    xvb = [carve(c.AR2, 24576, 128, NS * CB, BF16),
           carve(c.AR2, 53248, 128, NS * CB, BF16)]
    xrb = [carve(c.AR2, 28672, 128, NS * CB, BF16),
           carve(c.AR2, 57344, 128, NS * CB, BF16)]
    e0 = carve(c.AR2, 61440, 128, CB, F32)
    e1 = carve(c.AR2, 62464, 128, CB, F32)
    ffb = carve(c.AR2, 63488, 128, CB, F32)
    yA = carve(c.AR4, 0, 128, CB + 1, F32)
    yB = carve(c.AR4, 1056, 128, CB + 1, F32)
    den = carve(c.AR4, 2112, 128, CB, F32)
    rwk = [carve(c.AR4, 3136, 128, NS * CB, BF16),
           carve(c.AR4, 7232, 128, NS * CB, BF16)]
    e0b2 = [carve(c.AR4, 11328, 128, CB, F32),
            carve(c.AR4, 12352, 128, CB, F32)]
    e1b2 = [carve(c.AR4, 13376, 128, CB, F32),
            carve(c.AR4, 14400, 128, CB, F32)]
    btc = carve(c.AR1, 58144, 128, 4 * CB, F32)
    ps_m = c.psF[:, 0:2048]
    ps_m2 = c.psF[:, 2048:3072]

    b1b_state = {"mix": {}, "k": {}, "v": {}, "r": {}, "wkv": {}}

    lamb = carve(c.AR2, 64512, 128, NS, F32)
    kru = carve(c.AR2, 64544, 128, NS, F32)
    Ah = carve(c.AR2, 64576, 128, NS, F32)
    Bh = carve(c.AR2, 64608, 128, NS, F32)
    Mh = carve(c.AR2, 64640, 128, NS, F32)
    mu8 = carve(c.AR2, 64672, 128, NS, F32)
    nmu8 = carve(c.AR2, 64704, 128, NS, F32)
    sc8 = carve(c.AR2, 64736, 128, NS, F32)
    tf8 = carve(c.AR2, 64768, 128, NS, F32)

    P.wait("scalar", "ld_a", wld)
    P.op("scalar", lambda sc, o=lamb, i_=c.s_vec["time_decay"]:
         sc.activation(o, i_, AF.Exp), sem="scl")
    P.wait("scalar", "scl", P.count("scl"))
    P.op("scalar", lambda sc, o=lamb, i_=lamb:
         sc.activation(o, i_, AF.Exp, scale=-1.0), sem="scl")
    P.op("vector", lambda v, o=kru, i_=c.s_vec["time_first"]:
         v.tensor_scalar(o, i_, 0.0, None, ALU.max))
    P.op("vector", lambda v, o=Ah: v.memset(o, 0.0))
    P.op("vector", lambda v, o=Bh: v.memset(o, 0.0))
    P.op("vector", lambda v, o=Mh: v.memset(o, -1e30), sem="vec")

    def b1b_mix(ch):
        b = ch % 2
        t0 = ch * CB
        P.wait("sync", "vec", b1b_state["mix"].get(ch - 1, 0))
        P.wait("sync", "st_a", b1a_store[(t0 + CB - 1) // CA])
        if ch == 0:
            P.op("vector", lambda v, o=hcc[0]: v.memset(o, 0.0), sem="vec")
            P.wait("sync", "vec", P.count("vec"))
            hl = P.dma("sync", sl3(hcc[0], NS)[:, :, 1:CB + 1],
                       hT_r[:, :, 0:CB], "ld_a")
        else:
            hl = P.dma("sync", sl3(hcc[b], NS), hT_r[:, :, t0 - 1:t0 + CB],
                       "ld_a")
        P.wait("vector", "ld_a", hl)
        P.wait("vector", "mm", b1b_state["r"].get(ch - 2, 0))
        W1 = CB + 1
        for nm, dst in (("att_tmk", xkb), ("att_tmv", xvb), ("att_tmr", xrb)):
            for s in range(NS):
                P.op("vector",
                     lambda v, o=ffb, h_=hcc[b][:, s * W1 + 1:s * W1 + W1],
                     hh=hcc[b][:, s * W1:s * W1 + CB]:
                     v.tensor_sub(o, h_, hh))
                P.op("vector",
                     lambda v, o=dst[b][:, s * CB:(s + 1) * CB], i_=ffb,
                     w_=c.s_vec[nm][:, s:s + 1],
                     hh=hcc[b][:, s * W1:s * W1 + CB]:
                     v.scalar_tensor_tensor(o, i_, w_, hh, ALU.mult, ALU.add),
                     sem="vec" if (nm == "att_tmr" and s == NS - 1) else None)
        b1b_state["mix"][ch] = P.count("vec")

    def b1b_mms(ch):
        b = ch % 2
        for which, src_b, wsb, dstf in (("k", xkb, wk_sb, kf),
                                        ("v", xvb, wvv_sb, vf),
                                        ("r", xrb, wr_sb, srf)):
            P.wait("tensor", "vec", b1b_state["mix"][ch])
            P.wait("tensor", "ev", P.count("ev"))
            P.wait("tensor", "vec", b1b_state["wkv"].get(ch - 2, 0))
            for oc in range(NS):
                for s in range(NS):
                    P.op("tensor",
                         lambda te, o=ps_m[:, oc * CB:oc * CB + CB],
                         l=wsb[:, s * 1024 + oc * 128:s * 1024 + oc * 128 + 128],
                         r=src_b[b][:, s * CB:(s + 1) * CB],
                         st=(s == 0), sp=(s == NS - 1):
                         te.matmul(o, l, r, start=st, stop=sp,
                                   skip_group_check=True),
                         sem="mm" if (oc == NS - 1 and s == NS - 1) else None)
            P.wait("scalar", "mm", P.count("mm"))
            if which == "r":
                P.op("scalar", lambda sc, o=srf[b], i_=ps_m:
                     sc.activation(o, i_, AF.Sigmoid), sem="ev")
            else:
                P.op("scalar", lambda sc, o=dstf[b], i_=ps_m:
                     sc.copy(o, i_), sem="ev")
            b1b_state[which][ch] = P.count("mm")
            P.wait("tensor", "ev", P.count("ev"))

    def b1b_wkv(ch):
        b = ch % 2
        P.wait("vector", "ev", P.count("ev"))
        for s in range(NS):
            P.op("vector", lambda v, o=mu8[:, s:s + 1],
                 i_=kf[b][:, s * CB:(s + 1) * CB]:
                 v.reduce_max(o, i_, axis=AX.X),
                 sem="wbv" if s == NS - 1 else None)
        P.wait("vector", "wbv")
        P.op("vector", lambda v, o=mu8, i_=mu8, m=kru:
             v.tensor_tensor(o, i_, m, ALU.add))
        P.op("vector", lambda v, o=sc8, i_=Mh:
             v.tensor_scalar(o, i_, -CAP_DROP, None, ALU.add), sem="wbv")
        P.wait("vector", "wbv")
        P.op("vector", lambda v, o=mu8, i_=mu8, m=sc8:
             v.tensor_tensor(o, i_, m, ALU.max), sem="wbv")
        P.wait("vector", "wbv")
        P.op("vector", lambda v, o=nmu8, i_=mu8:
             v.tensor_scalar(o, i_, -1.0, None, ALU.mult))
        P.op("vector", lambda v, o=sc8, i_=Mh, m=mu8:
             v.tensor_tensor(o, i_, m, ALU.subtract))
        P.op("vector", lambda v, o=tf8, i_=c.s_vec["time_first"][:, 0:NS],
             m=mu8: v.tensor_tensor(o, i_, m, ALU.subtract), sem="vec")
        P.wait("scalar", "vec", P.count("vec"))
        P.op("scalar", lambda sc, o=sc8, i_=sc8:
             sc.activation(o, i_, AF.Exp), sem="scl")
        P.wait("vector", "scl", P.count("scl"))
        P.op("vector", lambda v, o=Ah, i_=Ah, m=sc8:
             v.tensor_tensor(o, i_, m, ALU.mult))
        P.op("vector", lambda v, o=Bh, i_=Bh, m=sc8:
             v.tensor_tensor(o, i_, m, ALU.mult))
        P.op("vector", lambda v, o=Mh, i_=mu8:
             v.tensor_copy(o, i_), sem="wbv")
        P.wait("vector", "wbv")
        ev_slab = {}
        for s in range(NS):
            eb = s % 2
            kfs = kf[b][:, s * CB:(s + 1) * CB]
            vfs = vf[b][:, s * CB:(s + 1) * CB]
            if s >= 2:
                P.wait("scalar", "vec", ev_slab[s - 2])
            P.op("scalar", lambda sc, o=e0b2[eb], i_=kfs, b_=nmu8[:, s:s + 1]:
                 sc.activation(o, i_, AF.Exp, bias=b_))
            P.op("scalar", lambda sc, o=e1b2[eb], i_=kfs, b_=tf8[:, s:s + 1]:
                 sc.activation(o, i_, AF.Exp, bias=b_), sem="scl")
            P.wait("vector", "scl", P.count("scl"))
            P.op("vector", lambda v, o=ffb, i_=e0b2[eb], m=vfs:
                 v.tensor_tensor(o, i_, m, ALU.mult))
            P.op("vector", lambda v, o=yA[:, 0:1], i_=Ah[:, s:s + 1]:
                 v.tensor_copy(o, i_))
            P.op("vector", lambda v, o=yB[:, 0:1], i_=Bh[:, s:s + 1]:
                 v.tensor_copy(o, i_), sem="wbv")
            P.wait("vector", "wbv")
            P.op("vector", lambda v, o=yA[:, 1:CB + 1],
                 d0=lamb[:, s:s + 1].broadcast_to((128, CB)), d1=ffb,
                 init=Ah[:, s:s + 1]:
                 v.tensor_tensor_scan(o, d0, d1, init, ALU.mult, ALU.add))
            P.op("vector", lambda v, o=yB[:, 1:CB + 1],
                 d0=lamb[:, s:s + 1].broadcast_to((128, CB)), d1=e0b2[eb],
                 init=Bh[:, s:s + 1]:
                 v.tensor_tensor_scan(o, d0, d1, init, ALU.mult, ALU.add),
                 sem="vec")
            P.wait("vector", "vec")
            P.op("vector", lambda v, o=den, i_=yB[:, 0:CB], m=e1b2[eb]:
                 v.tensor_tensor(o, i_, m, ALU.add))
            P.op("vector", lambda v, o=den, i_=den: v.reciprocal(o, i_))
            P.op("vector", lambda v, o=e1b2[eb], i_=e1b2[eb], m=vfs:
                 v.tensor_tensor(o, i_, m, ALU.mult))
            P.op("vector", lambda v, o=e1b2[eb], i_=e1b2[eb], m=yA[:, 0:CB]:
                 v.tensor_tensor(o, i_, m, ALU.add), sem="wbv")
            P.wait("vector", "wbv")
            P.op("vector", lambda v, o=e1b2[eb], i_=e1b2[eb], m=den:
                 v.tensor_tensor(o, i_, m, ALU.mult))
            P.op("vector", lambda v, o=rwk[b][:, s * CB:(s + 1) * CB],
                 i_=e1b2[eb], m=srf[b][:, s * CB:(s + 1) * CB]:
                 v.tensor_tensor(o, i_, m, ALU.mult))
            P.op("vector", lambda v, o=Ah[:, s:s + 1], i_=yA[:, CB:CB + 1]:
                 v.tensor_copy(o, i_))
            P.op("vector", lambda v, o=Bh[:, s:s + 1], i_=yB[:, CB:CB + 1]:
                 v.tensor_copy(o, i_), sem="vec")
            ev_slab[s] = P.count("vec")
        b1b_state["wkv"][ch] = P.count("vec")

    wo_ev = {}

    def b1b_wo(ch):
        b = ch % 2
        t0 = ch * CB
        P.wait("tensor", "vec", b1b_state["wkv"][ch])
        for hh2 in range(2):
            P.wait("tensor", "ev",
                   wo_ev.get((ch, 0), 0) if hh2 == 1
                   else wo_ev.get((ch - 1, 1), 0))
            for oc4 in range(4):
                oc = 4 * hh2 + oc4
                for s in range(NS):
                    P.op("tensor",
                         lambda te, o=ps_m2[:, oc4 * CB:oc4 * CB + CB],
                         l=wo_sb[:, s * 1024 + oc * 128:s * 1024 + oc * 128 + 128],
                         r=rwk[b][:, s * CB:(s + 1) * CB],
                         st=(s == 0), sp=(s == NS - 1):
                         te.matmul(o, l, r, start=st, stop=sp,
                                   skip_group_check=True),
                         sem="mm" if (oc4 == 3 and s == NS - 1) else None)
            mmw = P.count("mm")
            P.wait("sync", "ev", P.count("ev"))
            bl = P.dma("sync", sl3(btc, 4),
                       bT_r[:, hh2 * 4:(hh2 + 1) * 4, t0:t0 + CB], "ld_b")
            P.wait("vector", "mm", mmw)
            P.wait("vector", "ld_b", bl)
            P.op("vector", lambda v, o=btc, i_=btc, p=ps_m2:
                 v.tensor_tensor(o, i_, p, ALU.add), sem="ev")
            wo_ev[(ch, hh2)] = P.count("ev")
            P.wait("sync", "ev", P.count("ev"))
            P.dma("sync", bT_r[:, hh2 * 4:(hh2 + 1) * 4, t0:t0 + CB],
                  sl3(btc, 4), "st_a")

    emit_b1a(0)
    for ch in range(NCB):
        if ch % 2 == 0 and ch // 2 + 1 < NCA:
            emit_b1a(ch // 2 + 1)
        b1b_mix(ch)
        b1b_mms(ch)
        b1b_wkv(ch)
        if ch > 0:
            b1b_wo(ch - 1)
    b1b_wo(NCB - 1)
    P.barrier()


    CH2 = 256
    NCH2 = T // CH2
    wkey_sb = carve(c.AR3, 0, 128, NS * FFN, BF16)
    wval_sb = carve(c.AR1, 0, 128, 32 * 1024, BF16)
    wrec_sb = carve(c.AR4, 0, 128, NS * 1024, BF16)
    P.dma("sync", sl3(wkey_sb), dview(t["wkeyT_bf"]), "ld_a")
    wkey_n = P.count("ld_a")
    P.dma("sync", sl3(wval_sb, s=32),
          t["wvalT_bf"].rearrange("(s p) q -> p s q", p=128), "ld_a")
    wval_n = P.count("ld_a")
    P.dma("sync", sl3(wrec_sb), dview(t["wrecT_bf"]), "ld_a")
    wrec_n = P.count("ld_a")

    bcc = carve(c.AR2, 0, 128, NS * 257, F32)
    h2c = carve(c.AR2, 8224, 128, NS * 257, F32)
    sq2 = carve(c.AR2, 16448, 128, NS * CH2, F32)
    xk2 = carve(c.AR2, 24640, 128, NS * CH2, BF16)
    xr2 = carve(c.AR2, 28736, 128, NS * CH2, BF16)
    kkr = carve(c.AR2, 32832, 128, CH2, F32)
    kkr_b = carve(c.AR2, 58432, 128, CH2, F32)
    kk2 = carve(c.AR2, 33856, 128, 32 * CH2, BF16)
    rf = carve(c.AR2, 50240, 128, NS * CH2, F32)
    st1b = carve(c.AR2, 62528, 1, 257, F32)
    st2b = carve(c.AR2, 63680, 1, 257, F32)
    ps_kk = [c.psF[:, 0:256], c.psF[:, 512:768]]
    ps_kv = c.psF[:, 2048:4096]
    ps_rr = c.psF[:, 1024:2048]
    ps_stb = c.psF[:, 0:1024]

    for ch in range(NCH2):
        t0 = ch * CH2
        P.wait("sync", "vec", P.count("vec"))
        if ch == 0:
            P.op("vector", lambda v, o=bcc: v.memset(o, 0.0), sem="vec")
            P.wait("sync", "vec", P.count("vec"))
            bl = P.dma("sync", sl3(bcc, NS)[:, :, 1:257], bT_r[:, :, 0:CH2],
                       "ld_a")
        else:
            bl = P.dma("sync", sl3(bcc, NS), bT_r[:, :, t0 - 1:t0 + CH2], "ld_a")
        P.wait("tensor", "ld_a", bl)
        P.wait("tensor", "vec", P.count("vec"))
        P.wait("tensor", "scl", P.count("scl"))
        W = 257
        for s in range(NS):
            P.op("tensor", lambda te, o=ps_stb[0:1, 0:W], l=c.s_ones,
                 r=bcc[:, s * 257:(s + 1) * 257], st=(s == 0), sp=(s == NS - 1):
                 te.matmul(o, l, r, start=st, stop=sp, skip_group_check=True),
                 sem="mm" if s == NS - 1 else None)
        P.wait("vector", "mm", P.count("mm"))
        P.op("vector", lambda v, o=st1b, i_=ps_stb[0:1, 0:W]:
             v.tensor_scalar(o, i_, -1.0 / D, None, ALU.mult), sem="vec")
        for s in range(NS):
            P.op("vector", lambda v, o=h2c[:, s * 257:(s + 1) * 257],
                 i_=bcc[:, s * 257:(s + 1) * 257]:
                 v.tensor_mul(o, i_, i_), sem="vec" if s == NS - 1 else None)
        P.wait("tensor", "vec", P.count("vec"))
        for s in range(NS):
            P.op("tensor", lambda te, o=ps_stb[0:1, 512:512 + W], l=c.s_ones,
                 r=h2c[:, s * 257:(s + 1) * 257], st=(s == 0), sp=(s == NS - 1):
                 te.matmul(o, l, r, start=st, stop=sp, skip_group_check=True),
                 sem="mm" if s == NS - 1 else None)
        P.wait("vector", "mm", P.count("mm"))
        P.op("vector", lambda v, o=st2b, i_=st1b: v.tensor_mul(o, i_, i_),
             sem="wbv")
        P.wait("vector", "wbv")
        P.op("vector", lambda v, o=st2b, i_=ps_stb[0:1, 512:512 + W], m=st2b:
             v.scalar_tensor_tensor(o, i_, 1.0 / D, m, ALU.mult, ALU.subtract),
             sem="vec")
        P.wait("scalar", "vec", P.count("vec"))
        P.op("scalar", lambda sc, o=st2b, i_=st2b, e=c.s_eps_row:
             sc.activation(o, i_, AF.Sqrt, bias=e), sem="scl")
        P.wait("vector", "scl", P.count("scl"))
        P.op("vector", lambda v, o=st2b, i_=st2b: v.reciprocal(o, i_), sem="vec")
        P.wait("tensor", "vec", P.count("vec"))
        P.op("tensor", lambda te, o=ps_stb[:, 0:W].opt(), l=c.s_ones_row,
             r=st1b: te.matmul(o, l, r, start=True, stop=True,
                               skip_group_check=True))
        P.op("tensor", lambda te, o=ps_stb[:, 512:512 + W], l=c.s_ones_row,
             r=st2b: te.matmul(o, l, r, start=True, stop=True,
                               skip_group_check=True), sem="mm")
        P.wait("vector", "mm", P.count("mm"))
        for s in range(NS):
            P.op("vector", lambda v, o=h2c[:, s * 257:(s + 1) * 257],
                 i_=bcc[:, s * 257:(s + 1) * 257], m=ps_stb[:, 0:W]:
                 v.tensor_tensor(o, i_, m, ALU.add))
        for s in range(NS):
            P.op("vector", lambda v, o=h2c[:, s * 257:(s + 1) * 257],
                 i_=h2c[:, s * 257:(s + 1) * 257], m=ps_stb[:, 512:512 + W]:
                 v.tensor_tensor(o, i_, m, ALU.mult))
        for s in range(NS):
            P.op("vector", lambda v, o=h2c[:, s * 257:(s + 1) * 257],
                 i_=h2c[:, s * 257:(s + 1) * 257],
                 w_=c.s_vec["ln2_w"][:, s:s + 1], b_=c.s_vec["ln2_b"][:, s:s + 1]:
                 v.tensor_scalar(o, i_, w_, b_, ALU.mult, ALU.add),
                 sem="vec" if s == NS - 1 else None)
        if ch == 0:
            P.op("vector", lambda v, o=sl3(h2c, NS)[:, :, 0:1]:
                 v.memset(o, 0.0), sem="wbv")
            P.wait("vector", "wbv")
        for nm, dst in (("ffn_tmk", xk2), ("ffn_tmr", xr2)):
            for s in range(NS):
                P.op("vector",
                     lambda v, o=kkr, h_=h2c[:, s * 257 + 1:s * 257 + 257],
                     hh=h2c[:, s * 257:s * 257 + 256]:
                     v.tensor_sub(o, h_, hh))
                P.op("vector",
                     lambda v, o=dst[:, s * CH2:(s + 1) * CH2], i_=kkr,
                     w_=c.s_vec[nm][:, s:s + 1],
                     hh=h2c[:, s * 257:s * 257 + 256]:
                     v.scalar_tensor_tensor(o, i_, w_, hh, ALU.mult, ALU.add),
                     sem="vec" if (nm == "ffn_tmr" and s == NS - 1) else None)
        mixn = P.count("vec")
        P.wait("tensor", "vec", mixn)
        if ch == 0:
            P.wait("tensor", "ld_a", wkey_n)
        ev_after = {}
        kkr2 = [kkr, kkr_b]
        for oc in range(32):
            pb = oc % 2
            if oc >= 2:
                P.wait("tensor", "ev", ev_after[oc - 2])
            for s in range(NS):
                P.op("tensor", lambda te, o=ps_kk[pb],
                     l=wkey_sb[:, s * FFN + oc * 128:s * FFN + oc * 128 + 128],
                     r=xk2[:, s * CH2:(s + 1) * CH2],
                     st=(s == 0), sp=(s == NS - 1):
                     te.matmul(o, l, r, start=st, stop=sp,
                               skip_group_check=True),
                     sem="mm" if s == NS - 1 else None)
            P.wait("scalar", "mm", P.count("mm"))
            if oc >= 2:
                P.wait("scalar", "ev", ev_after[oc - 2])
            P.op("scalar", lambda sc, o=kkr2[pb], i_=ps_kk[pb]:
                 sc.activation(o, i_, AF.Relu), sem="scl")
            P.wait("vector", "scl", P.count("scl"))
            P.op("vector", lambda v, o=kk2[:, oc * CH2:(oc + 1) * CH2],
                 i_=kkr2[pb], m=kkr2[pb]:
                 v.tensor_tensor(o, i_, m, ALU.mult), sem="ev")
            ev_after[oc] = P.count("ev")
        P.wait("tensor", "ev", P.count("ev"))
        if ch == 0:
            P.wait("tensor", "ld_a", wval_n)
        for oc in range(NS):
            for s in range(32):
                P.op("tensor", lambda te, o=ps_kv[:, oc * 256:oc * 256 + 256],
                     l=wval_sb[:, s * 1024 + oc * 128:s * 1024 + oc * 128 + 128],
                     r=kk2[:, s * CH2:(s + 1) * CH2],
                     st=(s == 0), sp=(s == 31):
                     te.matmul(o, l, r, start=st, stop=sp,
                               skip_group_check=True),
                     sem="mm" if (oc == NS - 1 and s == 31) else None)
        rr_slots = [c.psF[:, 1024:1280], c.psF[:, 1536:1792]]
        if ch == 0:
            P.wait("tensor", "ld_a", wrec_n)
        scl_after_r = {}
        for oc in range(NS):
            pb = oc % 2
            if oc >= 2:
                P.wait("tensor", "scl", scl_after_r[oc - 2])
            for s in range(NS):
                P.op("tensor", lambda te, o=rr_slots[pb],
                     l=wrec_sb[:, s * 1024 + oc * 128:s * 1024 + oc * 128 + 128],
                     r=xr2[:, s * CH2:(s + 1) * CH2],
                     st=(s == 0), sp=(s == NS - 1):
                     te.matmul(o, l, r, start=st, stop=sp,
                               skip_group_check=True),
                     sem="mm" if s == NS - 1 else None)
            P.wait("scalar", "mm", P.count("mm"))
            P.op("scalar", lambda sc, o=rf[:, oc * CH2:(oc + 1) * CH2],
                 i_=rr_slots[pb]:
                 sc.activation(o, i_, AF.Sigmoid), sem="scl")
            scl_after_r[oc] = P.count("scl")
        P.wait("tensor", "scl", P.count("scl"))
        P.wait("sync", "scl", P.count("scl"))
        P.wait("sync", "st_a", P.count("st_a"))
        xl = P.dma("sync", sl3(sq2), xT_r[:, :, t0:t0 + CH2], "ld_b")
        P.wait("vector", "mm", P.count("mm"))
        P.wait("vector", "scl", P.count("scl"))
        P.wait("vector", "ld_b", xl)
        P.op("vector", lambda v, o=rf, i_=rf, p=ps_kv:
             v.tensor_tensor(o, i_, p, ALU.mult), sem="ev")
        P.op("vector", lambda v, o=sq2, i_=sq2,
             p=sl3(bcc, NS)[:, :, 1:257].rearrange("p s c -> p (s c)")
             if False else sq2: v.tensor_copy(o, i_)) if False else None
        for s in range(NS):
            P.op("vector", lambda v, o=sq2[:, s * CH2:(s + 1) * CH2],
                 i_=sq2[:, s * CH2:(s + 1) * CH2],
                 m=bcc[:, s * 257 + 1:s * 257 + 257]:
                 v.tensor_tensor(o, i_, m, ALU.add))
        P.op("vector", lambda v, o=sq2, i_=sq2, m=rf:
             v.tensor_tensor(o, i_, m, ALU.add), sem="vec")
        P.wait("sync", "vec", P.count("vec"))
        P.dma("sync", xT_r[:, :, t0:t0 + CH2], sl3(sq2), "st_a")
        P.wait("tensor", "vec", P.count("vec"))
    P.barrier()



def emit_final(c):
    P, t = c.P, c.t
    xT_r = dview(t["xT"])
    xst = [carve(c.AR1, 0, 128, 1024, F32), carve(c.AR1, 4096, 128, 1024, F32)]
    ost = [carve(c.AR1, 8192, 128, 1024, F32), carve(c.AR1, 12288, 128, 1024, F32)]
    ps32 = c.psF[:, 0:1024]
    for ti in range(T // 128):
        b = ti % 2
        if ti >= 2:
            P.wait("sync", "mm", P.count("mm"))
        xl = P.dma("sync", sl3(xst[b]), xT_r[:, :, ti * 128:(ti + 1) * 128], "ld_a")
        P.wait("tensor", "ld_a", xl)
        P.wait("tensor", "scl", P.count("scl"))
        for s in range(NS):
            P.op("tensor", lambda te, o=ps32[:, s * 128:(s + 1) * 128],
                 i_=xst[b][:, s * 128:(s + 1) * 128], idn=c.s_ident_f:
                 te.transpose(o, i_, idn), sem="mm" if s == NS - 1 else None)
        P.wait("scalar", "mm", P.count("mm"))
        P.wait("scalar", "st_a", P.count("st_a"))
        evn = P.op("scalar", lambda sc, o=ost[b], i_=ps32: sc.copy(o, i_),
                   sem="scl")
        P.wait("sync", "scl", evn)
        P.dma("sync", t["out"][ti * 128:(ti + 1) * 128, :], ost[b], "st_a")
    P.wait("sync", "st_a", P.count("st_a"))



def build_program(phases=ALL_PHASES, debug_names=(), debug_sbuf=()):
    nc = bass.Bass("TRN2", target_bir_lowering=False, debug=False,
                   num_devices=NCORES, dynamic_dma_scratch_size=8192)

    t = {}
    t["x_in"] = nc.dram_tensor("x", [T, D], F32, kind="ExternalInput").ap()
    t["wq_in"] = nc.dram_tensor("wq", [D, D], F32, kind="ExternalInput").ap()
    t["wk_a_in"] = nc.dram_tensor("wk_a", [D, D], F32, kind="ExternalInput").ap()
    t["wv_a_in"] = nc.dram_tensor("wv_a", [D, D], F32, kind="ExternalInput").ap()
    VEC_NAMES = ["ln1_w", "ln1_b", "ln2_w", "ln2_b", "att_tmk", "att_tmv",
                 "att_tmr", "time_decay", "time_first", "ffn_tmk", "ffn_tmr"]
    for nm in VEC_NAMES:
        t[nm + "_in"] = nc.dram_tensor(nm, [128, NS], F32, kind="ExternalInput").ap()
    for nm, shape in [("wkb", [D, D]), ("wvb", [D, D]), ("wrb", [D, D]),
                      ("wob", [D, D]), ("wkey", [FFN, D]), ("wrec", [D, D]),
                      ("wval", [D, FFN]), ("wg", [D, NE]),
                      ("w1", [NE, D, EH]), ("w2", [NE, EH, D]), ("w3", [NE, D, EH]),
                      ("ws1", [D, D]), ("ws2", [D, D]), ("ws3", [D, D])]:
        t[nm + "_in"] = nc.dram_tensor(nm, shape, F32, kind="ExternalInput").ap()
    t["ident_f_in"] = nc.dram_tensor("ident_f", [128, 128], F32, kind="ExternalInput").ap()
    t["ident_b_in"] = nc.dram_tensor("ident_b", [128, 128], BF16, kind="ExternalInput").ap()
    t["ones_in"] = nc.dram_tensor("ones", [128, 128], F32, kind="ExternalInput").ap()
    t["causal_in"] = nc.dram_tensor("causal", [128, 128], F32, kind="ExternalInput").ap()
    t["onehot_in"] = nc.dram_tensor("onehot", [NE, NE * 128], F32, kind="ExternalInput").ap()
    t["iota32_in"] = nc.dram_tensor("iota32", [128, 2 * NE], F32, kind="ExternalInput").ap()
    t["iotap_in"] = nc.dram_tensor("iotap", [32, 128], F32, kind="ExternalInput").ap()
    t["eps_in"] = nc.dram_tensor("epsv", [128, 2], F32, kind="ExternalInput").ap()
    t["out"] = nc.dram_tensor("out", [T, D], F32, kind="ExternalOutput").ap()

    t["xT"] = nc.dram_tensor("xT", [D, T], F32, kind="Internal").ap()
    t["bT"] = nc.dram_tensor("bT", [D, T], F32, kind="Internal").ap()
    t["hT"] = nc.dram_tensor("hT", [D, T], F32, kind="Internal").ap()
    t["qT"] = nc.dram_tensor("qT", [D, T], BF16, kind="Internal").ap()
    for nm, shape in [("wqpm", [128, NS * D]), ("wkapm", [128, NS * D]),
                      ("wvapm", [128, NS * D]),
                      ("wkT_bf", [D, D]), ("wvT_bf", [D, D]), ("wrT_bf", [D, D]),
                      ("woT_bf", [D, D]), ("wrecT_bf", [D, D]),
                      ("wkeyT_bf", [D, FFN]), ("wvalT_bf", [FFN, D]),
                      ("w1pm", [NE, 128, NS * EH]), ("w2pm", [NE, 128, 4 * D]),
                      ("w3pm", [NE, 128, NS * EH]),
                      ("ws1pm", [2, 128, NS * EH]), ("ws2pm", [2, 128, 4 * D]),
                      ("ws3pm", [2, 128, NS * EH])]:
        t[nm] = nc.dram_tensor(nm, shape, BF16, kind="Internal").ap()

    sem_names = ["bar", "ld_a", "ld_b", "st_a", "st_b", "mm", "ev", "vec",
                 "scl", "pool", "wbv", "wbs", "gpc", "pmm", "pvec", "pscl"]

    with contextlib.ExitStack() as stack:
        AR1h = stack.enter_context(nc.sbuf_tensor("AR1", [128, 32768], BF16))
        AR2h = stack.enter_context(nc.sbuf_tensor("AR2", [128, 32768], BF16))
        AR3h = stack.enter_context(nc.sbuf_tensor("AR3", [128, 32768], BF16))
        AR4h = stack.enter_context(nc.sbuf_tensor("AR4", [128, 8192], BF16))
        identfh = stack.enter_context(nc.sbuf_tensor("sb_idf", [128, 128], F32))
        identbh = stack.enter_context(nc.sbuf_tensor("sb_idb", [128, 128], BF16))
        onesh = stack.enter_context(nc.sbuf_tensor("sb_ones", [128, 128], F32))
        causalh = stack.enter_context(nc.sbuf_tensor("sb_causal", [128, 128], F32))
        iotah = stack.enter_context(nc.sbuf_tensor("sb_iota", [128, 2 * NE], F32))
        iotaph = stack.enter_context(nc.sbuf_tensor("sb_iotap", [32, 128], F32))
        epsh = stack.enter_context(nc.sbuf_tensor("sb_eps", [128, 2], F32))
        wgh = stack.enter_context(nc.sbuf_tensor("sb_wg", [128, NS * NE], F32))
        vech = stack.enter_context(nc.sbuf_tensor("sb_vecs", [128, 11 * NS], F32))
        psFh = stack.enter_context(nc.psum_tensor("psF", [128, 4096], F32))

        blk = stack.enter_context(nc.Block())
        sems = {nm: stack.enter_context(nc.semaphore(nm)) for nm in sem_names}

        P = Prog(nc)
        for nm in sem_names:
            P.register_sem(nm, sems[nm])

        c = Ctx()
        c.P, c.t, c.nc = P, t, nc
        c.AR1, c.AR2, c.AR3, c.AR4 = AR1h[:], AR2h[:], AR3h[:], AR4h[:]
        c.LN0T = AR2h[:]
        c.KT = AR3h[:]
        c.V = AR1h[:]
        c.s_ident_f, c.s_ident_bf = identfh[:], identbh[:]
        c.s_ones = onesh[:, 0:1]
        c.s_ones_row = onesh[0:1, :]
        c.s_causal = causalh[:]
        c.s_iota32 = iotah[:, 0:NE]
        c.s_iotap = iotaph[:]
        c.s_iotam64 = iotah[:, NE:2 * NE]
        c.s_eps = epsh[:, 0:1]
        c.s_eps_row = epsh[0:1, 1:2]
        c.WG = wgh[:]
        c.psF = psFh[:]
        c.s_vec = {nm: vech[:, i * NS:(i + 1) * NS]
                   for i, nm in enumerate(VEC_NAMES)}

        P.dma("sync", c.s_ident_f, t["ident_f_in"], "ld_a")
        P.dma("sync", c.s_ident_bf, t["ident_b_in"], "ld_a")
        P.dma("sync", onesh[:], t["ones_in"], "ld_a")
        P.dma("sync", c.s_causal, t["causal_in"], "ld_a")
        P.dma("sync", iotah[:], t["iota32_in"], "ld_a")
        P.dma("sync", iotaph[:], t["iotap_in"], "ld_a")
        P.dma("sync", epsh[:], t["eps_in"], "ld_a")
        for s in range(NS):
            P.dma("sync", wgh[:, s * NE:(s + 1) * NE],
                  t["wg_in"][s * 128:(s + 1) * 128, :], "ld_a")
        for nm in VEC_NAMES:
            P.dma("sync", c.s_vec[nm], t[nm + "_in"], "ld_a")
        P.wait("sync", "ld_a", P.count("ld_a"))
        P.barrier()

        if "prep" in phases:
            emit_prep(c)
        if "a1" in phases:
            emit_a1(c)
        if "a2" in phases:
            emit_a2(c)
        if "a3" in phases:
            emit_a3(c)
        if "moe1" in phases:
            emit_moe(c)
        if "attnb" in phases:
            emit_attnb(c)
        if "moe2" in phases:
            emit_moe(c)
        if "final" in phases:
            emit_final(c)
        else:
            for ti in range(0, T, 128):
                sb = carve(c.AR4, 0, 128, 1024, F32)
                P.wait("sync", "st_b", P.count("st_b"))
                P.dma("sync", sb, t["x_in"][ti:ti + 128, :], "ld_b")
                P.wait("gpsimd", "ld_b", P.count("ld_b"))
                P.dma("gpsimd", t["out"][ti:ti + 128, :], sb, "st_b")
            P.wait("sync", "st_b", P.count("st_b"))

        if debug_names:
            P.barrier()
            for nm in debug_names:
                src = t[nm]
                dbg = nc.dram_tensor("dbg_" + nm, list(src.shape), src.dtype,
                                     kind="ExternalOutput").ap()
                P.dma("gpsimd", dbg, src, "st_b")
            P.wait("gpsimd", "st_b", P.count("st_b"))
        if debug_sbuf:
            P.barrier()
            for nm, ap in debug_sbuf:
                dbg = nc.dram_tensor("dbg_" + nm, list(ap(c).shape), ap(c).dtype,
                                     kind="ExternalOutput").ap()
                P.dma("gpsimd", dbg, ap(c), "st_b")
            P.wait("gpsimd", "st_b", P.count("st_b"))

        for name, eng_sel in (("sync", blk.sync), ("vector", blk.vector),
                              ("scalar", blk.scalar), ("tensor", blk.tensor),
                              ("gpsimd", blk.gpsimd)):
            fns = P.plan[name]
            eng_sel(lambda e, fns=fns: [f(e) for f in fns])

    return nc


def make_inputs(inputs):
    x = np.ascontiguousarray(inputs["L0_attn_a_embeddings"], dtype=np.float32)

    def slabvec(v):
        return np.ascontiguousarray(np.asarray(v, np.float32).reshape(NS, 128).T)

    base = {
        "wq": inputs["L0_attn_a_W_Q"], "wk_a": inputs["L0_attn_a_W_K"],
        "wv_a": inputs["L0_attn_a_W_V"],
        "wkb": inputs["L0_attn_b_Wk"], "wvb": inputs["L0_attn_b_Wv"],
        "wrb": inputs["L0_attn_b_Wr"], "wob": inputs["L0_attn_b_Wo"],
        "wkey": inputs["L0_attn_b_Wkey"], "wrec": inputs["L0_attn_b_Wrec"],
        "wval": inputs["L0_attn_b_Wval"],
        "wg": inputs["L0_ffn_Wg"], "w1": inputs["L0_ffn_W1"],
        "w2": inputs["L0_ffn_W2"], "w3": inputs["L0_ffn_W3"],
        "ws1": inputs["L0_ffn_Ws1"], "ws2": inputs["L0_ffn_Ws2"],
        "ws3": inputs["L0_ffn_Ws3"],
    }
    base = {k: np.ascontiguousarray(v, dtype=np.float32) for k, v in base.items()}
    for nm, key in [("ln1_w", "L0_attn_b_ln1_w"), ("ln1_b", "L0_attn_b_ln1_b"),
                    ("ln2_w", "L0_attn_b_ln2_w"), ("ln2_b", "L0_attn_b_ln2_b"),
                    ("att_tmk", "L0_attn_b_att_tmk"),
                    ("att_tmv", "L0_attn_b_att_tmv"),
                    ("att_tmr", "L0_attn_b_att_tmr"),
                    ("time_decay", "L0_attn_b_time_decay"),
                    ("time_first", "L0_attn_b_time_first"),
                    ("ffn_tmk", "L0_attn_b_ffn_tmk"),
                    ("ffn_tmr", "L0_attn_b_ffn_tmr")]:
        base[nm] = slabvec(inputs[key])

    import ml_dtypes
    base["ident_f"] = np.eye(128, dtype=np.float32)
    base["ident_b"] = np.eye(128).astype(ml_dtypes.bfloat16)
    base["ones"] = np.ones((128, 128), np.float32)
    cz = np.zeros((128, 128), np.float32)
    i_idx = np.arange(128)[:, None]
    j_idx = np.arange(128)[None, :]
    cz[j_idx > i_idx] = -1e30 / SCALE_A
    base["causal"] = cz
    oh = np.zeros((NE, NE * 128), np.float32)
    for e in range(NE):
        oh[e, e * 128:(e + 1) * 128] = 1.0
    base["onehot"] = oh
    io = np.tile(np.arange(NE, dtype=np.float32), (128, 1))
    base["iota32"] = np.ascontiguousarray(
        np.concatenate([io, io - 64.0], axis=1))
    base["iotap"] = np.ascontiguousarray(
        np.tile(np.arange(32, dtype=np.float32)[:, None], (1, 128)))
    base["epsv"] = np.full((128, 2), EPS, np.float32)

    in_maps = []
    for cc in range(NCORES):
        m = dict(base)
        m["x"] = np.ascontiguousarray(x[cc])
        in_maps.append(m)
    return in_maps


def kernel(**inputs):
    in_maps = make_inputs(inputs)
    nc = build_program()
    res = run_bass_kernel_spmd(nc, in_maps, core_ids=list(range(NCORES)))
    out = np.stack([res.results[cc]["out"] for cc in range(NCORES)], axis=0)
    return out.astype(np.float32)


# version 14 (final)
# speedup vs baseline: 1.1287x; 1.0174x over previous
import contextlib
import numpy as np
import concourse.bass as bass
import concourse.mybir as mybir
from concourse.bass_utils import run_bass_kernel_spmd

F32 = mybir.dt.float32
BF16 = mybir.dt.bfloat16
AF = mybir.ActivationFunctionType
ALU = mybir.AluOpType
AX = mybir.AxisListType

D = 1024
T = 4096
NS = D // 128
NE = 32
EH = 512
FFN = 4096
NCORES = 8
EPS = 1e-5
TOPK = 4
CAP_DROP = 70.0
SCALE_A = 1.0 / 32.0

ENGS = ("sync", "vector", "scalar", "tensor", "gpsimd")
ALL_PHASES = ("prep", "a1", "a2", "a3", "moe1", "attnb", "moe2", "final")


class Prog:
    def __init__(self, nc):
        self.nc = nc
        self.plan = {e: [] for e in ENGS}
        self.sem_handles = {}
        self.sem_counts = {}

    def register_sem(self, name, handle):
        self.sem_handles[name] = handle
        self.sem_counts[name] = 0

    def bump(self, name, k):
        self.sem_counts[name] += k
        return self.sem_counts[name]

    def count(self, name):
        return self.sem_counts[name]

    def emit(self, eng, fn):
        self.plan[eng].append(fn)

    def dma(self, eng, out, in_, sem):
        h = self.sem_handles[sem]
        tgt = self.bump(sem, 16)

        def f(e, out=out, in_=in_, h=h):
            e.dma_start(out=out, in_=in_).then_inc(h, 16)
        self.emit(eng, f)
        return tgt

    def wait(self, eng, sem, val=None):
        h = self.sem_handles[sem]
        v = self.count(sem) if val is None else val
        if v <= 0:
            return

        def f(e, h=h, v=v):
            e.wait_ge(h, v)
        self.emit(eng, f)

    def op(self, eng, fn, sem=None, k=1):
        if sem is None:
            self.emit(eng, fn)
            return None
        h = self.sem_handles[sem]
        tgt = self.bump(sem, k)

        def f(e, fn=fn, h=h, k=k):
            fn(e).then_inc(h, k)
        self.emit(eng, f)
        return tgt

    def barrier(self):
        h = self.sem_handles["bar"]
        base = self.count("bar")
        for e in ENGS:
            self.bump("bar", 1)
        tgt = base + len(ENGS)
        for e in ENGS:
            def f(eng, h=h, tgt=tgt):
                eng.nop().then_inc(h, 1)
                eng.wait_ge(h, tgt)
            self.emit(e, f)


class Ctx:
    pass


def carve(arena_ap, byte_off, rows, cols, dtype):
    nb = mybir.dt.size(dtype)
    assert byte_off % 2 == 0
    start = byte_off // 2
    width = cols * nb // 2
    sl = arena_ap[0:rows, start:start + width]
    if dtype == BF16:
        return sl
    return sl.bitcast(dtype)


def sl3(ap, s=NS):
    return ap.rearrange("p (s c) -> p s c", s=s)


def dview(dram_ap):
    return dram_ap.rearrange("(s p) t -> p s t", p=128)



def emit_prep(c):
    P, t = c.P, c.t
    c.prep_w_done = {}

    def cast_pm(dst_pm, src, s_count, width):
        P.dma("gpsimd", dst_pm.rearrange("p (s h) -> p s h", s=s_count),
              src.rearrange("(s p) h -> p s h", p=128), "gpc")

    cast_pm(t["wqpm"], t["wq_in"], NS, D)
    cast_pm(t["wkapm"], t["wk_a_in"], NS, D)
    cast_pm(t["wvapm"], t["wv_a_in"], NS, D)
    c.prep_w_done["qkv"] = P.count("gpc")

    stg16 = [carve(c.AR1, 0, 128, 2048, BF16), carve(c.AR1, 4096, 128, 2048, BF16)]
    tstg = [carve(c.AR1, 8192, 128, 2048, BF16), carve(c.AR1, 12288, 128, 2048, BF16)]
    ps_bf2 = [c.psF[:, 0:1024].bitcast(BF16), c.psF[:, 1024:2048].bitcast(BF16)]
    tposes = [
        ("wkb_in", "wkT_bf", 1024, 1024), ("wvb_in", "wvT_bf", 1024, 1024),
        ("wrb_in", "wrT_bf", 1024, 1024), ("wob_in", "woT_bf", 1024, 1024),
        ("wrec_in", "wrecT_bf", 1024, 1024),
        ("wkey_in", "wkeyT_bf", 4096, 1024), ("wval_in", "wvalT_bf", 1024, 4096),
    ]
    ti = 0
    for nm_in, nm_out, R, Cc in tposes:
        srcw, dst = t[nm_in], t[nm_out]
        for r0 in range(0, R, 128):
            for c0 in range(0, Cc, 2048):
                cw = min(2048, Cc - c0)
                b = ti % 2
                ps_bf = ps_bf2[b]
                P.wait("gpsimd", "mm", P.count("mm"))
                ldn = P.dma("gpsimd", stg16[b][:, 0:cw],
                            srcw[r0:r0 + 128, c0:c0 + cw], "gpc")
                P.wait("tensor", "gpc", ldn)
                P.wait("tensor", "st_a", P.count("st_a"))
                P.wait("tensor", "scl", P.count("scl"))
                nblk = cw // 128
                for jb in range(nblk):
                    P.op("tensor",
                         lambda te, o=ps_bf[:, jb * 128:(jb + 1) * 128],
                         i_=stg16[b][:, jb * 128:(jb + 1) * 128],
                         idn=c.s_ident_bf:
                         te.transpose(o, i_, idn))
                P.op("tensor",
                     lambda te, o=c.psF[:, 2048:2112].bitcast(BF16),
                     i_=c.s_ident_bf, idn=c.s_ident_bf:
                     te.transpose(o, i_, idn), sem="mm")
                P.wait("scalar", "mm", P.count("mm"))
                P.op("scalar",
                     lambda sc, o=tstg[b][:, 0:cw], i_=ps_bf[:, 0:cw]:
                     sc.copy(o, i_), sem="scl")
                P.wait("sync", "scl", P.count("scl"))
                for jb in range(nblk):
                    P.dma("sync",
                          dst[c0 + jb * 128:c0 + (jb + 1) * 128, r0:r0 + 128],
                          tstg[b][:, jb * 128:(jb + 1) * 128], "st_a")
                ti += 1
    c.prep_w_done["attnb"] = P.count("st_a")

    P.wait("gpsimd", "st_a", P.count("st_a"))
    for h in range(2):
        cast_pm(t["ws1pm"][h], t["ws1_in"][:, h * EH:(h + 1) * EH], NS, EH)
        cast_pm(t["ws3pm"][h], t["ws3_in"][:, h * EH:(h + 1) * EH], NS, EH)
        cast_pm(t["ws2pm"][h], t["ws2_in"][h * EH:(h + 1) * EH, :], 4, D)
        c.prep_w_done[NE + h] = P.count("gpc")
    for e in range(NE):
        cast_pm(t["w1pm"][e], t["w1_in"][e], NS, EH)
        cast_pm(t["w3pm"][e], t["w3_in"][e], NS, EH)
        cast_pm(t["w2pm"][e], t["w2_in"][e], 4, D)
        c.prep_w_done[e] = P.count("gpc")
    c.prep_w_done["all"] = P.count("gpc")
    P.wait("sync", "st_a", P.count("st_a"))
    P.barrier()



def emit_a1(c):
    P, t = c.P, c.t
    xst = [carve(c.AR1, 0, 128, 1024, F32), carve(c.AR1, 4096, 128, 1024, F32)]
    tmp1 = carve(c.AR1, 8192, 128, 1024, F32)
    lnst = [carve(c.AR1, 12288, 128, 1024, BF16), carve(c.AR1, 14336, 128, 1024, BF16)]
    xTst = [carve(c.AR1, 16384, 128, 1024, F32), carve(c.AR1, 20480, 128, 1024, F32)]
    nmu = carve(c.AR1, 24576, 128, 1, F32)
    vsum = carve(c.AR1, 24608, 128, 1, F32)
    rstd = carve(c.AR1, 24640, 128, 1, F32)
    junk = carve(c.AR1, 28672, 128, 1024, F32)
    ps32 = c.psF[:, 0:1024]
    psbf = c.psF[:, 1024:1536].bitcast(BF16)
    xT_r = dview(t["xT"])

    for ti in range(T // 128):
        b = ti % 2
        if ti >= 2:
            P.wait("sync", "mm", P.count("mm"))
            P.wait("sync", "vec", P.count("vec"))
        ld_n = P.dma("sync", xst[b], t["x_in"][ti * 128:(ti + 1) * 128, :], "ld_a")

        P.wait("tensor", "ld_a", ld_n)
        if ti >= 2:
            P.wait("tensor", "st_a", P.count("st_a"))
        for s in range(NS):
            P.op("tensor",
                 lambda te, o=ps32[:, s * 128:(s + 1) * 128],
                 i_=xst[b][:, s * 128:(s + 1) * 128], idn=c.s_ident_f:
                 te.transpose(o, i_, idn),
                 sem="mm" if s == NS - 1 else None)
        P.wait("scalar", "mm", P.count("mm"))
        evn = P.op("scalar", lambda sc, o=xTst[b], i_=ps32: sc.copy(o, i_),
                   sem="scl")
        P.wait("sync", "scl", evn)
        P.dma("sync", xT_r[:, :, ti * 128:(ti + 1) * 128], sl3(xTst[b]), "st_a")

        P.wait("vector", "ld_a", ld_n)
        P.op("vector", lambda v, o=vsum, i_=xst[b]:
             v.reduce_sum(o, i_, axis=AX.X), sem="wbv")
        P.wait("vector", "wbv")
        P.op("vector", lambda v, o=nmu, i_=vsum:
             v.tensor_scalar(o, i_, -1.0 / D, None, ALU.mult), sem="wbv")
        P.wait("vector", "wbv")
        P.op("vector", lambda v, o=tmp1, i_=xst[b], m=nmu:
             v.tensor_scalar(o, i_, m, None, ALU.add))
        P.op("vector", lambda v, o=junk, i_=tmp1: v.tensor_mul(o, i_, i_))
        P.op("vector", lambda v, o=vsum, i_=junk:
             v.reduce_sum(o, i_, axis=AX.X), sem="vec")
        P.wait("scalar", "vec", P.count("vec"))
        P.op("scalar", lambda sc, o=rstd, i_=vsum, e=c.s_eps:
             sc.activation(o, i_, AF.Sqrt, bias=e, scale=1.0 / D), sem="scl")
        P.wait("vector", "scl", P.count("scl"))
        P.op("vector", lambda v, o=nmu, i_=rstd: v.reciprocal(o, i_), sem="wbv")
        P.wait("vector", "wbv")
        P.op("vector", lambda v, o=lnst[b], i_=tmp1, m=nmu:
             v.tensor_scalar(o, i_, m, None, ALU.mult), sem="vec")
        P.wait("tensor", "vec", P.count("vec"))
        for s in range(NS):
            P.op("tensor",
                 lambda te, o=psbf[:, s * 128:(s + 1) * 128],
                 i_=lnst[b][:, s * 128:(s + 1) * 128], idn=c.s_ident_bf:
                 te.transpose(o, i_, idn),
                 sem="mm" if s == NS - 1 else None)
        P.wait("vector", "mm", P.count("mm"))
        P.op("vector",
             lambda v, o=sl3(c.LN0T, NS)[:, :, ti * 128:(ti + 1) * 128].
             rearrange("p s c -> p (s c)") if False else
             sl3(c.LN0T)[:, :, ti * 128:(ti + 1) * 128],
             i_=sl3(psbf, NS): v.tensor_copy(o, i_), sem="vec")
    P.barrier()



def emit_a2(c):
    P, t = c.P, c.t
    wsl = carve(c.AR4, 0, 128, NS * 1024, BF16)
    qstg = [carve(c.AR1, 0, 128, 512, BF16), carve(c.AR1, 1024, 128, 512, BF16)]
    ps = [c.psF[:, 0:512], c.psF[:, 512:1024]]
    qT = t["qT"]

    P.wait("sync", "gpc", c.prep_w_done["qkv"])

    def load_w(nm):
        P.dma("sync", wsl, t[nm], "ld_a")
        return P.count("ld_a")

    for which, nm in (("k", "wkapm"), ("q", "wqpm")):
        P.wait("sync", "mm", P.count("mm"))
        wn = load_w(nm)
        P.wait("tensor", "ld_a", wn)
        pp = 0
        for tc in range(T // 512):
            for oc in range(NS):
                b = pp % 2
                P.wait("tensor", "ev", P.count("ev"))
                P.wait("tensor", "st_a", P.count("st_a"))
                for s in range(NS):
                    P.op("tensor",
                         lambda te, o=ps[b],
                         l=wsl[:, s * 1024 + oc * 128:s * 1024 + oc * 128 + 128],
                         r=c.LN0T[:, s * T + tc * 512:s * T + tc * 512 + 512],
                         st=(s == 0), sp=(s == NS - 1):
                         te.matmul(o, l, r, start=st, stop=sp,
                                   skip_group_check=True),
                         sem="mm" if s == NS - 1 else None)
                mmn = P.count("mm")
                if which == "k":
                    P.wait("vector", "mm", mmn)
                    P.op("vector",
                         lambda v, o=c.KT[:, oc * T + tc * 512:oc * T + tc * 512 + 512],
                         i_=ps[b]: v.tensor_copy(o, i_), sem="ev")
                else:
                    P.wait("scalar", "mm", mmn)
                    evn = P.op("scalar",
                               lambda sc, o=qstg[b], i_=ps[b]: sc.copy(o, i_),
                               sem="ev")
                    P.wait("sync", "ev", evn)
                    P.dma("sync", qT[oc * 128:(oc + 1) * 128, tc * 512:tc * 512 + 512],
                          qstg[b], "st_a")
                pp += 1

    P.wait("sync", "mm", P.count("mm"))
    P.wait("sync", "st_a", P.count("st_a"))
    wn = load_w("wvapm")
    P.wait("tensor", "ld_a", wn)
    pp = 0
    for ti in range(T // 128):
        for dc in range(2):
            b = pp % 2
            P.wait("tensor", "ev", P.count("ev"))
            for s in range(NS):
                P.op("tensor",
                     lambda te, o=ps[b],
                     l=c.LN0T[:, s * T + ti * 128:s * T + ti * 128 + 128],
                     r=wsl[:, s * 1024 + dc * 512:s * 1024 + dc * 512 + 512],
                     st=(s == 0), sp=(s == NS - 1):
                     te.matmul(o, l, r, start=st, stop=sp,
                               skip_group_check=True),
                     sem="mm" if s == NS - 1 else None)
            P.wait("vector", "mm", P.count("mm"))
            P.op("vector",
                 lambda v, o=c.V[:, ti * 1024 + dc * 512:ti * 1024 + dc * 512 + 512],
                 i_=ps[b]: v.tensor_copy(o, i_), sem="ev")
            pp += 1
    P.barrier()



def emit_a3(c):
    P, t = c.P, c.t
    qtile = [carve(c.AR2, 0, 128, 1024, BF16), carve(c.AR2, 2048, 128, 1024, BF16)]
    ptev_big = carve(c.AR2, 28672, 128, 4096, BF16)
    pbig = carve(c.AR2, 36864, 128, 4096, BF16)
    mg8 = carve(c.AR2, 45056, 128, 8, F32)
    gam8 = carve(c.AR2, 45088, 128, 8, F32)
    nbg = carve(c.AR2, 45120, 128, 1, F32)
    brow = carve(c.AR2, 45152, 128, 1, F32)
    mrow = carve(c.AR2, 6144, 128, 1, F32)
    lrow = carve(c.AR2, 6208, 128, 1, F32)
    tsum = carve(c.AR2, 6272, 128, 1, F32)
    tmax = carve(c.AR2, 6336, 128, 1, F32)
    alpha = carve(c.AR2, 6400, 128, 1, F32)
    nbias = carve(c.AR2, 6464, 128, 1, F32)
    junk = carve(c.AR2, 8192, 128, 512, F32)
    otst = [carve(c.AR2, 12288, 128, 1024, F32), carve(c.AR2, 16384, 128, 1024, F32)]
    xtst = [carve(c.AR2, 20480, 128, 1024, F32), carve(c.AR2, 24576, 128, 1024, F32)]
    ps_s = c.psF[:, 2048:2560]
    ps_pt = c.psF[:, 2560:2816].bitcast(BF16)
    ps_ot = c.psF[:, 0:1024]
    qT_r = dview(t["qT"])
    xT_r = dview(t["xT"])

    for qi in range(T // 128):
        bq = qi % 2
        if qi >= 2:
            P.wait("sync", "mm", P.count("mm"))
        qld = P.dma("sync", sl3(qtile[bq]), qT_r[:, :, qi * 128:(qi + 1) * 128],
                    "ld_a")
        P.op("vector", lambda v, o=mrow: v.memset(o, -1e30))
        P.op("vector", lambda v, o=lrow: v.memset(o, 0.0), sem="wbv")
        P.wait("vector", "wbv")
        P.wait("tensor", "ld_a", qld)

        ngrp = (qi * 128 + 128 + 511) // 512
        for g in range(ngrp):
            k0 = g * 512
            kw = min(512, qi * 128 + 128 - k0)
            P.wait("tensor", "vec", P.count("vec"))
            P.wait("tensor", "scl", P.count("scl"))
            for s in range(NS):
                P.op("tensor",
                     lambda te, o=ps_s[:, 0:kw], l=qtile[bq][:, s * 128:(s + 1) * 128],
                     r=c.KT[:, s * T + k0:s * T + k0 + kw],
                     st=(s == 0), sp=(s == NS - 1):
                     te.matmul(o, l, r, start=st, stop=sp, skip_group_check=True),
                     sem="mm" if s == NS - 1 else None)
            P.wait("vector", "mm", P.count("mm"))
            if kw > (qi * 128 - k0):
                P.op("vector", lambda v, o=ps_s[:, kw - 128:kw],
                     i_=ps_s[:, kw - 128:kw], m=c.s_causal:
                     v.tensor_tensor(o, i_, m, ALU.add))
            P.op("vector", lambda v, o=tmax, i_=ps_s[:, 0:kw]:
                 v.reduce_max(o, i_, axis=AX.X), sem="wbv")
            P.wait("vector", "wbv")
            P.op("vector", lambda v, o=alpha, i_=mrow, m=tmax:
                 v.tensor_scalar(o, i_, m, None, ALU.subtract))
            P.op("vector", lambda v, o=mg8[:, g:g + 1], i_=tmax:
                 v.tensor_copy(o, i_))
            P.op("vector", lambda v, o=nbg, i_=tmax:
                 v.tensor_scalar(o, i_, -SCALE_A, None, ALU.mult))
            P.op("vector", lambda v, o=mrow, i_=mrow, m=tmax:
                 v.tensor_scalar(o, i_, m, None, ALU.max), sem="wbv")
            P.wait("vector", "wbv")
            P.op("vector", lambda v, o=alpha, i_=alpha:
                 v.tensor_scalar(o, i_, 0.0, None, ALU.min))
            P.op("vector", lambda v, o=brow, i_=tmax, m=mrow:
                 v.tensor_scalar(o, i_, m, None, ALU.subtract))
            P.op("vector", lambda v, o=nbias, i_=mrow:
                 v.tensor_scalar(o, i_, -SCALE_A, None, ALU.mult), sem="vec")
            P.wait("scalar", "vec", P.count("vec"))
            P.op("scalar", lambda sc, o=alpha, i_=alpha:
                 sc.activation(o, i_, AF.Exp, scale=SCALE_A))
            P.op("scalar", lambda sc, o=brow, i_=brow:
                 sc.activation(o, i_, AF.Exp, scale=SCALE_A), sem="scl")
            P.op("scalar", lambda sc, o=junk[:, 0:kw], i_=ps_s[:, 0:kw],
                 b_=nbg, m=tsum:
                 sc.activation(o, i_, AF.Exp, bias=b_, scale=SCALE_A,
                               accum_out=m), sem="scl")
            P.op("scalar", lambda sc, o=pbig[:, k0:k0 + kw],
                 i_=junk[:, 0:kw]: sc.copy(o, i_), sem="scl")
            P.wait("vector", "scl", P.count("scl"))
            P.op("vector", lambda v, o=lrow, i_=lrow, a=alpha:
                 v.tensor_scalar(o, i_, a, None, ALU.mult))
            P.op("vector", lambda v, o=tsum, i_=tsum, b_=brow:
                 v.tensor_scalar(o, i_, b_, None, ALU.mult), sem="wbv")
            P.wait("vector", "wbv")
            P.op("vector", lambda v, o=lrow, i_=lrow, b_=tsum:
                 v.tensor_scalar(o, i_, b_, None, ALU.add), sem="vec")

        P.wait("scalar", "vec", P.count("vec"))
        P.op("scalar", lambda sc, o=tsum, i_=lrow:
             sc.activation(o, i_, AF.Ln), sem="scl")
        P.wait("vector", "scl", P.count("scl"))
        P.op("vector", lambda v, o=nbias, i_=nbias, m=tsum:
             v.tensor_scalar(o, i_, m, None, ALU.subtract), sem="vec")

        P.wait("scalar", "vec", P.count("vec"))
        P.op("scalar", lambda sc, o=gam8[:, 0:ngrp], i_=mg8[:, 0:ngrp],
             b_=nbias:
             sc.activation(o, i_, AF.Exp, bias=b_, scale=SCALE_A), sem="scl")
        P.wait("vector", "scl", P.count("scl"))
        P.wait("vector", "mm", P.count("mm"))
        for g in range(ngrp):
            k0 = g * 512
            kw = min(512, qi * 128 + 128 - k0)
            P.op("vector", lambda v, o=pbig[:, k0:k0 + kw],
                 i_=pbig[:, k0:k0 + kw], m=gam8[:, g:g + 1]:
                 v.tensor_scalar(o, i_, m, None, ALU.mult), sem="vec")
            P.wait("tensor", "vec", P.count("vec"))
            nkt = (kw + 127) // 128
            for j in range(nkt):
                P.op("tensor",
                     lambda te, o=ps_pt[:, j * 128:(j + 1) * 128],
                     i_=pbig[:, k0 + j * 128:k0 + (j + 1) * 128],
                     idn=c.s_ident_bf:
                     te.transpose(o, i_, idn),
                     sem="mm" if j == nkt - 1 else None)
            P.wait("vector", "mm", P.count("mm"))
            P.op("vector", lambda v, o=ptev_big[:, k0:k0 + kw], i_=ps_pt[:, 0:kw]:
                 v.tensor_copy(o, i_), sem="vec")
        P.wait("tensor", "vec", P.count("vec"))
        nkt_all = qi + 1
        for dc in range(NS):
            for kt in range(nkt_all):
                P.op("tensor",
                     lambda te, o=ps_ot[:, dc * 128:(dc + 1) * 128],
                     l=c.V[:, kt * 1024 + dc * 128:kt * 1024 + dc * 128 + 128],
                     r=ptev_big[:, kt * 128:(kt + 1) * 128],
                     st=(kt == 0), sp=(kt == nkt_all - 1):
                     te.matmul(o, l, r, start=st, stop=sp,
                               skip_group_check=True),
                     sem="mm" if (dc == NS - 1 and kt == nkt_all - 1) else None)

        bo = qi % 2
        P.wait("sync", "vec", P.count("vec"))
        xld = P.dma("sync", sl3(xtst[bo]), xT_r[:, :, qi * 128:(qi + 1) * 128],
                    "ld_b")
        P.wait("vector", "mm", P.count("mm"))
        P.wait("vector", "ld_b", xld)
        P.wait("vector", "st_a", P.count("st_a"))
        P.op("vector", lambda v, o=otst[bo], i_=xtst[bo], p=ps_ot:
             v.tensor_tensor(o, i_, p, ALU.add), sem="vec")
        P.wait("sync", "vec", P.count("vec"))
        P.dma("sync", xT_r[:, :, qi * 128:(qi + 1) * 128], sl3(otst[bo]), "st_a")
    P.barrier()



def emit_moe(c):
    P, t = c.P, c.t
    GT = 1024
    CH = 256
    NG = T // GT
    NCH = GT // CH
    NPE = NE + 2
    NPAIR = NPE // 2

    ln1t = [carve(c.AR2, 0, 128, NS * GT, BF16),
            carve(c.AR2, 16384, 128, NS * GT, BF16)]
    mw = [carve(c.AR2, 32768, 128, 4 * CH, BF16),
          carve(c.AR2, 34816, 128, 4 * CH, BF16),
          carve(c.AR2, 36864, 128, 4 * CH, BF16),
          carve(c.AR2, 38912, 128, 4 * CH, BF16)]
    logit = carve(c.AR2, 40960, 128, NE, F32)
    lmax = carve(c.AR2, 41088, 128, 1, F32)
    lsum = carve(c.AR2, 41120, 128, 1, F32)
    rrow = carve(c.AR2, 41152, 128, 1, F32)
    srow = carve(c.AR2, 41184, 128, 1, F32)
    eq = carve(c.AR2, 41216, 128, NE, F32)
    idxm = carve(c.AR2, 41344, 128, NE, F32)
    wsel = carve(c.AR2, 41472, 128, NE, F32)
    stat1 = carve(c.AR2, 41600, 1, CH, F32)
    stat2 = carve(c.AR2, 42624, 1, CH, F32)
    wbsb = [carve(c.AR2, 53888, 128, GT, F32),
            carve(c.AR2, 57984, 128, GT, F32)]
    silu1 = [carve(c.AR2, 45696, 128, 4 * CH, BF16),
             carve(c.AR2, 47744, 128, 4 * CH, BF16)]
    h3w = [carve(c.AR2, 49792, 128, 4 * CH, BF16),
           carve(c.AR2, 51840, 128, 4 * CH, BF16)]
    yacc = carve(c.AR1, 0, 128, NS * GT, F32)
    wt32 = [carve(c.AR1, 32768, 32, GT, F32),
            carve(c.AR1, 36864, 32, GT, F32)]
    onehot_dyn = [carve(c.AR1, 40960, 32, 128, F32),
                  carve(c.AR1, 41472, 32, 128, F32)]
    xchk = [carve(c.AR1, 49152, 128, NS * CH, F32),
            carve(c.AR1, 57344, 128, NS * CH, F32)]
    lnf = carve(c.AR4, 0, 128, NS * CH, F32)
    xchk2 = carve(c.AR4, 8192, 128, NS * CH, F32)
    wv = []
    for b in range(2):
        off = b * 24576
        wv.append({
            "w1": carve(c.AR3, off, 128, NS * 512, BF16),
            "w3": carve(c.AR3, off + 8192, 128, NS * 512, BF16),
            "w2": carve(c.AR3, off + 16384, 128, 4 * 1024, BF16),
        })
    ps_y = c.psF[:, 0:2048]
    ps_h1 = c.psF[:, 2048:3072]
    ps_h3 = c.psF[:, 3072:4096]
    ps_st = c.psF[:, 2560:3072]
    ps_lg = c.psF[:, 2560:2592]
    ps_wt = c.psF[0:32, 2592:2720]
    xT_r = dview(t["xT"])


    def load_w13(pe, buf):
        P.wait("sync", "gpc", c.prep_w_done["all"])
        if pe < NE:
            P.dma("sync", wv[buf]["w1"], t["w1pm"][pe], "ld_a")
            n1 = P.count("ld_a")
            P.dma("sync", wv[buf]["w3"], t["w3pm"][pe], "ld_a")
        else:
            h0 = pe - NE
            P.dma("sync", wv[buf]["w1"], t["ws1pm"][h0], "ld_a")
            n1 = P.count("ld_a")
            P.dma("sync", wv[buf]["w3"], t["ws3pm"][h0], "ld_a")
        return (n1, P.count("ld_a"))

    def load_w2(pe, buf):
        if pe < NE:
            P.dma("sync", wv[buf]["w2"], t["w2pm"][pe], "ld_a")
        else:
            P.dma("sync", wv[buf]["w2"], t["ws2pm"][pe - NE], "ld_a")
        return P.count("ld_a")



    ln1t_ready = {}
    c.prep_psum_free = 0
    evst = {"silu": [0, 0], "h3w": [0, 0], "bcast": 0}
    poolw = {0: [0, 0], 1: [0, 0]}

    def prep_gen(g, gb):
        t0g = g * GT
        xln_done = {}
        P.wait("sync", "pvec", P.count("pvec"))
        xln_done[0] = P.dma("sync", sl3(xchk[0]), xT_r[:, :, t0g:t0g + CH],
                            "ld_b")
        for ch in range(NCH):
            t0 = t0g + ch * CH
            if ch + 1 < NCH:
                P.wait("sync", "pvec", P.count("pvec"))
                xln_done[ch + 1] = P.dma(
                    "sync", sl3(xchk[(ch + 1) % 2]),
                    xT_r[:, :, t0 + CH:t0 + 2 * CH], "ld_b")
            xc = xchk[ch % 2]
            xln = xln_done[ch]
            P.wait("tensor", "ld_b", xln)
            P.wait("tensor", "pvec", P.count("pvec"))
            P.wait("tensor", "pscl", P.count("pscl"))
            P.wait("tensor", "scl", P.count("scl"))
            for s in range(NS):
                P.op("tensor", lambda te, o=ps_st[0:1, 0:CH], l=c.s_ones,
                     r=xc[:, s * CH:(s + 1) * CH],
                     st=(s == 0), sp=(s == NS - 1):
                     te.matmul(o, l, r, start=st, stop=sp, skip_group_check=True),
                     sem="pmm" if s == NS - 1 else None)
            P.wait("vector", "pmm", P.count("pmm"))
            P.op("vector", lambda v, o=stat1, i_=ps_st[0:1, 0:CH]:
                 v.tensor_scalar(o, i_, -1.0 / D, None, ALU.mult), sem="pvec")
            for s in range(NS):
                P.op("vector", lambda v, o=lnf[:, s * CH:(s + 1) * CH],
                     i_=xc[:, s * CH:(s + 1) * CH]:
                     v.tensor_mul(o, i_, i_),
                     sem="pvec" if s == NS - 1 else None)
            c.prep_psum_free = P.count("pvec")
            yield
            P.wait("tensor", "pvec", P.count("pvec"))
            P.wait("tensor", "scl", P.count("scl"))
            for s in range(NS):
                P.op("tensor", lambda te, o=ps_st[0:1, CH:2 * CH], l=c.s_ones,
                     r=lnf[:, s * CH:(s + 1) * CH],
                     st=(s == 0), sp=(s == NS - 1):
                     te.matmul(o, l, r, start=st, stop=sp, skip_group_check=True),
                     sem="pmm" if s == NS - 1 else None)
            P.wait("vector", "pmm", P.count("pmm"))
            P.op("vector", lambda v, o=stat2, i_=stat1: v.tensor_mul(o, i_, i_),
                 sem="pvec")
            P.wait("vector", "pvec")
            P.op("vector", lambda v, o=stat2, i_=ps_st[0:1, CH:2 * CH], m=stat2:
                 v.scalar_tensor_tensor(o, i_, 1.0 / D, m, ALU.mult,
                                        ALU.subtract), sem="pvec")
            P.wait("scalar", "pvec", P.count("pvec"))
            P.op("scalar", lambda sc, o=stat2, i_=stat2, e=c.s_eps_row:
                 sc.activation(o, i_, AF.Sqrt, bias=e), sem="pscl")
            P.wait("vector", "pscl", P.count("pscl"))
            P.op("vector", lambda v, o=stat2, i_=stat2:
                 v.reciprocal(o, i_), sem="pvec")
            c.prep_psum_free = P.count("pvec")
            yield
            P.wait("tensor", "pvec", P.count("pvec"))
            P.wait("tensor", "scl", P.count("scl"))
            P.op("tensor", lambda te, o=ps_st[:, 0:CH], l=c.s_ones_row,
                 r=stat1: te.matmul(o, l, r, start=True, stop=True,
                                    skip_group_check=True))
            P.op("tensor", lambda te, o=ps_st[:, CH:2 * CH], l=c.s_ones_row,
                 r=stat2: te.matmul(o, l, r, start=True, stop=True,
                                    skip_group_check=True), sem="pmm")
            P.wait("vector", "pmm", P.count("pmm"))
            for s in range(NS):
                P.op("vector", lambda v, o=lnf[:, s * CH:(s + 1) * CH],
                     i_=xc[:, s * CH:(s + 1) * CH], m=ps_st[:, 0:CH]:
                     v.tensor_tensor(o, i_, m, ALU.add))
            for s in range(NS):
                P.op("vector", lambda v, o=lnf[:, s * CH:(s + 1) * CH],
                     i_=lnf[:, s * CH:(s + 1) * CH], m=ps_st[:, CH:2 * CH]:
                     v.tensor_tensor(o, i_, m, ALU.mult),
                     sem="pvec" if s == NS - 1 else None)
            for s in range(NS):
                P.op("vector",
                     lambda v, o=sl3(ln1t[gb])[:, s, ch * CH:(ch + 1) * CH],
                     i_=lnf[:, s * CH:(s + 1) * CH]:
                     v.tensor_copy(o, i_), sem="pvec" if s == NS - 1 else None)
            c.prep_psum_free = P.count("pvec")
            yield
            for tt in range(CH // 128):
                P.wait("tensor", "pvec", P.count("pvec"))
                P.wait("tensor", "scl", P.count("scl"))
                for s in range(NS):
                    P.op("tensor", lambda te, o=ps_lg,
                         l=lnf[:, s * CH + tt * 128:s * CH + tt * 128 + 128],
                         r=c.WG[:, s * NE:(s + 1) * NE],
                         st=(s == 0), sp=(s == NS - 1):
                         te.matmul(o, l, r, start=st, stop=sp,
                                   skip_group_check=True),
                         sem="pmm" if s == NS - 1 else None)
                P.wait("vector", "pmm", P.count("pmm"))
                P.op("vector", lambda v, o=lmax, i_=ps_lg:
                     v.reduce_max(o, i_, axis=AX.X, negate=True))
                P.op("vector", lambda v, o=logit, i_=ps_lg:
                     v.tensor_copy(o, i_), sem="pvec")
                P.wait("scalar", "pvec", P.count("pvec"))
                P.op("scalar", lambda sc, o=logit, i_=logit, b_=lmax, m=lsum:
                     sc.activation(o, i_, AF.Exp, bias=b_, accum_out=m),
                     sem="pscl")
                P.wait("vector", "pscl", P.count("pscl"))
                P.op("vector", lambda v, o=wsel: v.memset(o, 0.0), sem="pvec")
                for r_ in range(TOPK):
                    P.wait("vector", "pvec")
                    P.op("vector", lambda v, o=lmax, i_=logit:
                         v.reduce_max(o, i_, axis=AX.X), sem="pvec")
                    P.wait("vector", "pvec")
                    P.op("vector", lambda v, o=eq, i_=logit, m=lmax:
                         v.tensor_scalar(o, i_, m, None, ALU.is_equal),
                         sem="pvec")
                    P.wait("vector", "pvec")
                    P.op("vector", lambda v, o=idxm, i_=eq, m=c.s_iotam64:
                         v.tensor_tensor(o, i_, m, ALU.mult), sem="pvec")
                    P.wait("vector", "pvec")
                    P.op("vector", lambda v, o=idxm, i_=idxm:
                         v.tensor_scalar(o, i_, 64.0, None, ALU.add),
                         sem="pvec")
                    P.wait("vector", "pvec")
                    P.op("vector", lambda v, o=srow, i_=idxm:
                         v.tensor_reduce(o, i_, axis=AX.X, op=ALU.min),
                         sem="pvec")
                    P.wait("vector", "pvec")
                    P.op("vector", lambda v, o=eq, i_=c.s_iota32, m=srow:
                         v.tensor_scalar(o, i_, m, None, ALU.is_equal),
                         sem="pvec")
                    P.wait("vector", "pvec")
                    P.op("vector", lambda v, o=wsel, i_=eq, m=lmax, a=wsel:
                         v.scalar_tensor_tensor(o, i_, m, a, ALU.mult, ALU.add))
                    P.op("vector", lambda v, o=logit, i_=eq, a=logit:
                         v.scalar_tensor_tensor(o, i_, -2.0, a, ALU.mult,
                                                ALU.add), sem="pvec")
                P.wait("vector", "pvec")
                P.op("vector", lambda v, o=rrow, i_=lsum: v.reciprocal(o, i_),
                     sem="pvec")
                P.wait("vector", "pvec")
                P.op("vector", lambda v, o=wsel, i_=wsel, m=rrow:
                     v.tensor_scalar(o, i_, m, None, ALU.mult), sem="pvec")
                c.prep_psum_free = P.count("pvec")
                yield
                P.wait("tensor", "pvec", P.count("pvec"))
                P.wait("tensor", "scl", P.count("scl"))
                P.op("tensor", lambda te, o=ps_wt, i_=wsel, idn=c.s_ident_f:
                     te.transpose(o, i_, idn), sem="pmm")
                P.wait("vector", "pmm", P.count("pmm"))
                P.op("vector",
                     lambda v, o=wt32[gb][0:32, ch * CH + tt * 128:
                                          ch * CH + tt * 128 + 128],
                     i_=ps_wt: v.tensor_copy(o, i_), sem="pvec")
                c.prep_psum_free = P.count("pvec")
                yield
        ln1t_ready[g] = (P.count("pvec"), P.count("pmm"))

    def emit_waves(g, gb, gen):
        t0g = g * GT
        P.op("vector", lambda v, o=yacc: v.memset(o, 0.0), sem="vec")
        P.wait("sync", "mm", P.count("mm"))
        w13_done = [load_w13(0, 0), load_w13(1, 1)]
        w2_done = [load_w2(0, 0), load_w2(1, 1)]
        pool_done = {}
        w2mm_done = {}

        def h_phase(pw, ch, ei):
            gci = pw * NCH + ch
            slot = gci % 2
            mwslot = slot * 2 + ei
            pe = 2 * pw + ei
            ch0 = ch * CH
            if ch == 0 and ei == 0 and pw == 0:
                P.wait("tensor", "pvec", ln1t_ready[g][0])
            if ch == 0 and ei == 0:
                P.wait("vector", "mm", P.count("mm"))
                for e2 in range(2):
                    if 2 * pw + e2 < NE:
                        P.op("vector", lambda v, o=onehot_dyn[e2],
                             i_=c.s_iotap, val=float(2 * pw + e2):
                             v.tensor_single_scalar(o, i_, val, ALU.is_equal),
                             sem="vec" if e2 == 1 else None)
                if 2 * pw < NE:
                    P.wait("tensor", "vec", P.count("vec"))
                    wbs_th = {}
                    for e2 in range(2):
                        if 2 * pw + e2 >= NE:
                            continue
                        for th in range(2):
                            if e2 == 1:
                                P.wait("tensor", "wbs", wbs_th[th])
                            P.op("tensor",
                                 lambda te, o=ps_h3[:, th * 512:th * 512 + 512],
                                 l=onehot_dyn[e2],
                                 r=wt32[gb][0:32, th * 512:th * 512 + 512]:
                                 te.matmul(o, l, r, start=True, stop=True,
                                           skip_group_check=True), sem="mm")
                            P.wait("vector", "mm", P.count("mm"))
                            P.op("vector",
                                 lambda v,
                                 o=wbsb[e2][:, th * 512:th * 512 + 512],
                                 i_=ps_h3[:, th * 512:th * 512 + 512]:
                                 v.tensor_copy(o, i_), sem="wbs")
                            wbs_th[th] = P.count("wbs")
                    evst["bcast"] = P.count("wbs")
            for half in range(2):
                h0, h1c = 2 * half * CH, (2 * half + 2) * CH
                if ch == 0 and half == 0:
                    P.wait("tensor", "ld_a", w13_done[ei][0])
                    P.wait("tensor", "ld_a", w13_done[ei][1])
                P.wait("tensor", "scl", evst["silu"][half])
                if half == 1:
                    P.wait("tensor", "pvec", c.prep_psum_free)
                for hc in (2 * half, 2 * half + 1):
                    for s in range(NS):
                        P.op("tensor",
                             lambda te, o=ps_h1[:, hc * CH:(hc + 1) * CH],
                             l=wv[ei]["w1"][:, s * 512 + hc * 128:
                                            s * 512 + hc * 128 + 128],
                             r=ln1t[gb][:, s * GT + ch0:s * GT + ch0 + CH],
                             st=(s == 0), sp=(s == NS - 1):
                             te.matmul(o, l, r, start=st, stop=sp,
                                       skip_group_check=True),
                             sem="mm" if (hc == 2 * half + 1 and s == NS - 1)
                             else None)
                mmh1 = P.count("mm")
                P.wait("tensor", "vec", evst["h3w"][half])
                P.wait("tensor", "wbs", evst["bcast"])
                for hc in (2 * half, 2 * half + 1):
                    for s in range(NS):
                        P.op("tensor",
                             lambda te, o=ps_h3[:, hc * CH:(hc + 1) * CH],
                             l=wv[ei]["w3"][:, s * 512 + hc * 128:
                                            s * 512 + hc * 128 + 128],
                             r=ln1t[gb][:, s * GT + ch0:s * GT + ch0 + CH],
                             st=(s == 0), sp=(s == NS - 1):
                             te.matmul(o, l, r, start=st, stop=sp,
                                       skip_group_check=True),
                             sem="mm" if (hc == 2 * half + 1 and s == NS - 1)
                             else None)
                mmh3 = P.count("mm")
                P.wait("scalar", "mm", mmh1)
                P.wait("scalar", "pool", poolw[ei][half])
                P.op("scalar", lambda sc, o=silu1[ei][:, h0:h1c],
                     i_=ps_h1[:, h0:h1c]:
                     sc.activation(o, i_, AF.Silu), sem="scl")
                evst["silu"][half] = P.count("scl")
                P.wait("vector", "mm", mmh3)
                P.wait("vector", "pool", poolw[ei][half])
                if pe < NE:
                    P.op("vector",
                         lambda v,
                         o=sl3(h3w[ei], 4)[:, 2 * half:2 * half + 2, :],
                         i_=sl3(ps_h3, 4)[:, 2 * half:2 * half + 2, :],
                         m=wbsb[ei][:, ch0:ch0 + CH].unsqueeze(1)
                         .broadcast_to((128, 2, CH)):
                         v.tensor_tensor(o, i_, m, ALU.mult), sem="vec")
                else:
                    P.op("vector", lambda v, o=h3w[ei][:, h0:h1c],
                         i_=ps_h3[:, h0:h1c]:
                         v.tensor_copy(o, i_), sem="vec")
                evst["h3w"][half] = P.count("vec")
                P.wait("gpsimd", "scl", evst["silu"][half])
                P.wait("gpsimd", "vec", evst["h3w"][half])
                P.op("gpsimd", lambda gp, o=mw[mwslot][:, h0:h1c],
                     i_=silu1[ei][:, h0:h1c], m=h3w[ei][:, h0:h1c]:
                     gp.tensor_tensor(o, i_, m, ALU.mult), sem="pool")
                poolw[ei][half] = P.count("pool")
            if ei == 1:
                pool_done[gci] = P.count("pool")
            if ch == NCH - 1 and ei == 1 and 2 * pw + 2 < NPE:
                P.wait("sync", "mm", P.count("mm"))
                w13_done[:] = [load_w13(2 * pw + 2, 0),
                               load_w13(2 * pw + 3, 1)]

        def w2_batch(pw, ch):
            gci = pw * NCH + ch
            slot = gci % 2
            ch0 = ch * CH
            P.wait("tensor", "pool", pool_done[gci])
            P.wait("tensor", "ld_a", max(w2_done))
            P.wait("tensor", "ev", P.count("ev"))
            for dc in range(NS):
                for ei in range(2):
                    for s in range(4):
                        P.op("tensor",
                             lambda te, o=ps_y[:, dc * CH:(dc + 1) * CH],
                             l=wv[ei]["w2"][:, s * 1024 + dc * 128:
                                            s * 1024 + dc * 128 + 128],
                             r=mw[slot * 2 + ei][:, s * CH:(s + 1) * CH],
                             st=(ei == 0 and s == 0),
                             sp=(ei == 1 and s == 3):
                             te.matmul(o, l, r, start=st, stop=sp,
                                       skip_group_check=True),
                             sem="mm" if (dc == NS - 1 and ei == 1
                                          and s == 3) else None)
            w2mm_done[pw] = P.count("mm")
            P.wait("vector", "mm", P.count("mm"))
            if pw + 1 < NPAIR:
                P.op("vector", lambda v, o=sl3(yacc)[:, :, ch0:ch0 + CH],
                     i_=sl3(yacc)[:, :, ch0:ch0 + CH], p=sl3(ps_y, NS):
                     v.tensor_tensor(o, i_, p, ALU.add), sem="ev")
            else:
                P.wait("sync", "vec", P.count("vec"))
                P.wait("sync", "st_a", P.count("st_a"))
                xl = P.dma("sync", sl3(xchk2),
                           xT_r[:, :, t0g + ch0:t0g + ch0 + CH], "ld_b")
                P.op("vector", lambda v, o=sl3(yacc)[:, :, ch0:ch0 + CH],
                     i_=sl3(yacc)[:, :, ch0:ch0 + CH], p=sl3(ps_y, NS):
                     v.tensor_tensor(o, i_, p, ALU.add), sem="ev")
                P.wait("vector", "ld_b", xl)
                P.wait("vector", "ev", P.count("ev"))
                P.op("vector", lambda v, o=sl3(xchk2), i_=sl3(xchk2),
                     p=sl3(yacc)[:, :, ch0:ch0 + CH]:
                     v.tensor_tensor(o, i_, p, ALU.add), sem="vec")
                P.wait("sync", "vec", P.count("vec"))
                P.dma("sync", xT_r[:, :, t0g + ch0:t0g + ch0 + CH],
                      sl3(xchk2), "st_a")

        def run_w2(task):
            tpw, tch = task
            w2_batch(tpw, tch)
            if tch == NCH - 1 and 2 * tpw + 2 < NPE:
                P.wait("sync", "mm", w2mm_done[tpw])
                w2_done[:] = [load_w2(2 * tpw + 2, 0), load_w2(2 * tpw + 3, 1)]

        pending = []
        for pw in range(NPAIR):
            for ch in range(NCH):
                h_phase(pw, ch, 0)
                if gen is not None:
                    next(gen, None)
                h_phase(pw, ch, 1)
                if gen is not None:
                    next(gen, None)
                pending.append((pw, ch))
                if len(pending) >= 2:
                    run_w2(pending.pop(0))
        while pending:
            run_w2(pending.pop(0))
        if gen is not None:
            for _ in gen:
                pass

    INTERLEAVE = True
    for g in range(NG):
        gb = g % 2
        if g == 0:
            for _ in prep_gen(0, 0):
                pass
        gen = prep_gen(g + 1, (g + 1) % 2) if g + 1 < NG else None
        if not INTERLEAVE and gen is not None:
            for _ in gen:
                pass
            gen = None
        emit_waves(g, gb, gen)
    P.wait("sync", "st_a", P.count("st_a"))
    P.barrier()



def emit_attnb(c):
    P, t = c.P, c.t
    xT_r = dview(t["xT"])
    bT_r = dview(t["bT"])
    hT_r = dview(t["hT"])
    CA = 512
    CB = 256
    NCA = T // CA
    NCB = T // CB

    xa = carve(c.AR1, 41760, 128, NS * CA, F32)
    hca = carve(c.AR2, 32768, 128, NS * CA, F32)
    sqf = hca
    st1a = carve(c.AR2, 49152, 1, CA, F32)
    st2a = carve(c.AR2, 51200, 1, CA, F32)
    ps_a = c.psF[:, 3072:4096]

    b1a_store = {}

    def emit_b1a(ch):
        t0 = ch * CA
        P.wait("sync", "pvec", P.count("pvec"))
        P.wait("sync", "st_a", P.count("st_a"))
        xl = P.dma("sync", sl3(xa), xT_r[:, :, t0:t0 + CA], "ld_b")
        for round_ in range(2):
            src = xa
            P.wait("tensor", "ld_b", xl)
            P.wait("tensor", "pvec", P.count("pvec"))
            P.wait("tensor", "pscl", P.count("pscl"))
            for s in range(NS):
                P.op("tensor", lambda te, o=ps_a[0:1, 0:CA], l=c.s_ones,
                     r=src[:, s * CA:(s + 1) * CA], st=(s == 0), sp=(s == NS - 1):
                     te.matmul(o, l, r, start=st, stop=sp, skip_group_check=True),
                     sem="pmm" if s == NS - 1 else None)
            P.wait("vector", "pmm", P.count("pmm"))
            P.op("vector", lambda v, o=st1a, i_=ps_a[0:1, 0:CA]:
                 v.tensor_scalar(o, i_, -1.0 / D, None, ALU.mult), sem="pvec")
            for s in range(NS):
                P.op("vector", lambda v, o=sqf[:, s * CA:(s + 1) * CA],
                     i_=src[:, s * CA:(s + 1) * CA]:
                     v.tensor_mul(o, i_, i_), sem="pvec" if s == NS - 1 else None)
            P.wait("tensor", "pvec", P.count("pvec"))
            for s in range(NS):
                P.op("tensor", lambda te, o=ps_a[0:1, CA:2 * CA], l=c.s_ones,
                     r=sqf[:, s * CA:(s + 1) * CA], st=(s == 0), sp=(s == NS - 1):
                     te.matmul(o, l, r, start=st, stop=sp, skip_group_check=True),
                     sem="pmm" if s == NS - 1 else None)
            P.wait("vector", "pmm", P.count("pmm"))
            P.op("vector", lambda v, o=st2a, i_=st1a: v.tensor_mul(o, i_, i_),
                 sem="pvec")
            P.wait("vector", "pvec")
            P.op("vector", lambda v, o=st2a, i_=ps_a[0:1, CA:2 * CA], m=st2a:
                 v.scalar_tensor_tensor(o, i_, 1.0 / D, m, ALU.mult,
                                        ALU.subtract), sem="pvec")
            P.wait("scalar", "pvec", P.count("pvec"))
            P.op("scalar", lambda sc, o=st2a, i_=st2a, e=c.s_eps_row:
                 sc.activation(o, i_, AF.Sqrt, bias=e), sem="pscl")
            P.wait("vector", "pscl", P.count("pscl"))
            P.op("vector", lambda v, o=st2a, i_=st2a:
                 v.reciprocal(o, i_), sem="pvec")
            P.wait("tensor", "pvec", P.count("pvec"))
            P.op("tensor", lambda te, o=ps_a[:, 0:CA], l=c.s_ones_row,
                 r=st1a: te.matmul(o, l, r, start=True, stop=True,
                                   skip_group_check=True))
            P.op("tensor", lambda te, o=ps_a[:, CA:2 * CA], l=c.s_ones_row,
                 r=st2a: te.matmul(o, l, r, start=True, stop=True,
                                   skip_group_check=True), sem="pmm")
            P.wait("vector", "pmm", P.count("pmm"))
            dst = xa if round_ == 0 else hca
            if round_ == 1:
                P.wait("vector", "st_a", P.count("st_a"))
            for s in range(NS):
                P.op("vector", lambda v, o=dst[:, s * CA:(s + 1) * CA],
                     i_=src[:, s * CA:(s + 1) * CA], m=ps_a[:, 0:CA]:
                     v.tensor_tensor(o, i_, m, ALU.add))
            for s in range(NS):
                P.op("vector", lambda v, o=dst[:, s * CA:(s + 1) * CA],
                     i_=dst[:, s * CA:(s + 1) * CA], m=ps_a[:, CA:2 * CA]:
                     v.tensor_tensor(o, i_, m, ALU.mult),
                     sem="pvec" if s == NS - 1 else None)
            if round_ == 1:
                for s in range(NS):
                    P.op("vector", lambda v, o=hca[:, s * CA:(s + 1) * CA],
                         i_=hca[:, s * CA:(s + 1) * CA],
                         w_=c.s_vec["ln1_w"][:, s:s + 1],
                         b_=c.s_vec["ln1_b"][:, s:s + 1]:
                         v.tensor_scalar(o, i_, w_, b_, ALU.mult, ALU.add),
                         sem="pvec" if s == NS - 1 else None)
        P.wait("sync", "pvec", P.count("pvec"))
        P.dma("sync", bT_r[:, :, t0:t0 + CA], sl3(xa), "st_a")
        P.dma("sync", hT_r[:, :, t0:t0 + CA], sl3(hca), "st_a")
        b1a_store[ch] = P.count("st_a")

    wk_sb = carve(c.AR3, 0, 128, NS * 1024, BF16)
    wvv_sb = carve(c.AR3, 16384, 128, NS * 1024, BF16)
    wr_sb = carve(c.AR3, 32768, 128, NS * 1024, BF16)
    wo_sb = carve(c.AR3, 49152, 128, NS * 1024, BF16)
    for nm, sb in (("wkT_bf", wk_sb), ("wvT_bf", wvv_sb), ("wrT_bf", wr_sb),
                   ("woT_bf", wo_sb)):
        P.dma("sync", sl3(sb), dview(t[nm]), "ld_a")
    wld = P.count("ld_a")

    hcc0 = carve(c.AR1, 0, 128, NS * 257, F32)
    hcc = [hcc0, hcc0]
    kf = [carve(c.AR1, 8224, 128, NS * CB, F32),
          carve(c.AR1, 16416, 128, NS * CB, F32)]
    vf = [carve(c.AR1, 24608, 128, NS * CB, F32),
          carve(c.AR1, 32800, 128, NS * CB, F32)]
    srf = [carve(c.AR2, 0, 128, NS * CB, F32),
           carve(c.AR2, 8192, 128, NS * CB, F32)]
    xkb = [carve(c.AR2, 16384, 128, NS * CB, BF16),
           carve(c.AR2, 20480, 128, NS * CB, BF16)]
    xvb = [carve(c.AR2, 24576, 128, NS * CB, BF16),
           carve(c.AR2, 53248, 128, NS * CB, BF16)]
    xrb = [carve(c.AR2, 28672, 128, NS * CB, BF16),
           carve(c.AR2, 57344, 128, NS * CB, BF16)]
    e0 = carve(c.AR2, 61440, 128, CB, F32)
    e1 = carve(c.AR2, 62464, 128, CB, F32)
    ffb = carve(c.AR2, 63488, 128, CB, F32)
    yA = carve(c.AR4, 0, 128, CB + 1, F32)
    yB = carve(c.AR4, 1056, 128, CB + 1, F32)
    den = carve(c.AR4, 2112, 128, CB, F32)
    rwk = [carve(c.AR4, 3136, 128, NS * CB, BF16),
           carve(c.AR4, 7232, 128, NS * CB, BF16)]
    e0b2 = [carve(c.AR4, 11328, 128, CB, F32),
            carve(c.AR4, 12352, 128, CB, F32)]
    e1b2 = [carve(c.AR4, 13376, 128, CB, F32),
            carve(c.AR4, 14400, 128, CB, F32)]
    btc = carve(c.AR1, 58144, 128, 4 * CB, F32)
    ps_m = c.psF[:, 0:2048]
    ps_m2 = c.psF[:, 2048:3072]

    b1b_state = {"mix": {}, "k": {}, "v": {}, "r": {}, "wkv": {}}

    lamb = carve(c.AR2, 64512, 128, NS, F32)
    kru = carve(c.AR2, 64544, 128, NS, F32)
    Ah = carve(c.AR2, 64576, 128, NS, F32)
    Bh = carve(c.AR2, 64608, 128, NS, F32)
    Mh = carve(c.AR2, 64640, 128, NS, F32)
    mu8 = carve(c.AR2, 64672, 128, NS, F32)
    nmu8 = carve(c.AR2, 64704, 128, NS, F32)
    sc8 = carve(c.AR2, 64736, 128, NS, F32)
    tf8 = carve(c.AR2, 64768, 128, NS, F32)

    P.wait("scalar", "ld_a", wld)
    P.op("scalar", lambda sc, o=lamb, i_=c.s_vec["time_decay"]:
         sc.activation(o, i_, AF.Exp), sem="scl")
    P.wait("scalar", "scl", P.count("scl"))
    P.op("scalar", lambda sc, o=lamb, i_=lamb:
         sc.activation(o, i_, AF.Exp, scale=-1.0), sem="scl")
    P.op("vector", lambda v, o=kru, i_=c.s_vec["time_first"]:
         v.tensor_scalar(o, i_, 0.0, None, ALU.max))
    P.op("vector", lambda v, o=Ah: v.memset(o, 0.0))
    P.op("vector", lambda v, o=Bh: v.memset(o, 0.0))
    P.op("vector", lambda v, o=Mh: v.memset(o, -1e30), sem="vec")

    def b1b_mix(ch):
        b = ch % 2
        t0 = ch * CB
        P.wait("sync", "vec", b1b_state["mix"].get(ch - 1, 0))
        P.wait("sync", "st_a", b1a_store[(t0 + CB - 1) // CA])
        if ch == 0:
            P.op("vector", lambda v, o=hcc[0]: v.memset(o, 0.0), sem="vec")
            P.wait("sync", "vec", P.count("vec"))
            hl = P.dma("sync", sl3(hcc[0], NS)[:, :, 1:CB + 1],
                       hT_r[:, :, 0:CB], "ld_a")
        else:
            hl = P.dma("sync", sl3(hcc[b], NS), hT_r[:, :, t0 - 1:t0 + CB],
                       "ld_a")
        P.wait("vector", "ld_a", hl)
        P.wait("vector", "mm", b1b_state["r"].get(ch - 2, 0))
        W1 = CB + 1
        for nm, dst in (("att_tmk", xkb), ("att_tmv", xvb), ("att_tmr", xrb)):
            for s in range(NS):
                P.op("vector",
                     lambda v, o=ffb, h_=hcc[b][:, s * W1 + 1:s * W1 + W1],
                     hh=hcc[b][:, s * W1:s * W1 + CB]:
                     v.tensor_sub(o, h_, hh))
                P.op("vector",
                     lambda v, o=dst[b][:, s * CB:(s + 1) * CB], i_=ffb,
                     w_=c.s_vec[nm][:, s:s + 1],
                     hh=hcc[b][:, s * W1:s * W1 + CB]:
                     v.scalar_tensor_tensor(o, i_, w_, hh, ALU.mult, ALU.add),
                     sem="vec" if (nm == "att_tmr" and s == NS - 1) else None)
        b1b_state["mix"][ch] = P.count("vec")

    def b1b_mms(ch):
        b = ch % 2
        for which, src_b, wsb, dstf in (("k", xkb, wk_sb, kf),
                                        ("v", xvb, wvv_sb, vf),
                                        ("r", xrb, wr_sb, srf)):
            P.wait("tensor", "vec", b1b_state["mix"][ch])
            P.wait("tensor", "ev", P.count("ev"))
            P.wait("tensor", "vec", b1b_state["wkv"].get(ch - 2, 0))
            for oc in range(NS):
                for s in range(NS):
                    P.op("tensor",
                         lambda te, o=ps_m[:, oc * CB:oc * CB + CB],
                         l=wsb[:, s * 1024 + oc * 128:s * 1024 + oc * 128 + 128],
                         r=src_b[b][:, s * CB:(s + 1) * CB],
                         st=(s == 0), sp=(s == NS - 1):
                         te.matmul(o, l, r, start=st, stop=sp,
                                   skip_group_check=True),
                         sem="mm" if (oc == NS - 1 and s == NS - 1) else None)
            P.wait("scalar", "mm", P.count("mm"))
            if which == "r":
                P.op("scalar", lambda sc, o=srf[b], i_=ps_m:
                     sc.activation(o, i_, AF.Sigmoid), sem="ev")
            else:
                P.op("scalar", lambda sc, o=dstf[b], i_=ps_m:
                     sc.copy(o, i_), sem="ev")
            b1b_state[which][ch] = P.count("mm")
            P.wait("tensor", "ev", P.count("ev"))

    def b1b_wkv(ch):
        b = ch % 2
        P.wait("vector", "ev", P.count("ev"))
        for s in range(NS):
            P.op("vector", lambda v, o=mu8[:, s:s + 1],
                 i_=kf[b][:, s * CB:(s + 1) * CB]:
                 v.reduce_max(o, i_, axis=AX.X),
                 sem="wbv" if s == NS - 1 else None)
        P.wait("vector", "wbv")
        P.op("vector", lambda v, o=mu8, i_=mu8, m=kru:
             v.tensor_tensor(o, i_, m, ALU.add))
        P.op("vector", lambda v, o=sc8, i_=Mh:
             v.tensor_scalar(o, i_, -CAP_DROP, None, ALU.add), sem="wbv")
        P.wait("vector", "wbv")
        P.op("vector", lambda v, o=mu8, i_=mu8, m=sc8:
             v.tensor_tensor(o, i_, m, ALU.max), sem="wbv")
        P.wait("vector", "wbv")
        P.op("vector", lambda v, o=nmu8, i_=mu8:
             v.tensor_scalar(o, i_, -1.0, None, ALU.mult))
        P.op("vector", lambda v, o=sc8, i_=Mh, m=mu8:
             v.tensor_tensor(o, i_, m, ALU.subtract))
        P.op("vector", lambda v, o=tf8, i_=c.s_vec["time_first"][:, 0:NS],
             m=mu8: v.tensor_tensor(o, i_, m, ALU.subtract), sem="vec")
        P.wait("scalar", "vec", P.count("vec"))
        P.op("scalar", lambda sc, o=sc8, i_=sc8:
             sc.activation(o, i_, AF.Exp), sem="scl")
        P.wait("vector", "scl", P.count("scl"))
        P.op("vector", lambda v, o=Ah, i_=Ah, m=sc8:
             v.tensor_tensor(o, i_, m, ALU.mult))
        P.op("vector", lambda v, o=Bh, i_=Bh, m=sc8:
             v.tensor_tensor(o, i_, m, ALU.mult))
        P.op("vector", lambda v, o=Mh, i_=mu8:
             v.tensor_copy(o, i_), sem="wbv")
        P.wait("vector", "wbv")
        ev_slab = {}
        for s in range(NS):
            eb = s % 2
            kfs = kf[b][:, s * CB:(s + 1) * CB]
            vfs = vf[b][:, s * CB:(s + 1) * CB]
            if s >= 2:
                P.wait("scalar", "vec", ev_slab[s - 2])
            P.op("scalar", lambda sc, o=e0b2[eb], i_=kfs, b_=nmu8[:, s:s + 1]:
                 sc.activation(o, i_, AF.Exp, bias=b_))
            P.op("scalar", lambda sc, o=e1b2[eb], i_=kfs, b_=tf8[:, s:s + 1]:
                 sc.activation(o, i_, AF.Exp, bias=b_), sem="scl")
            P.wait("vector", "scl", P.count("scl"))
            P.op("vector", lambda v, o=ffb, i_=e0b2[eb], m=vfs:
                 v.tensor_tensor(o, i_, m, ALU.mult))
            P.op("vector", lambda v, o=yA[:, 0:1], i_=Ah[:, s:s + 1]:
                 v.tensor_copy(o, i_))
            P.op("vector", lambda v, o=yB[:, 0:1], i_=Bh[:, s:s + 1]:
                 v.tensor_copy(o, i_), sem="wbv")
            P.wait("vector", "wbv")
            P.op("vector", lambda v, o=yA[:, 1:CB + 1],
                 d0=lamb[:, s:s + 1].broadcast_to((128, CB)), d1=ffb,
                 init=Ah[:, s:s + 1]:
                 v.tensor_tensor_scan(o, d0, d1, init, ALU.mult, ALU.add))
            P.op("vector", lambda v, o=yB[:, 1:CB + 1],
                 d0=lamb[:, s:s + 1].broadcast_to((128, CB)), d1=e0b2[eb],
                 init=Bh[:, s:s + 1]:
                 v.tensor_tensor_scan(o, d0, d1, init, ALU.mult, ALU.add),
                 sem="vec")
            P.wait("vector", "vec")
            P.op("vector", lambda v, o=den, i_=yB[:, 0:CB], m=e1b2[eb]:
                 v.tensor_tensor(o, i_, m, ALU.add))
            P.op("vector", lambda v, o=den, i_=den: v.reciprocal(o, i_))
            P.op("vector", lambda v, o=e1b2[eb], i_=e1b2[eb], m=vfs:
                 v.tensor_tensor(o, i_, m, ALU.mult))
            P.op("vector", lambda v, o=e1b2[eb], i_=e1b2[eb], m=yA[:, 0:CB]:
                 v.tensor_tensor(o, i_, m, ALU.add), sem="wbv")
            P.wait("vector", "wbv")
            P.op("vector", lambda v, o=e1b2[eb], i_=e1b2[eb], m=den:
                 v.tensor_tensor(o, i_, m, ALU.mult))
            P.op("vector", lambda v, o=rwk[b][:, s * CB:(s + 1) * CB],
                 i_=e1b2[eb], m=srf[b][:, s * CB:(s + 1) * CB]:
                 v.tensor_tensor(o, i_, m, ALU.mult))
            P.op("vector", lambda v, o=Ah[:, s:s + 1], i_=yA[:, CB:CB + 1]:
                 v.tensor_copy(o, i_))
            P.op("vector", lambda v, o=Bh[:, s:s + 1], i_=yB[:, CB:CB + 1]:
                 v.tensor_copy(o, i_), sem="vec")
            ev_slab[s] = P.count("vec")
        b1b_state["wkv"][ch] = P.count("vec")

    wo_ev = {}

    def b1b_wo(ch):
        b = ch % 2
        t0 = ch * CB
        P.wait("tensor", "vec", b1b_state["wkv"][ch])
        for hh2 in range(2):
            P.wait("tensor", "ev",
                   wo_ev.get((ch, 0), 0) if hh2 == 1
                   else wo_ev.get((ch - 1, 1), 0))
            for oc4 in range(4):
                oc = 4 * hh2 + oc4
                for s in range(NS):
                    P.op("tensor",
                         lambda te, o=ps_m2[:, oc4 * CB:oc4 * CB + CB],
                         l=wo_sb[:, s * 1024 + oc * 128:s * 1024 + oc * 128 + 128],
                         r=rwk[b][:, s * CB:(s + 1) * CB],
                         st=(s == 0), sp=(s == NS - 1):
                         te.matmul(o, l, r, start=st, stop=sp,
                                   skip_group_check=True),
                         sem="mm" if (oc4 == 3 and s == NS - 1) else None)
            mmw = P.count("mm")
            P.wait("sync", "ev", P.count("ev"))
            bl = P.dma("sync", sl3(btc, 4),
                       bT_r[:, hh2 * 4:(hh2 + 1) * 4, t0:t0 + CB], "ld_b")
            P.wait("vector", "mm", mmw)
            P.wait("vector", "ld_b", bl)
            P.op("vector", lambda v, o=btc, i_=btc, p=ps_m2:
                 v.tensor_tensor(o, i_, p, ALU.add), sem="ev")
            wo_ev[(ch, hh2)] = P.count("ev")
            P.wait("sync", "ev", P.count("ev"))
            P.dma("sync", bT_r[:, hh2 * 4:(hh2 + 1) * 4, t0:t0 + CB],
                  sl3(btc, 4), "st_a")

    emit_b1a(0)
    for ch in range(NCB):
        if ch % 2 == 0 and ch // 2 + 1 < NCA:
            emit_b1a(ch // 2 + 1)
        b1b_mix(ch)
        b1b_mms(ch)
        b1b_wkv(ch)
        if ch > 0:
            b1b_wo(ch - 1)
    b1b_wo(NCB - 1)
    P.barrier()


    CH2 = 256
    NCH2 = T // CH2
    wkey_sb = carve(c.AR3, 0, 128, NS * FFN, BF16)
    wval_sb = carve(c.AR1, 0, 128, 32 * 1024, BF16)
    wrec_sb = carve(c.AR4, 0, 128, NS * 1024, BF16)
    P.dma("sync", sl3(wkey_sb), dview(t["wkeyT_bf"]), "ld_a")
    wkey_n = P.count("ld_a")
    P.dma("sync", sl3(wval_sb, s=32),
          t["wvalT_bf"].rearrange("(s p) q -> p s q", p=128), "ld_a")
    wval_n = P.count("ld_a")
    P.dma("sync", sl3(wrec_sb), dview(t["wrecT_bf"]), "ld_a")
    wrec_n = P.count("ld_a")

    bcc = carve(c.AR2, 0, 128, NS * 257, F32)
    h2c = carve(c.AR2, 8224, 128, NS * 257, F32)
    sq2 = carve(c.AR2, 16448, 128, NS * CH2, F32)
    xk2 = carve(c.AR2, 24640, 128, NS * CH2, BF16)
    xr2 = carve(c.AR2, 28736, 128, NS * CH2, BF16)
    kkr = carve(c.AR2, 32832, 128, CH2, F32)
    kkr_b = carve(c.AR2, 58432, 128, CH2, F32)
    kk2 = carve(c.AR2, 33856, 128, 32 * CH2, BF16)
    rf = carve(c.AR2, 50240, 128, NS * CH2, F32)
    st1b = carve(c.AR2, 62528, 1, 257, F32)
    st2b = carve(c.AR2, 63680, 1, 257, F32)
    ps_kk = [c.psF[:, 0:256], c.psF[:, 512:768]]
    ps_kv = c.psF[:, 2048:4096]
    ps_rr = c.psF[:, 1024:2048]
    ps_stb = c.psF[:, 0:1024]

    for ch in range(NCH2):
        t0 = ch * CH2
        P.wait("sync", "vec", P.count("vec"))
        if ch == 0:
            P.op("vector", lambda v, o=bcc: v.memset(o, 0.0), sem="vec")
            P.wait("sync", "vec", P.count("vec"))
            bl = P.dma("sync", sl3(bcc, NS)[:, :, 1:257], bT_r[:, :, 0:CH2],
                       "ld_a")
        else:
            bl = P.dma("sync", sl3(bcc, NS), bT_r[:, :, t0 - 1:t0 + CH2], "ld_a")
        P.wait("tensor", "ld_a", bl)
        P.wait("tensor", "vec", P.count("vec"))
        P.wait("tensor", "scl", P.count("scl"))
        W = 257
        for s in range(NS):
            P.op("tensor", lambda te, o=ps_stb[0:1, 0:W], l=c.s_ones,
                 r=bcc[:, s * 257:(s + 1) * 257], st=(s == 0), sp=(s == NS - 1):
                 te.matmul(o, l, r, start=st, stop=sp, skip_group_check=True),
                 sem="mm" if s == NS - 1 else None)
        P.wait("vector", "mm", P.count("mm"))
        P.op("vector", lambda v, o=st1b, i_=ps_stb[0:1, 0:W]:
             v.tensor_scalar(o, i_, -1.0 / D, None, ALU.mult), sem="vec")
        for s in range(NS):
            P.op("vector", lambda v, o=h2c[:, s * 257:(s + 1) * 257],
                 i_=bcc[:, s * 257:(s + 1) * 257]:
                 v.tensor_mul(o, i_, i_), sem="vec" if s == NS - 1 else None)
        P.wait("tensor", "vec", P.count("vec"))
        for s in range(NS):
            P.op("tensor", lambda te, o=ps_stb[0:1, 512:512 + W], l=c.s_ones,
                 r=h2c[:, s * 257:(s + 1) * 257], st=(s == 0), sp=(s == NS - 1):
                 te.matmul(o, l, r, start=st, stop=sp, skip_group_check=True),
                 sem="mm" if s == NS - 1 else None)
        P.wait("vector", "mm", P.count("mm"))
        P.op("vector", lambda v, o=st2b, i_=st1b: v.tensor_mul(o, i_, i_),
             sem="wbv")
        P.wait("vector", "wbv")
        P.op("vector", lambda v, o=st2b, i_=ps_stb[0:1, 512:512 + W], m=st2b:
             v.scalar_tensor_tensor(o, i_, 1.0 / D, m, ALU.mult, ALU.subtract),
             sem="vec")
        P.wait("scalar", "vec", P.count("vec"))
        P.op("scalar", lambda sc, o=st2b, i_=st2b, e=c.s_eps_row:
             sc.activation(o, i_, AF.Sqrt, bias=e), sem="scl")
        P.wait("vector", "scl", P.count("scl"))
        P.op("vector", lambda v, o=st2b, i_=st2b: v.reciprocal(o, i_), sem="vec")
        P.wait("tensor", "vec", P.count("vec"))
        P.op("tensor", lambda te, o=ps_stb[:, 0:W].opt(), l=c.s_ones_row,
             r=st1b: te.matmul(o, l, r, start=True, stop=True,
                               skip_group_check=True))
        P.op("tensor", lambda te, o=ps_stb[:, 512:512 + W], l=c.s_ones_row,
             r=st2b: te.matmul(o, l, r, start=True, stop=True,
                               skip_group_check=True), sem="mm")
        P.wait("vector", "mm", P.count("mm"))
        for s in range(NS):
            P.op("vector", lambda v, o=h2c[:, s * 257:(s + 1) * 257],
                 i_=bcc[:, s * 257:(s + 1) * 257], m=ps_stb[:, 0:W]:
                 v.tensor_tensor(o, i_, m, ALU.add))
        for s in range(NS):
            P.op("vector", lambda v, o=h2c[:, s * 257:(s + 1) * 257],
                 i_=h2c[:, s * 257:(s + 1) * 257], m=ps_stb[:, 512:512 + W]:
                 v.tensor_tensor(o, i_, m, ALU.mult))
        for s in range(NS):
            P.op("vector", lambda v, o=h2c[:, s * 257:(s + 1) * 257],
                 i_=h2c[:, s * 257:(s + 1) * 257],
                 w_=c.s_vec["ln2_w"][:, s:s + 1], b_=c.s_vec["ln2_b"][:, s:s + 1]:
                 v.tensor_scalar(o, i_, w_, b_, ALU.mult, ALU.add),
                 sem="vec" if s == NS - 1 else None)
        if ch == 0:
            P.op("vector", lambda v, o=sl3(h2c, NS)[:, :, 0:1]:
                 v.memset(o, 0.0), sem="wbv")
            P.wait("vector", "wbv")
        for nm, dst in (("ffn_tmk", xk2), ("ffn_tmr", xr2)):
            for s in range(NS):
                P.op("vector",
                     lambda v, o=kkr, h_=h2c[:, s * 257 + 1:s * 257 + 257],
                     hh=h2c[:, s * 257:s * 257 + 256]:
                     v.tensor_sub(o, h_, hh))
                P.op("vector",
                     lambda v, o=dst[:, s * CH2:(s + 1) * CH2], i_=kkr,
                     w_=c.s_vec[nm][:, s:s + 1],
                     hh=h2c[:, s * 257:s * 257 + 256]:
                     v.scalar_tensor_tensor(o, i_, w_, hh, ALU.mult, ALU.add),
                     sem="vec" if (nm == "ffn_tmr" and s == NS - 1) else None)
        mixn = P.count("vec")
        P.wait("tensor", "vec", mixn)
        if ch == 0:
            P.wait("tensor", "ld_a", wkey_n)
        ev_after = {}
        kkr2 = [kkr, kkr_b]
        for oc in range(32):
            pb = oc % 2
            if oc >= 2:
                P.wait("tensor", "ev", ev_after[oc - 2])
            for s in range(NS):
                P.op("tensor", lambda te, o=ps_kk[pb],
                     l=wkey_sb[:, s * FFN + oc * 128:s * FFN + oc * 128 + 128],
                     r=xk2[:, s * CH2:(s + 1) * CH2],
                     st=(s == 0), sp=(s == NS - 1):
                     te.matmul(o, l, r, start=st, stop=sp,
                               skip_group_check=True),
                     sem="mm" if s == NS - 1 else None)
            P.wait("scalar", "mm", P.count("mm"))
            if oc >= 2:
                P.wait("scalar", "ev", ev_after[oc - 2])
            P.op("scalar", lambda sc, o=kkr2[pb], i_=ps_kk[pb]:
                 sc.activation(o, i_, AF.Relu), sem="scl")
            P.wait("vector", "scl", P.count("scl"))
            P.op("vector", lambda v, o=kk2[:, oc * CH2:(oc + 1) * CH2],
                 i_=kkr2[pb], m=kkr2[pb]:
                 v.tensor_tensor(o, i_, m, ALU.mult), sem="ev")
            ev_after[oc] = P.count("ev")
        P.wait("tensor", "ev", P.count("ev"))
        if ch == 0:
            P.wait("tensor", "ld_a", wval_n)
        for oc in range(NS):
            for s in range(32):
                P.op("tensor", lambda te, o=ps_kv[:, oc * 256:oc * 256 + 256],
                     l=wval_sb[:, s * 1024 + oc * 128:s * 1024 + oc * 128 + 128],
                     r=kk2[:, s * CH2:(s + 1) * CH2],
                     st=(s == 0), sp=(s == 31):
                     te.matmul(o, l, r, start=st, stop=sp,
                               skip_group_check=True),
                     sem="mm" if (oc == NS - 1 and s == 31) else None)
        rr_slots = [c.psF[:, 1024:1280], c.psF[:, 1536:1792]]
        if ch == 0:
            P.wait("tensor", "ld_a", wrec_n)
        scl_after_r = {}
        for oc in range(NS):
            pb = oc % 2
            if oc >= 2:
                P.wait("tensor", "scl", scl_after_r[oc - 2])
            for s in range(NS):
                P.op("tensor", lambda te, o=rr_slots[pb],
                     l=wrec_sb[:, s * 1024 + oc * 128:s * 1024 + oc * 128 + 128],
                     r=xr2[:, s * CH2:(s + 1) * CH2],
                     st=(s == 0), sp=(s == NS - 1):
                     te.matmul(o, l, r, start=st, stop=sp,
                               skip_group_check=True),
                     sem="mm" if s == NS - 1 else None)
            P.wait("scalar", "mm", P.count("mm"))
            P.op("scalar", lambda sc, o=rf[:, oc * CH2:(oc + 1) * CH2],
                 i_=rr_slots[pb]:
                 sc.activation(o, i_, AF.Sigmoid), sem="scl")
            scl_after_r[oc] = P.count("scl")
        P.wait("tensor", "scl", P.count("scl"))
        P.wait("sync", "scl", P.count("scl"))
        P.wait("sync", "st_a", P.count("st_a"))
        xl = P.dma("sync", sl3(sq2), xT_r[:, :, t0:t0 + CH2], "ld_b")
        P.wait("vector", "mm", P.count("mm"))
        P.wait("vector", "scl", P.count("scl"))
        P.wait("vector", "ld_b", xl)
        P.op("vector", lambda v, o=rf, i_=rf, p=ps_kv:
             v.tensor_tensor(o, i_, p, ALU.mult), sem="ev")
        P.op("vector", lambda v, o=sq2, i_=sq2,
             p=sl3(bcc, NS)[:, :, 1:257].rearrange("p s c -> p (s c)")
             if False else sq2: v.tensor_copy(o, i_)) if False else None
        for s in range(NS):
            P.op("vector", lambda v, o=sq2[:, s * CH2:(s + 1) * CH2],
                 i_=sq2[:, s * CH2:(s + 1) * CH2],
                 m=bcc[:, s * 257 + 1:s * 257 + 257]:
                 v.tensor_tensor(o, i_, m, ALU.add))
        P.op("vector", lambda v, o=sq2, i_=sq2, m=rf:
             v.tensor_tensor(o, i_, m, ALU.add), sem="vec")
        P.wait("sync", "vec", P.count("vec"))
        P.dma("sync", xT_r[:, :, t0:t0 + CH2], sl3(sq2), "st_a")
        P.wait("tensor", "vec", P.count("vec"))
    P.barrier()



def emit_final(c):
    P, t = c.P, c.t
    xT_r = dview(t["xT"])
    xst = [carve(c.AR1, 0, 128, 1024, F32), carve(c.AR1, 4096, 128, 1024, F32)]
    ost = [carve(c.AR1, 8192, 128, 1024, F32), carve(c.AR1, 12288, 128, 1024, F32)]
    ps32 = c.psF[:, 0:1024]
    for ti in range(T // 128):
        b = ti % 2
        if ti >= 2:
            P.wait("sync", "mm", P.count("mm"))
        xl = P.dma("sync", sl3(xst[b]), xT_r[:, :, ti * 128:(ti + 1) * 128], "ld_a")
        P.wait("tensor", "ld_a", xl)
        P.wait("tensor", "scl", P.count("scl"))
        for s in range(NS):
            P.op("tensor", lambda te, o=ps32[:, s * 128:(s + 1) * 128],
                 i_=xst[b][:, s * 128:(s + 1) * 128], idn=c.s_ident_f:
                 te.transpose(o, i_, idn), sem="mm" if s == NS - 1 else None)
        P.wait("scalar", "mm", P.count("mm"))
        P.wait("scalar", "st_a", P.count("st_a"))
        evn = P.op("scalar", lambda sc, o=ost[b], i_=ps32: sc.copy(o, i_),
                   sem="scl")
        P.wait("sync", "scl", evn)
        P.dma("sync", t["out"][ti * 128:(ti + 1) * 128, :], ost[b], "st_a")
    P.wait("sync", "st_a", P.count("st_a"))



def build_program(phases=ALL_PHASES, debug_names=(), debug_sbuf=()):
    nc = bass.Bass("TRN2", target_bir_lowering=False, debug=False,
                   num_devices=NCORES, dynamic_dma_scratch_size=8192)

    t = {}
    t["x_in"] = nc.dram_tensor("x", [T, D], F32, kind="ExternalInput").ap()
    t["wq_in"] = nc.dram_tensor("wq", [D, D], F32, kind="ExternalInput").ap()
    t["wk_a_in"] = nc.dram_tensor("wk_a", [D, D], F32, kind="ExternalInput").ap()
    t["wv_a_in"] = nc.dram_tensor("wv_a", [D, D], F32, kind="ExternalInput").ap()
    VEC_NAMES = ["ln1_w", "ln1_b", "ln2_w", "ln2_b", "att_tmk", "att_tmv",
                 "att_tmr", "time_decay", "time_first", "ffn_tmk", "ffn_tmr"]
    for nm in VEC_NAMES:
        t[nm + "_in"] = nc.dram_tensor(nm, [128, NS], F32, kind="ExternalInput").ap()
    for nm, shape in [("wkb", [D, D]), ("wvb", [D, D]), ("wrb", [D, D]),
                      ("wob", [D, D]), ("wkey", [FFN, D]), ("wrec", [D, D]),
                      ("wval", [D, FFN]), ("wg", [D, NE]),
                      ("w1", [NE, D, EH]), ("w2", [NE, EH, D]), ("w3", [NE, D, EH]),
                      ("ws1", [D, D]), ("ws2", [D, D]), ("ws3", [D, D])]:
        t[nm + "_in"] = nc.dram_tensor(nm, shape, F32, kind="ExternalInput").ap()
    t["ident_f_in"] = nc.dram_tensor("ident_f", [128, 128], F32, kind="ExternalInput").ap()
    t["ident_b_in"] = nc.dram_tensor("ident_b", [128, 128], BF16, kind="ExternalInput").ap()
    t["ones_in"] = nc.dram_tensor("ones", [128, 128], F32, kind="ExternalInput").ap()
    t["causal_in"] = nc.dram_tensor("causal", [128, 128], F32, kind="ExternalInput").ap()
    t["onehot_in"] = nc.dram_tensor("onehot", [NE, NE * 128], F32, kind="ExternalInput").ap()
    t["iota32_in"] = nc.dram_tensor("iota32", [128, 2 * NE], F32, kind="ExternalInput").ap()
    t["iotap_in"] = nc.dram_tensor("iotap", [32, 128], F32, kind="ExternalInput").ap()
    t["eps_in"] = nc.dram_tensor("epsv", [128, 2], F32, kind="ExternalInput").ap()
    t["out"] = nc.dram_tensor("out", [T, D], F32, kind="ExternalOutput").ap()

    t["xT"] = nc.dram_tensor("xT", [D, T], F32, kind="Internal").ap()
    t["bT"] = nc.dram_tensor("bT", [D, T], F32, kind="Internal").ap()
    t["hT"] = nc.dram_tensor("hT", [D, T], F32, kind="Internal").ap()
    t["qT"] = nc.dram_tensor("qT", [D, T], BF16, kind="Internal").ap()
    for nm, shape in [("wqpm", [128, NS * D]), ("wkapm", [128, NS * D]),
                      ("wvapm", [128, NS * D]),
                      ("wkT_bf", [D, D]), ("wvT_bf", [D, D]), ("wrT_bf", [D, D]),
                      ("woT_bf", [D, D]), ("wrecT_bf", [D, D]),
                      ("wkeyT_bf", [D, FFN]), ("wvalT_bf", [FFN, D]),
                      ("w1pm", [NE, 128, NS * EH]), ("w2pm", [NE, 128, 4 * D]),
                      ("w3pm", [NE, 128, NS * EH]),
                      ("ws1pm", [2, 128, NS * EH]), ("ws2pm", [2, 128, 4 * D]),
                      ("ws3pm", [2, 128, NS * EH])]:
        t[nm] = nc.dram_tensor(nm, shape, BF16, kind="Internal").ap()

    sem_names = ["bar", "ld_a", "ld_b", "st_a", "st_b", "mm", "ev", "vec",
                 "scl", "pool", "wbv", "wbs", "gpc", "pmm", "pvec", "pscl"]

    with contextlib.ExitStack() as stack:
        AR1h = stack.enter_context(nc.sbuf_tensor("AR1", [128, 32768], BF16))
        AR2h = stack.enter_context(nc.sbuf_tensor("AR2", [128, 32768], BF16))
        AR3h = stack.enter_context(nc.sbuf_tensor("AR3", [128, 32768], BF16))
        AR4h = stack.enter_context(nc.sbuf_tensor("AR4", [128, 8192], BF16))
        identfh = stack.enter_context(nc.sbuf_tensor("sb_idf", [128, 128], F32))
        identbh = stack.enter_context(nc.sbuf_tensor("sb_idb", [128, 128], BF16))
        onesh = stack.enter_context(nc.sbuf_tensor("sb_ones", [128, 128], F32))
        causalh = stack.enter_context(nc.sbuf_tensor("sb_causal", [128, 128], F32))
        iotah = stack.enter_context(nc.sbuf_tensor("sb_iota", [128, 2 * NE], F32))
        iotaph = stack.enter_context(nc.sbuf_tensor("sb_iotap", [32, 128], F32))
        epsh = stack.enter_context(nc.sbuf_tensor("sb_eps", [128, 2], F32))
        wgh = stack.enter_context(nc.sbuf_tensor("sb_wg", [128, NS * NE], F32))
        vech = stack.enter_context(nc.sbuf_tensor("sb_vecs", [128, 11 * NS], F32))
        psFh = stack.enter_context(nc.psum_tensor("psF", [128, 4096], F32))

        blk = stack.enter_context(nc.Block())
        sems = {nm: stack.enter_context(nc.semaphore(nm)) for nm in sem_names}

        P = Prog(nc)
        for nm in sem_names:
            P.register_sem(nm, sems[nm])

        c = Ctx()
        c.P, c.t, c.nc = P, t, nc
        c.AR1, c.AR2, c.AR3, c.AR4 = AR1h[:], AR2h[:], AR3h[:], AR4h[:]
        c.LN0T = AR2h[:]
        c.KT = AR3h[:]
        c.V = AR1h[:]
        c.s_ident_f, c.s_ident_bf = identfh[:], identbh[:]
        c.s_ones = onesh[:, 0:1]
        c.s_ones_row = onesh[0:1, :]
        c.s_causal = causalh[:]
        c.s_iota32 = iotah[:, 0:NE]
        c.s_iotap = iotaph[:]
        c.s_iotam64 = iotah[:, NE:2 * NE]
        c.s_eps = epsh[:, 0:1]
        c.s_eps_row = epsh[0:1, 1:2]
        c.WG = wgh[:]
        c.psF = psFh[:]
        c.s_vec = {nm: vech[:, i * NS:(i + 1) * NS]
                   for i, nm in enumerate(VEC_NAMES)}

        P.dma("sync", c.s_ident_f, t["ident_f_in"], "ld_a")
        P.dma("sync", c.s_ident_bf, t["ident_b_in"], "ld_a")
        P.dma("sync", onesh[:], t["ones_in"], "ld_a")
        P.dma("sync", c.s_causal, t["causal_in"], "ld_a")
        P.dma("sync", iotah[:], t["iota32_in"], "ld_a")
        P.dma("sync", iotaph[:], t["iotap_in"], "ld_a")
        P.dma("sync", epsh[:], t["eps_in"], "ld_a")
        for s in range(NS):
            P.dma("sync", wgh[:, s * NE:(s + 1) * NE],
                  t["wg_in"][s * 128:(s + 1) * 128, :], "ld_a")
        for nm in VEC_NAMES:
            P.dma("sync", c.s_vec[nm], t[nm + "_in"], "ld_a")
        P.wait("sync", "ld_a", P.count("ld_a"))
        P.barrier()

        if "prep" in phases:
            emit_prep(c)
        if "a1" in phases:
            emit_a1(c)
        if "a2" in phases:
            emit_a2(c)
        if "a3" in phases:
            emit_a3(c)
        if "moe1" in phases:
            emit_moe(c)
        if "attnb" in phases:
            emit_attnb(c)
        if "moe2" in phases:
            emit_moe(c)
        if "final" in phases:
            emit_final(c)
        else:
            for ti in range(0, T, 128):
                sb = carve(c.AR4, 0, 128, 1024, F32)
                P.wait("sync", "st_b", P.count("st_b"))
                P.dma("sync", sb, t["x_in"][ti:ti + 128, :], "ld_b")
                P.wait("gpsimd", "ld_b", P.count("ld_b"))
                P.dma("gpsimd", t["out"][ti:ti + 128, :], sb, "st_b")
            P.wait("sync", "st_b", P.count("st_b"))

        if debug_names:
            P.barrier()
            for nm in debug_names:
                src = t[nm]
                dbg = nc.dram_tensor("dbg_" + nm, list(src.shape), src.dtype,
                                     kind="ExternalOutput").ap()
                P.dma("gpsimd", dbg, src, "st_b")
            P.wait("gpsimd", "st_b", P.count("st_b"))
        if debug_sbuf:
            P.barrier()
            for nm, ap in debug_sbuf:
                dbg = nc.dram_tensor("dbg_" + nm, list(ap(c).shape), ap(c).dtype,
                                     kind="ExternalOutput").ap()
                P.dma("gpsimd", dbg, ap(c), "st_b")
            P.wait("gpsimd", "st_b", P.count("st_b"))

        for name, eng_sel in (("sync", blk.sync), ("vector", blk.vector),
                              ("scalar", blk.scalar), ("tensor", blk.tensor),
                              ("gpsimd", blk.gpsimd)):
            fns = P.plan[name]
            eng_sel(lambda e, fns=fns: [f(e) for f in fns])

    return nc


def make_inputs(inputs):
    x = np.ascontiguousarray(inputs["L0_attn_a_embeddings"], dtype=np.float32)

    def slabvec(v):
        return np.ascontiguousarray(np.asarray(v, np.float32).reshape(NS, 128).T)

    base = {
        "wq": inputs["L0_attn_a_W_Q"], "wk_a": inputs["L0_attn_a_W_K"],
        "wv_a": inputs["L0_attn_a_W_V"],
        "wkb": inputs["L0_attn_b_Wk"], "wvb": inputs["L0_attn_b_Wv"],
        "wrb": inputs["L0_attn_b_Wr"], "wob": inputs["L0_attn_b_Wo"],
        "wkey": inputs["L0_attn_b_Wkey"], "wrec": inputs["L0_attn_b_Wrec"],
        "wval": inputs["L0_attn_b_Wval"],
        "wg": inputs["L0_ffn_Wg"], "w1": inputs["L0_ffn_W1"],
        "w2": inputs["L0_ffn_W2"], "w3": inputs["L0_ffn_W3"],
        "ws1": inputs["L0_ffn_Ws1"], "ws2": inputs["L0_ffn_Ws2"],
        "ws3": inputs["L0_ffn_Ws3"],
    }
    base = {k: np.ascontiguousarray(v, dtype=np.float32) for k, v in base.items()}
    for nm, key in [("ln1_w", "L0_attn_b_ln1_w"), ("ln1_b", "L0_attn_b_ln1_b"),
                    ("ln2_w", "L0_attn_b_ln2_w"), ("ln2_b", "L0_attn_b_ln2_b"),
                    ("att_tmk", "L0_attn_b_att_tmk"),
                    ("att_tmv", "L0_attn_b_att_tmv"),
                    ("att_tmr", "L0_attn_b_att_tmr"),
                    ("time_decay", "L0_attn_b_time_decay"),
                    ("time_first", "L0_attn_b_time_first"),
                    ("ffn_tmk", "L0_attn_b_ffn_tmk"),
                    ("ffn_tmr", "L0_attn_b_ffn_tmr")]:
        base[nm] = slabvec(inputs[key])

    import ml_dtypes
    base["ident_f"] = np.eye(128, dtype=np.float32)
    base["ident_b"] = np.eye(128).astype(ml_dtypes.bfloat16)
    base["ones"] = np.ones((128, 128), np.float32)
    cz = np.zeros((128, 128), np.float32)
    i_idx = np.arange(128)[:, None]
    j_idx = np.arange(128)[None, :]
    cz[j_idx > i_idx] = -1e30 / SCALE_A
    base["causal"] = cz
    oh = np.zeros((NE, NE * 128), np.float32)
    for e in range(NE):
        oh[e, e * 128:(e + 1) * 128] = 1.0
    base["onehot"] = oh
    io = np.tile(np.arange(NE, dtype=np.float32), (128, 1))
    base["iota32"] = np.ascontiguousarray(
        np.concatenate([io, io - 64.0], axis=1))
    base["iotap"] = np.ascontiguousarray(
        np.tile(np.arange(32, dtype=np.float32)[:, None], (1, 128)))
    base["epsv"] = np.full((128, 2), EPS, np.float32)

    in_maps = []
    for cc in range(NCORES):
        m = dict(base)
        m["x"] = np.ascontiguousarray(x[cc])
        in_maps.append(m)
    return in_maps


def kernel(**inputs):
    in_maps = make_inputs(inputs)
    nc = build_program()
    res = run_bass_kernel_spmd(nc, in_maps, core_ids=list(range(NCORES)))
    out = np.stack([res.results[cc]["out"] for cc in range(NCORES)], axis=0)
    return out.astype(np.float32)
